# version 09
# baseline (speedup 1.0000x reference)
import numpy as np
import ml_dtypes
import concourse.bass as bass
import concourse.mybir as mybir
from concourse.bass_utils import run_bass_kernel_spmd

F32 = mybir.dt.float32
BF16 = mybir.dt.bfloat16
AF = mybir.ActivationFunctionType
ALU = mybir.AluOpType

NCORE = 8
B, S, D, H, Dh, C = 2, 16384, 1024, 16, 64, 2
T = 4096
NT = 8
TT = 512
NCH = 4
CH = 128
DC = 8
HC = H * C
E = 8
SHD = 2048
PT = 1024
NHALF = 2
HT = 2048
LN_EPS = 1e-5
EPS = 1e-6
NDST = Dh + 1
MCOLS = HC * NDST

DEBUG = False
PHASES = "AGBC"
CLEVEL = 4
ALSUB = 4
A_LEVEL = 4
NTILES = NT
SHARED_CC = True
STRICT_SELF_WAITS = True
_bf = ml_dtypes.bfloat16
LAST_RESULTS = None


class Walk:

    def __init__(self, eng_name, eng, sems):
        self.eng_name = eng_name
        self.eng = eng
        self.sems = sems
        self.counts = {k: 0 for k in sems}

    def mark(self, sem):
        return self.counts[sem]

    def on(self, eng_name):
        return eng_name == self.eng_name

    def wait(self, eng_name, sem, val=None):
        v = self.counts[sem] if val is None else val
        if self.on(eng_name) and v > 0:
            self.eng.wait_ge(self.sems[sem], v)

    def dma(self, qname, out, in_, sem, **kw):
        self.counts[sem] += 16
        if self.on(qname):
            self.eng.dma_start(out=out, in_=in_, **kw).then_inc(self.sems[sem], 16)

    _SELF = {"vector": "v", "scalar": "s"}

    def op(self, eng_name, fn, sem=None):
        own = self._SELF.get(eng_name) if STRICT_SELF_WAITS else None
        if own is not None and self.on(eng_name) and self.counts[own] > 0:
            self.eng.wait_ge(self.sems[own], self.counts[own])
        if sem is not None:
            self.counts[sem] += 1
        if self.on(eng_name):
            last = fn(self.eng)
            if sem is not None:
                last.then_inc(self.sems[sem], 1)


def build_program():
    nc = bass.Bass("TRN2", target_bir_lowering=False)

    def din(name, shape, dt=F32):
        return nc.dram_tensor(name, shape, dt, kind="ExternalInput").ap()

    x_fm = din("x_fm", [D, T])
    wq = din("wq", [D, D]); wk = din("wk", [D, D]); wv = din("wv", [D, D])
    wbeta = din("wbeta", [D, HC]); wo = din("wo", [D, D])
    bbeta = din("bbeta", [1, HC]); bo = din("bo", [128, DC])
    ln_g = din("ln_g", [128, DC]); ln_b = din("ln_b", [128, DC])
    lin0_w = din("lin0_w", [D, D]); lin0_b = din("lin0_b", [128, DC])
    s1w1 = din("s1w1", [D, D]); s1b1 = din("s1b1", [128, DC])
    s1w2 = din("s1w2", [D, D]); s1b2 = din("s1b2", [128, DC])
    gate_wt = din("gate_wt", [128, DC * E])
    exw1 = din("exw1", [E, D, D]); exw3 = din("exw3", [E, D, D]); exw2 = din("exw2", [E, D, D])
    shw1 = din("shw1", [D, SHD]); shw3 = din("shw3", [D, SHD]); shw2 = din("shw2", [SHD, D])
    lin1_w = din("lin1_w", [D, D]); lin1_b = din("lin1_b", [128, DC])
    s2w1 = din("s2w1", [D, D]); s2b1 = din("s2b1", [128, DC])
    s2w2 = din("s2w2", [D, D]); s2b2 = din("s2b2", [128, DC])
    cs1 = din("cs1", [CH, T], BF16); cs2 = din("cs2", [CH, T], BF16)
    tri_f = din("tri_f", [CH, CH]); tri_b = din("tri_b", [CH, CH], BF16)
    id_b = din("id_b", [CH, CH], BF16); id_f = din("id_f", [CH, CH])
    swap_b = din("swap_b", [CH, CH], BF16)
    ones_f = din("ones_f", [1, CH]); ones_b = din("ones_b", [1, CH], BF16)
    negones_f = din("negones_f", [1, CH])
    onescol_f = din("onescol_f", [CH, 1])
    onescol_b = din("onescol_b", [CH, 1], BF16)
    gmask = din("gmask", [1, NCORE])

    out_fm = nc.dram_tensor("out_fm", [D, T], F32, kind="ExternalOutput").ap()

    skind = "ExternalOutput" if DEBUG else "Internal"

    def scr(name, shape, dt=F32):
        return nc.dram_tensor(name, shape, dt, kind=skind).ap()

    sp_q = scr("sp_q", [D, T], BF16)
    sp_k = scr("sp_k", [D, T], BF16)
    sp_va = scr("sp_va", [T, H * NDST], BF16)
    sp_L = scr("sp_L", [T, 2 * HC])
    sp_Lr = scr("sp_Lr", [T // CH, HC, CH])
    sp_mst = scr("sp_mst", [T // CH, 64, MCOLS], BF16)
    sp_attn = scr("sp_attn", [D, T])
    sp_stats = scr("sp_stats", [2, T])
    wc = nc.dram_tensor("wc", [320, 128, 1024], BF16, kind="Internal").ap()
    UOF = {"wq": 0, "wk": 8, "wv": 16, "wo": 24, "lin0_w": 32, "s1w1": 40,
           "s1w2": 48, "lin1_w": 56, "s2w1": 64, "s2w2": 72,
           "shw1": 80, "shw3": 96, "shw2": 112,
           "exw1": 128, "exw3": 192, "exw2": 256}
    sp_comb = scr("sp_comb", [E, T], BF16)
    cc_in = nc.dram_tensor("cc_in", [1, HC + 64 * MCOLS], F32, kind="Internal").ap()
    dbg_cc = (nc.dram_tensor("dbg_cc", [1, HC + 64 * MCOLS], F32,
                             kind="ExternalOutput").ap() if DEBUG else None)
    cc_out = nc.dram_tensor("cc_out", [NCORE, HC + 64 * MCOLS], F32,
                            addr_space="Shared" if SHARED_CC else "Local",
                            kind="Internal").ap()

    from contextlib import ExitStack
    ctx = ExitStack()

    def mk_sb(name, shape, dt=F32):
        return ctx.enter_context(nc.sbuf_tensor(name, shape, dt))

    with ctx:
        biga = mk_sb("biga", [128, DC * T], BF16)
        bigb = mk_sb("bigb", [128, 19200], BF16)
        bigc = mk_sb("bigc", [128, 8192], F32)
        wring = mk_sb("wring", [128, 4096], BF16)
        m_sb = mk_sb("m_sb", [64, MCOLS], F32)
        sin_sb = mk_sb("sin_sb", [64, MCOLS], F32)
        sinb_sb = mk_sb("sinb_sb", [128, MCOLS], BF16)
        u64a = mk_sb("u64a", [128, 2 * MCOLS], BF16)
        gtmp_sb = u64a[:, :].bitcast(F32)[0:64, 0:MCOLS]
        arow_sb = mk_sb("arow_sb", [1, NCORE * HC], F32)
        gm_sb = mk_sb("gm_sb", [1, NCORE], F32)
        gmb_sb = mk_sb("gmb_sb", [64, NCORE], F32)
        ap_row = mk_sb("ap_row", [1, HC], F32)
        ap_bc = mk_sb("ap_bc", [64, HC], F32)
        cs1_sb = mk_sb("cs1_sb", [128, TT], BF16)
        cs2_sb = mk_sb("cs2_sb", [128, TT], BF16)
        tri_f_sb = mk_sb("tri_f_sb", [128, CH], F32)
        tri_b_sb = mk_sb("tri_b_sb", [128, CH], BF16)
        id_b_sb = mk_sb("id_b_sb", [128, CH], BF16)
        id_f_sb = mk_sb("id_f_sb", [128, CH], F32)
        swap_sb = mk_sb("swap_sb", [128, CH], BF16)
        ones_f_sb = mk_sb("ones_f_sb", [1, CH], F32)
        negones_sb = mk_sb("negones_sb", [1, CH], F32)
        ones_b_sb = mk_sb("ones_b_sb", [1, CH], BF16)
        onescol_f_sb = mk_sb("onescol_f_sb", [128, 1], F32)
        onescol_b_sb = mk_sb("onescol_b_sb", [128, 1], BF16)
        bbr_sb = mk_sb("bbr_sb", [1, HC], F32)
        bb_bc_sb = mk_sb("bb_bc_sb", [128, HC], F32)
        wbeta_sb = mk_sb("wbeta_sb", [128, DC * HC], BF16)
        gatew_sb = mk_sb("gatew_sb", [128, DC * E], BF16)
        vec_names = ["bo", "ln_g", "ln_b", "lin0_b", "s1b1", "s1b2",
                     "lin1_b", "s2b1", "s2b2"]
        vecs = {n: mk_sb("vec_" + n, [128, DC], F32) for n in vec_names}
        carry_sb = mk_sb("carry_sb", [1, HC], F32)
        dsum_sb = mk_sb("dsum_sb", [1, HC], F32)
        adsum_sb = mk_sb("adsum_sb", [1, HC], F32)
        acore_sb = mk_sb("acore_sb", [1, HC], F32)
        lrows_sb = mk_sb("lrows_sb", [1, HC * CH], F32)
        combrow_sb = mk_sb("combrow_sb", [1, PT], BF16)
        combf_sb = mk_sb("combf_sb", [8, TT], BF16)
        eps_sb = mk_sb("eps_sb", [1, 1], F32)
        dmy_sb = mk_sb("dmy_sb", [1, 1], F32)
        stats_sb = mk_sb("stats_sb", [1, 3 * TT], F32)
        lloc_sb = mk_sb("lloc_sb", [128, HC], F32)
        lglob_sb = mk_sb("lglob_sb", [128, HC], F32)
        negl_sb = mk_sb("negl_sb", [128, HC], F32)
        expll_sb = mk_sb("expll_sb", [128, HC], F32)
        explg_sb = mk_sb("explg_sb", [128, HC], F32)
        sc_sb = mk_sb("sc_sb", [128, HC], F32)
        asb_sb = mk_sb("asb_sb", [64, HC], F32)
        scarg_sb = mk_sb("scarg_sb", [128, HC], F32)
        zrow_sb = mk_sb("zrow_sb", [128, HC], F32)
        lb_sb = mk_sb("lb_sb", [128, HC], F32)
        num_sb = mk_sb("num_sb", [128, 2 * NDST], F32)
        rden_sb = mk_sb("rden_sb", [128, 4], F32)
        llocT_sb = mk_sb("llocT_sb", [HC, CH], F32)
        khat_sb = mk_sb("khat_sb", [128, 2 * Dh], BF16)
        mstl_sb = u64a[:, :]

        ps_main0 = ctx.enter_context(nc.psum_tensor("ps_main0", [128, 512], F32))
        ps_main1 = ctx.enter_context(nc.psum_tensor("ps_main1", [128, 512], F32))
        ps_aux = ctx.enter_context(nc.psum_tensor("ps_aux", [128, 512], F32))
        ps_scan0 = ctx.enter_context(nc.psum_tensor("ps_scan0", [128, 512], F32))
        ps_scan1 = ctx.enter_context(nc.psum_tensor("ps_scan1", [128, 512], F32))
        ps_scan2 = ctx.enter_context(nc.psum_tensor("ps_scan2", [128, 512], F32))
        ps_tp = ctx.enter_context(nc.psum_tensor("ps_tp", [128, 1024], BF16))
        ps_bcast = ctx.enter_context(nc.psum_tensor("ps_bcast", [128, 512], F32))

        pMain = [ps_main0[:, :], ps_main1[:, :]]
        pB1 = ps_bcast[:, :]
        pB2 = ps_aux[:, :]
        pRope = ps_aux[:, :]
        pSt = [ps_scan0[:, 0:128], ps_scan0[:, 128:256]]
        pBeta = ps_scan0[:, 256:288]
        pGate = ps_scan0[:, 288:296]
        pTcomb = ps_scan0[:, 352:480]
        pM0 = ps_scan1[:, 0:128]
        pM1 = ps_scan1[:, 128:256]
        pScT = ps_scan1[:, 256:288]
        pAbc = ps_scan1[:, 288:320]
        pStt = [ps_scan1[:, 320:385], ps_scan1[:, 385:450]]
        pMisc = ps_scan1[:, 450:482]
        pLloc = ps_scan2[:, 0:32]
        pLglob = ps_scan2[:, 32:64]
        pNum = [ps_scan2[:, 64:129], ps_scan2[:, 129:194]]
        pIntr = ps_scan2[:, 194:454]
        pTp = [ps_tp[:, 0:128], ps_tp[:, 128:256]]
        pBc = [ps_bcast[:, :], ps_aux[:, :]]

        bq = bigb[:, 0:1024]
        bk = bigb[:, 1024:2048]
        bv = bigb[:, 2048:3072]
        bva = bigb[:, 3072:7232]
        bkt = bigb[:, 7232:11328]
        bsc = bigb[:, 11328:15424]
        bqt = bigb[:, 0:4096]
        bkt2 = bigb[:, 4096:8192]
        bva2 = bigb[:, 8192:12352]
        byt = bigb[:, 12352:13376]
        byf = bigb[:, 13376:17472]
        bsc2 = bigb[:, 17472:19200]
        bf0 = bigb[:, 0:16384]
        bgg = bigb[:, 16384:17408]
        btmp = bigb[:, 17408:18432]
        bhh = bigb[:, 0:8192]
        bshh = bigb[:, 0:8192]
        bm1 = bigb[:, 8192:16384]
        bm2 = bigb[:, 0:4096]

        cx = bigc[:, 0:1024]
        ch_ = bigc[:, 1024:5120]
        ctmp = bigc[:, 5120:7168]
        csc = bigc[:, 7168:8192]
        routed = bigc[:, 0:8192]
        cffn = bigc[:, 0:4096]
        cxa = bigc[:, 4096:5120]
        ctmp2 = bigc[:, 5120:7168]

        sems = {}
        for n in ["ldw0", "ldw1", "ldx0", "ldx1", "ldcs", "ldm", "ldmg",
                  "spq0", "spq1", "spv", "spl", "spm", "spa", "rlq",
                  "rlm0", "rlm1", "rlm2",
                  "cc", "ccd", "bounce", "outd", "pcast", "t", "v", "s", "g"]:
            sems[n] = ctx.enter_context(nc.semaphore(n))

        block = ctx.enter_context(nc.Block())

        TE, VE, SC, SY, GP = "tensor", "vector", "scalar", "sync", "gpsimd"

        def guard(w):
            w.wait(TE, "v")
            w.wait(TE, "s")

        class Ring:

            def __init__(self, w, depth=2):
                self.w = w
                self.depth = depth
                self.hist = []

            def wait_te(self):
                if len(self.hist) >= self.depth:
                    vm, sm, _ = self.hist[-self.depth]
                    self.w.wait(TE, "v", vm)
                    self.w.wait(TE, "s", sm)
                else:
                    guard(self.w)

            def wait_gp(self):
                if len(self.hist) >= self.depth:
                    _, _, tm = self.hist[-self.depth]
                    self.w.wait(GP, "t", tm)
                else:
                    self.w.wait(GP, "t")

            def wait_sy(self):
                if len(self.hist) >= self.depth:
                    _, _, tm = self.hist[-self.depth]
                    self.w.wait(SY, "t", tm)
                else:
                    self.w.wait(SY, "t")

            def push(self):
                self.hist.append((self.w.mark("v"), self.w.mark("s"), self.w.mark("t")))

        def body(w: Walk):
            if PHASES == "MICRO":
                for c in range(DC):
                    w.dma(SY, out_fm[c * 128:(c + 1) * 128, :],
                          x_fm[c * 128:(c + 1) * 128, :], "outd")
                w.wait(SY, "outd")
                return
            w.dma(SY, tri_f_sb[:, :], tri_f, "ldm")
            w.dma(SY, tri_b_sb[:, :], tri_b, "ldm")
            w.dma(SY, id_b_sb[:, :], id_b, "ldm")
            w.dma(SY, id_f_sb[:, :], id_f, "ldm")
            w.dma(SY, swap_sb[:, :], swap_b, "ldm")
            w.dma(SY, ones_f_sb[:, :], ones_f, "ldm")
            w.dma(SY, negones_sb[:, :], negones_f, "ldm")
            w.dma(SY, ones_b_sb[:, :], ones_b, "ldm")
            w.dma(SY, onescol_f_sb[:, :], onescol_f, "ldm")
            w.dma(SY, onescol_b_sb[:, :], onescol_b, "ldm")
            w.dma(SY, gm_sb[:, :], gmask, "ldm")
            w.dma(SY, bbr_sb[:, :], bbeta, "ldm")
            bias_srcs = {"bo": bo, "ln_g": ln_g, "ln_b": ln_b, "lin0_b": lin0_b,
                         "s1b1": s1b1, "s1b2": s1b2, "lin1_b": lin1_b,
                         "s2b1": s2b1, "s2b2": s2b2}
            for n in vec_names:
                w.dma(SY, vecs[n][:, :], bias_srcs[n], "ldm")
            w.dma(GP, wbeta_sb[:, :].rearrange("p (c n) -> p c n", c=DC),
                  wbeta.rearrange("(c p) n -> p c n", p=128), "ldmg")
            w.dma(GP, gatew_sb[:, :], gate_wt, "ldmg")
            def precast_dxd(name, wt, nm):
                for m in range(nm):
                    w.dma(GP, wc[UOF[name] + m].rearrange("p (kc n) -> p kc n", kc=DC),
                          wt.rearrange("(kc p) n -> p kc n", p=128)[:, :, m * 128:(m + 1) * 128],
                          "pcast")
            precast_dxd("wq", wq, 8)
            precast_dxd("wk", wk, 8)
            precast_dxd("wv", wv, 8)
            precast_dxd("wo", wo, 8)
            w.wait(GP, "pcast")
            w.op(GP, lambda e: e.memset(dmy_sb[:, :], 0.0), sem="g")
            g1mark = w.mark("g")
            precast_dxd("lin0_w", lin0_w, 8)
            precast_dxd("s1w1", s1w1, 8)
            precast_dxd("s1w2", s1w2, 8)
            precast_dxd("lin1_w", lin1_w, 8)
            precast_dxd("s2w1", s2w1, 8)
            precast_dxd("s2w2", s2w2, 8)
            precast_dxd("shw1", shw1, 16)
            precast_dxd("shw3", shw3, 16)
            for m in range(DC):
                for u in range(2):
                    w.dma(GP, wc[UOF["shw2"] + 2 * m + u].rearrange("p (kc n) -> p kc n", kc=DC),
                          shw2.rearrange("(u kc p) n -> u p kc n", u=2, p=128)[u][:, :, m * 128:(m + 1) * 128],
                          "pcast")
            for ee in range(E):
                for nm, wt in (("exw1", exw1), ("exw3", exw3), ("exw2", exw2)):
                    for m in range(DC):
                        w.dma(GP, wc[UOF[nm] + ee * 8 + m].rearrange("p (kc n) -> p kc n", kc=DC),
                              wt[ee].rearrange("(kc p) n -> p kc n", p=128)[:, :, m * 128:(m + 1) * 128],
                              "pcast")
            w.wait(GP, "pcast")
            w.op(GP, lambda e: e.memset(dmy_sb[:, :], 0.0), sem="g")
            g2mark = w.mark("g")
            w.wait(SY, "g", g1mark)
            w.wait(TE, "ldm")
            w.wait(TE, "ldmg")
            w.op(TE, lambda e: e.matmul(pScT, ones_f_sb[0:1, :], bbr_sb[0:1, :],
                                        start=True, stop=True), sem="t")
            w.wait(VE, "t")
            w.wait(VE, "ldm")
            w.op(VE, lambda e: e.tensor_copy(bb_bc_sb[:, :], pScT), sem="v")
            if CLEVEL >= 3:
                w.op(VE, lambda e: e.memset(m_sb[:, :], 0.0), sem="v")
                w.op(VE, lambda e: e.memset(sin_sb[:, :], 0.0), sem="v")
                w.op(VE, lambda e: e.memset(carry_sb[:, :], 0.0), sem="v")
                w.op(VE, lambda e: e.memset(eps_sb[:, :], LN_EPS), sem="v")
            if CLEVEL >= 4:
                w.wait(TE, "v")
                w.op(TE, lambda e: e.matmul(pAbc[0:64, 0:NCORE], ones_f_sb[0:1, 0:64],
                                            gm_sb[0:1, :], start=True, stop=True), sem="t")
                w.wait(VE, "t")
                w.op(VE, lambda e: e.tensor_copy(gmb_sb[:, :], pAbc[0:64, 0:NCORE]), sem="v")

            if "A" in PHASES and A_LEVEL >= 1:
                for j in range(NTILES):
                    phase_a_tile(w, j)
            if "G" in PHASES:
                do_gather(w)
            if "B" in PHASES:
                for sname in ["spq0", "spq1", "spv", "spl", "spm"]:
                    w.wait(SY, sname)
                w.wait(TE, "ccd")
                w.wait(VE, "ccd")
                for j in range(NTILES):
                    phase_b_tile(w, j)
            if "C" in PHASES:
                w.wait(SY, "g", g2mark)
                w.wait(SY, "spa")
                w.wait(VE, "spa")
                w.wait(SC, "spa")
                w.wait(TE, "spa")
                for hf in range(NHALF if NTILES >= 8 else 1):
                    phase_c_half(w, hf)
            for sname in ["outd", "spq0", "spq1", "spv", "spl", "spa",
                          "ldx0", "ldx1", "rlq", "rlm0", "rlm1", "rlm2",
                          "ldcs", "ldm"]:
                w.wait(SY, sname)
            for sname in ["ldmg", "spm", "ccd", "bounce", "ldw0", "ldw1"]:
                w.wait(GP, sname)

        def do_gather(w: Walk):
            w.wait(SC, "v")
            w.op(SC, lambda e: e.activation(acore_sb[:, :], carry_sb[:, :], AF.Exp), sem="s")
            w.wait(GP, "s")
            w.wait(GP, "v")
            w.dma(GP, cc_in[0:1, 0:HC], acore_sb[:, :], "ccd")
            w.dma(GP, cc_in[0, HC:].rearrange("(p f) -> p f", p=64), m_sb[:, :], "ccd")
            w.wait(GP, "ccd")
            if DEBUG:
                w.dma(GP, dbg_cc, cc_in[:, :], "ccd")
                w.wait(GP, "ccd")

            w.op(GP, lambda e: e.collective_compute(
                "AllGather", ALU.bypass, replica_groups=[list(range(NCORE))],
                ins=[cc_in[:, :]], outs=[cc_out[:, :]]), sem="cc")
            w.wait(GP, "cc")
            w.dma(GP, arow_sb[:, :].rearrange("o (r a) -> o r a", r=NCORE),
                  cc_out[None, :, 0:HC], "ccd")
            for i in range(NCORE):
                w.wait(GP, "v")
                w.dma(GP, gtmp_sb[:, :],
                      cc_out[i, HC:].rearrange("(p f) -> p f", p=64), "ccd")
                w.wait(VE, "ccd")
                w.op(VE, lambda e, i=i: e.tensor_single_scalar(
                    ap_row[:, :], arow_sb[0:1, i * HC:(i + 1) * HC], -1.0, ALU.add), sem="v")
                w.op(VE, lambda e, i=i: e.tensor_single_scalar(
                    ap_row[:, :], ap_row[:, :], gm_sb[0:1, i:i + 1], ALU.mult), sem="v")
                w.op(VE, lambda e: e.tensor_single_scalar(
                    ap_row[:, :], ap_row[:, :], 1.0, ALU.add), sem="v")
                w.wait(TE, "v")
                w.op(TE, lambda e: e.matmul(pAbc[0:64, 0:HC], ones_f_sb[0:1, 0:64],
                                            ap_row[0:1, :], start=True, stop=True), sem="t")
                w.wait(VE, "t")
                w.op(VE, lambda e: e.tensor_copy(ap_bc[:, :], pAbc[0:64, 0:HC]), sem="v")
                w.op(VE, lambda e, i=i: e.tensor_single_scalar(
                    gtmp_sb[:, :], gtmp_sb[:, :], gmb_sb[:, i:i + 1], ALU.mult), sem="v")
                for hcn in range(HC):
                    w.op(VE, lambda e, hcn=hcn: e.scalar_tensor_tensor(
                        sin_sb[:, hcn * NDST:(hcn + 1) * NDST],
                        sin_sb[:, hcn * NDST:(hcn + 1) * NDST],
                        ap_bc[:, hcn:hcn + 1],
                        gtmp_sb[:, hcn * NDST:(hcn + 1) * NDST],
                        ALU.mult, ALU.add), sem="v")
            w.op(VE, lambda e: e.tensor_copy(sinb_sb[0:64, :], sin_sb[:, :]), sem="v")
            w.wait(GP, "v")
            w.dma(GP, sinb_sb[64:128, :], sinb_sb[0:64, :], "ccd")

        def ln_sweep(w, read_chunk, nst, write_out):
            raise NotImplementedError

        def phase_a_tile(w: Walk, j):
            t0 = j * TT
            w.wait(VE, "t")
            w.wait(VE, "spq0")
            w.wait(VE, "spq1")
            w.wait(VE, "spv")
            w.wait(VE, "spl")
            w.wait(VE, "spm")
            w.wait(SY, "v")
            w.wait(SY, "s")
            if A_LEVEL >= 2 and not (A_LEVEL == 2 and ALSUB == 1):
                w.dma(SY, cs1_sb[:, :], cs1[:, t0:t0 + TT], "ldcs")
                w.dma(SY, cs2_sb[:, :], cs2[:, t0:t0 + TT], "ldcs")

            guard(w)
            cxa = bigc[:, 0:4096]
            for c in range(DC):
                w.dma(SY, cxa[:, c * TT:(c + 1) * TT],
                      x_fm[c * 128:(c + 1) * 128, t0:t0 + TT], "ldx0")
            w.wait(SC, "ldx0")
            for c in range(DC):
                xc = cxa[:, c * TT:(c + 1) * TT]
                xb = bsc[:, (c % 2) * TT:(c % 2) * TT + TT]
                x2b = bsc[:, 1024 + (c % 2) * TT:1024 + (c % 2) * TT + TT]
                if c >= 2:
                    w.wait(SC, "t")
                w.op(SC, lambda e, xc=xc, xb=xb: e.copy(xb, xc), sem="s")
                w.op(SC, lambda e, xc=xc, x2b=x2b: e.activation(x2b, xc, AF.Square), sem="s")
                w.wait(TE, "s")
                w.op(TE, lambda e, xb=xb, c=c: e.matmul(
                    pB1[0:1, 0:TT], onescol_b_sb[:, :], xb,
                    start=(c == 0), stop=(c == DC - 1)), sem=None)
                w.op(TE, lambda e, x2b=x2b, c=c: e.matmul(
                    pB2[0:1, 0:TT], onescol_b_sb[:, :], x2b,
                    start=(c == 0), stop=(c == DC - 1)), sem="t")
            if A_LEVEL == 1 and ALSUB < 2:
                return
            mu = stats_sb[0:1, 0:TT]
            ex2 = stats_sb[0:1, TT:2 * TT]
            musq = stats_sb[0:1, 2 * TT:3 * TT]
            var = stats_sb[0:1, TT:2 * TT]
            sd = stats_sb[0:1, 2 * TT:3 * TT]
            rstd = stats_sb[0:1, TT:2 * TT]
            murstd = stats_sb[0:1, 2 * TT:3 * TT]
            w.wait(SC, "t")
            w.op(SC, lambda e: e.activation(mu, pB1[0:1, 0:TT], AF.Copy, scale=1.0 / D), sem="s")
            w.op(SC, lambda e: e.activation(ex2, pB2[0:1, 0:TT], AF.Copy, scale=1.0 / D), sem="s")
            w.wait(VE, "s")
            w.op(VE, lambda e: e.tensor_mul(musq, mu, mu), sem="v")
            w.op(VE, lambda e: e.tensor_sub(var, ex2, musq), sem="v")
            w.wait(SC, "v")
            w.op(SC, lambda e: e.activation(sd, var, AF.Sqrt, bias=eps_sb[0:1, 0:1]), sem="s")
            w.wait(VE, "s")
            w.op(VE, lambda e: e.reciprocal(rstd, sd), sem="v")
            w.op(VE, lambda e: e.tensor_mul(murstd, mu, rstd), sem="v")
            if A_LEVEL == 1 and ALSUB < 3:
                return
            w.wait(SY, "v")
            w.dma(SY, sp_stats[0:1, t0:t0 + TT], rstd, "spl")
            w.dma(SY, sp_stats[1:2, t0:t0 + TT], murstd, "spl")
            w.wait(TE, "v")
            w.op(TE, lambda e: e.matmul(pB1[:, 0:TT], ones_f_sb[0:1, :], rstd,
                                        start=True, stop=True), sem=None)
            w.op(TE, lambda e: e.matmul(pB2[:, 0:TT], ones_f_sb[0:1, :], murstd,
                                        start=True, stop=True), sem="t")
            mark_bc = w.mark("t")
            if A_LEVEL == 1 and ALSUB < 4:
                return
            w.wait(VE, "t", mark_bc)
            for c in range(DC):
                xc = cxa[:, c * TT:(c + 1) * TT]
                xnc = biga[:, c * T + t0:c * T + t0 + TT]
                tmp = ctmp[:, (c % 2) * TT:(c % 2) * TT + TT]
                w.op(VE, lambda e, xc=xc, tmp=tmp: e.tensor_mul(tmp, xc, pB1[:, 0:TT]), sem="v")
                w.op(VE, lambda e, tmp=tmp, xnc=xnc: e.tensor_sub(xnc, tmp, pB2[:, 0:TT]), sem="v")
            mark_xn = w.mark("v")
            if A_LEVEL < 2:
                return

            qkv_list = [(None, sp_q), (None, sp_k), (None, None)]
            marks_c = [None] * 24
            vmark_t12 = [0] * 24
            vmark_mn = [0]
            gmarks = [0] * 24

            def a_stage1(i):
                wi, m = divmod(i, DC)
                uof = [UOF["wq"], UOF["wk"], UOF["wv"]][wi]
                ws = wring[:, (i % 2) * 1024:(i % 2) * 1024 + 1024]
                if i >= 2:
                    w.wait(SY, "t", gmarks[i - 2])
                else:
                    w.wait(SY, "t")
                cache_load(w, uof + m, 1, ws, "ldw%d" % (i % 2))
                pm = pMain[i % 2]
                w.wait(TE, "ldw%d" % (i % 2))
                w.wait(TE, "v", mark_xn)
                if i >= 2 and marks_c[i - 2] is not None:
                    sem_n, val = marks_c[i - 2]
                    w.wait(TE, sem_n, val)
                else:
                    guard(w)
                for kc in range(DC):
                    w.op(TE, lambda e, ws=ws, kc=kc, pm=pm: e.matmul(
                        pm[:, 0:TT], ws[:, kc * 128:(kc + 1) * 128],
                        biga[:, kc * T + t0:kc * T + t0 + TT],
                        start=(kc == 0), stop=(kc == DC - 1)),
                        sem="t" if kc == DC - 1 else None)
                tmark = w.mark("t")
                gmarks[i] = tmark
                if wi < 2:
                    t1 = bsc[:, 2048 + (i % 2) * 2 * TT:2048 + (i % 2) * 2 * TT + TT]
                    t2 = bsc[:, 2048 + (i % 2) * 2 * TT + TT:2048 + (i % 2) * 2 * TT + 2 * TT]
                    w.wait(VE, "t", tmark)
                    w.wait(VE, "ldcs")
                    w.op(VE, lambda e, pm=pm, t1=t1: e.tensor_mul(t1, pm[:, 0:TT], cs1_sb[:, :]), sem="v")
                    w.op(VE, lambda e, pm=pm, t2=t2: e.tensor_mul(t2, pm[:, 0:TT], cs2_sb[:, :]), sem="v")
                    vmark_t12[i] = w.mark("v")
                    marks_c[i] = ("v", vmark_t12[i])
                else:
                    vb = bv[:, (i % 2) * TT:(i % 2) * TT + TT]
                    w.wait(SC, "t", tmark)
                    w.op(SC, lambda e, pm=pm, vb=vb: e.copy(vb, pm[:, 0:TT]), sem="s")
                    marks_c[i] = ("s", w.mark("s"))

            def a_stage2(i):
                wi, m = divmod(i, DC)
                _, dst_spill = qkv_list[wi]
                if wi < 2:
                    t1 = bsc[:, 2048 + (i % 2) * 2 * TT:2048 + (i % 2) * 2 * TT + TT]
                    t2 = bsc[:, 2048 + (i % 2) * 2 * TT + TT:2048 + (i % 2) * 2 * TT + 2 * TT]
                    w.wait(TE, "v", vmark_t12[i])
                    w.wait(TE, "v", vmark_mn[0])
                    w.op(TE, lambda e, t1=t1: e.matmul(
                        pRope[:, 0:TT], id_b_sb[:, :], t1, start=True, stop=False), sem=None)
                    w.op(TE, lambda e, t2=t2: e.matmul(
                        pRope[:, 0:TT], swap_sb[:, :], t2, start=False, stop=True), sem="t")
                    rmark = w.mark("t")
                    mn = bsc[:, 0:TT]
                    ex = bsc[:, TT:2 * TT]
                    qk_out = (bq if wi == 0 else bk)[:, (m % 2) * TT:(m % 2) * TT + TT]
                    w.wait(VE, "t", rmark)
                    w.op(VE, lambda e, mn=mn: e.tensor_single_scalar(
                        mn, pRope[:, 0:TT], 0.0, ALU.min), sem="v")
                    w.wait(SC, "v")
                    w.op(SC, lambda e, mn=mn, ex=ex: e.activation(ex, mn, AF.Exp), sem="s")
                    smark = w.mark("s")
                    w.wait(VE, "s", smark)
                    w.wait(VE, "spq%d" % (m % 2))
                    w.op(VE, lambda e, ex=ex, qk_out=qk_out: e.scalar_tensor_tensor(
                        qk_out, pRope[:, 0:TT], 0.0, ex, ALU.max, ALU.add), sem="v")
                    vmark_mn[0] = w.mark("v")
                    w.wait(SY, "v")
                    w.dma(SY, dst_spill[m * 128:(m + 1) * 128, t0:t0 + TT], qk_out,
                          "spq%d" % (m % 2))
                    if wi == 1:
                        for tc in range(NCH):
                            w.wait(TE, "v")
                            w.op(TE, lambda e, qk_out=qk_out, tc=tc: e.transpose(
                                pTp[tc % 2], qk_out[:, tc * 128:(tc + 1) * 128],
                                id_b_sb[:, :]), sem="t")
                            w.wait(VE, "t")
                            w.op(VE, lambda e, tc=tc, m=m: e.tensor_copy(
                                bkt[:, tc * 1024 + m * 128:tc * 1024 + (m + 1) * 128],
                                pTp[tc % 2]), sem="v")
                else:
                    vb = bv[:, (i % 2) * TT:(i % 2) * TT + TT]
                    for tc in range(NCH):
                        w.wait(TE, "s")
                        w.wait(TE, "v")
                        w.op(TE, lambda e, vb=vb, tc=tc: e.transpose(
                            pTp[tc % 2], vb[:, tc * 128:(tc + 1) * 128],
                            id_b_sb[:, :]), sem="t")
                        w.wait(VE, "t")
                        w.op(VE, lambda e, tc=tc, m=m: e.tensor_copy(
                            bva[:, tc * 1040 + 2 * m * NDST:tc * 1040 + (2 * m + 2) * NDST]
                            .rearrange("p (hh dd) -> p hh dd", hh=2)[:, :, 0:Dh],
                            pTp[tc % 2].rearrange("p (hh dd) -> p hh dd", hh=2)), sem="v")

            for step in range(25):
                if step < 24:
                    a_stage1(step)
                if step >= 1:
                    a_stage2(step - 1)
            w.op(VE, lambda e: e.memset(
                bva.rearrange("p (cc hh dd) -> p cc hh dd", cc=NCH, hh=H)[:, :, :, Dh:],
                1.0), sem="v")
            w.wait(SY, "v")
            w.dma(SY, sp_va[t0:t0 + TT, :].rearrange("(cc p) f -> p cc f", p=128),
                  bva.rearrange("p (cc f) -> p cc f", cc=NCH), "spv")
            if A_LEVEL < 3:
                return

            for tc in range(NCH):
                g = j * NCH + tc
                tc0 = t0 + tc * CH
                guard(w)
                for kc in range(DC):
                    w.op(TE, lambda e, kc=kc, tc0=tc0: e.matmul(
                        pBeta, biga[:, kc * T + tc0:kc * T + tc0 + CH],
                        wbeta_sb[:, kc * HC:(kc + 1) * HC],
                        start=(kc == 0), stop=(kc == DC - 1)), sem="t" if kc == DC - 1 else None)
                w.wait(VE, "t")
                w.op(VE, lambda e: e.tensor_add(zrow_sb[:, :], pBeta, bb_bc_sb[:, :]), sem="v")
                w.wait(SC, "v")
                w.op(SC, lambda e: e.activation(zrow_sb[:, :], zrow_sb[:, :], AF.Sigmoid), sem="s")
                w.wait(VE, "s")
                w.op(VE, lambda e: e.tensor_scalar(
                    zrow_sb[:, :], zrow_sb[:, :], 0.001, 0.999, ALU.max, ALU.min), sem="v")
                w.wait(SC, "v")
                w.op(SC, lambda e: e.activation(lb_sb[:, :], zrow_sb[:, :], AF.Ln), sem="s")
                w.wait(TE, "s")
                guard(w)
                w.op(TE, lambda e: e.matmul(pLloc, tri_f_sb[:, :], lb_sb[:, :],
                                            start=True, stop=True), sem=None)
                w.op(TE, lambda e: e.matmul(pLglob, tri_f_sb[:, :], lb_sb[:, :],
                                            start=True, stop=False), sem=None)
                w.op(TE, lambda e: e.matmul(pLglob, ones_f_sb[0:1, :], carry_sb[0:1, :],
                                            start=False, stop=True), sem=None)
                w.op(TE, lambda e: e.matmul(pMisc[0:1, :], onescol_f_sb[:, :], lb_sb[:, :],
                                            start=True, stop=True), sem="t")
                w.wait(VE, "t")
                w.wait(VE, "spl")
                w.op(VE, lambda e: e.tensor_copy(lloc_sb[:, :], pLloc), sem="v")
                w.op(VE, lambda e: e.tensor_copy(lglob_sb[:, :], pLglob), sem="v")
                w.op(VE, lambda e: e.tensor_copy(dsum_sb[:, :], pMisc[0:1, :]), sem="v")
                w.wait(SY, "v")
                w.dma(SY, sp_L[tc0:tc0 + CH, 0:HC], lloc_sb[:, :], "spl")
                w.dma(SY, sp_L[tc0:tc0 + CH, HC:2 * HC], lglob_sb[:, :], "spl")
                w.wait(TE, "v")
                w.op(TE, lambda e: e.transpose(pTcomb[0:HC, 0:CH], lloc_sb[:, :],
                                               id_f_sb[:, :]), sem="t")
                w.wait(VE, "t")
                w.op(VE, lambda e: e.tensor_copy(llocT_sb[:, :], pTcomb[0:HC, 0:CH]), sem="v")
                w.wait(SY, "v")
                w.dma(SY, sp_Lr[g, :, :], llocT_sb[:, :], "spl")
                w.wait(TE, "v")
                w.op(TE, lambda e: e.matmul(pScT, ones_f_sb[0:1, :], dsum_sb[0:1, :],
                                            start=True, stop=True), sem="t")
                w.wait(VE, "t")
                w.op(VE, lambda e: e.tensor_sub(scarg_sb[:, :], pScT, lloc_sb[:, :]), sem="v")
                w.wait(SC, "v")
                w.op(SC, lambda e: e.activation(sc_sb[:, :], scarg_sb[:, :], AF.Exp), sem="s")
                w.op(SC, lambda e: e.activation(adsum_sb[:, :], dsum_sb[:, :], AF.Exp), sem="s")
                w.wait(TE, "s")
                w.op(TE, lambda e: e.matmul(pAbc[0:64, 0:HC], ones_f_sb[0:1, 0:64],
                                            adsum_sb[0:1, :], start=True, stop=True), sem="t")
                w.wait(VE, "t")
                w.op(VE, lambda e: e.tensor_copy(asb_sb[:, :], pAbc[0:64, 0:HC]), sem="v")
                w.op(VE, lambda e: e.tensor_add(carry_sb[:, :], carry_sb[:, :], dsum_sb[:, :]), sem="v")
                if A_LEVEL < 4:
                    continue
                w.wait(GP, "v")
                w.dma(GP, sp_mst[g, :, :], m_sb[:, :], "spm")
                khat_all = bigb[:, 15424:17472]
                w.wait(VE, "s")
                w.op(VE, lambda e, tc=tc: e.tensor_tensor(
                    khat_all[:, :].rearrange("p (h c d) -> p h c d", h=H, c=C),
                    bkt[:, tc * 1024:(tc + 1) * 1024]
                    .rearrange("p (h d) -> p h d", h=H)[:, :, None, :]
                    .broadcast_to([128, H, C, Dh]),
                    sc_sb[:, :].rearrange("p (h c) -> p h c", h=H)[:, :, :, None]
                    .broadcast_to([128, H, C, Dh]),
                    ALU.mult), sem="v")
                dbanks = [pMain[0], pMain[1], ps_scan2[:, :], ps_bcast[:, :], ps_scan1[:, :]]
                w.wait(TE, "v")
                w.wait(TE, "s")
                for hcn in range(HC):
                    h = hcn // 2
                    dv = dbanks[hcn // 7][0:64, (hcn % 7) * 65:(hcn % 7) * 65 + 65]
                    w.op(TE, lambda e, hcn=hcn, h=h, tc=tc, dv=dv: e.matmul(
                        dv, khat_all[:, hcn * Dh:(hcn + 1) * Dh],
                        bva[:, tc * 1040 + h * NDST:tc * 1040 + (h + 1) * NDST],
                        start=True, stop=True), sem="t" if hcn == HC - 1 else None)
                w.wait(VE, "t")
                w.wait(VE, "spm")
                for hcn in range(HC):
                    dv = dbanks[hcn // 7][0:64, (hcn % 7) * 65:(hcn % 7) * 65 + 65]
                    w.op(VE, lambda e, hcn=hcn, dv=dv: e.scalar_tensor_tensor(
                        m_sb[:, hcn * NDST:(hcn + 1) * NDST],
                        m_sb[:, hcn * NDST:(hcn + 1) * NDST],
                        asb_sb[:, hcn:hcn + 1],
                        dv, ALU.mult, ALU.add), sem="v")

        def phase_b_tile(w: Walk, j):
            t0 = j * TT
            w.wait(SY, "t")
            w.wait(SY, "v")
            w.wait(SC, "spa")
            w.wait(VE, "spa")
            w.dma(SY, bqt.rearrange("p (cc tt) -> p cc tt", cc=DC),
                  sp_q.rearrange("(cc p) tt -> p cc tt", p=128)[:, :, t0:t0 + TT], "rlq")
            w.dma(SY, bkt2.rearrange("p (cc tt) -> p cc tt", cc=DC),
                  sp_k.rearrange("(cc p) tt -> p cc tt", p=128)[:, :, t0:t0 + TT], "rlq")
            w.dma(SY, bva2.rearrange("p (cc f) -> p cc f", cc=NCH),
                  sp_va[t0:t0 + TT, :].rearrange("(cc p) f -> p cc f", p=128), "rlq")
            for tc in range(NCH):
                g = j * NCH + tc
                tc0 = t0 + tc * CH
                msl = mstl_sb[:, (tc % 2) * MCOLS:(tc % 2) * MCOLS + MCOLS]
                w.wait(SY, "t")
                w.wait(SY, "s")
                w.wait(SY, "v")
                w.dma(SY, msl[0:64, :], sp_mst[g, :, :], "rlm%d" % (tc % 2))
                w.dma(SY, msl[64:128, :], sp_mst[g, :, :], "rlm%d" % (tc % 2))
                w.dma(SY, lloc_sb[:, :], sp_L[tc0:tc0 + CH, 0:HC], "rlm2")
                w.dma(SY, lglob_sb[:, :], sp_L[tc0:tc0 + CH, HC:2 * HC], "rlm2")
                w.dma(SY, lrows_sb[:, :],
                      sp_Lr[g].rearrange("hh tt -> (hh tt)")[None, :], "rlm2")
                w.wait(VE, "rlm2")
                w.op(VE, lambda e: e.tensor_single_scalar(
                    negl_sb[:, :], lloc_sb[:, :], -1.0, ALU.mult), sem="v")
                w.wait(SC, "rlm2")
                w.op(SC, lambda e: e.activation(expll_sb[:, :], lloc_sb[:, :], AF.Exp), sem="s")
                w.op(SC, lambda e: e.activation(explg_sb[:, :], lglob_sb[:, :], AF.Exp), sem="s")
                w.wait(TE, "rlq")
                w.wait(TE, "rlm%d" % (tc % 2))
                w.wait(TE, "rlm2")
                w.wait(TE, "s")
                w.wait(TE, "v")
                bSt = [ps_scan0[:, 0:128], ps_scan2[:, 0:128]]
                bM0 = [ps_scan0[:, 128:256], ps_scan2[:, 128:256]]
                bM1 = [ps_scan0[:, 256:384], ps_scan2[:, 256:384]]
                bNum = [ps_scan1[:, 264:329], ps_main0[:, 264:329]]
                bInt = [ps_scan1[:, 0:260], ps_main0[:, 0:260]]
                vmark_p = [0] * H
                smark2 = [0] * H
                vmark2 = [0] * H

                def b_stage1(h):
                    hc0, hc1 = 2 * h, 2 * h + 1
                    cb = h // 2
                    pb = (h % 2) * 64
                    par = h % 2
                    qf = bqt[pb:pb + 64, cb * TT + tc * CH:cb * TT + tc * CH + CH]
                    kf = bkt2[pb:pb + 64, cb * TT + tc * CH:cb * TT + tc * CH + CH]
                    if h >= 2:
                        w.wait(TE, "v", vmark_p[h - 2])
                    else:
                        w.wait(TE, "v")
                        w.wait(TE, "s")
                    w.op(TE, lambda e: e.matmul(bSt[par], kf, qf, start=True, stop=True), sem=None)
                    for hci, (hcx, bM) in enumerate(((hc0, bM0[par]), (hc1, bM1[par]))):
                        w.op(TE, lambda e, hcx=hcx, bM=bM: e.matmul(
                            bM, ones_f_sb[0:1, :], lrows_sb[0:1, hcx * CH:(hcx + 1) * CH],
                            start=True, stop=False), sem=None)
                        w.op(TE, lambda e, hcx=hcx, bM=bM: e.matmul(
                            bM, lrows_sb[0:1, hcx * CH:(hcx + 1) * CH], negones_sb[0:1, :],
                            start=False, stop=True), sem="t" if hci == 1 else None)
                    tmark_st = w.mark("t")
                    stm = bsc2[:, par * 640:par * 640 + 128]
                    mk01 = ctmp[:, par * 512:par * 512 + 256]
                    a01 = ctmp[:, 1024 + par * 256:1024 + par * 256 + 256]
                    p01 = bsc2[:, par * 640 + 384:par * 640 + 640]
                    pM01 = [ps_scan0[:, 128:384], ps_scan2[:, 128:384]][par]
                    w.wait(VE, "t", tmark_st)
                    w.op(VE, lambda e: e.tensor_mul(stm, bSt[par], tri_b_sb[:, :]), sem="v")
                    w.op(VE, lambda e: e.tensor_single_scalar(
                        a01, pM01, 0.0, ALU.min), sem="v")
                    w.wait(SC, "v")
                    w.op(SC, lambda e: e.activation(mk01, a01, AF.Exp), sem="s")
                    w.wait(VE, "s")
                    w.op(VE, lambda e: e.tensor_tensor(
                        p01.rearrange("p (c f) -> p c f", c=2),
                        stm[:, None, :].broadcast_to([128, 2, 128]),
                        mk01.rearrange("p (c f) -> p c f", c=2),
                        ALU.mult), sem="v")
                    vmark_p[h] = w.mark("v")

                def b_stage2(h):
                    hc0, hc1 = 2 * h, 2 * h + 1
                    cb = h // 2
                    pb = (h % 2) * 64
                    par = h % 2
                    qf = bqt[pb:pb + 64, cb * TT + tc * CH:cb * TT + tc * CH + CH]
                    vaug_h = bva2[:, tc * 1040 + h * NDST:tc * 1040 + (h + 1) * NDST]
                    p0 = bsc2[:, par * 640 + 384:par * 640 + 512]
                    p1 = bsc2[:, par * 640 + 512:par * 640 + 640]
                    w.wait(TE, "v", vmark_p[h])
                    if h >= 2:
                        w.wait(TE, "s", smark2[h - 2])
                        w.wait(TE, "v", vmark2[h - 2])
                    w.op(TE, lambda e: e.matmul(bNum[par], p0, vaug_h, start=True, stop=False), sem=None)
                    w.op(TE, lambda e: e.matmul(bNum[par], p1, vaug_h, start=False, stop=True), sem=None)
                    w.op(TE, lambda e: e.matmul(
                        bInt[par][:, 0:65], qf, msl[pb:pb + 64, hc0 * NDST:(hc0 + 1) * NDST],
                        start=True, stop=True), sem=None)
                    w.op(TE, lambda e: e.matmul(
                        bInt[par][:, 65:130], qf, msl[pb:pb + 64, hc1 * NDST:(hc1 + 1) * NDST],
                        start=True, stop=True), sem=None)
                    w.op(TE, lambda e: e.matmul(
                        bInt[par][:, 130:195], qf, sinb_sb[pb:pb + 64, hc0 * NDST:(hc0 + 1) * NDST],
                        start=True, stop=True), sem=None)
                    w.op(TE, lambda e: e.matmul(
                        bInt[par][:, 195:260], qf, sinb_sb[pb:pb + 64, hc1 * NDST:(hc1 + 1) * NDST],
                        start=True, stop=True), sem="t")
                    tmark_num = w.mark("t")
                    nsb = num_sb[:, par * NDST:(par + 1) * NDST]
                    w.wait(SC, "t", tmark_num)
                    w.op(SC, lambda e: e.copy(nsb, bNum[par]), sem="s")
                    smark2[h] = w.mark("s")
                    w.wait(VE, "s", smark2[h])
                    for (off, esb, hcx) in [(0, expll_sb, hc0), (65, expll_sb, hc1),
                                            (130, explg_sb, hc0), (195, explg_sb, hc1)]:
                        w.op(VE, lambda e, off=off, esb=esb, hcx=hcx: e.scalar_tensor_tensor(
                            nsb, bInt[par][:, off:off + 65], esb[:, hcx:hcx + 1],
                            nsb, ALU.mult, ALU.add), sem="v")
                    w.op(VE, lambda e: e.tensor_single_scalar(
                        rden_sb[:, par * 2:par * 2 + 1], nsb[:, Dh:Dh + 1], EPS, ALU.add), sem="v")
                    w.op(VE, lambda e: e.reciprocal(
                        rden_sb[:, par * 2 + 1:par * 2 + 2], rden_sb[:, par * 2:par * 2 + 1]), sem="v")
                    w.op(VE, lambda e: e.tensor_single_scalar(
                        byt[:, h * Dh:(h + 1) * Dh], nsb[:, 0:Dh],
                        rden_sb[:, par * 2 + 1:par * 2 + 2], ALU.mult), sem="v")
                    vmark2[h] = w.mark("v")

                for step in range(H + 1):
                    if step < H:
                        b_stage1(step)
                    if step >= 1:
                        b_stage2(step - 1)
                for m in range(DC):
                    w.wait(TE, "v")
                    w.op(TE, lambda e, m=m: e.transpose(
                        pTp[m % 2], byt[:, m * 128:(m + 1) * 128], id_b_sb[:, :]), sem="t")
                    w.wait(VE, "t")
                    w.op(VE, lambda e, m=m, tc=tc: e.tensor_copy(
                        byf[:, m * TT + tc * CH:m * TT + tc * CH + CH], pTp[m % 2]), sem="v")
            w.dma(SY, stats_sb[0:1, 0:TT], sp_stats[0:1, t0:t0 + TT], "rlq")
            w.dma(SY, stats_sb[0:1, TT:2 * TT], sp_stats[1:2, t0:t0 + TT], "rlq")
            w.wait(TE, "rlq")
            guard(w)
            w.op(TE, lambda e: e.matmul(pB1[:, 0:TT], ones_f_sb[0:1, :],
                                        stats_sb[0:1, 0:TT], start=True, stop=True), sem=None)
            w.op(TE, lambda e: e.matmul(pB2[:, 0:TT], ones_f_sb[0:1, :],
                                        stats_sb[0:1, TT:2 * TT], start=True, stop=True), sem="t")
            mark_bcq = w.mark("t")
            w.wait(VE, "spa")
            for m in range(DC):
                pm = pMain[m % 2]
                ws = wring[:, (m % 2) * 1024:(m % 2) * 1024 + 1024]
                xc = cx[:, (m % 2) * TT:(m % 2) * TT + TT]
                xnf = ctmp[:, (m % 2) * TT:(m % 2) * TT + TT]
                w.wait(SY, "v")
                w.dma(SY, xc, x_fm[m * 128:(m + 1) * 128, t0:t0 + TT], "ldx%d" % (m % 2))
                w.wait(SY, "t")
                cache_load(w, UOF["wo"] + m, 1, ws, "ldw%d" % (m % 2))
                w.wait(TE, "ldw%d" % (m % 2))
                guard(w)
                for kc in range(DC):
                    w.op(TE, lambda e, kc=kc, pm=pm, ws=ws: e.matmul(
                        pm[:, 0:TT], ws[:, kc * 128:(kc + 1) * 128],
                        byf[:, kc * TT:(kc + 1) * TT],
                        start=(kc == 0), stop=(kc == DC - 1)), sem="t" if kc == DC - 1 else None)
                w.wait(VE, "ldx%d" % (m % 2))
                w.wait(VE, "t", mark_bcq)
                w.op(VE, lambda e, xc=xc, xnf=xnf: e.tensor_mul(xnf, xc, pB1[:, 0:TT]), sem="v")
                w.op(VE, lambda e, xnf=xnf: e.tensor_sub(xnf, xnf, pB2[:, 0:TT]), sem="v")
                w.wait(VE, "t")
                w.op(VE, lambda e, m=m, pm=pm, xnf=xnf: e.scalar_tensor_tensor(
                    ch_[:, m * TT:(m + 1) * TT], pm[:, 0:TT], vecs["bo"][:, m:m + 1],
                    xnf, ALU.add, ALU.add), sem="v")
            guard(w)
            for c in range(DC):
                hb = bsc2[:, 640:640 + TT]
                h2b = bsc2[:, 640 + TT:640 + 2 * TT]
                w.wait(SC, "v")
                if c >= 1:
                    w.wait(SC, "t")
                w.op(SC, lambda e, c=c, hb=hb: e.copy(hb, ch_[:, c * TT:(c + 1) * TT]), sem="s")
                w.op(SC, lambda e, c=c, h2b=h2b: e.activation(
                    h2b, ch_[:, c * TT:(c + 1) * TT], AF.Square), sem="s")
                w.wait(TE, "s")
                w.op(TE, lambda e, hb=hb, c=c: e.matmul(
                    pB1[0:1, 0:TT], onescol_b_sb[:, :], hb,
                    start=(c == 0), stop=(c == DC - 1)), sem=None)
                w.op(TE, lambda e, h2b=h2b, c=c: e.matmul(
                    pB2[0:1, 0:TT], onescol_b_sb[:, :], h2b,
                    start=(c == 0), stop=(c == DC - 1)), sem="t")
            mu = stats_sb[0:1, 0:TT]
            ex2 = stats_sb[0:1, TT:2 * TT]
            musq = stats_sb[0:1, 2 * TT:3 * TT]
            var = stats_sb[0:1, TT:2 * TT]
            sd = stats_sb[0:1, 2 * TT:3 * TT]
            rstd = stats_sb[0:1, TT:2 * TT]
            murstd = stats_sb[0:1, 2 * TT:3 * TT]
            w.wait(SC, "t")
            w.op(SC, lambda e: e.activation(mu, pB1[0:1, 0:TT], AF.Copy, scale=1.0 / D), sem="s")
            w.op(SC, lambda e: e.activation(ex2, pB2[0:1, 0:TT], AF.Copy, scale=1.0 / D), sem="s")
            w.wait(VE, "s")
            w.op(VE, lambda e: e.tensor_mul(musq, mu, mu), sem="v")
            w.op(VE, lambda e: e.tensor_sub(var, ex2, musq), sem="v")
            w.wait(SC, "v")
            w.op(SC, lambda e: e.activation(sd, var, AF.Sqrt, bias=eps_sb[0:1, 0:1]), sem="s")
            w.wait(VE, "s")
            w.op(VE, lambda e: e.reciprocal(rstd, sd), sem="v")
            w.op(VE, lambda e: e.tensor_mul(murstd, mu, rstd), sem="v")
            w.wait(TE, "v")
            w.op(TE, lambda e: e.matmul(pB1[:, 0:TT], ones_f_sb[0:1, :], rstd,
                                        start=True, stop=True), sem=None)
            w.op(TE, lambda e: e.matmul(pB2[:, 0:TT], ones_f_sb[0:1, :], murstd,
                                        start=True, stop=True), sem="t")
            mark_bc2 = w.mark("t")
            for c in range(DC):
                tmp = ctmp[:, 0:TT]
                tmp2 = ctmp[:, TT:2 * TT]
                w.wait(VE, "t", mark_bc2)
                w.wait(VE, "spa")
                w.op(VE, lambda e, c=c, tmp=tmp: e.tensor_mul(
                    tmp, ch_[:, c * TT:(c + 1) * TT], pB1[:, 0:TT]), sem="v")
                w.op(VE, lambda e, tmp=tmp, tmp2=tmp2: e.tensor_sub(tmp2, tmp, pB2[:, 0:TT]), sem="v")
                w.op(VE, lambda e, c=c, tmp2=tmp2: e.tensor_scalar(
                    tmp2, tmp2, vecs["ln_g"][:, c:c + 1], vecs["ln_b"][:, c:c + 1],
                    ALU.mult, ALU.add), sem="v")
                w.wait(SY, "v")
                w.dma(SY, sp_attn[c * 128:(c + 1) * 128, t0:t0 + TT], tmp2, "spa")

        def cache_load(w, unit, count, slot, sem):
            if count == 1:
                w.dma(SY, slot, wc[unit], sem)
            else:
                w.dma(SY, slot.rearrange("p (u f) -> p u f", u=count),
                      wc[unit:unit + count].rearrange("u p f -> p u f"), sem)

        def phase_c_half(w: Walk, hf):
            h0 = hf * HT
            njj = min(4, NTILES)
            npass = 2 if NTILES >= 4 else 1
            ring = Ring(w, depth=4)
            gring = Ring(w, depth=2)
            p4 = [pMain[0], pMain[1], ps_aux[:, :], ps_bcast[:, :]]
            for m in range(DC):
                ws = wring[:, (m % 2) * 1024:(m % 2) * 1024 + 1024]
                gring.wait_sy()
                cache_load(w, UOF["lin0_w"] + m, 1, ws, "ldw%d" % (m % 2))
                w.wait(TE, "ldw%d" % (m % 2))
                for jj in range(njj):
                    i = m * njj + jj
                    pm = p4[i % 4]
                    tl0 = h0 + jj * TT
                    ring.wait_te()
                    for kc in range(DC):
                        w.op(TE, lambda e, kc=kc, pm=pm, ws=ws, tl0=tl0: e.matmul(
                            pm[:, 0:TT], ws[:, kc * 128:(kc + 1) * 128],
                            biga[:, kc * T + tl0:kc * T + tl0 + TT],
                            start=(kc == 0), stop=(kc == DC - 1)), sem="t" if kc == DC - 1 else None)
                    w.wait(SC, "t")
                    w.op(SC, lambda e, m=m, jj=jj, pm=pm: e.activation(
                        bf0[:, m * HT + jj * TT:m * HT + (jj + 1) * TT], pm[:, 0:TT],
                        AF.Identity, bias=vecs["lin0_b"][:, m:m + 1]), sem="s")
                    ring.push()
                gring.push()
            ring = Ring(w, depth=2)
            gring = Ring(w, depth=2)
            pairs = [(pMain[0], pMain[1]), (ps_aux[:, :], ps_bcast[:, :])]
            for m in range(DC):
                ws1 = wring[:, (m % 2) * 1024:(m % 2) * 1024 + 1024]
                ws2 = wring[:, 2048 + (m % 2) * 1024:2048 + (m % 2) * 1024 + 1024]
                gring.wait_sy()
                cache_load(w, UOF["s1w1"] + m, 1, ws1, "ldw%d" % (m % 2))
                cache_load(w, UOF["s1w2"] + m, 1, ws2, "ldw%d" % (m % 2))
                w.wait(TE, "ldw%d" % (m % 2))
                for jj in range(njj):
                    i = m * njj + jj
                    pa, pb = pairs[i % 2]
                    tl0 = h0 + jj * TT
                    ring.wait_te()
                    for kc in range(DC):
                        w.op(TE, lambda e, kc=kc, ws1=ws1, jj=jj, pa=pa: e.matmul(
                            pa[:, 0:TT], ws1[:, kc * 128:(kc + 1) * 128],
                            bf0[:, kc * HT + jj * TT:kc * HT + (jj + 1) * TT],
                            start=(kc == 0), stop=(kc == DC - 1)), sem="t" if kc == DC - 1 else None)
                    for kc in range(DC):
                        w.op(TE, lambda e, kc=kc, ws2=ws2, jj=jj, pb=pb: e.matmul(
                            pb[:, 0:TT], ws2[:, kc * 128:(kc + 1) * 128],
                            bf0[:, kc * HT + jj * TT:kc * HT + (jj + 1) * TT],
                            start=(kc == 0), stop=(kc == DC - 1)), sem="t" if kc == DC - 1 else None)
                    gsl = bgg[:, (i % 2) * TT:(i % 2) * TT + TT]
                    w.wait(SC, "t")
                    w.op(SC, lambda e, m=m, gsl=gsl, pa=pa: e.activation(
                        gsl, pa[:, 0:TT], AF.Silu, bias=vecs["s1b1"][:, m:m + 1]), sem="s")
                    w.wait(VE, "s")
                    w.op(VE, lambda e, m=m, tl0=tl0, gsl=gsl, pb=pb: e.scalar_tensor_tensor(
                        biga[:, m * T + tl0:m * T + tl0 + TT],
                        pb[:, 0:TT], vecs["s1b2"][:, m:m + 1], gsl,
                        ALU.add, ALU.mult), sem="v")
                    ring.push()
                gring.push()
            for jj in range(njj):
                tl0 = h0 + jj * TT
                for tc in range(NCH):
                    tt0 = tl0 + tc * CH
                    guard(w)
                    for kc in range(DC):
                        w.op(TE, lambda e, kc=kc, tt0=tt0: e.matmul(
                            pGate, biga[:, kc * T + tt0:kc * T + tt0 + CH],
                            gatew_sb[:, kc * E:(kc + 1) * E],
                            start=(kc == 0), stop=(kc == DC - 1)), sem="t" if kc == DC - 1 else None)
                    sc_scores = csc[:, 0:8]
                    prob = csc[:, 8:16]
                    work = csc[:, 16:24]
                    sel = csc[:, 24:32]
                    msk = csc[:, 32:40]
                    rowm = csc[:, 40:41]
                    rows = csc[:, 41:42]
                    comb = csc[:, 48:56]
                    w.wait(VE, "t")
                    w.op(VE, lambda e: e.reduce_max(rowm, pGate, mybir.AxisListType.X), sem="v")
                    w.op(VE, lambda e: e.tensor_single_scalar(
                        sc_scores, pGate, rowm[:, 0:1], ALU.subtract), sem="v")
                    w.wait(SC, "v")
                    w.op(SC, lambda e: e.activation(prob, sc_scores, AF.Exp), sem="s")
                    w.wait(VE, "s")
                    w.op(VE, lambda e: e.reduce_sum(rows, prob, mybir.AxisListType.X), sem="v")
                    w.op(VE, lambda e: e.reciprocal(rowm[:, 0:1], rows[:, 0:1]), sem="v")
                    w.op(VE, lambda e: e.tensor_single_scalar(
                        prob, prob, rowm[:, 0:1], ALU.mult), sem="v")
                    w.op(VE, lambda e: e.tensor_copy(work, prob), sem="v")
                    w.op(VE, lambda e: e.memset(sel, 0.0), sem="v")
                    for it in range(4):
                        w.op(VE, lambda e: e.reduce_max(rowm, work, mybir.AxisListType.X), sem="v")
                        w.op(VE, lambda e: e.tensor_single_scalar(
                            msk, work, rowm[:, 0:1], ALU.is_ge), sem="v")
                        w.op(VE, lambda e: e.tensor_max(sel, sel, msk), sem="v")
                        w.op(VE, lambda e: e.scalar_tensor_tensor(
                            work, msk, -1e30, work, ALU.mult, ALU.add), sem="v")
                    w.op(VE, lambda e: e.tensor_mul(comb, prob, sel), sem="v")
                    w.op(VE, lambda e: e.reduce_sum(rows, comb, mybir.AxisListType.X), sem="v")
                    w.op(VE, lambda e: e.reciprocal(rowm[:, 0:1], rows[:, 0:1]), sem="v")
                    w.op(VE, lambda e: e.tensor_single_scalar(
                        comb, comb, rowm[:, 0:1], ALU.mult), sem="v")
                    w.wait(TE, "v")
                    w.op(TE, lambda e, comb=comb: e.transpose(
                        pTcomb[0:8, 0:128], comb, id_f_sb[:, :]), sem="t")
                    w.wait(VE, "t")
                    w.wait(VE, "bounce")
                    w.op(VE, lambda e, tc=tc: e.tensor_copy(
                        combf_sb[:, tc * CH:(tc + 1) * CH], pTcomb[0:8, 0:128]), sem="v")
                w.wait(GP, "v")
                w.dma(GP, sp_comb[:, tl0:tl0 + TT], combf_sb[:, :], "bounce")
            bc_sb = bigb[:, 8192:9216]
            for pp in range(npass):
                p0 = h0 + pp * PT
                w.wait(VE, "t")
                w.op(VE, lambda e: e.memset(routed[:, :], 0.0), sem="v")
                w.wait(GP, "bounce")
                pairs = [(pMain[0], pMain[1]), (ps_aux[:, :], ps_bcast[:, :])]
                for ee in range(E):
                    w.wait(GP, "t")
                    w.dma(GP, combrow_sb[:, :], sp_comb[ee:ee + 1, p0:p0 + PT], "bounce")
                    w.wait(TE, "bounce")
                    w.wait(TE, "v")
                    w.wait(TE, "s")
                    for t2 in range(2):
                        w.op(TE, lambda e, t2=t2: e.matmul(
                            pMain[t2][:, 0:TT], ones_b_sb[0:1, :],
                            combrow_sb[0:1, t2 * TT:(t2 + 1) * TT],
                            start=True, stop=True), sem="t")
                    w.wait(VE, "t")
                    w.op(VE, lambda e: e.tensor_copy(bc_sb[:, 0:TT], pMain[0][:, 0:TT]), sem="v")
                    w.op(VE, lambda e: e.tensor_copy(bc_sb[:, TT:2 * TT], pMain[1][:, 0:TT]), sem="v")
                    ring = Ring(w, depth=2)
                    gring = Ring(w, depth=2)
                    for m in range(DC):
                        ws1 = wring[:, (m % 2) * 1024:(m % 2) * 1024 + 1024]
                        ws3 = wring[:, 2048 + (m % 2) * 1024:2048 + (m % 2) * 1024 + 1024]
                        gring.wait_sy()
                        cache_load(w, UOF["exw1"] + ee * 8 + m, 1, ws1, "ldw%d" % (m % 2))
                        cache_load(w, UOF["exw3"] + ee * 8 + m, 1, ws3, "ldw%d" % (m % 2))
                        w.wait(TE, "ldw%d" % (m % 2))
                        for t2 in range(2):
                            i = m * 2 + t2
                            pa, pb = pairs[i % 2]
                            tl0 = p0 + t2 * TT
                            ring.wait_te()
                            for kc in range(DC):
                                w.op(TE, lambda e, kc=kc, ws1=ws1, tl0=tl0, pa=pa: e.matmul(
                                    pa[:, 0:TT], ws1[:, kc * 128:(kc + 1) * 128],
                                    biga[:, kc * T + tl0:kc * T + tl0 + TT],
                                    start=(kc == 0), stop=(kc == DC - 1)), sem="t" if kc == DC - 1 else None)
                            for kc in range(DC):
                                w.op(TE, lambda e, kc=kc, ws3=ws3, tl0=tl0, pb=pb: e.matmul(
                                    pb[:, 0:TT], ws3[:, kc * 128:(kc + 1) * 128],
                                    biga[:, kc * T + tl0:kc * T + tl0 + TT],
                                    start=(kc == 0), stop=(kc == DC - 1)), sem="t" if kc == DC - 1 else None)
                            gsl = bgg[:, (i % 2) * TT:(i % 2) * TT + TT]
                            tb = btmp[:, (i % 2) * TT:(i % 2) * TT + TT]
                            w.wait(SC, "t")
                            w.op(SC, lambda e, gsl=gsl, pa=pa: e.activation(
                                gsl, pa[:, 0:TT], AF.Silu), sem="s")
                            w.wait(VE, "s")
                            w.op(VE, lambda e, tb=tb, gsl=gsl, pb=pb: e.tensor_mul(
                                tb, pb[:, 0:TT], gsl), sem="v")
                            w.op(VE, lambda e, m=m, t2=t2, tb=tb: e.tensor_mul(
                                bhh[:, m * PT + t2 * TT:m * PT + (t2 + 1) * TT],
                                tb, bc_sb[:, t2 * TT:(t2 + 1) * TT]), sem="v")
                            ring.push()
                        gring.push()
                    ring = Ring(w, depth=4)
                    gring = Ring(w, depth=2)
                    p4 = [pMain[0], pMain[1], ps_aux[:, :], ps_bcast[:, :]]
                    for mo in range(DC):
                        ws2 = wring[:, (mo % 2) * 1024:(mo % 2) * 1024 + 1024]
                        gring.wait_sy()
                        cache_load(w, UOF["exw2"] + ee * 8 + mo, 1, ws2, "ldw%d" % (mo % 2))
                        w.wait(TE, "ldw%d" % (mo % 2))
                        for t2 in range(2):
                            i = mo * 2 + t2
                            pm = p4[i % 4]
                            ring.wait_te()
                            for kc in range(DC):
                                w.op(TE, lambda e, kc=kc, ws2=ws2, t2=t2, pm=pm: e.matmul(
                                    pm[:, 0:TT], ws2[:, kc * 128:(kc + 1) * 128],
                                    bhh[:, kc * PT + t2 * TT:kc * PT + (t2 + 1) * TT],
                                    start=(kc == 0), stop=(kc == DC - 1)), sem="t" if kc == DC - 1 else None)
                            w.wait(VE, "t")
                            w.op(VE, lambda e, mo=mo, t2=t2, pm=pm: e.tensor_add(
                                routed[:, mo * PT + t2 * TT:mo * PT + (t2 + 1) * TT],
                                routed[:, mo * PT + t2 * TT:mo * PT + (t2 + 1) * TT],
                                pm[:, 0:TT]), sem="v")
                            ring.push()
                        gring.push()
                phase_c4_pass(w, p0)

        def phase_c4_pass(w: Walk, p0):
            for t2 in range(2):
                tl0 = p0 + t2 * TT
                ring = Ring(w, depth=2)
                gring = Ring(w, depth=2)
                pairs = [(pMain[0], pMain[1]), (ps_aux[:, :], ps_bcast[:, :])]
                for mh in range(16):
                    ws1 = wring[:, (mh % 2) * 1024:(mh % 2) * 1024 + 1024]
                    ws3 = wring[:, 2048 + (mh % 2) * 1024:2048 + (mh % 2) * 1024 + 1024]
                    gring.wait_sy()
                    cache_load(w, UOF["shw1"] + mh, 1, ws1, "ldw%d" % (mh % 2))
                    cache_load(w, UOF["shw3"] + mh, 1, ws3, "ldw%d" % (mh % 2))
                    w.wait(TE, "ldw%d" % (mh % 2))
                    pa, pb = pairs[mh % 2]
                    ring.wait_te()
                    for kc in range(DC):
                        w.op(TE, lambda e, kc=kc, ws1=ws1, tl0=tl0, pa=pa: e.matmul(
                            pa[:, 0:TT], ws1[:, kc * 128:(kc + 1) * 128],
                            biga[:, kc * T + tl0:kc * T + tl0 + TT],
                            start=(kc == 0), stop=(kc == DC - 1)), sem="t" if kc == DC - 1 else None)
                    for kc in range(DC):
                        w.op(TE, lambda e, kc=kc, ws3=ws3, tl0=tl0, pb=pb: e.matmul(
                            pb[:, 0:TT], ws3[:, kc * 128:(kc + 1) * 128],
                            biga[:, kc * T + tl0:kc * T + tl0 + TT],
                            start=(kc == 0), stop=(kc == DC - 1)), sem="t" if kc == DC - 1 else None)
                    gsl = bgg[:, (mh % 2) * TT:(mh % 2) * TT + TT]
                    w.wait(SC, "t")
                    w.op(SC, lambda e, gsl=gsl, pa=pa: e.activation(gsl, pa[:, 0:TT], AF.Silu), sem="s")
                    w.wait(VE, "s")
                    w.op(VE, lambda e, mh=mh, gsl=gsl, pb=pb: e.tensor_mul(
                        bshh[:, mh * TT:(mh + 1) * TT], pb[:, 0:TT], gsl), sem="v")
                    ring.push()
                    gring.push()
                ring = Ring(w, depth=4)
                gring = Ring(w, depth=2)
                p4 = [pMain[0], pMain[1], ps_aux[:, :], ps_bcast[:, :]]
                for mo in range(DC):
                    ws2 = wring[:, (mo % 2) * 2048:(mo % 2) * 2048 + 2048]
                    gring.wait_sy()
                    cache_load(w, UOF["shw2"] + 2 * mo, 2, ws2, "ldw%d" % (mo % 2))
                    w.wait(TE, "ldw%d" % (mo % 2))
                    pm = p4[mo % 4]
                    ring.wait_te()
                    for kc in range(16):
                        w.op(TE, lambda e, kc=kc, ws2=ws2, pm=pm: e.matmul(
                            pm[:, 0:TT], ws2[:, kc * 128:(kc + 1) * 128],
                            bshh[:, kc * TT:(kc + 1) * TT],
                            start=(kc == 0), stop=(kc == 15)), sem="t" if kc == 15 else None)
                    w.wait(VE, "t")
                    w.op(VE, lambda e, mo=mo, t2=t2, pm=pm: e.tensor_add(
                        bm1[:, t2 * 4096 + mo * TT:t2 * 4096 + (mo + 1) * TT], pm[:, 0:TT],
                        routed[:, mo * PT + t2 * TT:mo * PT + (t2 + 1) * TT]), sem="v")
                    ring.push()
                    gring.push()
            for t2 in range(2):
                tl0 = p0 + t2 * TT
                ring = Ring(w, depth=4)
                gring = Ring(w, depth=2)
                p4 = [pMain[0], pMain[1], ps_aux[:, :], ps_bcast[:, :]]
                for m in range(DC):
                    ws = wring[:, (m % 2) * 1024:(m % 2) * 1024 + 1024]
                    gring.wait_sy()
                    cache_load(w, UOF["lin1_w"] + m, 1, ws, "ldw%d" % (m % 2))
                    w.wait(TE, "ldw%d" % (m % 2))
                    pm = p4[m % 4]
                    ring.wait_te()
                    for kc in range(DC):
                        w.op(TE, lambda e, kc=kc, ws=ws, t2=t2, pm=pm: e.matmul(
                            pm[:, 0:TT], ws[:, kc * 128:(kc + 1) * 128],
                            bm1[:, t2 * 4096 + kc * TT:t2 * 4096 + (kc + 1) * TT],
                            start=(kc == 0), stop=(kc == DC - 1)), sem="t" if kc == DC - 1 else None)
                    w.wait(SC, "t")
                    w.op(SC, lambda e, m=m, pm=pm: e.activation(
                        bm2[:, m * TT:(m + 1) * TT], pm[:, 0:TT], AF.Identity,
                        bias=vecs["lin1_b"][:, m:m + 1]), sem="s")
                    ring.push()
                    gring.push()
                ring = Ring(w, depth=2)
                gring = Ring(w, depth=2)
                pairs = [(pMain[0], pMain[1]), (ps_aux[:, :], ps_bcast[:, :])]
                for m in range(DC):
                    ws1 = wring[:, (m % 2) * 1024:(m % 2) * 1024 + 1024]
                    ws2 = wring[:, 2048 + (m % 2) * 1024:2048 + (m % 2) * 1024 + 1024]
                    gring.wait_sy()
                    cache_load(w, UOF["s2w1"] + m, 1, ws1, "ldw%d" % (m % 2))
                    cache_load(w, UOF["s2w2"] + m, 1, ws2, "ldw%d" % (m % 2))
                    w.wait(TE, "ldw%d" % (m % 2))
                    pa, pb = pairs[m % 2]
                    ring.wait_te()
                    for kc in range(DC):
                        w.op(TE, lambda e, kc=kc, ws1=ws1, pa=pa: e.matmul(
                            pa[:, 0:TT], ws1[:, kc * 128:(kc + 1) * 128],
                            bm2[:, kc * TT:(kc + 1) * TT],
                            start=(kc == 0), stop=(kc == DC - 1)), sem="t" if kc == DC - 1 else None)
                    for kc in range(DC):
                        w.op(TE, lambda e, kc=kc, ws2=ws2, pb=pb: e.matmul(
                            pb[:, 0:TT], ws2[:, kc * 128:(kc + 1) * 128],
                            bm2[:, kc * TT:(kc + 1) * TT],
                            start=(kc == 0), stop=(kc == DC - 1)), sem="t" if kc == DC - 1 else None)
                    gsl = bgg[:, (m % 2) * TT:(m % 2) * TT + TT]
                    w.wait(SC, "t")
                    w.op(SC, lambda e, m=m, gsl=gsl, pa=pa: e.activation(
                        gsl, pa[:, 0:TT], AF.Silu, bias=vecs["s2b1"][:, m:m + 1]), sem="s")
                    w.wait(VE, "s")
                    if m == 0:
                        w.wait(VE, "outd")
                    w.op(VE, lambda e, m=m, gsl=gsl, pb=pb: e.scalar_tensor_tensor(
                        cffn[:, m * TT:(m + 1) * TT], pb[:, 0:TT],
                        vecs["s2b2"][:, m:m + 1], gsl, ALU.add, ALU.mult), sem="v")
                    ring.push()
                    gring.push()
                    xa = cxa[:, 0:TT]
                    at = ctmp2[:, 0:TT]
                    w.wait(SY, "v")
                    w.dma(SY, xa, x_fm[m * 128:(m + 1) * 128, tl0:tl0 + TT], "ldx0")
                    w.dma(SY, at, sp_attn[m * 128:(m + 1) * 128, tl0:tl0 + TT], "ldx0")
                    w.wait(VE, "ldx0")
                    w.op(VE, lambda e, m=m, xa=xa: e.tensor_add(
                        cffn[:, m * TT:(m + 1) * TT], cffn[:, m * TT:(m + 1) * TT], xa), sem="v")
                    w.op(VE, lambda e, m=m, at=at: e.tensor_add(
                        cffn[:, m * TT:(m + 1) * TT], cffn[:, m * TT:(m + 1) * TT], at), sem="v")
                    w.wait(SY, "v")
                    w.dma(SY, out_fm[m * 128:(m + 1) * 128, tl0:tl0 + TT],
                          cffn[:, m * TT:(m + 1) * TT], "outd")

        @block.sync
        def _(e):
            body(Walk("sync", e, sems))

        @block.tensor
        def _(e):
            body(Walk("tensor", e, sems))

        @block.vector
        def _(e):
            body(Walk("vector", e, sems))

        @block.scalar
        def _(e):
            body(Walk("scalar", e, sems))

        @block.gpsimd
        def _(e):
            body(Walk("gpsimd", e, sems))

    return nc


def _rope_tables(core):
    inv = 1.0 / 10000.0 ** (np.arange(0, Dh, 2, dtype=np.float64) / Dh)
    t0 = (core % 4) * T
    t = np.arange(t0, t0 + T, dtype=np.float64)
    fr = t[:, None] * inv[None, :]
    c = np.cos(fr).astype(np.float32).T
    s = np.sin(fr).astype(np.float32).T
    cs1 = np.zeros((128, T), np.float32)
    cs2 = np.zeros((128, T), np.float32)
    for blk in range(2):
        b0 = blk * 64
        cs1[b0:b0 + 32] = c
        cs1[b0 + 32:b0 + 64] = c
        cs2[b0:b0 + 32] = s
        cs2[b0 + 32:b0 + 64] = -s
    return cs1, cs2


_nc_cache = {}


def kernel(**inputs):
    f32 = lambda k: np.ascontiguousarray(np.asarray(inputs[k], np.float32))
    x = f32("L0_attn_x")

    tri = np.triu(np.ones((CH, CH), np.float32))
    ident = np.eye(CH, dtype=np.float32)
    swap = np.zeros((CH, CH), np.float32)
    for mcol in range(CH):
        blk, i = mcol // 64, mcol % 64
        swap[blk * 64 + (i + 32) % 64, mcol] = 1.0
    ones_row = np.ones((1, CH), np.float32)
    ones_col = np.ones((CH, 1), np.float32)

    vec = lambda k: np.ascontiguousarray(f32(k).reshape(DC, 128).T)
    gate_wt = np.ascontiguousarray(
        f32("L0_ffn_gate_w").reshape(E, DC, 128).transpose(2, 1, 0).reshape(128, DC * E))

    common = dict(
        wq=f32("L0_attn_Wq"), wk=f32("L0_attn_Wk"), wv=f32("L0_attn_Wv"),
        wbeta=f32("L0_attn_Wbeta"), wo=f32("L0_attn_Wo"),
        bbeta=f32("L0_attn_bbeta").reshape(1, HC), bo=vec("L0_attn_bo"),
        ln_g=vec("L0_attn_ln_gamma"), ln_b=vec("L0_attn_ln_beta"),
        lin0_w=f32("L0_ffn_lin0_w"), lin0_b=vec("L0_ffn_lin0_b"),
        s1w1=f32("L0_ffn_swi1_w1"), s1b1=vec("L0_ffn_swi1_b1"),
        s1w2=f32("L0_ffn_swi1_w2"), s1b2=vec("L0_ffn_swi1_b2"),
        gate_wt=gate_wt,
        exw1=f32("L0_ffn_exp_w1"), exw3=f32("L0_ffn_exp_w3"), exw2=f32("L0_ffn_exp_w2"),
        shw1=f32("L0_ffn_sh_w1"), shw3=f32("L0_ffn_sh_w3"), shw2=f32("L0_ffn_sh_w2"),
        lin1_w=f32("L0_ffn_lin1_w"), lin1_b=vec("L0_ffn_lin1_b"),
        s2w1=f32("L0_ffn_swi2_w1"), s2b1=vec("L0_ffn_swi2_b1"),
        s2w2=f32("L0_ffn_swi2_w2"), s2b2=vec("L0_ffn_swi2_b2"),
        tri_f=tri, tri_b=tri.astype(_bf), id_b=ident.astype(_bf), id_f=ident,
        swap_b=swap.astype(_bf), ones_f=ones_row, ones_b=ones_row.astype(_bf),
        onescol_f=ones_col, onescol_b=ones_col.astype(_bf),
        negones_f=-ones_row,
    )

    in_maps = []
    for core in range(NCORE):
        b = core // 4
        r = core % 4
        xs = np.ascontiguousarray(x[b, r * T:(r + 1) * T, :].T)
        cs1v, cs2v = _rope_tables(core)
        gm = np.zeros((1, NCORE), np.float32)
        for i in range(NCORE):
            gm[0, i] = 1.0 if (i // 4 == b and i % 4 < r) else 0.0
        m = dict(common)
        m.update(x_fm=xs, cs1=cs1v.astype(_bf), cs2=cs2v.astype(_bf), gmask=gm)
        in_maps.append(m)

    if "prog" not in _nc_cache:
        _nc_cache["prog"] = build_program()
    nc = _nc_cache["prog"]

    import os as _os
    res = run_bass_kernel_spmd(nc, in_maps, core_ids=list(range(NCORE)),
                               trace=bool(_os.environ.get("KTRACE")))
    global LAST_RESULTS
    LAST_RESULTS = res

    out = np.empty((B, S, D), np.float32)
    for core in range(NCORE):
        b = core // 4
        r = core % 4
        out[b, r * T:(r + 1) * T, :] = np.asarray(res.results[core]["out_fm"]).T
    return out


# version 11
# speedup vs baseline: 1.0142x; 1.0142x over previous
import numpy as np
import ml_dtypes
import concourse.bass as bass
import concourse.mybir as mybir
from concourse.bass_utils import run_bass_kernel_spmd

F32 = mybir.dt.float32
BF16 = mybir.dt.bfloat16
AF = mybir.ActivationFunctionType
ALU = mybir.AluOpType

NCORE = 8
B, S, D, H, Dh, C = 2, 16384, 1024, 16, 64, 2
T = 4096
NT = 8
TT = 512
NCH = 4
CH = 128
DC = 8
HC = H * C
E = 8
SHD = 2048
PT = 1024
NHALF = 2
HT = 2048
LN_EPS = 1e-5
EPS = 1e-6
NDST = Dh + 1
MCOLS = HC * NDST

DEBUG = False
PHASES = "AGBC"
CLEVEL = 4
ALSUB = 4
A_LEVEL = 4
NTILES = NT
SHARED_CC = True
STRICT_SELF_WAITS = True
_bf = ml_dtypes.bfloat16
LAST_RESULTS = None


class Walk:

    def __init__(self, eng_name, eng, sems):
        self.eng_name = eng_name
        self.eng = eng
        self.sems = sems
        self.counts = {k: 0 for k in sems}

    def mark(self, sem):
        return self.counts[sem]

    def on(self, eng_name):
        return eng_name == self.eng_name

    def wait(self, eng_name, sem, val=None):
        v = self.counts[sem] if val is None else val
        if self.on(eng_name) and v > 0:
            self.eng.wait_ge(self.sems[sem], v)

    def dma(self, qname, out, in_, sem, **kw):
        self.counts[sem] += 16
        if self.on(qname):
            self.eng.dma_start(out=out, in_=in_, **kw).then_inc(self.sems[sem], 16)

    _SELF = {"vector": "v", "scalar": "s"}

    def op(self, eng_name, fn, sem=None):
        own = self._SELF.get(eng_name) if STRICT_SELF_WAITS else None
        if own is not None and self.on(eng_name) and self.counts[own] > 0:
            self.eng.wait_ge(self.sems[own], self.counts[own])
        if sem is not None:
            self.counts[sem] += 1
        if self.on(eng_name):
            last = fn(self.eng)
            if sem is not None:
                last.then_inc(self.sems[sem], 1)


def build_program():
    nc = bass.Bass("TRN2", target_bir_lowering=False)

    def din(name, shape, dt=F32):
        return nc.dram_tensor(name, shape, dt, kind="ExternalInput").ap()

    x_fm = din("x_fm", [D, T])
    wq = din("wq", [D, D]); wk = din("wk", [D, D]); wv = din("wv", [D, D])
    wbeta = din("wbeta", [D, HC]); wo = din("wo", [D, D])
    bbeta = din("bbeta", [1, HC]); bo = din("bo", [128, DC])
    ln_g = din("ln_g", [128, DC]); ln_b = din("ln_b", [128, DC])
    lin0_w = din("lin0_w", [D, D]); lin0_b = din("lin0_b", [128, DC])
    s1w1 = din("s1w1", [D, D]); s1b1 = din("s1b1", [128, DC])
    s1w2 = din("s1w2", [D, D]); s1b2 = din("s1b2", [128, DC])
    gate_wt = din("gate_wt", [128, DC * E])
    exw1 = din("exw1", [E, D, D]); exw3 = din("exw3", [E, D, D]); exw2 = din("exw2", [E, D, D])
    shw1 = din("shw1", [D, SHD]); shw3 = din("shw3", [D, SHD]); shw2 = din("shw2", [SHD, D])
    lin1_w = din("lin1_w", [D, D]); lin1_b = din("lin1_b", [128, DC])
    s2w1 = din("s2w1", [D, D]); s2b1 = din("s2b1", [128, DC])
    s2w2 = din("s2w2", [D, D]); s2b2 = din("s2b2", [128, DC])
    cs1 = din("cs1", [CH, T], BF16); cs2 = din("cs2", [CH, T], BF16)
    tri_f = din("tri_f", [CH, CH]); tri_b = din("tri_b", [CH, CH], BF16)
    id_b = din("id_b", [CH, CH], BF16); id_f = din("id_f", [CH, CH])
    swap_b = din("swap_b", [CH, CH], BF16)
    ones_f = din("ones_f", [1, CH]); ones_b = din("ones_b", [1, CH], BF16)
    negones_f = din("negones_f", [1, CH])
    onescol_f = din("onescol_f", [CH, 1])
    onescol_b = din("onescol_b", [CH, 1], BF16)
    gmask = din("gmask", [1, NCORE])

    out_fm = nc.dram_tensor("out_fm", [D, T], F32, kind="ExternalOutput").ap()

    skind = "ExternalOutput" if DEBUG else "Internal"

    def scr(name, shape, dt=F32):
        return nc.dram_tensor(name, shape, dt, kind=skind).ap()

    sp_q = scr("sp_q", [D, T], BF16)
    sp_k = scr("sp_k", [D, T], BF16)
    sp_va = scr("sp_va", [T, H * NDST], BF16)
    sp_L = scr("sp_L", [T, 2 * HC])
    sp_Lr = scr("sp_Lr", [T // CH, HC, CH])
    sp_mst = scr("sp_mst", [T // CH, 64, MCOLS], BF16)
    sp_attn = scr("sp_attn", [D, T])
    sp_stats = scr("sp_stats", [2, T])
    wc = nc.dram_tensor("wc", [320, 128, 1024], BF16, kind="Internal").ap()
    UOF = {"wq": 0, "wk": 8, "wv": 16, "wo": 24, "lin0_w": 32, "s1w1": 40,
           "s1w2": 48, "lin1_w": 56, "s2w1": 64, "s2w2": 72,
           "shw1": 80, "shw3": 96, "shw2": 112,
           "exw1": 128, "exw3": 192, "exw2": 256}
    sp_comb = scr("sp_comb", [E, T], BF16)
    cc_in = nc.dram_tensor("cc_in", [1, HC + 64 * MCOLS], F32, kind="Internal").ap()
    dbg_cc = (nc.dram_tensor("dbg_cc", [1, HC + 64 * MCOLS], F32,
                             kind="ExternalOutput").ap() if DEBUG else None)
    cc_out = nc.dram_tensor("cc_out", [NCORE, HC + 64 * MCOLS], F32,
                            addr_space="Shared" if SHARED_CC else "Local",
                            kind="Internal").ap()

    from contextlib import ExitStack
    ctx = ExitStack()

    def mk_sb(name, shape, dt=F32):
        return ctx.enter_context(nc.sbuf_tensor(name, shape, dt))

    with ctx:
        biga = mk_sb("biga", [128, DC * T], BF16)
        bigb = mk_sb("bigb", [128, 19200], BF16)
        bigc = mk_sb("bigc", [128, 8192], F32)
        wring = mk_sb("wring", [128, 4096], BF16)
        m_sb = mk_sb("m_sb", [64, MCOLS], F32)
        sin_sb = mk_sb("sin_sb", [64, MCOLS], F32)
        sinb_sb = mk_sb("sinb_sb", [128, MCOLS], BF16)
        m_bf = mk_sb("m_bf", [64, MCOLS], BF16)
        u64a = mk_sb("u64a", [128, 2 * MCOLS], BF16)
        gtmp_sb = u64a[:, :].bitcast(F32)[0:64, 0:MCOLS]
        arow_sb = mk_sb("arow_sb", [1, NCORE * HC], F32)
        gm_sb = mk_sb("gm_sb", [1, NCORE], F32)
        gmb_sb = mk_sb("gmb_sb", [64, NCORE], F32)
        ap_row = mk_sb("ap_row", [1, HC], F32)
        ap_bc = mk_sb("ap_bc", [64, HC], F32)
        cs1_sb = mk_sb("cs1_sb", [128, TT], BF16)
        cs2_sb = mk_sb("cs2_sb", [128, TT], BF16)
        tri_f_sb = mk_sb("tri_f_sb", [128, CH], F32)
        tri_b_sb = mk_sb("tri_b_sb", [128, CH], BF16)
        id_b_sb = mk_sb("id_b_sb", [128, CH], BF16)
        id_f_sb = mk_sb("id_f_sb", [128, CH], F32)
        swap_sb = mk_sb("swap_sb", [128, CH], BF16)
        ones_f_sb = mk_sb("ones_f_sb", [1, CH], F32)
        negones_sb = mk_sb("negones_sb", [1, CH], F32)
        ones_b_sb = mk_sb("ones_b_sb", [1, CH], BF16)
        onescol_f_sb = mk_sb("onescol_f_sb", [128, 1], F32)
        onescol_b_sb = mk_sb("onescol_b_sb", [128, 1], BF16)
        bbr_sb = mk_sb("bbr_sb", [1, HC], F32)
        bb_bc_sb = mk_sb("bb_bc_sb", [128, HC], F32)
        wbeta_sb = mk_sb("wbeta_sb", [128, DC * HC], BF16)
        gatew_sb = mk_sb("gatew_sb", [128, DC * E], BF16)
        vec_names = ["bo", "ln_g", "ln_b", "lin0_b", "s1b1", "s1b2",
                     "lin1_b", "s2b1", "s2b2"]
        vecs = {n: mk_sb("vec_" + n, [128, DC], F32) for n in vec_names}
        carry_sb = mk_sb("carry_sb", [1, HC], F32)
        dsum_sb = mk_sb("dsum_sb", [1, HC], F32)
        adsum_sb = mk_sb("adsum_sb", [1, HC], F32)
        acore_sb = mk_sb("acore_sb", [1, HC], F32)
        lrows_sb = mk_sb("lrows_sb", [1, HC * CH], F32)
        combrow_sb = mk_sb("combrow_sb", [1, PT], BF16)
        combf_sb = mk_sb("combf_sb", [8, TT], BF16)
        eps_sb = mk_sb("eps_sb", [1, 1], F32)
        dmy_sb = mk_sb("dmy_sb", [1, 1], F32)
        lloc_sb = mk_sb("lloc_sb", [128, HC], F32)
        lglob_sb = mk_sb("lglob_sb", [128, HC], F32)
        negl_sb = mk_sb("negl_sb", [128, HC], F32)
        expll_sb = mk_sb("expll_sb", [128, HC], F32)
        explg_sb = mk_sb("explg_sb", [128, HC], F32)
        sc_sb = mk_sb("sc_sb", [128, HC], F32)
        asb_sb = mk_sb("asb_sb", [64, HC], F32)
        scarg_sb = mk_sb("scarg_sb", [128, HC], F32)
        zrow_sb = mk_sb("zrow_sb", [128, HC], F32)
        lb_sb = mk_sb("lb_sb", [128, HC], F32)
        num_sb = mk_sb("num_sb", [128, 2 * NDST], F32)
        rden_sb = mk_sb("rden_sb", [128, 4], F32)
        llocT_sb = mk_sb("llocT_sb", [HC, CH], F32)
        mstl_sb = u64a[:, :]

        ps_main0 = ctx.enter_context(nc.psum_tensor("ps_main0", [128, 512], F32))
        ps_main1 = ctx.enter_context(nc.psum_tensor("ps_main1", [128, 512], F32))
        ps_aux = ctx.enter_context(nc.psum_tensor("ps_aux", [128, 512], F32))
        ps_scan0 = ctx.enter_context(nc.psum_tensor("ps_scan0", [128, 512], F32))
        ps_scan1 = ctx.enter_context(nc.psum_tensor("ps_scan1", [128, 512], F32))
        ps_scan2 = ctx.enter_context(nc.psum_tensor("ps_scan2", [128, 512], F32))
        ps_tp = ctx.enter_context(nc.psum_tensor("ps_tp", [128, 1024], BF16))
        ps_bcast = ctx.enter_context(nc.psum_tensor("ps_bcast", [128, 512], F32))

        pMain = [ps_main0[:, :], ps_main1[:, :]]
        pB1 = ps_bcast[:, :]
        pB2 = ps_aux[:, :]
        pRope = ps_aux[:, :]
        pSt = [ps_scan0[:, 0:128], ps_scan0[:, 128:256]]
        pBeta = ps_scan0[:, 256:288]
        pGate = ps_scan0[:, 288:296]
        pTcomb = ps_scan0[:, 352:480]
        pM0 = ps_scan1[:, 0:128]
        pM1 = ps_scan1[:, 128:256]
        pScT = ps_scan1[:, 256:288]
        pAbc = ps_scan1[:, 288:320]
        pStt = [ps_scan1[:, 320:385], ps_scan1[:, 385:450]]
        pMisc = ps_scan1[:, 450:482]
        pLloc = ps_scan2[:, 0:32]
        pLglob = ps_scan2[:, 32:64]
        pNum = [ps_scan2[:, 64:129], ps_scan2[:, 129:194]]
        pIntr = ps_scan2[:, 194:454]
        pTp = [ps_tp[:, 0:128], ps_tp[:, 128:256]]
        pBc = [ps_bcast[:, :], ps_aux[:, :]]

        bq = bigb[:, 0:1024]
        bk = bigb[:, 1024:2048]
        bv = bigb[:, 2048:3072]
        bva = bigb[:, 3072:7232]
        bkt = bigb[:, 7232:11328]
        bsc = bigb[:, 11328:15424]
        bqt = bigb[:, 0:4096]
        bkt2 = bigb[:, 4096:8192]
        bva2 = bigb[:, 8192:12352]
        byt = bigb[:, 12352:13376]
        byf = bigb[:, 13376:17472]
        bsc2 = bigb[:, 17472:19200]
        bf0 = bigb[:, 0:16384]
        bgg = bigb[:, 16384:17408]
        btmp = bigb[:, 17408:18432]
        bhh = bigb[:, 0:8192]
        bshh = bigb[:, 0:8192]
        bm1 = bigb[:, 8192:16384]
        bm2 = bigb[:, 0:4096]

        cx = bigc[:, 0:1024]
        ch_ = bigc[:, 1024:5120]
        ctmp = bigc[:, 5120:7168]
        csc = bigc[:, 7168:8192]
        stats_sb = bigc[0:1, 6656:6656 + 3 * TT].rearrange("o f -> o f")
        routed = bigc[:, 0:8192]
        cffn = bigc[:, 0:4096]
        cxa = bigc[:, 4096:5120]
        ctmp2 = bigc[:, 5120:7168]

        sems = {}
        for n in ["ldw0", "ldw1", "ldx0", "ldx1", "ldcs", "ldm", "ldmg",
                  "spq0", "spq1", "spv", "spl", "spm", "spa", "rlq",
                  "rlm0", "rlm1", "rlm2",
                  "cc", "ccd", "bounce", "outd", "pcast", "t", "v", "s", "g"]:
            sems[n] = ctx.enter_context(nc.semaphore(n))

        block = ctx.enter_context(nc.Block())

        TE, VE, SC, SY, GP = "tensor", "vector", "scalar", "sync", "gpsimd"

        def guard(w):
            w.wait(TE, "v")
            w.wait(TE, "s")

        class Ring:

            def __init__(self, w, depth=2):
                self.w = w
                self.depth = depth
                self.hist = []

            def wait_te(self):
                if len(self.hist) >= self.depth:
                    vm, sm, _ = self.hist[-self.depth]
                    self.w.wait(TE, "v", vm)
                    self.w.wait(TE, "s", sm)
                else:
                    guard(self.w)

            def wait_gp(self):
                if len(self.hist) >= self.depth:
                    _, _, tm = self.hist[-self.depth]
                    self.w.wait(GP, "t", tm)
                else:
                    self.w.wait(GP, "t")

            def wait_sy(self):
                if len(self.hist) >= self.depth:
                    _, _, tm = self.hist[-self.depth]
                    self.w.wait(SY, "t", tm)
                else:
                    self.w.wait(SY, "t")

            def push(self):
                self.hist.append((self.w.mark("v"), self.w.mark("s"), self.w.mark("t")))

        def body(w: Walk):
            if PHASES == "MICRO":
                for c in range(DC):
                    w.dma(SY, out_fm[c * 128:(c + 1) * 128, :],
                          x_fm[c * 128:(c + 1) * 128, :], "outd")
                w.wait(SY, "outd")
                return
            w.dma(SY, tri_f_sb[:, :], tri_f, "ldm")
            w.dma(SY, tri_b_sb[:, :], tri_b, "ldm")
            w.dma(SY, id_b_sb[:, :], id_b, "ldm")
            w.dma(SY, id_f_sb[:, :], id_f, "ldm")
            w.dma(SY, swap_sb[:, :], swap_b, "ldm")
            w.dma(SY, ones_f_sb[:, :], ones_f, "ldm")
            w.dma(SY, negones_sb[:, :], negones_f, "ldm")
            w.dma(SY, ones_b_sb[:, :], ones_b, "ldm")
            w.dma(SY, onescol_f_sb[:, :], onescol_f, "ldm")
            w.dma(SY, onescol_b_sb[:, :], onescol_b, "ldm")
            w.dma(SY, gm_sb[:, :], gmask, "ldm")
            w.dma(SY, bbr_sb[:, :], bbeta, "ldm")
            bias_srcs = {"bo": bo, "ln_g": ln_g, "ln_b": ln_b, "lin0_b": lin0_b,
                         "s1b1": s1b1, "s1b2": s1b2, "lin1_b": lin1_b,
                         "s2b1": s2b1, "s2b2": s2b2}
            for n in vec_names:
                w.dma(SY, vecs[n][:, :], bias_srcs[n], "ldm")
            w.dma(GP, wbeta_sb[:, :].rearrange("p (c n) -> p c n", c=DC),
                  wbeta.rearrange("(c p) n -> p c n", p=128), "ldmg")
            w.dma(GP, gatew_sb[:, :], gate_wt, "ldmg")
            def precast_dxd(name, wt, nm):
                for m in range(nm):
                    w.dma(GP, wc[UOF[name] + m].rearrange("p (kc n) -> p kc n", kc=DC),
                          wt.rearrange("(kc p) n -> p kc n", p=128)[:, :, m * 128:(m + 1) * 128],
                          "pcast")
            precast_dxd("wq", wq, 8)
            precast_dxd("wk", wk, 8)
            precast_dxd("wv", wv, 8)
            precast_dxd("wo", wo, 8)
            w.wait(GP, "pcast")
            w.op(GP, lambda e: e.memset(dmy_sb[:, :], 0.0), sem="g")
            g1mark = w.mark("g")
            precast_dxd("lin0_w", lin0_w, 8)
            precast_dxd("s1w1", s1w1, 8)
            precast_dxd("s1w2", s1w2, 8)
            precast_dxd("lin1_w", lin1_w, 8)
            precast_dxd("s2w1", s2w1, 8)
            precast_dxd("s2w2", s2w2, 8)
            precast_dxd("shw1", shw1, 16)
            precast_dxd("shw3", shw3, 16)
            for m in range(DC):
                for u in range(2):
                    w.dma(GP, wc[UOF["shw2"] + 2 * m + u].rearrange("p (kc n) -> p kc n", kc=DC),
                          shw2.rearrange("(u kc p) n -> u p kc n", u=2, p=128)[u][:, :, m * 128:(m + 1) * 128],
                          "pcast")
            for ee in range(E):
                for nm, wt in (("exw1", exw1), ("exw3", exw3), ("exw2", exw2)):
                    for m in range(DC):
                        w.dma(GP, wc[UOF[nm] + ee * 8 + m].rearrange("p (kc n) -> p kc n", kc=DC),
                              wt[ee].rearrange("(kc p) n -> p kc n", p=128)[:, :, m * 128:(m + 1) * 128],
                              "pcast")
            w.wait(GP, "pcast")
            w.op(GP, lambda e: e.memset(dmy_sb[:, :], 0.0), sem="g")
            g2mark = w.mark("g")
            w.wait(SY, "g", g1mark)
            w.wait(TE, "ldm")
            w.wait(TE, "ldmg")
            w.op(TE, lambda e: e.matmul(pScT, ones_f_sb[0:1, :], bbr_sb[0:1, :],
                                        start=True, stop=True), sem="t")
            w.wait(VE, "t")
            w.wait(VE, "ldm")
            w.op(VE, lambda e: e.tensor_copy(bb_bc_sb[:, :], pScT), sem="v")
            if CLEVEL >= 3:
                w.op(VE, lambda e: e.memset(m_sb[:, :], 0.0), sem="v")
                w.op(VE, lambda e: e.memset(m_bf[:, :], 0.0), sem="v")
                w.op(VE, lambda e: e.memset(sin_sb[:, :], 0.0), sem="v")
                w.op(VE, lambda e: e.memset(carry_sb[:, :], 0.0), sem="v")
                w.op(VE, lambda e: e.memset(eps_sb[:, :], LN_EPS), sem="v")
            if CLEVEL >= 4:
                w.wait(TE, "v")
                w.op(TE, lambda e: e.matmul(pAbc[0:64, 0:NCORE], ones_f_sb[0:1, 0:64],
                                            gm_sb[0:1, :], start=True, stop=True), sem="t")
                w.wait(VE, "t")
                w.op(VE, lambda e: e.tensor_copy(gmb_sb[:, :], pAbc[0:64, 0:NCORE]), sem="v")

            if "A" in PHASES and A_LEVEL >= 1:
                for j in range(NTILES):
                    phase_a_tile(w, j)
            if "G" in PHASES:
                do_gather(w)
            if "B" in PHASES:
                for sname in ["spq0", "spq1", "spv", "spl", "spm"]:
                    w.wait(SY, sname)
                w.wait(TE, "ccd")
                w.wait(VE, "ccd")
                for j in range(NTILES):
                    phase_b_tile(w, j)
            if "C" in PHASES:
                w.wait(SY, "g", g2mark)
                w.wait(SY, "spa")
                w.wait(VE, "spa")
                w.wait(SC, "spa")
                w.wait(TE, "spa")
                for hf in range(NHALF if NTILES >= 8 else 1):
                    phase_c_half(w, hf)
            for sname in ["outd", "spq0", "spq1", "spv", "spl", "spa",
                          "ldx0", "ldx1", "rlq", "rlm0", "rlm1", "rlm2",
                          "ldcs", "ldm"]:
                w.wait(SY, sname)
            for sname in ["ldmg", "spm", "ccd", "bounce", "ldw0", "ldw1"]:
                w.wait(GP, sname)

        def do_gather(w: Walk):
            w.wait(SC, "v")
            w.op(SC, lambda e: e.activation(acore_sb[:, :], carry_sb[:, :], AF.Exp), sem="s")
            w.wait(GP, "s")
            w.wait(GP, "v")
            w.dma(GP, cc_in[0:1, 0:HC], acore_sb[:, :], "ccd")
            w.dma(GP, cc_in[0, HC:].rearrange("(p f) -> p f", p=64), m_sb[:, :], "ccd")
            w.wait(GP, "ccd")
            if DEBUG:
                w.dma(GP, dbg_cc, cc_in[:, :], "ccd")
                w.wait(GP, "ccd")

            w.op(GP, lambda e: e.collective_compute(
                "AllGather", ALU.bypass, replica_groups=[list(range(NCORE))],
                ins=[cc_in[:, :]], outs=[cc_out[:, :]]), sem="cc")
            w.wait(GP, "cc")
            w.dma(GP, arow_sb[:, :].rearrange("o (r a) -> o r a", r=NCORE),
                  cc_out[None, :, 0:HC], "ccd")
            for i in range(NCORE):
                w.wait(GP, "v")
                w.dma(GP, gtmp_sb[:, :],
                      cc_out[i, HC:].rearrange("(p f) -> p f", p=64), "ccd")
                w.wait(VE, "ccd")
                w.op(VE, lambda e, i=i: e.tensor_single_scalar(
                    ap_row[:, :], arow_sb[0:1, i * HC:(i + 1) * HC], -1.0, ALU.add), sem="v")
                w.op(VE, lambda e, i=i: e.tensor_single_scalar(
                    ap_row[:, :], ap_row[:, :], gm_sb[0:1, i:i + 1], ALU.mult), sem="v")
                w.op(VE, lambda e: e.tensor_single_scalar(
                    ap_row[:, :], ap_row[:, :], 1.0, ALU.add), sem="v")
                w.wait(TE, "v")
                w.op(TE, lambda e: e.matmul(pAbc[0:64, 0:HC], ones_f_sb[0:1, 0:64],
                                            ap_row[0:1, :], start=True, stop=True), sem="t")
                w.wait(VE, "t")
                w.op(VE, lambda e: e.tensor_copy(ap_bc[:, :], pAbc[0:64, 0:HC]), sem="v")
                w.op(VE, lambda e, i=i: e.tensor_single_scalar(
                    gtmp_sb[:, :], gtmp_sb[:, :], gmb_sb[:, i:i + 1], ALU.mult), sem="v")
                for hcn in range(HC):
                    w.op(VE, lambda e, hcn=hcn: e.scalar_tensor_tensor(
                        sin_sb[:, hcn * NDST:(hcn + 1) * NDST],
                        sin_sb[:, hcn * NDST:(hcn + 1) * NDST],
                        ap_bc[:, hcn:hcn + 1],
                        gtmp_sb[:, hcn * NDST:(hcn + 1) * NDST],
                        ALU.mult, ALU.add), sem="v")
            w.op(VE, lambda e: e.tensor_copy(sinb_sb[0:64, :], sin_sb[:, :]), sem="v")
            w.wait(GP, "v")
            w.dma(GP, sinb_sb[64:128, :], sinb_sb[0:64, :], "ccd")

        def ln_sweep(w, read_chunk, nst, write_out):
            raise NotImplementedError

        def phase_a_tile(w: Walk, j):
            t0 = j * TT
            w.wait(VE, "t")
            w.wait(VE, "spq0")
            w.wait(VE, "spq1")
            w.wait(VE, "spv")
            w.wait(VE, "spl")
            w.wait(VE, "spm")
            w.wait(SY, "v")
            w.wait(SY, "s")
            if A_LEVEL >= 2 and not (A_LEVEL == 2 and ALSUB == 1):
                w.dma(SY, cs1_sb[:, :], cs1[:, t0:t0 + TT], "ldcs")
                w.dma(SY, cs2_sb[:, :], cs2[:, t0:t0 + TT], "ldcs")

            guard(w)
            cxa = bigc[:, 0:4096]
            for c in range(DC):
                w.dma(SY, cxa[:, c * TT:(c + 1) * TT],
                      x_fm[c * 128:(c + 1) * 128, t0:t0 + TT], "ldx0")
            w.wait(SC, "ldx0")
            for c in range(DC):
                xc = cxa[:, c * TT:(c + 1) * TT]
                xb = bsc[:, (c % 2) * TT:(c % 2) * TT + TT]
                x2b = bsc[:, 1024 + (c % 2) * TT:1024 + (c % 2) * TT + TT]
                if c >= 2:
                    w.wait(SC, "t")
                w.op(SC, lambda e, xc=xc, xb=xb: e.copy(xb, xc), sem="s")
                w.op(SC, lambda e, xc=xc, x2b=x2b: e.activation(x2b, xc, AF.Square), sem="s")
                w.wait(TE, "s")
                w.op(TE, lambda e, xb=xb, c=c: e.matmul(
                    pB1[0:1, 0:TT], onescol_b_sb[:, :], xb,
                    start=(c == 0), stop=(c == DC - 1)), sem=None)
                w.op(TE, lambda e, x2b=x2b, c=c: e.matmul(
                    pB2[0:1, 0:TT], onescol_b_sb[:, :], x2b,
                    start=(c == 0), stop=(c == DC - 1)), sem="t")
            if A_LEVEL == 1 and ALSUB < 2:
                return
            mu = stats_sb[0:1, 0:TT]
            ex2 = stats_sb[0:1, TT:2 * TT]
            musq = stats_sb[0:1, 2 * TT:3 * TT]
            var = stats_sb[0:1, TT:2 * TT]
            sd = stats_sb[0:1, 2 * TT:3 * TT]
            rstd = stats_sb[0:1, TT:2 * TT]
            murstd = stats_sb[0:1, 2 * TT:3 * TT]
            w.wait(SC, "t")
            w.op(SC, lambda e: e.activation(mu, pB1[0:1, 0:TT], AF.Copy, scale=1.0 / D), sem="s")
            w.op(SC, lambda e: e.activation(ex2, pB2[0:1, 0:TT], AF.Copy, scale=1.0 / D), sem="s")
            w.wait(VE, "s")
            w.op(VE, lambda e: e.tensor_mul(musq, mu, mu), sem="v")
            w.op(VE, lambda e: e.tensor_sub(var, ex2, musq), sem="v")
            w.wait(SC, "v")
            w.op(SC, lambda e: e.activation(sd, var, AF.Sqrt, bias=eps_sb[0:1, 0:1]), sem="s")
            w.wait(VE, "s")
            w.op(VE, lambda e: e.reciprocal(rstd, sd), sem="v")
            w.op(VE, lambda e: e.tensor_mul(murstd, mu, rstd), sem="v")
            if A_LEVEL == 1 and ALSUB < 3:
                return
            w.wait(SY, "v")
            w.dma(SY, sp_stats[0:1, t0:t0 + TT], rstd, "spl")
            w.dma(SY, sp_stats[1:2, t0:t0 + TT], murstd, "spl")
            w.wait(TE, "v")
            w.op(TE, lambda e: e.matmul(pB1[:, 0:TT], ones_f_sb[0:1, :], rstd,
                                        start=True, stop=True), sem=None)
            w.op(TE, lambda e: e.matmul(pB2[:, 0:TT], ones_f_sb[0:1, :], murstd,
                                        start=True, stop=True), sem="t")
            mark_bc = w.mark("t")
            if A_LEVEL == 1 and ALSUB < 4:
                return
            w.wait(VE, "t", mark_bc)
            for c in range(DC):
                xc = cxa[:, c * TT:(c + 1) * TT]
                xnc = biga[:, c * T + t0:c * T + t0 + TT]
                tmp = ctmp[:, (c % 2) * TT:(c % 2) * TT + TT]
                w.op(VE, lambda e, xc=xc, tmp=tmp: e.tensor_mul(tmp, xc, pB1[:, 0:TT]), sem="v")
                w.op(VE, lambda e, tmp=tmp, xnc=xnc: e.tensor_sub(xnc, tmp, pB2[:, 0:TT]), sem="v")
            mark_xn = w.mark("v")
            if A_LEVEL < 2:
                return

            qkv_list = [(None, sp_q), (None, sp_k), (None, None)]
            marks_c = [None] * 24
            vmark_t12 = [0] * 24
            vmark_mn = [0]
            gmarks = [0] * 24

            def a_stage1(i):
                wi, m = divmod(i, DC)
                uof = [UOF["wq"], UOF["wk"], UOF["wv"]][wi]
                ws = wring[:, (i % 2) * 1024:(i % 2) * 1024 + 1024]
                if i >= 2:
                    w.wait(SY, "t", gmarks[i - 2])
                else:
                    w.wait(SY, "t")
                cache_load(w, uof + m, 1, ws, "ldw%d" % (i % 2))
                pm = pMain[i % 2]
                w.wait(TE, "ldw%d" % (i % 2))
                w.wait(TE, "v", mark_xn)
                if i >= 2 and marks_c[i - 2] is not None:
                    sem_n, val = marks_c[i - 2]
                    w.wait(TE, sem_n, val)
                else:
                    guard(w)
                for kc in range(DC):
                    w.op(TE, lambda e, ws=ws, kc=kc, pm=pm: e.matmul(
                        pm[:, 0:TT], ws[:, kc * 128:(kc + 1) * 128],
                        biga[:, kc * T + t0:kc * T + t0 + TT],
                        start=(kc == 0), stop=(kc == DC - 1)),
                        sem="t" if kc == DC - 1 else None)
                tmark = w.mark("t")
                gmarks[i] = tmark
                if wi < 2:
                    t1 = bsc[:, 2048 + (i % 2) * 2 * TT:2048 + (i % 2) * 2 * TT + TT]
                    t2 = bsc[:, 2048 + (i % 2) * 2 * TT + TT:2048 + (i % 2) * 2 * TT + 2 * TT]
                    w.wait(VE, "t", tmark)
                    w.wait(VE, "ldcs")
                    w.op(VE, lambda e, pm=pm, t1=t1: e.tensor_mul(t1, pm[:, 0:TT], cs1_sb[:, :]), sem="v")
                    w.op(VE, lambda e, pm=pm, t2=t2: e.tensor_mul(t2, pm[:, 0:TT], cs2_sb[:, :]), sem="v")
                    vmark_t12[i] = w.mark("v")
                    marks_c[i] = ("v", vmark_t12[i])
                else:
                    vb = bv[:, (i % 2) * TT:(i % 2) * TT + TT]
                    w.wait(SC, "t", tmark)
                    w.op(SC, lambda e, pm=pm, vb=vb: e.copy(vb, pm[:, 0:TT]), sem="s")
                    marks_c[i] = ("s", w.mark("s"))

            def a_stage2(i):
                wi, m = divmod(i, DC)
                _, dst_spill = qkv_list[wi]
                if wi < 2:
                    t1 = bsc[:, 2048 + (i % 2) * 2 * TT:2048 + (i % 2) * 2 * TT + TT]
                    t2 = bsc[:, 2048 + (i % 2) * 2 * TT + TT:2048 + (i % 2) * 2 * TT + 2 * TT]
                    w.wait(TE, "v", vmark_t12[i])
                    w.wait(TE, "v", vmark_mn[0])
                    w.op(TE, lambda e, t1=t1: e.matmul(
                        pRope[:, 0:TT], id_b_sb[:, :], t1, start=True, stop=False), sem=None)
                    w.op(TE, lambda e, t2=t2: e.matmul(
                        pRope[:, 0:TT], swap_sb[:, :], t2, start=False, stop=True), sem="t")
                    rmark = w.mark("t")
                    mn = bsc[:, 0:TT]
                    ex = bsc[:, TT:2 * TT]
                    qk_out = (bq if wi == 0 else bk)[:, (m % 2) * TT:(m % 2) * TT + TT]
                    w.wait(VE, "t", rmark)
                    w.op(VE, lambda e, mn=mn: e.tensor_single_scalar(
                        mn, pRope[:, 0:TT], 0.0, ALU.min), sem="v")
                    w.wait(SC, "v")
                    w.op(SC, lambda e, mn=mn, ex=ex: e.activation(ex, mn, AF.Exp), sem="s")
                    smark = w.mark("s")
                    w.wait(VE, "s", smark)
                    w.wait(VE, "spq%d" % (m % 2))
                    w.op(VE, lambda e, ex=ex, qk_out=qk_out: e.scalar_tensor_tensor(
                        qk_out, pRope[:, 0:TT], 0.0, ex, ALU.max, ALU.add), sem="v")
                    vmark_mn[0] = w.mark("v")
                    w.wait(SY, "v")
                    w.dma(SY, dst_spill[m * 128:(m + 1) * 128, t0:t0 + TT], qk_out,
                          "spq%d" % (m % 2))
                    if wi == 1:
                        for tc in range(NCH):
                            w.wait(TE, "v")
                            w.op(TE, lambda e, qk_out=qk_out, tc=tc: e.transpose(
                                pTp[tc % 2], qk_out[:, tc * 128:(tc + 1) * 128],
                                id_b_sb[:, :]), sem="t")
                            w.wait(VE, "t")
                            w.op(VE, lambda e, tc=tc, m=m: e.tensor_copy(
                                bkt[:, tc * 1024 + m * 128:tc * 1024 + (m + 1) * 128],
                                pTp[tc % 2]), sem="v")
                else:
                    vb = bv[:, (i % 2) * TT:(i % 2) * TT + TT]
                    for tc in range(NCH):
                        w.wait(TE, "s")
                        w.wait(TE, "v")
                        w.op(TE, lambda e, vb=vb, tc=tc: e.transpose(
                            pTp[tc % 2], vb[:, tc * 128:(tc + 1) * 128],
                            id_b_sb[:, :]), sem="t")
                        w.wait(VE, "t")
                        w.op(VE, lambda e, tc=tc, m=m: e.tensor_copy(
                            bva[:, tc * 1040 + 2 * m * NDST:tc * 1040 + (2 * m + 2) * NDST]
                            .rearrange("p (hh dd) -> p hh dd", hh=2)[:, :, 0:Dh],
                            pTp[tc % 2].rearrange("p (hh dd) -> p hh dd", hh=2)), sem="v")

            for step in range(25):
                if step < 24:
                    a_stage1(step)
                if step >= 1:
                    a_stage2(step - 1)
            w.op(VE, lambda e: e.memset(
                bva.rearrange("p (cc hh dd) -> p cc hh dd", cc=NCH, hh=H)[:, :, :, Dh:],
                1.0), sem="v")
            w.wait(SY, "v")
            w.dma(SY, sp_va[t0:t0 + TT, :].rearrange("(cc p) f -> p cc f", p=128),
                  bva.rearrange("p (cc f) -> p cc f", cc=NCH), "spv")
            if A_LEVEL < 3:
                return

            for tc in range(NCH):
                g = j * NCH + tc
                tc0 = t0 + tc * CH
                guard(w)
                for kc in range(DC):
                    w.op(TE, lambda e, kc=kc, tc0=tc0: e.matmul(
                        pBeta, biga[:, kc * T + tc0:kc * T + tc0 + CH],
                        wbeta_sb[:, kc * HC:(kc + 1) * HC],
                        start=(kc == 0), stop=(kc == DC - 1)), sem="t" if kc == DC - 1 else None)
                w.wait(VE, "t")
                w.op(VE, lambda e: e.tensor_add(zrow_sb[:, :], pBeta, bb_bc_sb[:, :]), sem="v")
                w.wait(SC, "v")
                w.op(SC, lambda e: e.activation(zrow_sb[:, :], zrow_sb[:, :], AF.Sigmoid), sem="s")
                w.wait(VE, "s")
                w.op(VE, lambda e: e.tensor_scalar(
                    zrow_sb[:, :], zrow_sb[:, :], 0.001, 0.999, ALU.max, ALU.min), sem="v")
                w.wait(SC, "v")
                w.op(SC, lambda e: e.activation(lb_sb[:, :], zrow_sb[:, :], AF.Ln), sem="s")
                w.wait(TE, "s")
                guard(w)
                w.op(TE, lambda e: e.matmul(pLloc, tri_f_sb[:, :], lb_sb[:, :],
                                            start=True, stop=True), sem=None)
                w.op(TE, lambda e: e.matmul(pLglob, tri_f_sb[:, :], lb_sb[:, :],
                                            start=True, stop=False), sem=None)
                w.op(TE, lambda e: e.matmul(pLglob, ones_f_sb[0:1, :], carry_sb[0:1, :],
                                            start=False, stop=True), sem=None)
                w.op(TE, lambda e: e.matmul(pMisc[0:1, :], onescol_f_sb[:, :], lb_sb[:, :],
                                            start=True, stop=True), sem="t")
                w.wait(VE, "t")
                w.wait(VE, "spl")
                w.op(VE, lambda e: e.tensor_copy(lloc_sb[:, :], pLloc), sem="v")
                w.op(VE, lambda e: e.tensor_copy(lglob_sb[:, :], pLglob), sem="v")
                w.op(VE, lambda e: e.tensor_copy(dsum_sb[:, :], pMisc[0:1, :]), sem="v")
                w.wait(SY, "v")
                w.dma(SY, sp_L[tc0:tc0 + CH, 0:HC], lloc_sb[:, :], "spl")
                w.dma(SY, sp_L[tc0:tc0 + CH, HC:2 * HC], lglob_sb[:, :], "spl")
                w.wait(TE, "v")
                w.op(TE, lambda e: e.transpose(pTcomb[0:HC, 0:CH], lloc_sb[:, :],
                                               id_f_sb[:, :]), sem="t")
                w.wait(VE, "t")
                w.op(VE, lambda e: e.tensor_copy(llocT_sb[:, :], pTcomb[0:HC, 0:CH]), sem="v")
                w.wait(SY, "v")
                w.dma(SY, sp_Lr[g, :, :], llocT_sb[:, :], "spl")
                w.wait(TE, "v")
                w.op(TE, lambda e: e.matmul(pScT, ones_f_sb[0:1, :], dsum_sb[0:1, :],
                                            start=True, stop=True), sem="t")
                w.wait(VE, "t")
                w.op(VE, lambda e: e.tensor_sub(scarg_sb[:, :], pScT, lloc_sb[:, :]), sem="v")
                w.wait(SC, "v")
                w.op(SC, lambda e: e.activation(sc_sb[:, :], scarg_sb[:, :], AF.Exp), sem="s")
                w.op(SC, lambda e: e.activation(adsum_sb[:, :], dsum_sb[:, :], AF.Exp), sem="s")
                w.wait(TE, "s")
                w.op(TE, lambda e: e.matmul(pAbc[0:64, 0:HC], ones_f_sb[0:1, 0:64],
                                            adsum_sb[0:1, :], start=True, stop=True), sem="t")
                w.wait(VE, "t")
                w.op(VE, lambda e: e.tensor_copy(asb_sb[:, :], pAbc[0:64, 0:HC]), sem="v")
                w.op(VE, lambda e: e.tensor_add(carry_sb[:, :], carry_sb[:, :], dsum_sb[:, :]), sem="v")
                if A_LEVEL < 4:
                    continue
                w.wait(SY, "v")
                w.dma(SY, sp_mst[g, :, :], m_bf[:, :], "spm")
                khat_all = bigb[:, 15424:17472]
                w.wait(VE, "s")
                w.op(VE, lambda e, tc=tc: e.tensor_tensor(
                    khat_all[:, :].rearrange("p (h c d) -> p h c d", h=H, c=C),
                    bkt[:, tc * 1024:(tc + 1) * 1024]
                    .rearrange("p (h d) -> p h d", h=H)[:, :, None, :]
                    .broadcast_to([128, H, C, Dh]),
                    sc_sb[:, :].rearrange("p (h c) -> p h c", h=H)[:, :, :, None]
                    .broadcast_to([128, H, C, Dh]),
                    ALU.mult), sem="v")
                dbanks = [pMain[0], pMain[1], ps_scan2[:, :], ps_bcast[:, :], ps_scan1[:, :]]
                w.wait(TE, "v")
                w.wait(TE, "s")
                for hcn in range(HC):
                    h = hcn // 2
                    dv = dbanks[hcn // 7][0:64, (hcn % 7) * 65:(hcn % 7) * 65 + 65]
                    w.op(TE, lambda e, hcn=hcn, h=h, tc=tc, dv=dv: e.matmul(
                        dv, khat_all[:, hcn * Dh:(hcn + 1) * Dh],
                        bva[:, tc * 1040 + h * NDST:tc * 1040 + (h + 1) * NDST],
                        start=True, stop=True), sem="t" if hcn == HC - 1 else None)
                w.wait(VE, "t")
                for hcn in range(HC):
                    dv = dbanks[hcn // 7][0:64, (hcn % 7) * 65:(hcn % 7) * 65 + 65]
                    w.op(VE, lambda e, hcn=hcn, dv=dv: e.scalar_tensor_tensor(
                        m_sb[:, hcn * NDST:(hcn + 1) * NDST],
                        m_sb[:, hcn * NDST:(hcn + 1) * NDST],
                        asb_sb[:, hcn:hcn + 1],
                        dv, ALU.mult, ALU.add), sem="v")
                w.wait(VE, "spm")
                w.op(VE, lambda e: e.tensor_copy(m_bf[:, :], m_sb[:, :]), sem="v")

        def phase_b_tile(w: Walk, j):
            t0 = j * TT
            w.wait(SY, "t")
            w.wait(SY, "v")
            w.wait(SC, "spa")
            w.wait(VE, "spa")
            w.dma(SY, bqt.rearrange("p (cc tt) -> p cc tt", cc=DC),
                  sp_q.rearrange("(cc p) tt -> p cc tt", p=128)[:, :, t0:t0 + TT], "rlq")
            w.dma(SY, bkt2.rearrange("p (cc tt) -> p cc tt", cc=DC),
                  sp_k.rearrange("(cc p) tt -> p cc tt", p=128)[:, :, t0:t0 + TT], "rlq")
            w.dma(SY, bva2.rearrange("p (cc f) -> p cc f", cc=NCH),
                  sp_va[t0:t0 + TT, :].rearrange("(cc p) f -> p cc f", p=128), "rlq")
            for tc in range(NCH):
                g = j * NCH + tc
                tc0 = t0 + tc * CH
                msl = mstl_sb[:, (tc % 2) * MCOLS:(tc % 2) * MCOLS + MCOLS]
                w.wait(SY, "t")
                w.wait(SY, "s")
                w.wait(SY, "v")
                w.dma(SY, msl[0:64, :], sp_mst[g, :, :], "rlm%d" % (tc % 2))
                w.dma(SY, msl[64:128, :], sp_mst[g, :, :], "rlm%d" % (tc % 2))
                w.dma(SY, lloc_sb[:, :], sp_L[tc0:tc0 + CH, 0:HC], "rlm2")
                w.dma(SY, lglob_sb[:, :], sp_L[tc0:tc0 + CH, HC:2 * HC], "rlm2")
                w.dma(SY, lrows_sb[:, :],
                      sp_Lr[g].rearrange("hh tt -> (hh tt)")[None, :], "rlm2")
                w.wait(VE, "rlm2")
                w.op(VE, lambda e: e.tensor_single_scalar(
                    negl_sb[:, :], lloc_sb[:, :], -1.0, ALU.mult), sem="v")
                w.wait(SC, "rlm2")
                w.op(SC, lambda e: e.activation(expll_sb[:, :], lloc_sb[:, :], AF.Exp), sem="s")
                w.op(SC, lambda e: e.activation(explg_sb[:, :], lglob_sb[:, :], AF.Exp), sem="s")
                w.wait(TE, "rlq")
                w.wait(TE, "rlm%d" % (tc % 2))
                w.wait(TE, "rlm2")
                w.wait(TE, "s")
                w.wait(TE, "v")
                bSt = [ps_scan0[:, 0:128], ps_scan2[:, 0:128]]
                bM0 = [ps_scan0[:, 128:256], ps_scan2[:, 128:256]]
                bM1 = [ps_scan0[:, 256:384], ps_scan2[:, 256:384]]
                bNum = [ps_scan1[:, 264:329], ps_main0[:, 264:329]]
                bInt = [ps_scan1[:, 0:260], ps_main0[:, 0:260]]
                vmark_p = [0] * H
                smark2 = [0] * H
                vmark2 = [0] * H

                def b_stage1(h):
                    hc0, hc1 = 2 * h, 2 * h + 1
                    cb = h // 2
                    pb = (h % 2) * 64
                    par = h % 2
                    qf = bqt[pb:pb + 64, cb * TT + tc * CH:cb * TT + tc * CH + CH]
                    kf = bkt2[pb:pb + 64, cb * TT + tc * CH:cb * TT + tc * CH + CH]
                    if h >= 2:
                        w.wait(TE, "v", vmark_p[h - 2])
                    else:
                        w.wait(TE, "v")
                        w.wait(TE, "s")
                    w.op(TE, lambda e: e.matmul(bSt[par], kf, qf, start=True, stop=True), sem=None)
                    for hci, (hcx, bM) in enumerate(((hc0, bM0[par]), (hc1, bM1[par]))):
                        w.op(TE, lambda e, hcx=hcx, bM=bM: e.matmul(
                            bM, ones_f_sb[0:1, :], lrows_sb[0:1, hcx * CH:(hcx + 1) * CH],
                            start=True, stop=False), sem=None)
                        w.op(TE, lambda e, hcx=hcx, bM=bM: e.matmul(
                            bM, lrows_sb[0:1, hcx * CH:(hcx + 1) * CH], negones_sb[0:1, :],
                            start=False, stop=True), sem="t" if hci == 1 else None)
                    tmark_st = w.mark("t")
                    stm = bsc2[:, par * 640:par * 640 + 128]
                    mk01 = ctmp[:, par * 512:par * 512 + 256]
                    a01 = ctmp[:, 1024 + par * 256:1024 + par * 256 + 256]
                    p01 = bsc2[:, par * 640 + 384:par * 640 + 640]
                    pM01 = [ps_scan0[:, 128:384], ps_scan2[:, 128:384]][par]
                    w.wait(VE, "t", tmark_st)
                    w.op(VE, lambda e: e.tensor_mul(stm, bSt[par], tri_b_sb[:, :]), sem="v")
                    w.op(VE, lambda e: e.tensor_single_scalar(
                        a01, pM01, 0.0, ALU.min), sem="v")
                    w.wait(SC, "v")
                    w.op(SC, lambda e: e.activation(mk01, a01, AF.Exp), sem="s")
                    w.wait(VE, "s")
                    w.op(VE, lambda e: e.tensor_tensor(
                        p01.rearrange("p (c f) -> p c f", c=2),
                        stm[:, None, :].broadcast_to([128, 2, 128]),
                        mk01.rearrange("p (c f) -> p c f", c=2),
                        ALU.mult), sem="v")
                    vmark_p[h] = w.mark("v")

                def b_stage2(h):
                    hc0, hc1 = 2 * h, 2 * h + 1
                    cb = h // 2
                    pb = (h % 2) * 64
                    par = h % 2
                    qf = bqt[pb:pb + 64, cb * TT + tc * CH:cb * TT + tc * CH + CH]
                    vaug_h = bva2[:, tc * 1040 + h * NDST:tc * 1040 + (h + 1) * NDST]
                    p0 = bsc2[:, par * 640 + 384:par * 640 + 512]
                    p1 = bsc2[:, par * 640 + 512:par * 640 + 640]
                    w.wait(TE, "v", vmark_p[h])
                    if h >= 2:
                        w.wait(TE, "s", smark2[h - 2])
                        w.wait(TE, "v", vmark2[h - 2])
                    w.op(TE, lambda e: e.matmul(bNum[par], p0, vaug_h, start=True, stop=False), sem=None)
                    w.op(TE, lambda e: e.matmul(bNum[par], p1, vaug_h, start=False, stop=True), sem=None)
                    w.op(TE, lambda e: e.matmul(
                        bInt[par][:, 0:65], qf, msl[pb:pb + 64, hc0 * NDST:(hc0 + 1) * NDST],
                        start=True, stop=True), sem=None)
                    w.op(TE, lambda e: e.matmul(
                        bInt[par][:, 65:130], qf, msl[pb:pb + 64, hc1 * NDST:(hc1 + 1) * NDST],
                        start=True, stop=True), sem=None)
                    w.op(TE, lambda e: e.matmul(
                        bInt[par][:, 130:195], qf, sinb_sb[pb:pb + 64, hc0 * NDST:(hc0 + 1) * NDST],
                        start=True, stop=True), sem=None)
                    w.op(TE, lambda e: e.matmul(
                        bInt[par][:, 195:260], qf, sinb_sb[pb:pb + 64, hc1 * NDST:(hc1 + 1) * NDST],
                        start=True, stop=True), sem="t")
                    tmark_num = w.mark("t")
                    nsb = num_sb[:, par * NDST:(par + 1) * NDST]
                    w.wait(SC, "t", tmark_num)
                    w.op(SC, lambda e: e.copy(nsb, bNum[par]), sem="s")
                    smark2[h] = w.mark("s")
                    w.wait(VE, "s", smark2[h])
                    for (off, esb, hcx) in [(0, expll_sb, hc0), (65, expll_sb, hc1),
                                            (130, explg_sb, hc0), (195, explg_sb, hc1)]:
                        w.op(VE, lambda e, off=off, esb=esb, hcx=hcx: e.scalar_tensor_tensor(
                            nsb, bInt[par][:, off:off + 65], esb[:, hcx:hcx + 1],
                            nsb, ALU.mult, ALU.add), sem="v")
                    w.op(VE, lambda e: e.tensor_single_scalar(
                        rden_sb[:, par * 2:par * 2 + 1], nsb[:, Dh:Dh + 1], EPS, ALU.add), sem="v")
                    w.op(VE, lambda e: e.reciprocal(
                        rden_sb[:, par * 2 + 1:par * 2 + 2], rden_sb[:, par * 2:par * 2 + 1]), sem="v")
                    w.op(VE, lambda e: e.tensor_single_scalar(
                        byt[:, h * Dh:(h + 1) * Dh], nsb[:, 0:Dh],
                        rden_sb[:, par * 2 + 1:par * 2 + 2], ALU.mult), sem="v")
                    vmark2[h] = w.mark("v")

                for step in range(H + 1):
                    if step < H:
                        b_stage1(step)
                    if step >= 1:
                        b_stage2(step - 1)
                for m in range(DC):
                    w.wait(TE, "v")
                    w.op(TE, lambda e, m=m: e.transpose(
                        pTp[m % 2], byt[:, m * 128:(m + 1) * 128], id_b_sb[:, :]), sem="t")
                    w.wait(VE, "t")
                    w.op(VE, lambda e, m=m, tc=tc: e.tensor_copy(
                        byf[:, m * TT + tc * CH:m * TT + tc * CH + CH], pTp[m % 2]), sem="v")
            w.dma(SY, stats_sb[0:1, 0:TT], sp_stats[0:1, t0:t0 + TT], "rlq")
            w.dma(SY, stats_sb[0:1, TT:2 * TT], sp_stats[1:2, t0:t0 + TT], "rlq")
            w.wait(TE, "rlq")
            guard(w)
            w.op(TE, lambda e: e.matmul(pB1[:, 0:TT], ones_f_sb[0:1, :],
                                        stats_sb[0:1, 0:TT], start=True, stop=True), sem=None)
            w.op(TE, lambda e: e.matmul(pB2[:, 0:TT], ones_f_sb[0:1, :],
                                        stats_sb[0:1, TT:2 * TT], start=True, stop=True), sem="t")
            mark_bcq = w.mark("t")
            w.wait(VE, "spa")
            for m in range(DC):
                pm = pMain[m % 2]
                ws = wring[:, (m % 2) * 1024:(m % 2) * 1024 + 1024]
                xc = cx[:, (m % 2) * TT:(m % 2) * TT + TT]
                xnf = ctmp[:, (m % 2) * TT:(m % 2) * TT + TT]
                w.wait(SY, "v")
                w.dma(SY, xc, x_fm[m * 128:(m + 1) * 128, t0:t0 + TT], "ldx%d" % (m % 2))
                w.wait(SY, "t")
                cache_load(w, UOF["wo"] + m, 1, ws, "ldw%d" % (m % 2))
                w.wait(TE, "ldw%d" % (m % 2))
                guard(w)
                for kc in range(DC):
                    w.op(TE, lambda e, kc=kc, pm=pm, ws=ws: e.matmul(
                        pm[:, 0:TT], ws[:, kc * 128:(kc + 1) * 128],
                        byf[:, kc * TT:(kc + 1) * TT],
                        start=(kc == 0), stop=(kc == DC - 1)), sem="t" if kc == DC - 1 else None)
                w.wait(VE, "ldx%d" % (m % 2))
                w.wait(VE, "t", mark_bcq)
                w.op(VE, lambda e, xc=xc, xnf=xnf: e.tensor_mul(xnf, xc, pB1[:, 0:TT]), sem="v")
                w.op(VE, lambda e, xnf=xnf: e.tensor_sub(xnf, xnf, pB2[:, 0:TT]), sem="v")
                w.wait(VE, "t")
                w.op(VE, lambda e, m=m, pm=pm, xnf=xnf: e.scalar_tensor_tensor(
                    ch_[:, m * TT:(m + 1) * TT], pm[:, 0:TT], vecs["bo"][:, m:m + 1],
                    xnf, ALU.add, ALU.add), sem="v")
            guard(w)
            for c in range(DC):
                hb = bsc2[:, 640:640 + TT]
                h2b = bsc2[:, 640 + TT:640 + 2 * TT]
                w.wait(SC, "v")
                if c >= 1:
                    w.wait(SC, "t")
                w.op(SC, lambda e, c=c, hb=hb: e.copy(hb, ch_[:, c * TT:(c + 1) * TT]), sem="s")
                w.op(SC, lambda e, c=c, h2b=h2b: e.activation(
                    h2b, ch_[:, c * TT:(c + 1) * TT], AF.Square), sem="s")
                w.wait(TE, "s")
                w.op(TE, lambda e, hb=hb, c=c: e.matmul(
                    pB1[0:1, 0:TT], onescol_b_sb[:, :], hb,
                    start=(c == 0), stop=(c == DC - 1)), sem=None)
                w.op(TE, lambda e, h2b=h2b, c=c: e.matmul(
                    pB2[0:1, 0:TT], onescol_b_sb[:, :], h2b,
                    start=(c == 0), stop=(c == DC - 1)), sem="t")
            mu = stats_sb[0:1, 0:TT]
            ex2 = stats_sb[0:1, TT:2 * TT]
            musq = stats_sb[0:1, 2 * TT:3 * TT]
            var = stats_sb[0:1, TT:2 * TT]
            sd = stats_sb[0:1, 2 * TT:3 * TT]
            rstd = stats_sb[0:1, TT:2 * TT]
            murstd = stats_sb[0:1, 2 * TT:3 * TT]
            w.wait(SC, "t")
            w.op(SC, lambda e: e.activation(mu, pB1[0:1, 0:TT], AF.Copy, scale=1.0 / D), sem="s")
            w.op(SC, lambda e: e.activation(ex2, pB2[0:1, 0:TT], AF.Copy, scale=1.0 / D), sem="s")
            w.wait(VE, "s")
            w.op(VE, lambda e: e.tensor_mul(musq, mu, mu), sem="v")
            w.op(VE, lambda e: e.tensor_sub(var, ex2, musq), sem="v")
            w.wait(SC, "v")
            w.op(SC, lambda e: e.activation(sd, var, AF.Sqrt, bias=eps_sb[0:1, 0:1]), sem="s")
            w.wait(VE, "s")
            w.op(VE, lambda e: e.reciprocal(rstd, sd), sem="v")
            w.op(VE, lambda e: e.tensor_mul(murstd, mu, rstd), sem="v")
            w.wait(TE, "v")
            w.op(TE, lambda e: e.matmul(pB1[:, 0:TT], ones_f_sb[0:1, :], rstd,
                                        start=True, stop=True), sem=None)
            w.op(TE, lambda e: e.matmul(pB2[:, 0:TT], ones_f_sb[0:1, :], murstd,
                                        start=True, stop=True), sem="t")
            mark_bc2 = w.mark("t")
            for c in range(DC):
                tmp = ctmp[:, 0:TT]
                tmp2 = ctmp[:, TT:2 * TT]
                w.wait(VE, "t", mark_bc2)
                w.wait(VE, "spa")
                w.op(VE, lambda e, c=c, tmp=tmp: e.tensor_mul(
                    tmp, ch_[:, c * TT:(c + 1) * TT], pB1[:, 0:TT]), sem="v")
                w.op(VE, lambda e, tmp=tmp, tmp2=tmp2: e.tensor_sub(tmp2, tmp, pB2[:, 0:TT]), sem="v")
                w.op(VE, lambda e, c=c, tmp2=tmp2: e.tensor_scalar(
                    tmp2, tmp2, vecs["ln_g"][:, c:c + 1], vecs["ln_b"][:, c:c + 1],
                    ALU.mult, ALU.add), sem="v")
                w.wait(SY, "v")
                w.dma(SY, sp_attn[c * 128:(c + 1) * 128, t0:t0 + TT], tmp2, "spa")

        def cache_load(w, unit, count, slot, sem):
            if count == 1:
                w.dma(SY, slot, wc[unit], sem)
            else:
                w.dma(SY, slot.rearrange("p (u f) -> p u f", u=count),
                      wc[unit:unit + count].rearrange("u p f -> p u f"), sem)

        def phase_c_half(w: Walk, hf):
            h0 = hf * HT
            njj = min(4, NTILES)
            npass = 2 if NTILES >= 4 else 1
            ring = Ring(w, depth=4)
            gring = Ring(w, depth=2)
            p4 = [pMain[0], pMain[1], ps_aux[:, :], ps_bcast[:, :]]
            for m in range(DC):
                ws = wring[:, (m % 2) * 1024:(m % 2) * 1024 + 1024]
                gring.wait_sy()
                cache_load(w, UOF["lin0_w"] + m, 1, ws, "ldw%d" % (m % 2))
                w.wait(TE, "ldw%d" % (m % 2))
                for jj in range(njj):
                    i = m * njj + jj
                    pm = p4[i % 4]
                    tl0 = h0 + jj * TT
                    ring.wait_te()
                    for kc in range(DC):
                        w.op(TE, lambda e, kc=kc, pm=pm, ws=ws, tl0=tl0: e.matmul(
                            pm[:, 0:TT], ws[:, kc * 128:(kc + 1) * 128],
                            biga[:, kc * T + tl0:kc * T + tl0 + TT],
                            start=(kc == 0), stop=(kc == DC - 1)), sem="t" if kc == DC - 1 else None)
                    w.wait(SC, "t")
                    w.op(SC, lambda e, m=m, jj=jj, pm=pm: e.activation(
                        bf0[:, m * HT + jj * TT:m * HT + (jj + 1) * TT], pm[:, 0:TT],
                        AF.Identity, bias=vecs["lin0_b"][:, m:m + 1]), sem="s")
                    ring.push()
                gring.push()
            ring = Ring(w, depth=2)
            gring = Ring(w, depth=2)
            pairs = [(pMain[0], pMain[1]), (ps_aux[:, :], ps_bcast[:, :])]
            for m in range(DC):
                ws1 = wring[:, (m % 2) * 1024:(m % 2) * 1024 + 1024]
                ws2 = wring[:, 2048 + (m % 2) * 1024:2048 + (m % 2) * 1024 + 1024]
                gring.wait_sy()
                cache_load(w, UOF["s1w1"] + m, 1, ws1, "ldw%d" % (m % 2))
                cache_load(w, UOF["s1w2"] + m, 1, ws2, "ldw%d" % (m % 2))
                w.wait(TE, "ldw%d" % (m % 2))
                for jj in range(njj):
                    i = m * njj + jj
                    pa, pb = pairs[i % 2]
                    tl0 = h0 + jj * TT
                    ring.wait_te()
                    for kc in range(DC):
                        w.op(TE, lambda e, kc=kc, ws1=ws1, jj=jj, pa=pa: e.matmul(
                            pa[:, 0:TT], ws1[:, kc * 128:(kc + 1) * 128],
                            bf0[:, kc * HT + jj * TT:kc * HT + (jj + 1) * TT],
                            start=(kc == 0), stop=(kc == DC - 1)), sem="t" if kc == DC - 1 else None)
                    for kc in range(DC):
                        w.op(TE, lambda e, kc=kc, ws2=ws2, jj=jj, pb=pb: e.matmul(
                            pb[:, 0:TT], ws2[:, kc * 128:(kc + 1) * 128],
                            bf0[:, kc * HT + jj * TT:kc * HT + (jj + 1) * TT],
                            start=(kc == 0), stop=(kc == DC - 1)), sem="t" if kc == DC - 1 else None)
                    gsl = bgg[:, (i % 2) * TT:(i % 2) * TT + TT]
                    w.wait(SC, "t")
                    w.op(SC, lambda e, m=m, gsl=gsl, pa=pa: e.activation(
                        gsl, pa[:, 0:TT], AF.Silu, bias=vecs["s1b1"][:, m:m + 1]), sem="s")
                    w.wait(VE, "s")
                    w.op(VE, lambda e, m=m, tl0=tl0, gsl=gsl, pb=pb: e.scalar_tensor_tensor(
                        biga[:, m * T + tl0:m * T + tl0 + TT],
                        pb[:, 0:TT], vecs["s1b2"][:, m:m + 1], gsl,
                        ALU.add, ALU.mult), sem="v")
                    ring.push()
                gring.push()
            for jj in range(njj):
                tl0 = h0 + jj * TT
                for tc in range(NCH):
                    tt0 = tl0 + tc * CH
                    guard(w)
                    for kc in range(DC):
                        w.op(TE, lambda e, kc=kc, tt0=tt0: e.matmul(
                            pGate, biga[:, kc * T + tt0:kc * T + tt0 + CH],
                            gatew_sb[:, kc * E:(kc + 1) * E],
                            start=(kc == 0), stop=(kc == DC - 1)), sem="t" if kc == DC - 1 else None)
                    sc_scores = csc[:, 0:8]
                    prob = csc[:, 8:16]
                    work = csc[:, 16:24]
                    sel = csc[:, 24:32]
                    msk = csc[:, 32:40]
                    rowm = csc[:, 40:41]
                    rows = csc[:, 41:42]
                    comb = csc[:, 48:56]
                    w.wait(VE, "t")
                    w.op(VE, lambda e: e.reduce_max(rowm, pGate, mybir.AxisListType.X), sem="v")
                    w.op(VE, lambda e: e.tensor_single_scalar(
                        sc_scores, pGate, rowm[:, 0:1], ALU.subtract), sem="v")
                    w.wait(SC, "v")
                    w.op(SC, lambda e: e.activation(prob, sc_scores, AF.Exp), sem="s")
                    w.wait(VE, "s")
                    w.op(VE, lambda e: e.reduce_sum(rows, prob, mybir.AxisListType.X), sem="v")
                    w.op(VE, lambda e: e.reciprocal(rowm[:, 0:1], rows[:, 0:1]), sem="v")
                    w.op(VE, lambda e: e.tensor_single_scalar(
                        prob, prob, rowm[:, 0:1], ALU.mult), sem="v")
                    w.op(VE, lambda e: e.tensor_copy(work, prob), sem="v")
                    w.op(VE, lambda e: e.memset(sel, 0.0), sem="v")
                    for it in range(4):
                        w.op(VE, lambda e: e.reduce_max(rowm, work, mybir.AxisListType.X), sem="v")
                        w.op(VE, lambda e: e.tensor_single_scalar(
                            msk, work, rowm[:, 0:1], ALU.is_ge), sem="v")
                        w.op(VE, lambda e: e.tensor_max(sel, sel, msk), sem="v")
                        w.op(VE, lambda e: e.scalar_tensor_tensor(
                            work, msk, -1e30, work, ALU.mult, ALU.add), sem="v")
                    w.op(VE, lambda e: e.tensor_mul(comb, prob, sel), sem="v")
                    w.op(VE, lambda e: e.reduce_sum(rows, comb, mybir.AxisListType.X), sem="v")
                    w.op(VE, lambda e: e.reciprocal(rowm[:, 0:1], rows[:, 0:1]), sem="v")
                    w.op(VE, lambda e: e.tensor_single_scalar(
                        comb, comb, rowm[:, 0:1], ALU.mult), sem="v")
                    w.wait(TE, "v")
                    w.op(TE, lambda e, comb=comb: e.transpose(
                        pTcomb[0:8, 0:128], comb, id_f_sb[:, :]), sem="t")
                    w.wait(VE, "t")
                    w.wait(VE, "bounce")
                    w.op(VE, lambda e, tc=tc: e.tensor_copy(
                        combf_sb[:, tc * CH:(tc + 1) * CH], pTcomb[0:8, 0:128]), sem="v")
                w.wait(GP, "v")
                w.dma(GP, sp_comb[:, tl0:tl0 + TT], combf_sb[:, :], "bounce")
            bc_sb = bigb[:, 8192:9216]
            for pp in range(npass):
                p0 = h0 + pp * PT
                w.wait(VE, "t")
                w.op(VE, lambda e: e.memset(routed[:, :], 0.0), sem="v")
                w.wait(GP, "bounce")
                pairs = [(pMain[0], pMain[1]), (ps_aux[:, :], ps_bcast[:, :])]
                for ee in range(E):
                    w.wait(GP, "t")
                    w.dma(GP, combrow_sb[:, :], sp_comb[ee:ee + 1, p0:p0 + PT], "bounce")
                    w.wait(TE, "bounce")
                    w.wait(TE, "v")
                    w.wait(TE, "s")
                    for t2 in range(2):
                        w.op(TE, lambda e, t2=t2: e.matmul(
                            pMain[t2][:, 0:TT], ones_b_sb[0:1, :],
                            combrow_sb[0:1, t2 * TT:(t2 + 1) * TT],
                            start=True, stop=True), sem="t")
                    w.wait(VE, "t")
                    w.op(VE, lambda e: e.tensor_copy(bc_sb[:, 0:TT], pMain[0][:, 0:TT]), sem="v")
                    w.op(VE, lambda e: e.tensor_copy(bc_sb[:, TT:2 * TT], pMain[1][:, 0:TT]), sem="v")
                    ring = Ring(w, depth=2)
                    gring = Ring(w, depth=2)
                    for m in range(DC):
                        ws1 = wring[:, (m % 2) * 1024:(m % 2) * 1024 + 1024]
                        ws3 = wring[:, 2048 + (m % 2) * 1024:2048 + (m % 2) * 1024 + 1024]
                        gring.wait_sy()
                        cache_load(w, UOF["exw1"] + ee * 8 + m, 1, ws1, "ldw%d" % (m % 2))
                        cache_load(w, UOF["exw3"] + ee * 8 + m, 1, ws3, "ldw%d" % (m % 2))
                        w.wait(TE, "ldw%d" % (m % 2))
                        for t2 in range(2):
                            i = m * 2 + t2
                            pa, pb = pairs[i % 2]
                            tl0 = p0 + t2 * TT
                            ring.wait_te()
                            for kc in range(DC):
                                w.op(TE, lambda e, kc=kc, ws1=ws1, tl0=tl0, pa=pa: e.matmul(
                                    pa[:, 0:TT], ws1[:, kc * 128:(kc + 1) * 128],
                                    biga[:, kc * T + tl0:kc * T + tl0 + TT],
                                    start=(kc == 0), stop=(kc == DC - 1)), sem="t" if kc == DC - 1 else None)
                            for kc in range(DC):
                                w.op(TE, lambda e, kc=kc, ws3=ws3, tl0=tl0, pb=pb: e.matmul(
                                    pb[:, 0:TT], ws3[:, kc * 128:(kc + 1) * 128],
                                    biga[:, kc * T + tl0:kc * T + tl0 + TT],
                                    start=(kc == 0), stop=(kc == DC - 1)), sem="t" if kc == DC - 1 else None)
                            gsl = bgg[:, (i % 2) * TT:(i % 2) * TT + TT]
                            tb = btmp[:, (i % 2) * TT:(i % 2) * TT + TT]
                            w.wait(SC, "t")
                            w.op(SC, lambda e, gsl=gsl, pa=pa: e.activation(
                                gsl, pa[:, 0:TT], AF.Silu), sem="s")
                            w.wait(VE, "s")
                            w.op(VE, lambda e, tb=tb, gsl=gsl, pb=pb: e.tensor_mul(
                                tb, pb[:, 0:TT], gsl), sem="v")
                            w.op(VE, lambda e, m=m, t2=t2, tb=tb: e.tensor_mul(
                                bhh[:, m * PT + t2 * TT:m * PT + (t2 + 1) * TT],
                                tb, bc_sb[:, t2 * TT:(t2 + 1) * TT]), sem="v")
                            ring.push()
                        gring.push()
                    ring = Ring(w, depth=4)
                    gring = Ring(w, depth=2)
                    p4 = [pMain[0], pMain[1], ps_aux[:, :], ps_bcast[:, :]]
                    for mo in range(DC):
                        ws2 = wring[:, (mo % 2) * 1024:(mo % 2) * 1024 + 1024]
                        gring.wait_sy()
                        cache_load(w, UOF["exw2"] + ee * 8 + mo, 1, ws2, "ldw%d" % (mo % 2))
                        w.wait(TE, "ldw%d" % (mo % 2))
                        for t2 in range(2):
                            i = mo * 2 + t2
                            pm = p4[i % 4]
                            ring.wait_te()
                            for kc in range(DC):
                                w.op(TE, lambda e, kc=kc, ws2=ws2, t2=t2, pm=pm: e.matmul(
                                    pm[:, 0:TT], ws2[:, kc * 128:(kc + 1) * 128],
                                    bhh[:, kc * PT + t2 * TT:kc * PT + (t2 + 1) * TT],
                                    start=(kc == 0), stop=(kc == DC - 1)), sem="t" if kc == DC - 1 else None)
                            w.wait(VE, "t")
                            w.op(VE, lambda e, mo=mo, t2=t2, pm=pm: e.tensor_add(
                                routed[:, mo * PT + t2 * TT:mo * PT + (t2 + 1) * TT],
                                routed[:, mo * PT + t2 * TT:mo * PT + (t2 + 1) * TT],
                                pm[:, 0:TT]), sem="v")
                            ring.push()
                        gring.push()
                phase_c4_pass(w, p0)

        def phase_c4_pass(w: Walk, p0):
            for t2 in range(2):
                tl0 = p0 + t2 * TT
                ring = Ring(w, depth=2)
                gring = Ring(w, depth=2)
                pairs = [(pMain[0], pMain[1]), (ps_aux[:, :], ps_bcast[:, :])]
                for mh in range(16):
                    ws1 = wring[:, (mh % 2) * 1024:(mh % 2) * 1024 + 1024]
                    ws3 = wring[:, 2048 + (mh % 2) * 1024:2048 + (mh % 2) * 1024 + 1024]
                    gring.wait_sy()
                    cache_load(w, UOF["shw1"] + mh, 1, ws1, "ldw%d" % (mh % 2))
                    cache_load(w, UOF["shw3"] + mh, 1, ws3, "ldw%d" % (mh % 2))
                    w.wait(TE, "ldw%d" % (mh % 2))
                    pa, pb = pairs[mh % 2]
                    ring.wait_te()
                    for kc in range(DC):
                        w.op(TE, lambda e, kc=kc, ws1=ws1, tl0=tl0, pa=pa: e.matmul(
                            pa[:, 0:TT], ws1[:, kc * 128:(kc + 1) * 128],
                            biga[:, kc * T + tl0:kc * T + tl0 + TT],
                            start=(kc == 0), stop=(kc == DC - 1)), sem="t" if kc == DC - 1 else None)
                    for kc in range(DC):
                        w.op(TE, lambda e, kc=kc, ws3=ws3, tl0=tl0, pb=pb: e.matmul(
                            pb[:, 0:TT], ws3[:, kc * 128:(kc + 1) * 128],
                            biga[:, kc * T + tl0:kc * T + tl0 + TT],
                            start=(kc == 0), stop=(kc == DC - 1)), sem="t" if kc == DC - 1 else None)
                    gsl = bgg[:, (mh % 2) * TT:(mh % 2) * TT + TT]
                    w.wait(SC, "t")
                    w.op(SC, lambda e, gsl=gsl, pa=pa: e.activation(gsl, pa[:, 0:TT], AF.Silu), sem="s")
                    w.wait(VE, "s")
                    w.op(VE, lambda e, mh=mh, gsl=gsl, pb=pb: e.tensor_mul(
                        bshh[:, mh * TT:(mh + 1) * TT], pb[:, 0:TT], gsl), sem="v")
                    ring.push()
                    gring.push()
                ring = Ring(w, depth=4)
                gring = Ring(w, depth=2)
                p4 = [pMain[0], pMain[1], ps_aux[:, :], ps_bcast[:, :]]
                for mo in range(DC):
                    ws2 = wring[:, (mo % 2) * 2048:(mo % 2) * 2048 + 2048]
                    gring.wait_sy()
                    cache_load(w, UOF["shw2"] + 2 * mo, 2, ws2, "ldw%d" % (mo % 2))
                    w.wait(TE, "ldw%d" % (mo % 2))
                    pm = p4[mo % 4]
                    ring.wait_te()
                    for kc in range(16):
                        w.op(TE, lambda e, kc=kc, ws2=ws2, pm=pm: e.matmul(
                            pm[:, 0:TT], ws2[:, kc * 128:(kc + 1) * 128],
                            bshh[:, kc * TT:(kc + 1) * TT],
                            start=(kc == 0), stop=(kc == 15)), sem="t" if kc == 15 else None)
                    w.wait(VE, "t")
                    w.op(VE, lambda e, mo=mo, t2=t2, pm=pm: e.tensor_add(
                        bm1[:, t2 * 4096 + mo * TT:t2 * 4096 + (mo + 1) * TT], pm[:, 0:TT],
                        routed[:, mo * PT + t2 * TT:mo * PT + (t2 + 1) * TT]), sem="v")
                    ring.push()
                    gring.push()
            for t2 in range(2):
                tl0 = p0 + t2 * TT
                ring = Ring(w, depth=4)
                gring = Ring(w, depth=2)
                p4 = [pMain[0], pMain[1], ps_aux[:, :], ps_bcast[:, :]]
                for m in range(DC):
                    ws = wring[:, (m % 2) * 1024:(m % 2) * 1024 + 1024]
                    gring.wait_sy()
                    cache_load(w, UOF["lin1_w"] + m, 1, ws, "ldw%d" % (m % 2))
                    w.wait(TE, "ldw%d" % (m % 2))
                    pm = p4[m % 4]
                    ring.wait_te()
                    for kc in range(DC):
                        w.op(TE, lambda e, kc=kc, ws=ws, t2=t2, pm=pm: e.matmul(
                            pm[:, 0:TT], ws[:, kc * 128:(kc + 1) * 128],
                            bm1[:, t2 * 4096 + kc * TT:t2 * 4096 + (kc + 1) * TT],
                            start=(kc == 0), stop=(kc == DC - 1)), sem="t" if kc == DC - 1 else None)
                    w.wait(SC, "t")
                    w.op(SC, lambda e, m=m, pm=pm: e.activation(
                        bm2[:, m * TT:(m + 1) * TT], pm[:, 0:TT], AF.Identity,
                        bias=vecs["lin1_b"][:, m:m + 1]), sem="s")
                    ring.push()
                    gring.push()
                ring = Ring(w, depth=2)
                gring = Ring(w, depth=2)
                pairs = [(pMain[0], pMain[1]), (ps_aux[:, :], ps_bcast[:, :])]
                for m in range(DC):
                    ws1 = wring[:, (m % 2) * 1024:(m % 2) * 1024 + 1024]
                    ws2 = wring[:, 2048 + (m % 2) * 1024:2048 + (m % 2) * 1024 + 1024]
                    gring.wait_sy()
                    cache_load(w, UOF["s2w1"] + m, 1, ws1, "ldw%d" % (m % 2))
                    cache_load(w, UOF["s2w2"] + m, 1, ws2, "ldw%d" % (m % 2))
                    w.wait(TE, "ldw%d" % (m % 2))
                    pa, pb = pairs[m % 2]
                    ring.wait_te()
                    for kc in range(DC):
                        w.op(TE, lambda e, kc=kc, ws1=ws1, pa=pa: e.matmul(
                            pa[:, 0:TT], ws1[:, kc * 128:(kc + 1) * 128],
                            bm2[:, kc * TT:(kc + 1) * TT],
                            start=(kc == 0), stop=(kc == DC - 1)), sem="t" if kc == DC - 1 else None)
                    for kc in range(DC):
                        w.op(TE, lambda e, kc=kc, ws2=ws2, pb=pb: e.matmul(
                            pb[:, 0:TT], ws2[:, kc * 128:(kc + 1) * 128],
                            bm2[:, kc * TT:(kc + 1) * TT],
                            start=(kc == 0), stop=(kc == DC - 1)), sem="t" if kc == DC - 1 else None)
                    gsl = bgg[:, (m % 2) * TT:(m % 2) * TT + TT]
                    w.wait(SC, "t")
                    w.op(SC, lambda e, m=m, gsl=gsl, pa=pa: e.activation(
                        gsl, pa[:, 0:TT], AF.Silu, bias=vecs["s2b1"][:, m:m + 1]), sem="s")
                    w.wait(VE, "s")
                    if m == 0:
                        w.wait(VE, "outd")
                    w.op(VE, lambda e, m=m, gsl=gsl, pb=pb: e.scalar_tensor_tensor(
                        cffn[:, m * TT:(m + 1) * TT], pb[:, 0:TT],
                        vecs["s2b2"][:, m:m + 1], gsl, ALU.add, ALU.mult), sem="v")
                    ring.push()
                    gring.push()
                    xa = cxa[:, 0:TT]
                    at = ctmp2[:, 0:TT]
                    w.wait(SY, "v")
                    w.dma(SY, xa, x_fm[m * 128:(m + 1) * 128, tl0:tl0 + TT], "ldx0")
                    w.dma(SY, at, sp_attn[m * 128:(m + 1) * 128, tl0:tl0 + TT], "ldx0")
                    w.wait(VE, "ldx0")
                    w.op(VE, lambda e, m=m, xa=xa: e.tensor_add(
                        cffn[:, m * TT:(m + 1) * TT], cffn[:, m * TT:(m + 1) * TT], xa), sem="v")
                    w.op(VE, lambda e, m=m, at=at: e.tensor_add(
                        cffn[:, m * TT:(m + 1) * TT], cffn[:, m * TT:(m + 1) * TT], at), sem="v")
                    w.wait(SY, "v")
                    w.dma(SY, out_fm[m * 128:(m + 1) * 128, tl0:tl0 + TT],
                          cffn[:, m * TT:(m + 1) * TT], "outd")

        @block.sync
        def _(e):
            body(Walk("sync", e, sems))

        @block.tensor
        def _(e):
            body(Walk("tensor", e, sems))

        @block.vector
        def _(e):
            body(Walk("vector", e, sems))

        @block.scalar
        def _(e):
            body(Walk("scalar", e, sems))

        @block.gpsimd
        def _(e):
            body(Walk("gpsimd", e, sems))

    return nc


def _rope_tables(core):
    inv = 1.0 / 10000.0 ** (np.arange(0, Dh, 2, dtype=np.float64) / Dh)
    t0 = (core % 4) * T
    t = np.arange(t0, t0 + T, dtype=np.float64)
    fr = t[:, None] * inv[None, :]
    c = np.cos(fr).astype(np.float32).T
    s = np.sin(fr).astype(np.float32).T
    cs1 = np.zeros((128, T), np.float32)
    cs2 = np.zeros((128, T), np.float32)
    for blk in range(2):
        b0 = blk * 64
        cs1[b0:b0 + 32] = c
        cs1[b0 + 32:b0 + 64] = c
        cs2[b0:b0 + 32] = s
        cs2[b0 + 32:b0 + 64] = -s
    return cs1, cs2


_nc_cache = {}


def kernel(**inputs):
    f32 = lambda k: np.ascontiguousarray(np.asarray(inputs[k], np.float32))
    x = f32("L0_attn_x")

    tri = np.triu(np.ones((CH, CH), np.float32))
    ident = np.eye(CH, dtype=np.float32)
    swap = np.zeros((CH, CH), np.float32)
    for mcol in range(CH):
        blk, i = mcol // 64, mcol % 64
        swap[blk * 64 + (i + 32) % 64, mcol] = 1.0
    ones_row = np.ones((1, CH), np.float32)
    ones_col = np.ones((CH, 1), np.float32)

    vec = lambda k: np.ascontiguousarray(f32(k).reshape(DC, 128).T)
    gate_wt = np.ascontiguousarray(
        f32("L0_ffn_gate_w").reshape(E, DC, 128).transpose(2, 1, 0).reshape(128, DC * E))

    common = dict(
        wq=f32("L0_attn_Wq"), wk=f32("L0_attn_Wk"), wv=f32("L0_attn_Wv"),
        wbeta=f32("L0_attn_Wbeta"), wo=f32("L0_attn_Wo"),
        bbeta=f32("L0_attn_bbeta").reshape(1, HC), bo=vec("L0_attn_bo"),
        ln_g=vec("L0_attn_ln_gamma"), ln_b=vec("L0_attn_ln_beta"),
        lin0_w=f32("L0_ffn_lin0_w"), lin0_b=vec("L0_ffn_lin0_b"),
        s1w1=f32("L0_ffn_swi1_w1"), s1b1=vec("L0_ffn_swi1_b1"),
        s1w2=f32("L0_ffn_swi1_w2"), s1b2=vec("L0_ffn_swi1_b2"),
        gate_wt=gate_wt,
        exw1=f32("L0_ffn_exp_w1"), exw3=f32("L0_ffn_exp_w3"), exw2=f32("L0_ffn_exp_w2"),
        shw1=f32("L0_ffn_sh_w1"), shw3=f32("L0_ffn_sh_w3"), shw2=f32("L0_ffn_sh_w2"),
        lin1_w=f32("L0_ffn_lin1_w"), lin1_b=vec("L0_ffn_lin1_b"),
        s2w1=f32("L0_ffn_swi2_w1"), s2b1=vec("L0_ffn_swi2_b1"),
        s2w2=f32("L0_ffn_swi2_w2"), s2b2=vec("L0_ffn_swi2_b2"),
        tri_f=tri, tri_b=tri.astype(_bf), id_b=ident.astype(_bf), id_f=ident,
        swap_b=swap.astype(_bf), ones_f=ones_row, ones_b=ones_row.astype(_bf),
        onescol_f=ones_col, onescol_b=ones_col.astype(_bf),
        negones_f=-ones_row,
    )

    in_maps = []
    for core in range(NCORE):
        b = core // 4
        r = core % 4
        xs = np.ascontiguousarray(x[b, r * T:(r + 1) * T, :].T)
        cs1v, cs2v = _rope_tables(core)
        gm = np.zeros((1, NCORE), np.float32)
        for i in range(NCORE):
            gm[0, i] = 1.0 if (i // 4 == b and i % 4 < r) else 0.0
        m = dict(common)
        m.update(x_fm=xs, cs1=cs1v.astype(_bf), cs2=cs2v.astype(_bf), gmask=gm)
        in_maps.append(m)

    if "prog" not in _nc_cache:
        _nc_cache["prog"] = build_program()
    nc = _nc_cache["prog"]

    import os as _os
    res = run_bass_kernel_spmd(nc, in_maps, core_ids=list(range(NCORE)),
                               trace=bool(_os.environ.get("KTRACE")))
    global LAST_RESULTS
    LAST_RESULTS = res

    out = np.empty((B, S, D), np.float32)
    for core in range(NCORE):
        b = core // 4
        r = core % 4
        out[b, r * T:(r + 1) * T, :] = np.asarray(res.results[core]["out_fm"]).T
    return out


# version 12
# speedup vs baseline: 1.0603x; 1.0455x over previous
import numpy as np
import ml_dtypes
import concourse.bass as bass
import concourse.mybir as mybir
from concourse.bass_utils import run_bass_kernel_spmd

F32 = mybir.dt.float32
BF16 = mybir.dt.bfloat16
AF = mybir.ActivationFunctionType
ALU = mybir.AluOpType

NCORE = 8
B, S, D, H, Dh, C = 2, 16384, 1024, 16, 64, 2
T = 4096
NT = 8
TT = 512
NCH = 4
CH = 128
DC = 8
HC = H * C
E = 8
SHD = 2048
PT = 1024
NHALF = 2
HT = 2048
LN_EPS = 1e-5
EPS = 1e-6
NDST = Dh + 1
MCOLS = HC * NDST

DEBUG = False
PHASES = "AGBC"
CLEVEL = 4
ALSUB = 4
A_LEVEL = 4
NTILES = NT
SHARED_CC = True
STRICT_SELF_WAITS = True
_bf = ml_dtypes.bfloat16
LAST_RESULTS = None


class Walk:

    def __init__(self, eng_name, eng, sems):
        self.eng_name = eng_name
        self.eng = eng
        self.sems = sems
        self.counts = {k: 0 for k in sems}

    def mark(self, sem):
        return self.counts[sem]

    def on(self, eng_name):
        return eng_name == self.eng_name

    def wait(self, eng_name, sem, val=None):
        v = self.counts[sem] if val is None else val
        if self.on(eng_name) and v > 0:
            self.eng.wait_ge(self.sems[sem], v)

    def dma(self, qname, out, in_, sem, **kw):
        self.counts[sem] += 16
        if self.on(qname):
            self.eng.dma_start(out=out, in_=in_, **kw).then_inc(self.sems[sem], 16)

    _SELF = {"vector": "v", "scalar": "s"}

    def op(self, eng_name, fn, sem=None):
        own = self._SELF.get(eng_name) if STRICT_SELF_WAITS else None
        if own is not None and self.on(eng_name) and self.counts[own] > 0:
            self.eng.wait_ge(self.sems[own], self.counts[own])
        if sem is not None:
            self.counts[sem] += 1
        if self.on(eng_name):
            last = fn(self.eng)
            if sem is not None:
                last.then_inc(self.sems[sem], 1)


def build_program():
    nc = bass.Bass("TRN2", target_bir_lowering=False)

    def din(name, shape, dt=F32):
        return nc.dram_tensor(name, shape, dt, kind="ExternalInput").ap()

    x_fm = din("x_fm", [D, T])
    wq = din("wq", [D, D]); wk = din("wk", [D, D]); wv = din("wv", [D, D])
    wbeta = din("wbeta", [D, HC]); wo = din("wo", [D, D])
    bbeta = din("bbeta", [1, HC]); bo = din("bo", [128, DC])
    ln_g = din("ln_g", [128, DC]); ln_b = din("ln_b", [128, DC])
    lin0_w = din("lin0_w", [D, D]); lin0_b = din("lin0_b", [128, DC])
    s1w1 = din("s1w1", [D, D]); s1b1 = din("s1b1", [128, DC])
    s1w2 = din("s1w2", [D, D]); s1b2 = din("s1b2", [128, DC])
    gate_wt = din("gate_wt", [128, DC * E])
    exw1 = din("exw1", [E, D, D]); exw3 = din("exw3", [E, D, D]); exw2 = din("exw2", [E, D, D])
    shw1 = din("shw1", [D, SHD]); shw3 = din("shw3", [D, SHD]); shw2 = din("shw2", [SHD, D])
    lin1_w = din("lin1_w", [D, D]); lin1_b = din("lin1_b", [128, DC])
    s2w1 = din("s2w1", [D, D]); s2b1 = din("s2b1", [128, DC])
    s2w2 = din("s2w2", [D, D]); s2b2 = din("s2b2", [128, DC])
    cs1 = din("cs1", [CH, T], BF16); cs2 = din("cs2", [CH, T], BF16)
    tri_f = din("tri_f", [CH, CH]); tri_b = din("tri_b", [CH, CH], BF16)
    id_b = din("id_b", [CH, CH], BF16); id_f = din("id_f", [CH, CH])
    swap_b = din("swap_b", [CH, CH], BF16)
    ones_f = din("ones_f", [1, CH]); ones_b = din("ones_b", [1, CH], BF16)
    negones_f = din("negones_f", [1, CH])
    onescol_f = din("onescol_f", [CH, 1])
    onescol_b = din("onescol_b", [CH, 1], BF16)
    gmask = din("gmask", [1, NCORE])

    out_fm = nc.dram_tensor("out_fm", [D, T], F32, kind="ExternalOutput").ap()

    skind = "ExternalOutput" if DEBUG else "Internal"

    def scr(name, shape, dt=F32):
        return nc.dram_tensor(name, shape, dt, kind=skind).ap()

    sp_q = scr("sp_q", [D, T], BF16)
    sp_k = scr("sp_k", [D, T], BF16)
    sp_va = scr("sp_va", [T, H * NDST], BF16)
    sp_L = scr("sp_L", [T, 2 * HC])
    sp_Lr = scr("sp_Lr", [T // CH, HC, CH])
    sp_mst = scr("sp_mst", [T // CH, 64, MCOLS], BF16)
    sp_attn = scr("sp_attn", [D, T])
    sp_stats = scr("sp_stats", [2, T])
    wc = nc.dram_tensor("wc", [320, 128, 1024], BF16, kind="Internal").ap()
    UOF = {"wq": 0, "wk": 8, "wv": 16, "wo": 24, "lin0_w": 32, "s1w1": 40,
           "s1w2": 48, "lin1_w": 56, "s2w1": 64, "s2w2": 72,
           "shw1": 80, "shw3": 96, "shw2": 112,
           "exw1": 128, "exw3": 192, "exw2": 256}
    sp_comb = scr("sp_comb", [E, T], BF16)
    cc_in = nc.dram_tensor("cc_in", [1, HC + 64 * MCOLS], F32, kind="Internal").ap()
    dbg_cc = (nc.dram_tensor("dbg_cc", [1, HC + 64 * MCOLS], F32,
                             kind="ExternalOutput").ap() if DEBUG else None)
    cc_out = nc.dram_tensor("cc_out", [NCORE, HC + 64 * MCOLS], F32,
                            addr_space="Shared" if SHARED_CC else "Local",
                            kind="Internal").ap()

    from contextlib import ExitStack
    ctx = ExitStack()

    def mk_sb(name, shape, dt=F32):
        return ctx.enter_context(nc.sbuf_tensor(name, shape, dt))

    with ctx:
        biga = mk_sb("biga", [128, DC * T], BF16)
        bigb = mk_sb("bigb", [128, 19200], BF16)
        bigc = mk_sb("bigc", [128, 8192], F32)
        wring = mk_sb("wring", [128, 4096], BF16)
        m_sb = mk_sb("m_sb", [64, MCOLS], F32)
        sin_sb = mk_sb("sin_sb", [64, MCOLS], F32)
        sinb_sb = mk_sb("sinb_sb", [128, MCOLS], BF16)
        m_bf = mk_sb("m_bf", [64, MCOLS], BF16)
        u64a = mk_sb("u64a", [128, 2 * MCOLS], BF16)
        gtmp_sb = u64a[:, :].bitcast(F32)[0:64, 0:MCOLS]
        arow_sb = mk_sb("arow_sb", [1, NCORE * HC], F32)
        gm_sb = mk_sb("gm_sb", [1, NCORE], F32)
        gmb_sb = mk_sb("gmb_sb", [64, NCORE], F32)
        ap_row = mk_sb("ap_row", [1, HC], F32)
        ap_bc = mk_sb("ap_bc", [64, HC], F32)
        cs1_sb = mk_sb("cs1_sb", [128, TT], BF16)
        cs2_sb = mk_sb("cs2_sb", [128, TT], BF16)
        tri_f_sb = mk_sb("tri_f_sb", [128, CH], F32)
        tri_b_sb = mk_sb("tri_b_sb", [128, CH], BF16)
        id_b_sb = mk_sb("id_b_sb", [128, CH], BF16)
        id_f_sb = mk_sb("id_f_sb", [128, CH], F32)
        swap_sb = mk_sb("swap_sb", [128, CH], BF16)
        ones_f_sb = mk_sb("ones_f_sb", [1, CH], F32)
        negones_sb = mk_sb("negones_sb", [1, CH], F32)
        ones_b_sb = mk_sb("ones_b_sb", [1, CH], BF16)
        onescol_f_sb = mk_sb("onescol_f_sb", [128, 1], F32)
        onescol_b_sb = mk_sb("onescol_b_sb", [128, 1], BF16)
        bbr_sb = mk_sb("bbr_sb", [1, HC], F32)
        bb_bc_sb = mk_sb("bb_bc_sb", [128, HC], F32)
        wbeta_sb = mk_sb("wbeta_sb", [128, DC * HC], BF16)
        gatew_sb = mk_sb("gatew_sb", [128, DC * E], BF16)
        vec_names = ["bo", "ln_g", "ln_b", "lin0_b", "s1b1", "s1b2",
                     "lin1_b", "s2b1", "s2b2"]
        vecs = {n: mk_sb("vec_" + n, [128, DC], F32) for n in vec_names}
        carry_sb = mk_sb("carry_sb", [1, HC], F32)
        dsum_sb = mk_sb("dsum_sb", [1, HC], F32)
        adsum_sb = mk_sb("adsum_sb", [1, HC], F32)
        acore_sb = mk_sb("acore_sb", [1, HC], F32)
        lrows_sb = mk_sb("lrows_sb", [1, HC * CH], F32)
        combrow_sb = mk_sb("combrow_sb", [1, PT], BF16)
        combf_sb = mk_sb("combf_sb", [8, TT], BF16)
        eps_sb = mk_sb("eps_sb", [1, 1], F32)
        dmy_sb = mk_sb("dmy_sb", [1, 1], F32)
        lloc_sb = mk_sb("lloc_sb", [128, HC], F32)
        lglob_sb = mk_sb("lglob_sb", [128, HC], F32)
        negl_sb = mk_sb("negl_sb", [128, HC], F32)
        expll_sb = mk_sb("expll_sb", [128, HC], F32)
        explg_sb = mk_sb("explg_sb", [128, HC], F32)
        sc_sb = mk_sb("sc_sb", [128, HC], F32)
        asb_sb = mk_sb("asb_sb", [64, HC], F32)
        scarg_sb = mk_sb("scarg_sb", [128, HC], F32)
        zrow_sb = mk_sb("zrow_sb", [128, HC], F32)
        lb_sb = mk_sb("lb_sb", [128, HC], F32)
        num_sb = mk_sb("num_sb", [128, 2 * NDST], F32)
        rden_sb = mk_sb("rden_sb", [128, 4], F32)
        llocT_sb = mk_sb("llocT_sb", [HC, CH], F32)
        mstl_sb = u64a[:, :]

        ps_main0 = ctx.enter_context(nc.psum_tensor("ps_main0", [128, 512], F32))
        ps_main1 = ctx.enter_context(nc.psum_tensor("ps_main1", [128, 512], F32))
        ps_aux = ctx.enter_context(nc.psum_tensor("ps_aux", [128, 512], F32))
        ps_scan0 = ctx.enter_context(nc.psum_tensor("ps_scan0", [128, 512], F32))
        ps_scan1 = ctx.enter_context(nc.psum_tensor("ps_scan1", [128, 512], F32))
        ps_scan2 = ctx.enter_context(nc.psum_tensor("ps_scan2", [128, 512], F32))
        ps_tp = ctx.enter_context(nc.psum_tensor("ps_tp", [128, 1024], BF16))
        ps_bcast = ctx.enter_context(nc.psum_tensor("ps_bcast", [128, 512], F32))

        pMain = [ps_main0[:, :], ps_main1[:, :]]
        pB1 = ps_bcast[:, :]
        pB2 = ps_aux[:, :]
        pRope = ps_aux[:, :]
        pSt = [ps_scan0[:, 0:128], ps_scan0[:, 128:256]]
        pBeta = ps_scan0[:, 256:288]
        pGate = ps_scan0[:, 288:296]
        pTcomb = ps_scan0[:, 352:480]
        pM0 = ps_scan1[:, 0:128]
        pM1 = ps_scan1[:, 128:256]
        pScT = ps_scan1[:, 256:288]
        pAbc = ps_scan1[:, 288:320]
        pStt = [ps_scan1[:, 320:385], ps_scan1[:, 385:450]]
        pMisc = ps_scan1[:, 450:482]
        pLloc = ps_scan2[:, 0:32]
        pLglob = ps_scan2[:, 32:64]
        pNum = [ps_scan2[:, 64:129], ps_scan2[:, 129:194]]
        pIntr = ps_scan2[:, 194:454]
        pTp = [ps_tp[:, 0:128], ps_tp[:, 128:256]]
        pBc = [ps_bcast[:, :], ps_aux[:, :]]

        bq = bigb[:, 0:1024]
        bk = bigb[:, 1024:2048]
        bv = bigb[:, 2048:3072]
        bva = bigb[:, 3072:7232]
        bkt = bigb[:, 7232:11328]
        bsc = bigb[:, 11328:15424]
        bqt = bigb[:, 0:4096]
        bkt2 = bigb[:, 4096:8192]
        bva2 = bigb[:, 8192:12352]
        byt = bigb[:, 12352:13376]
        byf = bigb[:, 13376:17472]
        bsc2 = bigb[:, 17472:19200]
        bf0 = bigb[:, 0:16384]
        bgg = bigb[:, 16384:17408]
        btmp = bigb[:, 17408:18432]
        bhh = bigb[:, 0:8192]
        bshh = bigb[:, 0:8192]
        bm1 = bigb[:, 8192:16384]
        bm2 = bigb[:, 0:4096]

        cx = bigc[:, 0:1024]
        ch_ = bigc[:, 1024:5120]
        ctmp = bigc[:, 5120:7168]
        csc = bigc[:, 7168:8192]
        stats_sb = bigc[0:1, 6656:6656 + 3 * TT].rearrange("o f -> o f")
        routed = bigc[:, 0:8192]
        cffn = bigc[:, 0:4096]
        cxa = bigc[:, 4096:5120]
        ctmp2 = bigc[:, 5120:7168]

        sems = {}
        for n in ["ldw0", "ldw1", "lwg0", "lwg1", "ldx0", "ldx1", "ldcs", "ldm", "ldmg",
                  "spq0", "spq1", "spv", "spl", "spm", "spa", "rlq",
                  "rlm0", "rlm1", "rlm2",
                  "cc", "ccd", "bounce", "outd", "pcast", "t", "v", "s", "g"]:
            sems[n] = ctx.enter_context(nc.semaphore(n))

        block = ctx.enter_context(nc.Block())

        TE, VE, SC, SY, GP = "tensor", "vector", "scalar", "sync", "gpsimd"

        def guard(w):
            w.wait(TE, "v")
            w.wait(TE, "s")

        class Ring:

            def __init__(self, w, depth=2):
                self.w = w
                self.depth = depth
                self.hist = []

            def wait_te(self):
                if len(self.hist) >= self.depth:
                    vm, sm, _ = self.hist[-self.depth]
                    self.w.wait(TE, "v", vm)
                    self.w.wait(TE, "s", sm)
                else:
                    guard(self.w)

            def wait_gp(self):
                if len(self.hist) >= self.depth:
                    _, _, tm = self.hist[-self.depth]
                    self.w.wait(GP, "t", tm)
                else:
                    self.w.wait(GP, "t")

            def wait_sy(self):
                if len(self.hist) >= self.depth:
                    _, _, tm = self.hist[-self.depth]
                    self.w.wait(SY, "t", tm)
                else:
                    self.w.wait(SY, "t")

            def push(self):
                self.hist.append((self.w.mark("v"), self.w.mark("s"), self.w.mark("t")))

        def body(w: Walk):
            if PHASES == "MICRO":
                for c in range(DC):
                    w.dma(SY, out_fm[c * 128:(c + 1) * 128, :],
                          x_fm[c * 128:(c + 1) * 128, :], "outd")
                w.wait(SY, "outd")
                return
            w.dma(SY, tri_f_sb[:, :], tri_f, "ldm")
            w.dma(SY, tri_b_sb[:, :], tri_b, "ldm")
            w.dma(SY, id_b_sb[:, :], id_b, "ldm")
            w.dma(SY, id_f_sb[:, :], id_f, "ldm")
            w.dma(SY, swap_sb[:, :], swap_b, "ldm")
            w.dma(SY, ones_f_sb[:, :], ones_f, "ldm")
            w.dma(SY, negones_sb[:, :], negones_f, "ldm")
            w.dma(SY, ones_b_sb[:, :], ones_b, "ldm")
            w.dma(SY, onescol_f_sb[:, :], onescol_f, "ldm")
            w.dma(SY, onescol_b_sb[:, :], onescol_b, "ldm")
            w.dma(SY, gm_sb[:, :], gmask, "ldm")
            w.dma(SY, bbr_sb[:, :], bbeta, "ldm")
            bias_srcs = {"bo": bo, "ln_g": ln_g, "ln_b": ln_b, "lin0_b": lin0_b,
                         "s1b1": s1b1, "s1b2": s1b2, "lin1_b": lin1_b,
                         "s2b1": s2b1, "s2b2": s2b2}
            for n in vec_names:
                w.dma(SY, vecs[n][:, :], bias_srcs[n], "ldm")
            w.dma(GP, wbeta_sb[:, :].rearrange("p (c n) -> p c n", c=DC),
                  wbeta.rearrange("(c p) n -> p c n", p=128), "ldmg")
            w.dma(GP, gatew_sb[:, :], gate_wt, "ldmg")
            def precast_dxd(name, wt, nm):
                for m in range(nm):
                    w.dma(GP, wc[UOF[name] + m].rearrange("p (kc n) -> p kc n", kc=DC),
                          wt.rearrange("(kc p) n -> p kc n", p=128)[:, :, m * 128:(m + 1) * 128],
                          "pcast")
            precast_dxd("wo", wo, 8)
            w.wait(GP, "pcast")
            w.op(GP, lambda e: e.memset(dmy_sb[:, :], 0.0), sem="g")
            g1mark = w.mark("g")
            w.wait(SY, "g", g1mark)
            w.wait(TE, "ldm")
            w.wait(TE, "ldmg")
            w.op(TE, lambda e: e.matmul(pScT, ones_f_sb[0:1, :], bbr_sb[0:1, :],
                                        start=True, stop=True), sem="t")
            w.wait(VE, "t")
            w.wait(VE, "ldm")
            w.op(VE, lambda e: e.tensor_copy(bb_bc_sb[:, :], pScT), sem="v")
            if CLEVEL >= 3:
                w.op(VE, lambda e: e.memset(m_sb[:, :], 0.0), sem="v")
                w.op(VE, lambda e: e.memset(m_bf[:, :], 0.0), sem="v")
                w.op(VE, lambda e: e.memset(sin_sb[:, :], 0.0), sem="v")
                w.op(VE, lambda e: e.memset(carry_sb[:, :], 0.0), sem="v")
                w.op(VE, lambda e: e.memset(eps_sb[:, :], LN_EPS), sem="v")
            if CLEVEL >= 4:
                w.wait(TE, "v")
                w.op(TE, lambda e: e.matmul(pAbc[0:64, 0:NCORE], ones_f_sb[0:1, 0:64],
                                            gm_sb[0:1, :], start=True, stop=True), sem="t")
                w.wait(VE, "t")
                w.op(VE, lambda e: e.tensor_copy(gmb_sb[:, :], pAbc[0:64, 0:NCORE]), sem="v")

            if "A" in PHASES and A_LEVEL >= 1:
                for j in range(NTILES):
                    phase_a_tile(w, j)
            if "G" in PHASES:
                do_gather(w)
            precast_dxd("lin0_w", lin0_w, 8)
            precast_dxd("s1w1", s1w1, 8)
            precast_dxd("s1w2", s1w2, 8)
            precast_dxd("lin1_w", lin1_w, 8)
            precast_dxd("s2w1", s2w1, 8)
            precast_dxd("s2w2", s2w2, 8)
            precast_dxd("shw1", shw1, 16)
            precast_dxd("shw3", shw3, 16)
            for m in range(DC):
                for u in range(2):
                    w.dma(GP, wc[UOF["shw2"] + 2 * m + u].rearrange("p (kc n) -> p kc n", kc=DC),
                          shw2.rearrange("(u kc p) n -> u p kc n", u=2, p=128)[u][:, :, m * 128:(m + 1) * 128],
                          "pcast")
            for ee in range(E):
                for nm, wt in (("exw1", exw1), ("exw3", exw3), ("exw2", exw2)):
                    for m in range(DC):
                        w.dma(GP, wc[UOF[nm] + ee * 8 + m].rearrange("p (kc n) -> p kc n", kc=DC),
                              wt[ee].rearrange("(kc p) n -> p kc n", p=128)[:, :, m * 128:(m + 1) * 128],
                              "pcast")
            w.wait(GP, "pcast")
            w.op(GP, lambda e: e.memset(dmy_sb[:, :], 0.0), sem="g")
            g2mark = w.mark("g")
            if "B" in PHASES:
                for sname in ["spq0", "spq1", "spv", "spl", "spm"]:
                    w.wait(SY, sname)
                w.wait(TE, "ccd")
                w.wait(VE, "ccd")
                for j in range(NTILES):
                    phase_b_tile(w, j)
            if "C" in PHASES:
                w.wait(SY, "g", g2mark)
                w.wait(SY, "spa")
                w.wait(VE, "spa")
                w.wait(SC, "spa")
                w.wait(TE, "spa")
                for hf in range(NHALF if NTILES >= 8 else 1):
                    phase_c_half(w, hf)
            for sname in ["outd", "spq0", "spq1", "spv", "spl", "spa", "spm",
                          "ldx0", "ldx1", "rlq", "rlm0", "rlm1", "rlm2",
                          "ldcs", "ldm", "ldw0", "ldw1"]:
                w.wait(SY, sname)
            for sname in ["ldmg", "ccd", "bounce", "lwg0", "lwg1"]:
                w.wait(GP, sname)

        def do_gather(w: Walk):
            w.wait(SC, "v")
            w.op(SC, lambda e: e.activation(acore_sb[:, :], carry_sb[:, :], AF.Exp), sem="s")
            w.wait(GP, "s")
            w.wait(GP, "v")
            w.dma(GP, cc_in[0:1, 0:HC], acore_sb[:, :], "ccd")
            w.dma(GP, cc_in[0, HC:].rearrange("(p f) -> p f", p=64), m_sb[:, :], "ccd")
            w.wait(GP, "ccd")
            if DEBUG:
                w.dma(GP, dbg_cc, cc_in[:, :], "ccd")
                w.wait(GP, "ccd")

            w.op(GP, lambda e: e.collective_compute(
                "AllGather", ALU.bypass, replica_groups=[list(range(NCORE))],
                ins=[cc_in[:, :]], outs=[cc_out[:, :]]), sem="cc")
            w.wait(GP, "cc")
            w.dma(GP, arow_sb[:, :].rearrange("o (r a) -> o r a", r=NCORE),
                  cc_out[None, :, 0:HC], "ccd")
            for i in range(NCORE):
                w.wait(GP, "v")
                w.dma(GP, gtmp_sb[:, :],
                      cc_out[i, HC:].rearrange("(p f) -> p f", p=64), "ccd")
                w.wait(VE, "ccd")
                w.op(VE, lambda e, i=i: e.tensor_single_scalar(
                    ap_row[:, :], arow_sb[0:1, i * HC:(i + 1) * HC], -1.0, ALU.add), sem="v")
                w.op(VE, lambda e, i=i: e.tensor_single_scalar(
                    ap_row[:, :], ap_row[:, :], gm_sb[0:1, i:i + 1], ALU.mult), sem="v")
                w.op(VE, lambda e: e.tensor_single_scalar(
                    ap_row[:, :], ap_row[:, :], 1.0, ALU.add), sem="v")
                w.wait(TE, "v")
                w.op(TE, lambda e: e.matmul(pAbc[0:64, 0:HC], ones_f_sb[0:1, 0:64],
                                            ap_row[0:1, :], start=True, stop=True), sem="t")
                w.wait(VE, "t")
                w.op(VE, lambda e: e.tensor_copy(ap_bc[:, :], pAbc[0:64, 0:HC]), sem="v")
                w.op(VE, lambda e, i=i: e.tensor_single_scalar(
                    gtmp_sb[:, :], gtmp_sb[:, :], gmb_sb[:, i:i + 1], ALU.mult), sem="v")
                for hcn in range(HC):
                    w.op(VE, lambda e, hcn=hcn: e.scalar_tensor_tensor(
                        sin_sb[:, hcn * NDST:(hcn + 1) * NDST],
                        sin_sb[:, hcn * NDST:(hcn + 1) * NDST],
                        ap_bc[:, hcn:hcn + 1],
                        gtmp_sb[:, hcn * NDST:(hcn + 1) * NDST],
                        ALU.mult, ALU.add), sem="v")
            w.op(VE, lambda e: e.tensor_copy(sinb_sb[0:64, :], sin_sb[:, :]), sem="v")
            w.wait(GP, "v")
            w.dma(GP, sinb_sb[64:128, :], sinb_sb[0:64, :], "ccd")

        def ln_sweep(w, read_chunk, nst, write_out):
            raise NotImplementedError

        def phase_a_tile(w: Walk, j):
            t0 = j * TT
            w.wait(VE, "t")
            w.wait(VE, "spq0")
            w.wait(VE, "spq1")
            w.wait(VE, "spv")
            w.wait(VE, "spl")
            w.wait(VE, "spm")
            w.wait(SY, "v")
            w.wait(SY, "s")
            if A_LEVEL >= 2 and not (A_LEVEL == 2 and ALSUB == 1):
                w.dma(SY, cs1_sb[:, :], cs1[:, t0:t0 + TT], "ldcs")
                w.dma(SY, cs2_sb[:, :], cs2[:, t0:t0 + TT], "ldcs")

            guard(w)
            cxa = bigc[:, 0:4096]
            for c in range(DC):
                w.dma(SY, cxa[:, c * TT:(c + 1) * TT],
                      x_fm[c * 128:(c + 1) * 128, t0:t0 + TT], "ldx0")
            w.wait(SC, "ldx0")
            for c in range(DC):
                xc = cxa[:, c * TT:(c + 1) * TT]
                xb = bsc[:, (c % 2) * TT:(c % 2) * TT + TT]
                x2b = bsc[:, 1024 + (c % 2) * TT:1024 + (c % 2) * TT + TT]
                if c >= 2:
                    w.wait(SC, "t")
                w.op(SC, lambda e, xc=xc, xb=xb: e.copy(xb, xc), sem="s")
                w.op(SC, lambda e, xc=xc, x2b=x2b: e.activation(x2b, xc, AF.Square), sem="s")
                w.wait(TE, "s")
                w.op(TE, lambda e, xb=xb, c=c: e.matmul(
                    pB1[0:1, 0:TT], onescol_b_sb[:, :], xb,
                    start=(c == 0), stop=(c == DC - 1)), sem=None)
                w.op(TE, lambda e, x2b=x2b, c=c: e.matmul(
                    pB2[0:1, 0:TT], onescol_b_sb[:, :], x2b,
                    start=(c == 0), stop=(c == DC - 1)), sem="t")
            if A_LEVEL == 1 and ALSUB < 2:
                return
            mu = stats_sb[0:1, 0:TT]
            ex2 = stats_sb[0:1, TT:2 * TT]
            musq = stats_sb[0:1, 2 * TT:3 * TT]
            var = stats_sb[0:1, TT:2 * TT]
            sd = stats_sb[0:1, 2 * TT:3 * TT]
            rstd = stats_sb[0:1, TT:2 * TT]
            murstd = stats_sb[0:1, 2 * TT:3 * TT]
            w.wait(SC, "t")
            w.op(SC, lambda e: e.activation(mu, pB1[0:1, 0:TT], AF.Copy, scale=1.0 / D), sem="s")
            w.op(SC, lambda e: e.activation(ex2, pB2[0:1, 0:TT], AF.Copy, scale=1.0 / D), sem="s")
            w.wait(VE, "s")
            w.op(VE, lambda e: e.tensor_mul(musq, mu, mu), sem="v")
            w.op(VE, lambda e: e.tensor_sub(var, ex2, musq), sem="v")
            w.wait(SC, "v")
            w.op(SC, lambda e: e.activation(sd, var, AF.Sqrt, bias=eps_sb[0:1, 0:1]), sem="s")
            w.wait(VE, "s")
            w.op(VE, lambda e: e.reciprocal(rstd, sd), sem="v")
            w.op(VE, lambda e: e.tensor_mul(murstd, mu, rstd), sem="v")
            if A_LEVEL == 1 and ALSUB < 3:
                return
            w.wait(SY, "v")
            w.dma(SY, sp_stats[0:1, t0:t0 + TT], rstd, "spl")
            w.dma(SY, sp_stats[1:2, t0:t0 + TT], murstd, "spl")
            w.wait(TE, "v")
            w.op(TE, lambda e: e.matmul(pB1[:, 0:TT], ones_f_sb[0:1, :], rstd,
                                        start=True, stop=True), sem=None)
            w.op(TE, lambda e: e.matmul(pB2[:, 0:TT], ones_f_sb[0:1, :], murstd,
                                        start=True, stop=True), sem="t")
            mark_bc = w.mark("t")
            if A_LEVEL == 1 and ALSUB < 4:
                return
            w.wait(VE, "t", mark_bc)
            for c in range(DC):
                xc = cxa[:, c * TT:(c + 1) * TT]
                xnc = biga[:, c * T + t0:c * T + t0 + TT]
                tmp = ctmp[:, (c % 2) * TT:(c % 2) * TT + TT]
                w.op(VE, lambda e, xc=xc, tmp=tmp: e.tensor_mul(tmp, xc, pB1[:, 0:TT]), sem="v")
                w.op(VE, lambda e, tmp=tmp, xnc=xnc: e.tensor_sub(xnc, tmp, pB2[:, 0:TT]), sem="v")
            mark_xn = w.mark("v")
            if A_LEVEL < 2:
                return

            qkv_list = [(None, sp_q), (None, sp_k), (None, None)]
            marks_c = [None] * 24
            vmark_t12 = [0] * 24
            vmark_mn = [0]
            gmarks = [0] * 24

            def a_stage1(i):
                wi, m = divmod(i, DC)
                wt = [wq, wk, wv][wi]
                ws = wring[:, (i % 2) * 1024:(i % 2) * 1024 + 1024]
                if i >= 2:
                    w.wait(GP, "t", gmarks[i - 2])
                else:
                    w.wait(GP, "t")
                w.dma(GP, ws.rearrange("p (c n) -> p c n", c=DC),
                      wt.rearrange("(c p) n -> p c n", p=128)[:, :, m * 128:(m + 1) * 128],
                      "lwg%d" % (i % 2))
                pm = pMain[i % 2]
                w.wait(TE, "lwg%d" % (i % 2))
                w.wait(TE, "v", mark_xn)
                if i >= 2 and marks_c[i - 2] is not None:
                    sem_n, val = marks_c[i - 2]
                    w.wait(TE, sem_n, val)
                else:
                    guard(w)
                for kc in range(DC):
                    w.op(TE, lambda e, ws=ws, kc=kc, pm=pm: e.matmul(
                        pm[:, 0:TT], ws[:, kc * 128:(kc + 1) * 128],
                        biga[:, kc * T + t0:kc * T + t0 + TT],
                        start=(kc == 0), stop=(kc == DC - 1)),
                        sem="t" if kc == DC - 1 else None)
                tmark = w.mark("t")
                gmarks[i] = tmark
                if wi < 2:
                    t1 = bsc[:, 2048 + (i % 2) * 2 * TT:2048 + (i % 2) * 2 * TT + TT]
                    t2 = bsc[:, 2048 + (i % 2) * 2 * TT + TT:2048 + (i % 2) * 2 * TT + 2 * TT]
                    w.wait(VE, "t", tmark)
                    w.wait(VE, "ldcs")
                    w.op(VE, lambda e, pm=pm, t1=t1: e.tensor_mul(t1, pm[:, 0:TT], cs1_sb[:, :]), sem="v")
                    w.op(VE, lambda e, pm=pm, t2=t2: e.tensor_mul(t2, pm[:, 0:TT], cs2_sb[:, :]), sem="v")
                    vmark_t12[i] = w.mark("v")
                    marks_c[i] = ("v", vmark_t12[i])
                else:
                    vb = bv[:, (i % 2) * TT:(i % 2) * TT + TT]
                    w.wait(SC, "t", tmark)
                    w.op(SC, lambda e, pm=pm, vb=vb: e.copy(vb, pm[:, 0:TT]), sem="s")
                    marks_c[i] = ("s", w.mark("s"))

            def a_stage2(i):
                wi, m = divmod(i, DC)
                _, dst_spill = qkv_list[wi]
                if wi < 2:
                    t1 = bsc[:, 2048 + (i % 2) * 2 * TT:2048 + (i % 2) * 2 * TT + TT]
                    t2 = bsc[:, 2048 + (i % 2) * 2 * TT + TT:2048 + (i % 2) * 2 * TT + 2 * TT]
                    w.wait(TE, "v", vmark_t12[i])
                    w.wait(TE, "v", vmark_mn[0])
                    w.op(TE, lambda e, t1=t1: e.matmul(
                        pRope[:, 0:TT], id_b_sb[:, :], t1, start=True, stop=False), sem=None)
                    w.op(TE, lambda e, t2=t2: e.matmul(
                        pRope[:, 0:TT], swap_sb[:, :], t2, start=False, stop=True), sem="t")
                    rmark = w.mark("t")
                    mn = bsc[:, 0:TT]
                    ex = bsc[:, TT:2 * TT]
                    qk_out = (bq if wi == 0 else bk)[:, (m % 2) * TT:(m % 2) * TT + TT]
                    w.wait(VE, "t", rmark)
                    w.op(VE, lambda e, mn=mn: e.tensor_single_scalar(
                        mn, pRope[:, 0:TT], 0.0, ALU.min), sem="v")
                    w.wait(SC, "v")
                    w.op(SC, lambda e, mn=mn, ex=ex: e.activation(ex, mn, AF.Exp), sem="s")
                    smark = w.mark("s")
                    w.wait(VE, "s", smark)
                    w.wait(VE, "spq%d" % (m % 2))
                    w.op(VE, lambda e, ex=ex, qk_out=qk_out: e.scalar_tensor_tensor(
                        qk_out, pRope[:, 0:TT], 0.0, ex, ALU.max, ALU.add), sem="v")
                    vmark_mn[0] = w.mark("v")
                    w.wait(SY, "v")
                    w.dma(SY, dst_spill[m * 128:(m + 1) * 128, t0:t0 + TT], qk_out,
                          "spq%d" % (m % 2))
                    if wi == 1:
                        for tc in range(NCH):
                            w.wait(TE, "v")
                            w.op(TE, lambda e, qk_out=qk_out, tc=tc: e.transpose(
                                pTp[tc % 2], qk_out[:, tc * 128:(tc + 1) * 128],
                                id_b_sb[:, :]), sem="t")
                            w.wait(VE, "t")
                            w.op(VE, lambda e, tc=tc, m=m: e.tensor_copy(
                                bkt[:, tc * 1024 + m * 128:tc * 1024 + (m + 1) * 128],
                                pTp[tc % 2]), sem="v")
                else:
                    vb = bv[:, (i % 2) * TT:(i % 2) * TT + TT]
                    for tc in range(NCH):
                        w.wait(TE, "s")
                        w.wait(TE, "v")
                        w.op(TE, lambda e, vb=vb, tc=tc: e.transpose(
                            pTp[tc % 2], vb[:, tc * 128:(tc + 1) * 128],
                            id_b_sb[:, :]), sem="t")
                        w.wait(VE, "t")
                        w.op(VE, lambda e, tc=tc, m=m: e.tensor_copy(
                            bva[:, tc * 1040 + 2 * m * NDST:tc * 1040 + (2 * m + 2) * NDST]
                            .rearrange("p (hh dd) -> p hh dd", hh=2)[:, :, 0:Dh],
                            pTp[tc % 2].rearrange("p (hh dd) -> p hh dd", hh=2)), sem="v")

            for step in range(25):
                if step < 24:
                    a_stage1(step)
                if step >= 1:
                    a_stage2(step - 1)
            w.op(VE, lambda e: e.memset(
                bva.rearrange("p (cc hh dd) -> p cc hh dd", cc=NCH, hh=H)[:, :, :, Dh:],
                1.0), sem="v")
            w.wait(SY, "v")
            w.dma(SY, sp_va[t0:t0 + TT, :].rearrange("(cc p) f -> p cc f", p=128),
                  bva.rearrange("p (cc f) -> p cc f", cc=NCH), "spv")
            if A_LEVEL < 3:
                return

            for tc in range(NCH):
                g = j * NCH + tc
                tc0 = t0 + tc * CH
                guard(w)
                for kc in range(DC):
                    w.op(TE, lambda e, kc=kc, tc0=tc0: e.matmul(
                        pBeta, biga[:, kc * T + tc0:kc * T + tc0 + CH],
                        wbeta_sb[:, kc * HC:(kc + 1) * HC],
                        start=(kc == 0), stop=(kc == DC - 1)), sem="t" if kc == DC - 1 else None)
                w.wait(VE, "t")
                w.op(VE, lambda e: e.tensor_add(zrow_sb[:, :], pBeta, bb_bc_sb[:, :]), sem="v")
                w.wait(SC, "v")
                w.op(SC, lambda e: e.activation(zrow_sb[:, :], zrow_sb[:, :], AF.Sigmoid), sem="s")
                w.wait(VE, "s")
                w.op(VE, lambda e: e.tensor_scalar(
                    zrow_sb[:, :], zrow_sb[:, :], 0.001, 0.999, ALU.max, ALU.min), sem="v")
                w.wait(SC, "v")
                w.op(SC, lambda e: e.activation(lb_sb[:, :], zrow_sb[:, :], AF.Ln), sem="s")
                w.wait(TE, "s")
                guard(w)
                w.op(TE, lambda e: e.matmul(pLloc, tri_f_sb[:, :], lb_sb[:, :],
                                            start=True, stop=True), sem=None)
                w.op(TE, lambda e: e.matmul(pLglob, tri_f_sb[:, :], lb_sb[:, :],
                                            start=True, stop=False), sem=None)
                w.op(TE, lambda e: e.matmul(pLglob, ones_f_sb[0:1, :], carry_sb[0:1, :],
                                            start=False, stop=True), sem=None)
                w.op(TE, lambda e: e.matmul(pMisc[0:1, :], onescol_f_sb[:, :], lb_sb[:, :],
                                            start=True, stop=True), sem="t")
                w.wait(VE, "t")
                w.wait(VE, "spl")
                w.op(VE, lambda e: e.tensor_copy(lloc_sb[:, :], pLloc), sem="v")
                w.op(VE, lambda e: e.tensor_copy(lglob_sb[:, :], pLglob), sem="v")
                w.op(VE, lambda e: e.tensor_copy(dsum_sb[:, :], pMisc[0:1, :]), sem="v")
                w.wait(SY, "v")
                w.dma(SY, sp_L[tc0:tc0 + CH, 0:HC], lloc_sb[:, :], "spl")
                w.dma(SY, sp_L[tc0:tc0 + CH, HC:2 * HC], lglob_sb[:, :], "spl")
                w.wait(TE, "v")
                w.op(TE, lambda e: e.transpose(pTcomb[0:HC, 0:CH], lloc_sb[:, :],
                                               id_f_sb[:, :]), sem="t")
                w.wait(VE, "t")
                w.op(VE, lambda e: e.tensor_copy(llocT_sb[:, :], pTcomb[0:HC, 0:CH]), sem="v")
                w.wait(SY, "v")
                w.dma(SY, sp_Lr[g, :, :], llocT_sb[:, :], "spl")
                w.wait(TE, "v")
                w.op(TE, lambda e: e.matmul(pScT, ones_f_sb[0:1, :], dsum_sb[0:1, :],
                                            start=True, stop=True), sem="t")
                w.wait(VE, "t")
                w.op(VE, lambda e: e.tensor_sub(scarg_sb[:, :], pScT, lloc_sb[:, :]), sem="v")
                w.wait(SC, "v")
                w.op(SC, lambda e: e.activation(sc_sb[:, :], scarg_sb[:, :], AF.Exp), sem="s")
                w.op(SC, lambda e: e.activation(adsum_sb[:, :], dsum_sb[:, :], AF.Exp), sem="s")
                w.wait(TE, "s")
                w.op(TE, lambda e: e.matmul(pAbc[0:64, 0:HC], ones_f_sb[0:1, 0:64],
                                            adsum_sb[0:1, :], start=True, stop=True), sem="t")
                w.wait(VE, "t")
                w.op(VE, lambda e: e.tensor_copy(asb_sb[:, :], pAbc[0:64, 0:HC]), sem="v")
                w.op(VE, lambda e: e.tensor_add(carry_sb[:, :], carry_sb[:, :], dsum_sb[:, :]), sem="v")
                if A_LEVEL < 4:
                    continue
                w.wait(SY, "v")
                w.dma(SY, sp_mst[g, :, :], m_bf[:, :], "spm")
                khat_all = bigb[:, 15424:17472]
                w.wait(VE, "s")
                w.op(VE, lambda e, tc=tc: e.tensor_tensor(
                    khat_all[:, :].rearrange("p (h c d) -> p h c d", h=H, c=C),
                    bkt[:, tc * 1024:(tc + 1) * 1024]
                    .rearrange("p (h d) -> p h d", h=H)[:, :, None, :]
                    .broadcast_to([128, H, C, Dh]),
                    sc_sb[:, :].rearrange("p (h c) -> p h c", h=H)[:, :, :, None]
                    .broadcast_to([128, H, C, Dh]),
                    ALU.mult), sem="v")
                dbanks = [pMain[0], pMain[1], ps_scan2[:, :], ps_bcast[:, :], ps_scan1[:, :]]
                w.wait(TE, "v")
                w.wait(TE, "s")
                for hcn in range(HC):
                    h = hcn // 2
                    dv = dbanks[hcn // 7][0:64, (hcn % 7) * 65:(hcn % 7) * 65 + 65]
                    w.op(TE, lambda e, hcn=hcn, h=h, tc=tc, dv=dv: e.matmul(
                        dv, khat_all[:, hcn * Dh:(hcn + 1) * Dh],
                        bva[:, tc * 1040 + h * NDST:tc * 1040 + (h + 1) * NDST],
                        start=True, stop=True), sem="t" if hcn == HC - 1 else None)
                w.wait(VE, "t")
                for hcn in range(HC):
                    dv = dbanks[hcn // 7][0:64, (hcn % 7) * 65:(hcn % 7) * 65 + 65]
                    w.op(VE, lambda e, hcn=hcn, dv=dv: e.scalar_tensor_tensor(
                        m_sb[:, hcn * NDST:(hcn + 1) * NDST],
                        m_sb[:, hcn * NDST:(hcn + 1) * NDST],
                        asb_sb[:, hcn:hcn + 1],
                        dv, ALU.mult, ALU.add), sem="v")
                w.wait(VE, "spm")
                w.op(VE, lambda e: e.tensor_copy(m_bf[:, :], m_sb[:, :]), sem="v")

        def phase_b_tile(w: Walk, j):
            t0 = j * TT
            w.wait(SY, "t")
            w.wait(SY, "v")
            w.wait(SC, "spa")
            w.wait(VE, "spa")
            w.dma(SY, bqt.rearrange("p (cc tt) -> p cc tt", cc=DC),
                  sp_q.rearrange("(cc p) tt -> p cc tt", p=128)[:, :, t0:t0 + TT], "rlq")
            w.dma(SY, bkt2.rearrange("p (cc tt) -> p cc tt", cc=DC),
                  sp_k.rearrange("(cc p) tt -> p cc tt", p=128)[:, :, t0:t0 + TT], "rlq")
            w.dma(SY, bva2.rearrange("p (cc f) -> p cc f", cc=NCH),
                  sp_va[t0:t0 + TT, :].rearrange("(cc p) f -> p cc f", p=128), "rlq")
            for tc in range(NCH):
                g = j * NCH + tc
                tc0 = t0 + tc * CH
                msl = mstl_sb[:, (tc % 2) * MCOLS:(tc % 2) * MCOLS + MCOLS]
                w.wait(SY, "t")
                w.wait(SY, "s")
                w.wait(SY, "v")
                w.dma(SY, msl[0:64, :], sp_mst[g, :, :], "rlm%d" % (tc % 2))
                w.dma(SY, msl[64:128, :], sp_mst[g, :, :], "rlm%d" % (tc % 2))
                w.dma(SY, lloc_sb[:, :], sp_L[tc0:tc0 + CH, 0:HC], "rlm2")
                w.dma(SY, lglob_sb[:, :], sp_L[tc0:tc0 + CH, HC:2 * HC], "rlm2")
                w.dma(SY, lrows_sb[:, :],
                      sp_Lr[g].rearrange("hh tt -> (hh tt)")[None, :], "rlm2")
                w.wait(VE, "rlm2")
                w.op(VE, lambda e: e.tensor_single_scalar(
                    negl_sb[:, :], lloc_sb[:, :], -1.0, ALU.mult), sem="v")
                w.wait(SC, "rlm2")
                w.op(SC, lambda e: e.activation(expll_sb[:, :], lloc_sb[:, :], AF.Exp), sem="s")
                w.op(SC, lambda e: e.activation(explg_sb[:, :], lglob_sb[:, :], AF.Exp), sem="s")
                w.wait(TE, "rlq")
                w.wait(TE, "rlm%d" % (tc % 2))
                w.wait(TE, "rlm2")
                w.wait(TE, "s")
                w.wait(TE, "v")
                bSt = [ps_scan0[:, 0:128], ps_scan2[:, 0:128]]
                bM0 = [ps_scan0[:, 128:256], ps_scan2[:, 128:256]]
                bM1 = [ps_scan0[:, 256:384], ps_scan2[:, 256:384]]
                bNum = [ps_scan1[:, 264:329], ps_main0[:, 264:329]]
                bInt = [ps_scan1[:, 0:260], ps_main0[:, 0:260]]
                vmark_p = [0] * H
                smark2 = [0] * H
                vmark2 = [0] * H

                def b_stage1(h):
                    hc0, hc1 = 2 * h, 2 * h + 1
                    cb = h // 2
                    pb = (h % 2) * 64
                    par = h % 2
                    qf = bqt[pb:pb + 64, cb * TT + tc * CH:cb * TT + tc * CH + CH]
                    kf = bkt2[pb:pb + 64, cb * TT + tc * CH:cb * TT + tc * CH + CH]
                    if h >= 2:
                        w.wait(TE, "v", vmark_p[h - 2])
                    else:
                        w.wait(TE, "v")
                        w.wait(TE, "s")
                    w.op(TE, lambda e: e.matmul(bSt[par], kf, qf, start=True, stop=True), sem=None)
                    for hci, (hcx, bM) in enumerate(((hc0, bM0[par]), (hc1, bM1[par]))):
                        w.op(TE, lambda e, hcx=hcx, bM=bM: e.matmul(
                            bM, ones_f_sb[0:1, :], lrows_sb[0:1, hcx * CH:(hcx + 1) * CH],
                            start=True, stop=False), sem=None)
                        w.op(TE, lambda e, hcx=hcx, bM=bM: e.matmul(
                            bM, lrows_sb[0:1, hcx * CH:(hcx + 1) * CH], negones_sb[0:1, :],
                            start=False, stop=True), sem="t" if hci == 1 else None)
                    tmark_st = w.mark("t")
                    stm = bsc2[:, par * 640:par * 640 + 128]
                    mk01 = ctmp[:, par * 512:par * 512 + 256]
                    a01 = ctmp[:, 1024 + par * 256:1024 + par * 256 + 256]
                    p01 = bsc2[:, par * 640 + 384:par * 640 + 640]
                    pM01 = [ps_scan0[:, 128:384], ps_scan2[:, 128:384]][par]
                    w.wait(VE, "t", tmark_st)
                    w.op(VE, lambda e: e.tensor_mul(stm, bSt[par], tri_b_sb[:, :]), sem="v")
                    w.op(VE, lambda e: e.tensor_single_scalar(
                        a01, pM01, 0.0, ALU.min), sem="v")
                    w.wait(SC, "v")
                    w.op(SC, lambda e: e.activation(mk01, a01, AF.Exp), sem="s")
                    w.wait(VE, "s")
                    w.op(VE, lambda e: e.tensor_tensor(
                        p01.rearrange("p (c f) -> p c f", c=2),
                        stm[:, None, :].broadcast_to([128, 2, 128]),
                        mk01.rearrange("p (c f) -> p c f", c=2),
                        ALU.mult), sem="v")
                    vmark_p[h] = w.mark("v")

                def b_stage2(h):
                    hc0, hc1 = 2 * h, 2 * h + 1
                    cb = h // 2
                    pb = (h % 2) * 64
                    par = h % 2
                    qf = bqt[pb:pb + 64, cb * TT + tc * CH:cb * TT + tc * CH + CH]
                    vaug_h = bva2[:, tc * 1040 + h * NDST:tc * 1040 + (h + 1) * NDST]
                    p0 = bsc2[:, par * 640 + 384:par * 640 + 512]
                    p1 = bsc2[:, par * 640 + 512:par * 640 + 640]
                    w.wait(TE, "v", vmark_p[h])
                    if h >= 2:
                        w.wait(TE, "s", smark2[h - 2])
                        w.wait(TE, "v", vmark2[h - 2])
                    w.op(TE, lambda e: e.matmul(bNum[par], p0, vaug_h, start=True, stop=False), sem=None)
                    w.op(TE, lambda e: e.matmul(bNum[par], p1, vaug_h, start=False, stop=True), sem=None)
                    w.op(TE, lambda e: e.matmul(
                        bInt[par][:, 0:65], qf, msl[pb:pb + 64, hc0 * NDST:(hc0 + 1) * NDST],
                        start=True, stop=True), sem=None)
                    w.op(TE, lambda e: e.matmul(
                        bInt[par][:, 65:130], qf, msl[pb:pb + 64, hc1 * NDST:(hc1 + 1) * NDST],
                        start=True, stop=True), sem=None)
                    w.op(TE, lambda e: e.matmul(
                        bInt[par][:, 130:195], qf, sinb_sb[pb:pb + 64, hc0 * NDST:(hc0 + 1) * NDST],
                        start=True, stop=True), sem=None)
                    w.op(TE, lambda e: e.matmul(
                        bInt[par][:, 195:260], qf, sinb_sb[pb:pb + 64, hc1 * NDST:(hc1 + 1) * NDST],
                        start=True, stop=True), sem="t")
                    tmark_num = w.mark("t")
                    nsb = num_sb[:, par * NDST:(par + 1) * NDST]
                    w.wait(SC, "t", tmark_num)
                    w.op(SC, lambda e: e.copy(nsb, bNum[par]), sem="s")
                    smark2[h] = w.mark("s")
                    w.wait(VE, "s", smark2[h])
                    for (off, esb, hcx) in [(0, expll_sb, hc0), (65, expll_sb, hc1),
                                            (130, explg_sb, hc0), (195, explg_sb, hc1)]:
                        w.op(VE, lambda e, off=off, esb=esb, hcx=hcx: e.scalar_tensor_tensor(
                            nsb, bInt[par][:, off:off + 65], esb[:, hcx:hcx + 1],
                            nsb, ALU.mult, ALU.add), sem="v")
                    w.op(VE, lambda e: e.tensor_single_scalar(
                        rden_sb[:, par * 2:par * 2 + 1], nsb[:, Dh:Dh + 1], EPS, ALU.add), sem="v")
                    w.op(VE, lambda e: e.reciprocal(
                        rden_sb[:, par * 2 + 1:par * 2 + 2], rden_sb[:, par * 2:par * 2 + 1]), sem="v")
                    w.op(VE, lambda e: e.tensor_single_scalar(
                        byt[:, h * Dh:(h + 1) * Dh], nsb[:, 0:Dh],
                        rden_sb[:, par * 2 + 1:par * 2 + 2], ALU.mult), sem="v")
                    vmark2[h] = w.mark("v")

                for step in range(H + 1):
                    if step < H:
                        b_stage1(step)
                    if step >= 1:
                        b_stage2(step - 1)
                for m in range(DC):
                    w.wait(TE, "v")
                    w.op(TE, lambda e, m=m: e.transpose(
                        pTp[m % 2], byt[:, m * 128:(m + 1) * 128], id_b_sb[:, :]), sem="t")
                    w.wait(VE, "t")
                    w.op(VE, lambda e, m=m, tc=tc: e.tensor_copy(
                        byf[:, m * TT + tc * CH:m * TT + tc * CH + CH], pTp[m % 2]), sem="v")
            w.dma(SY, stats_sb[0:1, 0:TT], sp_stats[0:1, t0:t0 + TT], "rlq")
            w.dma(SY, stats_sb[0:1, TT:2 * TT], sp_stats[1:2, t0:t0 + TT], "rlq")
            w.wait(TE, "rlq")
            guard(w)
            w.op(TE, lambda e: e.matmul(pB1[:, 0:TT], ones_f_sb[0:1, :],
                                        stats_sb[0:1, 0:TT], start=True, stop=True), sem=None)
            w.op(TE, lambda e: e.matmul(pB2[:, 0:TT], ones_f_sb[0:1, :],
                                        stats_sb[0:1, TT:2 * TT], start=True, stop=True), sem="t")
            mark_bcq = w.mark("t")
            w.wait(VE, "spa")
            for m in range(DC):
                pm = pMain[m % 2]
                ws = wring[:, (m % 2) * 1024:(m % 2) * 1024 + 1024]
                xc = cx[:, (m % 2) * TT:(m % 2) * TT + TT]
                xnf = ctmp[:, (m % 2) * TT:(m % 2) * TT + TT]
                w.wait(SY, "v")
                w.dma(SY, xc, x_fm[m * 128:(m + 1) * 128, t0:t0 + TT], "ldx%d" % (m % 2))
                w.wait(SY, "t")
                cache_load(w, UOF["wo"] + m, 1, ws, "ldw%d" % (m % 2))
                w.wait(TE, "ldw%d" % (m % 2))
                guard(w)
                for kc in range(DC):
                    w.op(TE, lambda e, kc=kc, pm=pm, ws=ws: e.matmul(
                        pm[:, 0:TT], ws[:, kc * 128:(kc + 1) * 128],
                        byf[:, kc * TT:(kc + 1) * TT],
                        start=(kc == 0), stop=(kc == DC - 1)), sem="t" if kc == DC - 1 else None)
                w.wait(VE, "ldx%d" % (m % 2))
                w.wait(VE, "t", mark_bcq)
                w.op(VE, lambda e, xc=xc, xnf=xnf: e.tensor_mul(xnf, xc, pB1[:, 0:TT]), sem="v")
                w.op(VE, lambda e, xnf=xnf: e.tensor_sub(xnf, xnf, pB2[:, 0:TT]), sem="v")
                w.wait(VE, "t")
                w.op(VE, lambda e, m=m, pm=pm, xnf=xnf: e.scalar_tensor_tensor(
                    ch_[:, m * TT:(m + 1) * TT], pm[:, 0:TT], vecs["bo"][:, m:m + 1],
                    xnf, ALU.add, ALU.add), sem="v")
            guard(w)
            for c in range(DC):
                hb = bsc2[:, 640:640 + TT]
                h2b = bsc2[:, 640 + TT:640 + 2 * TT]
                w.wait(SC, "v")
                if c >= 1:
                    w.wait(SC, "t")
                w.op(SC, lambda e, c=c, hb=hb: e.copy(hb, ch_[:, c * TT:(c + 1) * TT]), sem="s")
                w.op(SC, lambda e, c=c, h2b=h2b: e.activation(
                    h2b, ch_[:, c * TT:(c + 1) * TT], AF.Square), sem="s")
                w.wait(TE, "s")
                w.op(TE, lambda e, hb=hb, c=c: e.matmul(
                    pB1[0:1, 0:TT], onescol_b_sb[:, :], hb,
                    start=(c == 0), stop=(c == DC - 1)), sem=None)
                w.op(TE, lambda e, h2b=h2b, c=c: e.matmul(
                    pB2[0:1, 0:TT], onescol_b_sb[:, :], h2b,
                    start=(c == 0), stop=(c == DC - 1)), sem="t")
            mu = stats_sb[0:1, 0:TT]
            ex2 = stats_sb[0:1, TT:2 * TT]
            musq = stats_sb[0:1, 2 * TT:3 * TT]
            var = stats_sb[0:1, TT:2 * TT]
            sd = stats_sb[0:1, 2 * TT:3 * TT]
            rstd = stats_sb[0:1, TT:2 * TT]
            murstd = stats_sb[0:1, 2 * TT:3 * TT]
            w.wait(SC, "t")
            w.op(SC, lambda e: e.activation(mu, pB1[0:1, 0:TT], AF.Copy, scale=1.0 / D), sem="s")
            w.op(SC, lambda e: e.activation(ex2, pB2[0:1, 0:TT], AF.Copy, scale=1.0 / D), sem="s")
            w.wait(VE, "s")
            w.op(VE, lambda e: e.tensor_mul(musq, mu, mu), sem="v")
            w.op(VE, lambda e: e.tensor_sub(var, ex2, musq), sem="v")
            w.wait(SC, "v")
            w.op(SC, lambda e: e.activation(sd, var, AF.Sqrt, bias=eps_sb[0:1, 0:1]), sem="s")
            w.wait(VE, "s")
            w.op(VE, lambda e: e.reciprocal(rstd, sd), sem="v")
            w.op(VE, lambda e: e.tensor_mul(murstd, mu, rstd), sem="v")
            w.wait(TE, "v")
            w.op(TE, lambda e: e.matmul(pB1[:, 0:TT], ones_f_sb[0:1, :], rstd,
                                        start=True, stop=True), sem=None)
            w.op(TE, lambda e: e.matmul(pB2[:, 0:TT], ones_f_sb[0:1, :], murstd,
                                        start=True, stop=True), sem="t")
            mark_bc2 = w.mark("t")
            for c in range(DC):
                tmp = ctmp[:, 0:TT]
                tmp2 = ctmp[:, TT:2 * TT]
                w.wait(VE, "t", mark_bc2)
                w.wait(VE, "spa")
                w.op(VE, lambda e, c=c, tmp=tmp: e.tensor_mul(
                    tmp, ch_[:, c * TT:(c + 1) * TT], pB1[:, 0:TT]), sem="v")
                w.op(VE, lambda e, tmp=tmp, tmp2=tmp2: e.tensor_sub(tmp2, tmp, pB2[:, 0:TT]), sem="v")
                w.op(VE, lambda e, c=c, tmp2=tmp2: e.tensor_scalar(
                    tmp2, tmp2, vecs["ln_g"][:, c:c + 1], vecs["ln_b"][:, c:c + 1],
                    ALU.mult, ALU.add), sem="v")
                w.wait(SY, "v")
                w.dma(SY, sp_attn[c * 128:(c + 1) * 128, t0:t0 + TT], tmp2, "spa")

        def cache_load(w, unit, count, slot, sem):
            if count == 1:
                w.dma(SY, slot, wc[unit], sem)
            else:
                w.dma(SY, slot.rearrange("p (u f) -> p u f", u=count),
                      wc[unit:unit + count].rearrange("u p f -> p u f"), sem)

        def phase_c_half(w: Walk, hf):
            h0 = hf * HT
            njj = min(4, NTILES)
            npass = 2 if NTILES >= 4 else 1
            ring = Ring(w, depth=4)
            gring = Ring(w, depth=2)
            p4 = [pMain[0], pMain[1], ps_aux[:, :], ps_bcast[:, :]]
            for m in range(DC):
                ws = wring[:, (m % 2) * 1024:(m % 2) * 1024 + 1024]
                gring.wait_sy()
                cache_load(w, UOF["lin0_w"] + m, 1, ws, "ldw%d" % (m % 2))
                w.wait(TE, "ldw%d" % (m % 2))
                for jj in range(njj):
                    i = m * njj + jj
                    pm = p4[i % 4]
                    tl0 = h0 + jj * TT
                    ring.wait_te()
                    for kc in range(DC):
                        w.op(TE, lambda e, kc=kc, pm=pm, ws=ws, tl0=tl0: e.matmul(
                            pm[:, 0:TT], ws[:, kc * 128:(kc + 1) * 128],
                            biga[:, kc * T + tl0:kc * T + tl0 + TT],
                            start=(kc == 0), stop=(kc == DC - 1)), sem="t" if kc == DC - 1 else None)
                    w.wait(SC, "t")
                    w.op(SC, lambda e, m=m, jj=jj, pm=pm: e.activation(
                        bf0[:, m * HT + jj * TT:m * HT + (jj + 1) * TT], pm[:, 0:TT],
                        AF.Identity, bias=vecs["lin0_b"][:, m:m + 1]), sem="s")
                    ring.push()
                gring.push()
            ring = Ring(w, depth=2)
            gring = Ring(w, depth=2)
            pairs = [(pMain[0], pMain[1]), (ps_aux[:, :], ps_bcast[:, :])]
            for m in range(DC):
                ws1 = wring[:, (m % 2) * 1024:(m % 2) * 1024 + 1024]
                ws2 = wring[:, 2048 + (m % 2) * 1024:2048 + (m % 2) * 1024 + 1024]
                gring.wait_sy()
                cache_load(w, UOF["s1w1"] + m, 1, ws1, "ldw%d" % (m % 2))
                cache_load(w, UOF["s1w2"] + m, 1, ws2, "ldw%d" % (m % 2))
                w.wait(TE, "ldw%d" % (m % 2))
                for jj in range(njj):
                    i = m * njj + jj
                    pa, pb = pairs[i % 2]
                    tl0 = h0 + jj * TT
                    ring.wait_te()
                    for kc in range(DC):
                        w.op(TE, lambda e, kc=kc, ws1=ws1, jj=jj, pa=pa: e.matmul(
                            pa[:, 0:TT], ws1[:, kc * 128:(kc + 1) * 128],
                            bf0[:, kc * HT + jj * TT:kc * HT + (jj + 1) * TT],
                            start=(kc == 0), stop=(kc == DC - 1)), sem="t" if kc == DC - 1 else None)
                    for kc in range(DC):
                        w.op(TE, lambda e, kc=kc, ws2=ws2, jj=jj, pb=pb: e.matmul(
                            pb[:, 0:TT], ws2[:, kc * 128:(kc + 1) * 128],
                            bf0[:, kc * HT + jj * TT:kc * HT + (jj + 1) * TT],
                            start=(kc == 0), stop=(kc == DC - 1)), sem="t" if kc == DC - 1 else None)
                    gsl = bgg[:, (i % 2) * TT:(i % 2) * TT + TT]
                    w.wait(SC, "t")
                    w.op(SC, lambda e, m=m, gsl=gsl, pa=pa: e.activation(
                        gsl, pa[:, 0:TT], AF.Silu, bias=vecs["s1b1"][:, m:m + 1]), sem="s")
                    w.wait(VE, "s")
                    w.op(VE, lambda e, m=m, tl0=tl0, gsl=gsl, pb=pb: e.scalar_tensor_tensor(
                        biga[:, m * T + tl0:m * T + tl0 + TT],
                        pb[:, 0:TT], vecs["s1b2"][:, m:m + 1], gsl,
                        ALU.add, ALU.mult), sem="v")
                    ring.push()
                gring.push()
            for jj in range(njj):
                tl0 = h0 + jj * TT
                for tc in range(NCH):
                    tt0 = tl0 + tc * CH
                    guard(w)
                    for kc in range(DC):
                        w.op(TE, lambda e, kc=kc, tt0=tt0: e.matmul(
                            pGate, biga[:, kc * T + tt0:kc * T + tt0 + CH],
                            gatew_sb[:, kc * E:(kc + 1) * E],
                            start=(kc == 0), stop=(kc == DC - 1)), sem="t" if kc == DC - 1 else None)
                    sc_scores = csc[:, 0:8]
                    prob = csc[:, 8:16]
                    work = csc[:, 16:24]
                    sel = csc[:, 24:32]
                    msk = csc[:, 32:40]
                    rowm = csc[:, 40:41]
                    rows = csc[:, 41:42]
                    comb = csc[:, 48:56]
                    w.wait(VE, "t")
                    w.op(VE, lambda e: e.reduce_max(rowm, pGate, mybir.AxisListType.X), sem="v")
                    w.op(VE, lambda e: e.tensor_single_scalar(
                        sc_scores, pGate, rowm[:, 0:1], ALU.subtract), sem="v")
                    w.wait(SC, "v")
                    w.op(SC, lambda e: e.activation(prob, sc_scores, AF.Exp), sem="s")
                    w.wait(VE, "s")
                    w.op(VE, lambda e: e.reduce_sum(rows, prob, mybir.AxisListType.X), sem="v")
                    w.op(VE, lambda e: e.reciprocal(rowm[:, 0:1], rows[:, 0:1]), sem="v")
                    w.op(VE, lambda e: e.tensor_single_scalar(
                        prob, prob, rowm[:, 0:1], ALU.mult), sem="v")
                    w.op(VE, lambda e: e.tensor_copy(work, prob), sem="v")
                    w.op(VE, lambda e: e.memset(sel, 0.0), sem="v")
                    for it in range(4):
                        w.op(VE, lambda e: e.reduce_max(rowm, work, mybir.AxisListType.X), sem="v")
                        w.op(VE, lambda e: e.tensor_single_scalar(
                            msk, work, rowm[:, 0:1], ALU.is_ge), sem="v")
                        w.op(VE, lambda e: e.tensor_max(sel, sel, msk), sem="v")
                        w.op(VE, lambda e: e.scalar_tensor_tensor(
                            work, msk, -1e30, work, ALU.mult, ALU.add), sem="v")
                    w.op(VE, lambda e: e.tensor_mul(comb, prob, sel), sem="v")
                    w.op(VE, lambda e: e.reduce_sum(rows, comb, mybir.AxisListType.X), sem="v")
                    w.op(VE, lambda e: e.reciprocal(rowm[:, 0:1], rows[:, 0:1]), sem="v")
                    w.op(VE, lambda e: e.tensor_single_scalar(
                        comb, comb, rowm[:, 0:1], ALU.mult), sem="v")
                    w.wait(TE, "v")
                    w.op(TE, lambda e, comb=comb: e.transpose(
                        pTcomb[0:8, 0:128], comb, id_f_sb[:, :]), sem="t")
                    w.wait(VE, "t")
                    w.wait(VE, "bounce")
                    w.op(VE, lambda e, tc=tc: e.tensor_copy(
                        combf_sb[:, tc * CH:(tc + 1) * CH], pTcomb[0:8, 0:128]), sem="v")
                w.wait(GP, "v")
                w.dma(GP, sp_comb[:, tl0:tl0 + TT], combf_sb[:, :], "bounce")
            bc_sb = bigb[:, 8192:9216]
            for pp in range(npass):
                p0 = h0 + pp * PT
                w.wait(VE, "t")
                w.op(VE, lambda e: e.memset(routed[:, :], 0.0), sem="v")
                w.wait(GP, "bounce")
                pairs = [(pMain[0], pMain[1]), (ps_aux[:, :], ps_bcast[:, :])]
                for ee in range(E):
                    w.wait(GP, "t")
                    w.dma(GP, combrow_sb[:, :], sp_comb[ee:ee + 1, p0:p0 + PT], "bounce")
                    w.wait(TE, "bounce")
                    w.wait(TE, "v")
                    w.wait(TE, "s")
                    for t2 in range(2):
                        w.op(TE, lambda e, t2=t2: e.matmul(
                            pMain[t2][:, 0:TT], ones_b_sb[0:1, :],
                            combrow_sb[0:1, t2 * TT:(t2 + 1) * TT],
                            start=True, stop=True), sem="t")
                    w.wait(VE, "t")
                    w.op(VE, lambda e: e.tensor_copy(bc_sb[:, 0:TT], pMain[0][:, 0:TT]), sem="v")
                    w.op(VE, lambda e: e.tensor_copy(bc_sb[:, TT:2 * TT], pMain[1][:, 0:TT]), sem="v")
                    ring = Ring(w, depth=2)
                    gring = Ring(w, depth=2)
                    for m in range(DC):
                        ws1 = wring[:, (m % 2) * 1024:(m % 2) * 1024 + 1024]
                        ws3 = wring[:, 2048 + (m % 2) * 1024:2048 + (m % 2) * 1024 + 1024]
                        gring.wait_sy()
                        cache_load(w, UOF["exw1"] + ee * 8 + m, 1, ws1, "ldw%d" % (m % 2))
                        cache_load(w, UOF["exw3"] + ee * 8 + m, 1, ws3, "ldw%d" % (m % 2))
                        w.wait(TE, "ldw%d" % (m % 2))
                        for t2 in range(2):
                            i = m * 2 + t2
                            pa, pb = pairs[i % 2]
                            tl0 = p0 + t2 * TT
                            ring.wait_te()
                            for kc in range(DC):
                                w.op(TE, lambda e, kc=kc, ws1=ws1, tl0=tl0, pa=pa: e.matmul(
                                    pa[:, 0:TT], ws1[:, kc * 128:(kc + 1) * 128],
                                    biga[:, kc * T + tl0:kc * T + tl0 + TT],
                                    start=(kc == 0), stop=(kc == DC - 1)), sem="t" if kc == DC - 1 else None)
                            for kc in range(DC):
                                w.op(TE, lambda e, kc=kc, ws3=ws3, tl0=tl0, pb=pb: e.matmul(
                                    pb[:, 0:TT], ws3[:, kc * 128:(kc + 1) * 128],
                                    biga[:, kc * T + tl0:kc * T + tl0 + TT],
                                    start=(kc == 0), stop=(kc == DC - 1)), sem="t" if kc == DC - 1 else None)
                            gsl = bgg[:, (i % 2) * TT:(i % 2) * TT + TT]
                            tb = btmp[:, (i % 2) * TT:(i % 2) * TT + TT]
                            w.wait(SC, "t")
                            w.op(SC, lambda e, gsl=gsl, pa=pa: e.activation(
                                gsl, pa[:, 0:TT], AF.Silu), sem="s")
                            w.wait(VE, "s")
                            w.op(VE, lambda e, tb=tb, gsl=gsl, pb=pb: e.tensor_mul(
                                tb, pb[:, 0:TT], gsl), sem="v")
                            w.op(VE, lambda e, m=m, t2=t2, tb=tb: e.tensor_mul(
                                bhh[:, m * PT + t2 * TT:m * PT + (t2 + 1) * TT],
                                tb, bc_sb[:, t2 * TT:(t2 + 1) * TT]), sem="v")
                            ring.push()
                        gring.push()
                    ring = Ring(w, depth=4)
                    gring = Ring(w, depth=2)
                    p4 = [pMain[0], pMain[1], ps_aux[:, :], ps_bcast[:, :]]
                    for mo in range(DC):
                        ws2 = wring[:, (mo % 2) * 1024:(mo % 2) * 1024 + 1024]
                        gring.wait_sy()
                        cache_load(w, UOF["exw2"] + ee * 8 + mo, 1, ws2, "ldw%d" % (mo % 2))
                        w.wait(TE, "ldw%d" % (mo % 2))
                        for t2 in range(2):
                            i = mo * 2 + t2
                            pm = p4[i % 4]
                            ring.wait_te()
                            for kc in range(DC):
                                w.op(TE, lambda e, kc=kc, ws2=ws2, t2=t2, pm=pm: e.matmul(
                                    pm[:, 0:TT], ws2[:, kc * 128:(kc + 1) * 128],
                                    bhh[:, kc * PT + t2 * TT:kc * PT + (t2 + 1) * TT],
                                    start=(kc == 0), stop=(kc == DC - 1)), sem="t" if kc == DC - 1 else None)
                            w.wait(VE, "t")
                            w.op(VE, lambda e, mo=mo, t2=t2, pm=pm: e.tensor_add(
                                routed[:, mo * PT + t2 * TT:mo * PT + (t2 + 1) * TT],
                                routed[:, mo * PT + t2 * TT:mo * PT + (t2 + 1) * TT],
                                pm[:, 0:TT]), sem="v")
                            ring.push()
                        gring.push()
                phase_c4_pass(w, p0)

        def phase_c4_pass(w: Walk, p0):
            for t2 in range(2):
                tl0 = p0 + t2 * TT
                ring = Ring(w, depth=2)
                gring = Ring(w, depth=2)
                pairs = [(pMain[0], pMain[1]), (ps_aux[:, :], ps_bcast[:, :])]
                for mh in range(16):
                    ws1 = wring[:, (mh % 2) * 1024:(mh % 2) * 1024 + 1024]
                    ws3 = wring[:, 2048 + (mh % 2) * 1024:2048 + (mh % 2) * 1024 + 1024]
                    gring.wait_sy()
                    cache_load(w, UOF["shw1"] + mh, 1, ws1, "ldw%d" % (mh % 2))
                    cache_load(w, UOF["shw3"] + mh, 1, ws3, "ldw%d" % (mh % 2))
                    w.wait(TE, "ldw%d" % (mh % 2))
                    pa, pb = pairs[mh % 2]
                    ring.wait_te()
                    for kc in range(DC):
                        w.op(TE, lambda e, kc=kc, ws1=ws1, tl0=tl0, pa=pa: e.matmul(
                            pa[:, 0:TT], ws1[:, kc * 128:(kc + 1) * 128],
                            biga[:, kc * T + tl0:kc * T + tl0 + TT],
                            start=(kc == 0), stop=(kc == DC - 1)), sem="t" if kc == DC - 1 else None)
                    for kc in range(DC):
                        w.op(TE, lambda e, kc=kc, ws3=ws3, tl0=tl0, pb=pb: e.matmul(
                            pb[:, 0:TT], ws3[:, kc * 128:(kc + 1) * 128],
                            biga[:, kc * T + tl0:kc * T + tl0 + TT],
                            start=(kc == 0), stop=(kc == DC - 1)), sem="t" if kc == DC - 1 else None)
                    gsl = bgg[:, (mh % 2) * TT:(mh % 2) * TT + TT]
                    w.wait(SC, "t")
                    w.op(SC, lambda e, gsl=gsl, pa=pa: e.activation(gsl, pa[:, 0:TT], AF.Silu), sem="s")
                    w.wait(VE, "s")
                    w.op(VE, lambda e, mh=mh, gsl=gsl, pb=pb: e.tensor_mul(
                        bshh[:, mh * TT:(mh + 1) * TT], pb[:, 0:TT], gsl), sem="v")
                    ring.push()
                    gring.push()
                ring = Ring(w, depth=4)
                gring = Ring(w, depth=2)
                p4 = [pMain[0], pMain[1], ps_aux[:, :], ps_bcast[:, :]]
                for mo in range(DC):
                    ws2 = wring[:, (mo % 2) * 2048:(mo % 2) * 2048 + 2048]
                    gring.wait_sy()
                    cache_load(w, UOF["shw2"] + 2 * mo, 2, ws2, "ldw%d" % (mo % 2))
                    w.wait(TE, "ldw%d" % (mo % 2))
                    pm = p4[mo % 4]
                    ring.wait_te()
                    for kc in range(16):
                        w.op(TE, lambda e, kc=kc, ws2=ws2, pm=pm: e.matmul(
                            pm[:, 0:TT], ws2[:, kc * 128:(kc + 1) * 128],
                            bshh[:, kc * TT:(kc + 1) * TT],
                            start=(kc == 0), stop=(kc == 15)), sem="t" if kc == 15 else None)
                    w.wait(VE, "t")
                    w.op(VE, lambda e, mo=mo, t2=t2, pm=pm: e.tensor_add(
                        bm1[:, t2 * 4096 + mo * TT:t2 * 4096 + (mo + 1) * TT], pm[:, 0:TT],
                        routed[:, mo * PT + t2 * TT:mo * PT + (t2 + 1) * TT]), sem="v")
                    ring.push()
                    gring.push()
            for t2 in range(2):
                tl0 = p0 + t2 * TT
                ring = Ring(w, depth=4)
                gring = Ring(w, depth=2)
                p4 = [pMain[0], pMain[1], ps_aux[:, :], ps_bcast[:, :]]
                for m in range(DC):
                    ws = wring[:, (m % 2) * 1024:(m % 2) * 1024 + 1024]
                    gring.wait_sy()
                    cache_load(w, UOF["lin1_w"] + m, 1, ws, "ldw%d" % (m % 2))
                    w.wait(TE, "ldw%d" % (m % 2))
                    pm = p4[m % 4]
                    ring.wait_te()
                    for kc in range(DC):
                        w.op(TE, lambda e, kc=kc, ws=ws, t2=t2, pm=pm: e.matmul(
                            pm[:, 0:TT], ws[:, kc * 128:(kc + 1) * 128],
                            bm1[:, t2 * 4096 + kc * TT:t2 * 4096 + (kc + 1) * TT],
                            start=(kc == 0), stop=(kc == DC - 1)), sem="t" if kc == DC - 1 else None)
                    w.wait(SC, "t")
                    w.op(SC, lambda e, m=m, pm=pm: e.activation(
                        bm2[:, m * TT:(m + 1) * TT], pm[:, 0:TT], AF.Identity,
                        bias=vecs["lin1_b"][:, m:m + 1]), sem="s")
                    ring.push()
                    gring.push()
                ring = Ring(w, depth=2)
                gring = Ring(w, depth=2)
                pairs = [(pMain[0], pMain[1]), (ps_aux[:, :], ps_bcast[:, :])]
                for m in range(DC):
                    ws1 = wring[:, (m % 2) * 1024:(m % 2) * 1024 + 1024]
                    ws2 = wring[:, 2048 + (m % 2) * 1024:2048 + (m % 2) * 1024 + 1024]
                    gring.wait_sy()
                    cache_load(w, UOF["s2w1"] + m, 1, ws1, "ldw%d" % (m % 2))
                    cache_load(w, UOF["s2w2"] + m, 1, ws2, "ldw%d" % (m % 2))
                    w.wait(TE, "ldw%d" % (m % 2))
                    pa, pb = pairs[m % 2]
                    ring.wait_te()
                    for kc in range(DC):
                        w.op(TE, lambda e, kc=kc, ws1=ws1, pa=pa: e.matmul(
                            pa[:, 0:TT], ws1[:, kc * 128:(kc + 1) * 128],
                            bm2[:, kc * TT:(kc + 1) * TT],
                            start=(kc == 0), stop=(kc == DC - 1)), sem="t" if kc == DC - 1 else None)
                    for kc in range(DC):
                        w.op(TE, lambda e, kc=kc, ws2=ws2, pb=pb: e.matmul(
                            pb[:, 0:TT], ws2[:, kc * 128:(kc + 1) * 128],
                            bm2[:, kc * TT:(kc + 1) * TT],
                            start=(kc == 0), stop=(kc == DC - 1)), sem="t" if kc == DC - 1 else None)
                    gsl = bgg[:, (m % 2) * TT:(m % 2) * TT + TT]
                    w.wait(SC, "t")
                    w.op(SC, lambda e, m=m, gsl=gsl, pa=pa: e.activation(
                        gsl, pa[:, 0:TT], AF.Silu, bias=vecs["s2b1"][:, m:m + 1]), sem="s")
                    w.wait(VE, "s")
                    if m == 0:
                        w.wait(VE, "outd")
                    w.op(VE, lambda e, m=m, gsl=gsl, pb=pb: e.scalar_tensor_tensor(
                        cffn[:, m * TT:(m + 1) * TT], pb[:, 0:TT],
                        vecs["s2b2"][:, m:m + 1], gsl, ALU.add, ALU.mult), sem="v")
                    ring.push()
                    gring.push()
                    xa = cxa[:, 0:TT]
                    at = ctmp2[:, 0:TT]
                    w.wait(SY, "v")
                    w.dma(SY, xa, x_fm[m * 128:(m + 1) * 128, tl0:tl0 + TT], "ldx0")
                    w.dma(SY, at, sp_attn[m * 128:(m + 1) * 128, tl0:tl0 + TT], "ldx0")
                    w.wait(VE, "ldx0")
                    w.op(VE, lambda e, m=m, xa=xa: e.tensor_add(
                        cffn[:, m * TT:(m + 1) * TT], cffn[:, m * TT:(m + 1) * TT], xa), sem="v")
                    w.op(VE, lambda e, m=m, at=at: e.tensor_add(
                        cffn[:, m * TT:(m + 1) * TT], cffn[:, m * TT:(m + 1) * TT], at), sem="v")
                    w.wait(SY, "v")
                    w.dma(SY, out_fm[m * 128:(m + 1) * 128, tl0:tl0 + TT],
                          cffn[:, m * TT:(m + 1) * TT], "outd")

        @block.sync
        def _(e):
            body(Walk("sync", e, sems))

        @block.tensor
        def _(e):
            body(Walk("tensor", e, sems))

        @block.vector
        def _(e):
            body(Walk("vector", e, sems))

        @block.scalar
        def _(e):
            body(Walk("scalar", e, sems))

        @block.gpsimd
        def _(e):
            body(Walk("gpsimd", e, sems))

    return nc


def _rope_tables(core):
    inv = 1.0 / 10000.0 ** (np.arange(0, Dh, 2, dtype=np.float64) / Dh)
    t0 = (core % 4) * T
    t = np.arange(t0, t0 + T, dtype=np.float64)
    fr = t[:, None] * inv[None, :]
    c = np.cos(fr).astype(np.float32).T
    s = np.sin(fr).astype(np.float32).T
    cs1 = np.zeros((128, T), np.float32)
    cs2 = np.zeros((128, T), np.float32)
    for blk in range(2):
        b0 = blk * 64
        cs1[b0:b0 + 32] = c
        cs1[b0 + 32:b0 + 64] = c
        cs2[b0:b0 + 32] = s
        cs2[b0 + 32:b0 + 64] = -s
    return cs1, cs2


_nc_cache = {}


def kernel(**inputs):
    f32 = lambda k: np.ascontiguousarray(np.asarray(inputs[k], np.float32))
    x = f32("L0_attn_x")

    tri = np.triu(np.ones((CH, CH), np.float32))
    ident = np.eye(CH, dtype=np.float32)
    swap = np.zeros((CH, CH), np.float32)
    for mcol in range(CH):
        blk, i = mcol // 64, mcol % 64
        swap[blk * 64 + (i + 32) % 64, mcol] = 1.0
    ones_row = np.ones((1, CH), np.float32)
    ones_col = np.ones((CH, 1), np.float32)

    vec = lambda k: np.ascontiguousarray(f32(k).reshape(DC, 128).T)
    gate_wt = np.ascontiguousarray(
        f32("L0_ffn_gate_w").reshape(E, DC, 128).transpose(2, 1, 0).reshape(128, DC * E))

    common = dict(
        wq=f32("L0_attn_Wq"), wk=f32("L0_attn_Wk"), wv=f32("L0_attn_Wv"),
        wbeta=f32("L0_attn_Wbeta"), wo=f32("L0_attn_Wo"),
        bbeta=f32("L0_attn_bbeta").reshape(1, HC), bo=vec("L0_attn_bo"),
        ln_g=vec("L0_attn_ln_gamma"), ln_b=vec("L0_attn_ln_beta"),
        lin0_w=f32("L0_ffn_lin0_w"), lin0_b=vec("L0_ffn_lin0_b"),
        s1w1=f32("L0_ffn_swi1_w1"), s1b1=vec("L0_ffn_swi1_b1"),
        s1w2=f32("L0_ffn_swi1_w2"), s1b2=vec("L0_ffn_swi1_b2"),
        gate_wt=gate_wt,
        exw1=f32("L0_ffn_exp_w1"), exw3=f32("L0_ffn_exp_w3"), exw2=f32("L0_ffn_exp_w2"),
        shw1=f32("L0_ffn_sh_w1"), shw3=f32("L0_ffn_sh_w3"), shw2=f32("L0_ffn_sh_w2"),
        lin1_w=f32("L0_ffn_lin1_w"), lin1_b=vec("L0_ffn_lin1_b"),
        s2w1=f32("L0_ffn_swi2_w1"), s2b1=vec("L0_ffn_swi2_b1"),
        s2w2=f32("L0_ffn_swi2_w2"), s2b2=vec("L0_ffn_swi2_b2"),
        tri_f=tri, tri_b=tri.astype(_bf), id_b=ident.astype(_bf), id_f=ident,
        swap_b=swap.astype(_bf), ones_f=ones_row, ones_b=ones_row.astype(_bf),
        onescol_f=ones_col, onescol_b=ones_col.astype(_bf),
        negones_f=-ones_row,
    )

    in_maps = []
    for core in range(NCORE):
        b = core // 4
        r = core % 4
        xs = np.ascontiguousarray(x[b, r * T:(r + 1) * T, :].T)
        cs1v, cs2v = _rope_tables(core)
        gm = np.zeros((1, NCORE), np.float32)
        for i in range(NCORE):
            gm[0, i] = 1.0 if (i // 4 == b and i % 4 < r) else 0.0
        m = dict(common)
        m.update(x_fm=xs, cs1=cs1v.astype(_bf), cs2=cs2v.astype(_bf), gmask=gm)
        in_maps.append(m)

    if "prog" not in _nc_cache:
        _nc_cache["prog"] = build_program()
    nc = _nc_cache["prog"]

    import os as _os
    res = run_bass_kernel_spmd(nc, in_maps, core_ids=list(range(NCORE)),
                               trace=bool(_os.environ.get("KTRACE")))
    global LAST_RESULTS
    LAST_RESULTS = res

    out = np.empty((B, S, D), np.float32)
    for core in range(NCORE):
        b = core // 4
        r = core % 4
        out[b, r * T:(r + 1) * T, :] = np.asarray(res.results[core]["out_fm"]).T
    return out


# version 13
# speedup vs baseline: 1.0968x; 1.0344x over previous
import numpy as np
import ml_dtypes
import concourse.bass as bass
import concourse.mybir as mybir
from concourse.bass_utils import run_bass_kernel_spmd

F32 = mybir.dt.float32
BF16 = mybir.dt.bfloat16
AF = mybir.ActivationFunctionType
ALU = mybir.AluOpType

NCORE = 8
B, S, D, H, Dh, C = 2, 16384, 1024, 16, 64, 2
T = 4096
NT = 8
TT = 512
NCH = 4
CH = 128
DC = 8
HC = H * C
E = 8
SHD = 2048
PT = 1024
NHALF = 2
HT = 2048
LN_EPS = 1e-5
EPS = 1e-6
NDST = Dh + 1
MCOLS = HC * NDST

DEBUG = False
PHASES = "AGBC"
CLEVEL = 4
ALSUB = 4
A_LEVEL = 4
NTILES = NT
SHARED_CC = True
STRICT_SELF_WAITS = True
_bf = ml_dtypes.bfloat16
LAST_RESULTS = None


class Walk:

    def __init__(self, eng_name, eng, sems):
        self.eng_name = eng_name
        self.eng = eng
        self.sems = sems
        self.counts = {k: 0 for k in sems}

    def mark(self, sem):
        return self.counts[sem]

    def on(self, eng_name):
        return eng_name == self.eng_name

    def wait(self, eng_name, sem, val=None):
        v = self.counts[sem] if val is None else val
        if self.on(eng_name) and v > 0:
            self.eng.wait_ge(self.sems[sem], v)

    def dma(self, qname, out, in_, sem, **kw):
        self.counts[sem] += 16
        if self.on(qname):
            self.eng.dma_start(out=out, in_=in_, **kw).then_inc(self.sems[sem], 16)

    _SELF = {"vector": "v", "scalar": "s"}

    def op(self, eng_name, fn, sem=None):
        own = self._SELF.get(eng_name) if STRICT_SELF_WAITS else None
        if own is not None and self.on(eng_name) and self.counts[own] > 0:
            self.eng.wait_ge(self.sems[own], self.counts[own])
        if sem is not None:
            self.counts[sem] += 1
        if self.on(eng_name):
            last = fn(self.eng)
            if sem is not None:
                last.then_inc(self.sems[sem], 1)


def build_program():
    nc = bass.Bass("TRN2", target_bir_lowering=False)

    def din(name, shape, dt=F32):
        return nc.dram_tensor(name, shape, dt, kind="ExternalInput").ap()

    x_fm = din("x_fm", [D, T])
    wq = din("wq", [D, D]); wk = din("wk", [D, D]); wv = din("wv", [D, D])
    wbeta = din("wbeta", [D, HC]); wo = din("wo", [D, D])
    bbeta = din("bbeta", [1, HC]); bo = din("bo", [128, DC])
    ln_g = din("ln_g", [128, DC]); ln_b = din("ln_b", [128, DC])
    lin0_w = din("lin0_w", [D, D]); lin0_b = din("lin0_b", [128, DC])
    s1w1 = din("s1w1", [D, D]); s1b1 = din("s1b1", [128, DC])
    s1w2 = din("s1w2", [D, D]); s1b2 = din("s1b2", [128, DC])
    gate_wt = din("gate_wt", [128, DC * E])
    exw1 = din("exw1", [E, D, D]); exw3 = din("exw3", [E, D, D]); exw2 = din("exw2", [E, D, D])
    shw1 = din("shw1", [D, SHD]); shw3 = din("shw3", [D, SHD]); shw2 = din("shw2", [SHD, D])
    lin1_w = din("lin1_w", [D, D]); lin1_b = din("lin1_b", [128, DC])
    s2w1 = din("s2w1", [D, D]); s2b1 = din("s2b1", [128, DC])
    s2w2 = din("s2w2", [D, D]); s2b2 = din("s2b2", [128, DC])
    cs1 = din("cs1", [CH, T], BF16); cs2 = din("cs2", [CH, T], BF16)
    tri_f = din("tri_f", [CH, CH]); tri_b = din("tri_b", [CH, CH], BF16)
    id_b = din("id_b", [CH, CH], BF16); id_f = din("id_f", [CH, CH])
    swap_b = din("swap_b", [CH, CH], BF16)
    ones_f = din("ones_f", [1, CH]); ones_b = din("ones_b", [1, CH], BF16)
    negones_f = din("negones_f", [1, CH])
    onescol_f = din("onescol_f", [CH, 1])
    onescol_b = din("onescol_b", [CH, 1], BF16)
    gmask = din("gmask", [1, NCORE])

    out_fm = nc.dram_tensor("out_fm", [D, T], F32, kind="ExternalOutput").ap()

    skind = "ExternalOutput" if DEBUG else "Internal"

    def scr(name, shape, dt=F32):
        return nc.dram_tensor(name, shape, dt, kind=skind).ap()

    sp_q = scr("sp_q", [D, T], BF16)
    sp_k = scr("sp_k", [D, T], BF16)
    sp_va = scr("sp_va", [T, H * NDST], BF16)
    sp_L = scr("sp_L", [T, 2 * HC])
    sp_Lr = scr("sp_Lr", [T // CH, HC, CH])
    sp_mst = scr("sp_mst", [T // CH, 64, MCOLS], BF16)
    sp_attn = scr("sp_attn", [D, T])
    sp_stats = scr("sp_stats", [2, T])
    wc = nc.dram_tensor("wc", [320, 128, 1024], BF16, kind="Internal").ap()
    UOF = {"wq": 0, "wk": 8, "wv": 16, "wo": 24, "lin0_w": 32, "s1w1": 40,
           "s1w2": 48, "lin1_w": 56, "s2w1": 64, "s2w2": 72,
           "shw1": 80, "shw3": 96, "shw2": 112,
           "exw1": 128, "exw3": 192, "exw2": 256}
    sp_comb = scr("sp_comb", [E, T], BF16)
    cc_in = nc.dram_tensor("cc_in", [1, HC + 64 * MCOLS], F32, kind="Internal").ap()
    dbg_cc = (nc.dram_tensor("dbg_cc", [1, HC + 64 * MCOLS], F32,
                             kind="ExternalOutput").ap() if DEBUG else None)
    cc_out = nc.dram_tensor("cc_out", [NCORE, HC + 64 * MCOLS], F32,
                            addr_space="Shared" if SHARED_CC else "Local",
                            kind="Internal").ap()

    from contextlib import ExitStack
    ctx = ExitStack()

    def mk_sb(name, shape, dt=F32):
        return ctx.enter_context(nc.sbuf_tensor(name, shape, dt))

    with ctx:
        biga = mk_sb("biga", [128, DC * T], BF16)
        bigb = mk_sb("bigb", [128, 19200], BF16)
        bigc = mk_sb("bigc", [128, 8192], F32)
        wring = mk_sb("wring", [128, 4096], BF16)
        m_sb = mk_sb("m_sb", [64, MCOLS], F32)
        sin_sb = mk_sb("sin_sb", [64, MCOLS], F32)
        sinb_sb = mk_sb("sinb_sb", [128, MCOLS], BF16)
        m_bf = mk_sb("m_bf", [64, MCOLS], BF16)
        u64a = mk_sb("u64a", [128, 2 * MCOLS], BF16)
        gtmp_sb = u64a[:, :].bitcast(F32)[0:64, 0:MCOLS]
        arow_sb = mk_sb("arow_sb", [1, NCORE * HC], F32)
        gm_sb = mk_sb("gm_sb", [1, NCORE], F32)
        gmb_sb = mk_sb("gmb_sb", [64, NCORE], F32)
        ap_row = mk_sb("ap_row", [1, HC], F32)
        ap_bc = mk_sb("ap_bc", [64, HC], F32)
        cs1_sb = mk_sb("cs1_sb", [128, TT], BF16)
        cs2_sb = mk_sb("cs2_sb", [128, TT], BF16)
        tri_f_sb = mk_sb("tri_f_sb", [128, CH], F32)
        tri_b_sb = mk_sb("tri_b_sb", [128, CH], BF16)
        id_b_sb = mk_sb("id_b_sb", [128, CH], BF16)
        id_f_sb = mk_sb("id_f_sb", [128, CH], F32)
        swap_sb = mk_sb("swap_sb", [128, CH], BF16)
        ones_f_sb = mk_sb("ones_f_sb", [1, CH], F32)
        negones_sb = mk_sb("negones_sb", [1, CH], F32)
        ones_b_sb = mk_sb("ones_b_sb", [1, CH], BF16)
        onescol_f_sb = mk_sb("onescol_f_sb", [128, 1], F32)
        onescol_b_sb = mk_sb("onescol_b_sb", [128, 1], BF16)
        bbr_sb = mk_sb("bbr_sb", [1, HC], F32)
        bb_bc_sb = mk_sb("bb_bc_sb", [128, HC], F32)
        wbeta_sb = mk_sb("wbeta_sb", [128, DC * HC], BF16)
        gatew_sb = mk_sb("gatew_sb", [128, DC * E], BF16)
        vec_names = ["bo", "ln_g", "ln_b", "lin0_b", "s1b1", "s1b2",
                     "lin1_b", "s2b1", "s2b2"]
        vecs = {n: mk_sb("vec_" + n, [128, DC], F32) for n in vec_names}
        carry_sb = mk_sb("carry_sb", [1, HC], F32)
        dsum_sb = mk_sb("dsum_sb", [1, HC], F32)
        adsum_sb = mk_sb("adsum_sb", [1, HC], F32)
        acore_sb = mk_sb("acore_sb", [1, HC], F32)
        lrows_sb = mk_sb("lrows_sb", [1, HC * CH], F32)
        combrow_sb = mk_sb("combrow_sb", [1, PT], BF16)
        combf_sb = mk_sb("combf_sb", [8, TT], BF16)
        eps_sb = mk_sb("eps_sb", [1, 1], F32)
        dmy_sb = mk_sb("dmy_sb", [1, 1], F32)
        lloc_sb = mk_sb("lloc_sb", [128, HC], F32)
        lglob_sb = mk_sb("lglob_sb", [128, HC], F32)
        negl_sb = mk_sb("negl_sb", [128, HC], F32)
        expll_sb = mk_sb("expll_sb", [128, HC], F32)
        explg_sb = mk_sb("explg_sb", [128, HC], F32)
        sc_sb = mk_sb("sc_sb", [128, HC], F32)
        asb_sb = mk_sb("asb_sb", [64, HC], F32)
        scarg_sb = mk_sb("scarg_sb", [128, HC], F32)
        zrow_sb = mk_sb("zrow_sb", [128, HC], F32)
        lb_sb = mk_sb("lb_sb", [128, HC], F32)
        num_sb = mk_sb("num_sb", [128, 2 * NDST], F32)
        rden_sb = mk_sb("rden_sb", [128, 4], F32)
        llocT_sb = mk_sb("llocT_sb", [HC, CH], F32)
        mstl_sb = u64a[:, :]

        ps_main0 = ctx.enter_context(nc.psum_tensor("ps_main0", [128, 512], F32))
        ps_main1 = ctx.enter_context(nc.psum_tensor("ps_main1", [128, 512], F32))
        ps_aux = ctx.enter_context(nc.psum_tensor("ps_aux", [128, 512], F32))
        ps_scan0 = ctx.enter_context(nc.psum_tensor("ps_scan0", [128, 512], F32))
        ps_scan1 = ctx.enter_context(nc.psum_tensor("ps_scan1", [128, 512], F32))
        ps_scan2 = ctx.enter_context(nc.psum_tensor("ps_scan2", [128, 512], F32))
        ps_tp = ctx.enter_context(nc.psum_tensor("ps_tp", [128, 1024], BF16))
        ps_bcast = ctx.enter_context(nc.psum_tensor("ps_bcast", [128, 512], F32))

        pMain = [ps_main0[:, :], ps_main1[:, :]]
        pB1 = ps_bcast[:, :]
        pB2 = ps_aux[:, :]
        pRope = ps_aux[:, :]
        pSt = [ps_scan0[:, 0:128], ps_scan0[:, 128:256]]
        pBeta = ps_scan0[:, 256:288]
        pGate = ps_scan0[:, 288:296]
        pTcomb = ps_scan0[:, 352:480]
        pM0 = ps_scan1[:, 0:128]
        pM1 = ps_scan1[:, 128:256]
        pScT = ps_scan1[:, 256:288]
        pAbc = ps_scan1[:, 288:320]
        pStt = [ps_scan1[:, 320:385], ps_scan1[:, 385:450]]
        pMisc = ps_scan1[:, 450:482]
        pLloc = ps_scan2[:, 0:32]
        pLglob = ps_scan2[:, 32:64]
        pNum = [ps_scan2[:, 64:129], ps_scan2[:, 129:194]]
        pIntr = ps_scan2[:, 194:454]
        pTp = [ps_tp[:, 0:128], ps_tp[:, 128:256]]
        pBc = [ps_bcast[:, :], ps_aux[:, :]]

        bq = bigb[:, 0:1024]
        bk = bigb[:, 1024:2048]
        bv = bigb[:, 2048:3072]
        bva = bigb[:, 3072:7232]
        bkt = bigb[:, 7232:11328]
        bsc = bigb[:, 11328:15424]
        bqt = bigb[:, 0:4096]
        bkt2 = bigb[:, 4096:8192]
        bva2 = bigb[:, 8192:12352]
        byt = bigb[:, 12352:13376]
        byf = bigb[:, 13376:17472]
        bsc2 = bigb[:, 17472:19200]
        bf0 = bigb[:, 0:16384]
        bgg = bigb[:, 16384:17408]
        btmp = bigb[:, 17408:18432]
        bhh = bigb[:, 0:8192]
        bshh = bigb[:, 0:8192]
        bm1 = bigb[:, 8192:16384]
        bm2 = bigb[:, 0:4096]

        cx = bigc[:, 0:1024]
        ch_ = bigc[:, 1024:5120]
        ctmp = bigc[:, 5120:7168]
        csc = bigc[:, 7168:8192]
        stats_sb = bigc[0:1, 6656:6656 + 3 * TT].rearrange("o f -> o f")
        routed = bigc[:, 0:8192]
        cffn = bigc[:, 0:4096]
        cxa = bigc[:, 4096:5120]
        ctmp2 = bigc[:, 5120:7168]

        sems = {}
        for n in ["ldw0", "ldw1", "ldw2", "ldwa", "ldwb", "lwg0", "lwg1", "ldx0", "ldx1", "ldcs", "ldm", "ldmg",
                  "spq0", "spq1", "spv", "spl", "spm", "spa", "rlq",
                  "rlm0", "rlm1", "rlm2",
                  "cc", "ccd", "bounce", "outd", "pcast", "t", "v", "s", "g"]:
            sems[n] = ctx.enter_context(nc.semaphore(n))

        block = ctx.enter_context(nc.Block())

        TE, VE, SC, SY, GP = "tensor", "vector", "scalar", "sync", "gpsimd"

        def guard(w):
            w.wait(TE, "v")
            w.wait(TE, "s")

        class Ring:

            def __init__(self, w, depth=2):
                self.w = w
                self.depth = depth
                self.hist = []
                self.tm0 = w.mark("t")

            def wait_te(self):
                if len(self.hist) >= self.depth:
                    vm, sm, _ = self.hist[-self.depth]
                    self.w.wait(TE, "v", vm)
                    self.w.wait(TE, "s", sm)
                else:
                    guard(self.w)

            def wait_gp(self):
                if len(self.hist) >= self.depth:
                    _, _, tm = self.hist[-self.depth]
                    self.w.wait(GP, "t", tm)
                else:
                    self.w.wait(GP, "t", self.tm0)

            def wait_sy(self):
                if len(self.hist) >= self.depth:
                    _, _, tm = self.hist[-self.depth]
                    self.w.wait(SY, "t", tm)
                else:
                    self.w.wait(SY, "t", self.tm0)

            def push(self):
                self.hist.append((self.w.mark("v"), self.w.mark("s"), self.w.mark("t")))

        def body(w: Walk):
            if PHASES == "MICRO":
                for c in range(DC):
                    w.dma(SY, out_fm[c * 128:(c + 1) * 128, :],
                          x_fm[c * 128:(c + 1) * 128, :], "outd")
                w.wait(SY, "outd")
                return
            w.dma(SY, tri_f_sb[:, :], tri_f, "ldm")
            w.dma(SY, tri_b_sb[:, :], tri_b, "ldm")
            w.dma(SY, id_b_sb[:, :], id_b, "ldm")
            w.dma(SY, id_f_sb[:, :], id_f, "ldm")
            w.dma(SY, swap_sb[:, :], swap_b, "ldm")
            w.dma(SY, ones_f_sb[:, :], ones_f, "ldm")
            w.dma(SY, negones_sb[:, :], negones_f, "ldm")
            w.dma(SY, ones_b_sb[:, :], ones_b, "ldm")
            w.dma(SY, onescol_f_sb[:, :], onescol_f, "ldm")
            w.dma(SY, onescol_b_sb[:, :], onescol_b, "ldm")
            w.dma(SY, gm_sb[:, :], gmask, "ldm")
            w.dma(SY, bbr_sb[:, :], bbeta, "ldm")
            bias_srcs = {"bo": bo, "ln_g": ln_g, "ln_b": ln_b, "lin0_b": lin0_b,
                         "s1b1": s1b1, "s1b2": s1b2, "lin1_b": lin1_b,
                         "s2b1": s2b1, "s2b2": s2b2}
            for n in vec_names:
                w.dma(SY, vecs[n][:, :], bias_srcs[n], "ldm")
            w.dma(GP, wbeta_sb[:, :].rearrange("p (c n) -> p c n", c=DC),
                  wbeta.rearrange("(c p) n -> p c n", p=128), "ldmg")
            w.dma(GP, gatew_sb[:, :], gate_wt, "ldmg")
            def precast_dxd(name, wt, nm):
                for m in range(nm):
                    w.dma(GP, wc[UOF[name] + m].rearrange("p (kc n) -> p kc n", kc=DC),
                          wt.rearrange("(kc p) n -> p kc n", p=128)[:, :, m * 128:(m + 1) * 128],
                          "pcast")
            precast_dxd("wo", wo, 8)
            w.wait(GP, "pcast")
            w.op(GP, lambda e: e.memset(dmy_sb[:, :], 0.0), sem="g")
            g1mark = w.mark("g")
            w.wait(SY, "g", g1mark)
            w.wait(TE, "ldm")
            w.wait(TE, "ldmg")
            w.op(TE, lambda e: e.matmul(pScT, ones_f_sb[0:1, :], bbr_sb[0:1, :],
                                        start=True, stop=True), sem="t")
            w.wait(VE, "t")
            w.wait(VE, "ldm")
            w.op(VE, lambda e: e.tensor_copy(bb_bc_sb[:, :], pScT), sem="v")
            if CLEVEL >= 3:
                w.op(VE, lambda e: e.memset(m_sb[:, :], 0.0), sem="v")
                w.op(VE, lambda e: e.memset(m_bf[:, :], 0.0), sem="v")
                w.op(VE, lambda e: e.memset(sin_sb[:, :], 0.0), sem="v")
                w.op(VE, lambda e: e.memset(carry_sb[:, :], 0.0), sem="v")
                w.op(VE, lambda e: e.memset(eps_sb[:, :], LN_EPS), sem="v")
            if CLEVEL >= 4:
                w.wait(TE, "v")
                w.op(TE, lambda e: e.matmul(pAbc[0:64, 0:NCORE], ones_f_sb[0:1, 0:64],
                                            gm_sb[0:1, :], start=True, stop=True), sem="t")
                w.wait(VE, "t")
                w.op(VE, lambda e: e.tensor_copy(gmb_sb[:, :], pAbc[0:64, 0:NCORE]), sem="v")

            if "A" in PHASES and A_LEVEL >= 1:
                for j in range(NTILES):
                    phase_a_tile(w, j)
            if "G" in PHASES:
                do_gather(w)
            precast_dxd("lin0_w", lin0_w, 8)
            precast_dxd("s1w1", s1w1, 8)
            precast_dxd("s1w2", s1w2, 8)
            precast_dxd("lin1_w", lin1_w, 8)
            precast_dxd("s2w1", s2w1, 8)
            precast_dxd("s2w2", s2w2, 8)
            precast_dxd("shw1", shw1, 16)
            precast_dxd("shw3", shw3, 16)
            for m in range(DC):
                for u in range(2):
                    w.dma(GP, wc[UOF["shw2"] + 2 * m + u].rearrange("p (kc n) -> p kc n", kc=DC),
                          shw2.rearrange("(u kc p) n -> u p kc n", u=2, p=128)[u][:, :, m * 128:(m + 1) * 128],
                          "pcast")
            for ee in range(E):
                for nm, wt in (("exw1", exw1), ("exw3", exw3), ("exw2", exw2)):
                    for m in range(DC):
                        w.dma(GP, wc[UOF[nm] + ee * 8 + m].rearrange("p (kc n) -> p kc n", kc=DC),
                              wt[ee].rearrange("(kc p) n -> p kc n", p=128)[:, :, m * 128:(m + 1) * 128],
                              "pcast")
            w.wait(GP, "pcast")
            w.op(GP, lambda e: e.memset(dmy_sb[:, :], 0.0), sem="g")
            g2mark = w.mark("g")
            if "B" in PHASES:
                for sname in ["spq0", "spq1", "spv", "spl", "spm"]:
                    w.wait(SY, sname)
                w.wait(TE, "ccd")
                w.wait(VE, "ccd")
                for j in range(NTILES):
                    phase_b_tile(w, j)
            if "C" in PHASES:
                w.wait(SY, "g", g2mark)
                w.wait(SY, "spa")
                w.wait(VE, "spa")
                w.wait(SC, "spa")
                w.wait(TE, "spa")
                for hf in range(NHALF if NTILES >= 8 else 1):
                    phase_c_half(w, hf)
            for sname in ["outd", "spq0", "spq1", "spv", "spl", "spa", "spm",
                          "ldx0", "ldx1", "rlq", "rlm0", "rlm1", "rlm2",
                          "ldcs", "ldm", "ldw0", "ldw1", "ldw2", "ldwa", "ldwb"]:
                w.wait(SY, sname)
            for sname in ["ldmg", "ccd", "bounce", "lwg0", "lwg1"]:
                w.wait(GP, sname)

        def do_gather(w: Walk):
            w.wait(SC, "v")
            w.op(SC, lambda e: e.activation(acore_sb[:, :], carry_sb[:, :], AF.Exp), sem="s")
            w.wait(GP, "s")
            w.wait(GP, "v")
            w.dma(GP, cc_in[0:1, 0:HC], acore_sb[:, :], "ccd")
            w.dma(GP, cc_in[0, HC:].rearrange("(p f) -> p f", p=64), m_sb[:, :], "ccd")
            w.wait(GP, "ccd")
            if DEBUG:
                w.dma(GP, dbg_cc, cc_in[:, :], "ccd")
                w.wait(GP, "ccd")

            w.op(GP, lambda e: e.collective_compute(
                "AllGather", ALU.bypass, replica_groups=[list(range(NCORE))],
                ins=[cc_in[:, :]], outs=[cc_out[:, :]]), sem="cc")
            w.wait(GP, "cc")
            w.dma(GP, arow_sb[:, :].rearrange("o (r a) -> o r a", r=NCORE),
                  cc_out[None, :, 0:HC], "ccd")
            for i in range(NCORE):
                w.wait(GP, "v")
                w.dma(GP, gtmp_sb[:, :],
                      cc_out[i, HC:].rearrange("(p f) -> p f", p=64), "ccd")
                w.wait(VE, "ccd")
                w.op(VE, lambda e, i=i: e.tensor_single_scalar(
                    ap_row[:, :], arow_sb[0:1, i * HC:(i + 1) * HC], -1.0, ALU.add), sem="v")
                w.op(VE, lambda e, i=i: e.tensor_single_scalar(
                    ap_row[:, :], ap_row[:, :], gm_sb[0:1, i:i + 1], ALU.mult), sem="v")
                w.op(VE, lambda e: e.tensor_single_scalar(
                    ap_row[:, :], ap_row[:, :], 1.0, ALU.add), sem="v")
                w.wait(TE, "v")
                w.op(TE, lambda e: e.matmul(pAbc[0:64, 0:HC], ones_f_sb[0:1, 0:64],
                                            ap_row[0:1, :], start=True, stop=True), sem="t")
                w.wait(VE, "t")
                w.op(VE, lambda e: e.tensor_copy(ap_bc[:, :], pAbc[0:64, 0:HC]), sem="v")
                w.op(VE, lambda e, i=i: e.tensor_single_scalar(
                    gtmp_sb[:, :], gtmp_sb[:, :], gmb_sb[:, i:i + 1], ALU.mult), sem="v")
                for hcn in range(HC):
                    w.op(VE, lambda e, hcn=hcn: e.scalar_tensor_tensor(
                        sin_sb[:, hcn * NDST:(hcn + 1) * NDST],
                        sin_sb[:, hcn * NDST:(hcn + 1) * NDST],
                        ap_bc[:, hcn:hcn + 1],
                        gtmp_sb[:, hcn * NDST:(hcn + 1) * NDST],
                        ALU.mult, ALU.add), sem="v")
            w.op(VE, lambda e: e.tensor_copy(sinb_sb[0:64, :], sin_sb[:, :]), sem="v")
            w.wait(GP, "v")
            w.dma(GP, sinb_sb[64:128, :], sinb_sb[0:64, :], "ccd")

        def ln_sweep(w, read_chunk, nst, write_out):
            raise NotImplementedError

        def phase_a_tile(w: Walk, j):
            t0 = j * TT
            w.wait(VE, "t")
            w.wait(VE, "spq0")
            w.wait(VE, "spq1")
            w.wait(VE, "spv")
            w.wait(VE, "spl")
            w.wait(VE, "spm")
            w.wait(SY, "v")
            w.wait(SY, "s")
            if A_LEVEL >= 2 and not (A_LEVEL == 2 and ALSUB == 1):
                w.dma(SY, cs1_sb[:, :], cs1[:, t0:t0 + TT], "ldcs")
                w.dma(SY, cs2_sb[:, :], cs2[:, t0:t0 + TT], "ldcs")

            guard(w)
            cxa = bigc[:, 0:4096]
            for c in range(DC):
                w.dma(SY, cxa[:, c * TT:(c + 1) * TT],
                      x_fm[c * 128:(c + 1) * 128, t0:t0 + TT], "ldx0")
            w.wait(SC, "ldx0")
            for c in range(DC):
                xc = cxa[:, c * TT:(c + 1) * TT]
                xb = bsc[:, (c % 2) * TT:(c % 2) * TT + TT]
                x2b = bsc[:, 1024 + (c % 2) * TT:1024 + (c % 2) * TT + TT]
                if c >= 2:
                    w.wait(SC, "t")
                w.op(SC, lambda e, xc=xc, xb=xb: e.copy(xb, xc), sem="s")
                w.op(SC, lambda e, xc=xc, x2b=x2b: e.activation(x2b, xc, AF.Square), sem="s")
                w.wait(TE, "s")
                w.op(TE, lambda e, xb=xb, c=c: e.matmul(
                    pB1[0:1, 0:TT], onescol_b_sb[:, :], xb,
                    start=(c == 0), stop=(c == DC - 1)), sem=None)
                w.op(TE, lambda e, x2b=x2b, c=c: e.matmul(
                    pB2[0:1, 0:TT], onescol_b_sb[:, :], x2b,
                    start=(c == 0), stop=(c == DC - 1)), sem="t")
            if A_LEVEL == 1 and ALSUB < 2:
                return
            mu = stats_sb[0:1, 0:TT]
            ex2 = stats_sb[0:1, TT:2 * TT]
            musq = stats_sb[0:1, 2 * TT:3 * TT]
            var = stats_sb[0:1, TT:2 * TT]
            sd = stats_sb[0:1, 2 * TT:3 * TT]
            rstd = stats_sb[0:1, TT:2 * TT]
            murstd = stats_sb[0:1, 2 * TT:3 * TT]
            w.wait(SC, "t")
            w.op(SC, lambda e: e.activation(mu, pB1[0:1, 0:TT], AF.Copy, scale=1.0 / D), sem="s")
            w.op(SC, lambda e: e.activation(ex2, pB2[0:1, 0:TT], AF.Copy, scale=1.0 / D), sem="s")
            w.wait(VE, "s")
            w.op(VE, lambda e: e.tensor_mul(musq, mu, mu), sem="v")
            w.op(VE, lambda e: e.tensor_sub(var, ex2, musq), sem="v")
            w.wait(SC, "v")
            w.op(SC, lambda e: e.activation(sd, var, AF.Sqrt, bias=eps_sb[0:1, 0:1]), sem="s")
            w.wait(VE, "s")
            w.op(VE, lambda e: e.reciprocal(rstd, sd), sem="v")
            w.op(VE, lambda e: e.tensor_mul(murstd, mu, rstd), sem="v")
            if A_LEVEL == 1 and ALSUB < 3:
                return
            w.wait(SY, "v")
            w.dma(SY, sp_stats[0:1, t0:t0 + TT], rstd, "spl")
            w.dma(SY, sp_stats[1:2, t0:t0 + TT], murstd, "spl")
            w.wait(TE, "v")
            w.op(TE, lambda e: e.matmul(pB1[:, 0:TT], ones_f_sb[0:1, :], rstd,
                                        start=True, stop=True), sem=None)
            w.op(TE, lambda e: e.matmul(pB2[:, 0:TT], ones_f_sb[0:1, :], murstd,
                                        start=True, stop=True), sem="t")
            mark_bc = w.mark("t")
            if A_LEVEL == 1 and ALSUB < 4:
                return
            w.wait(VE, "t", mark_bc)
            for c in range(DC):
                xc = cxa[:, c * TT:(c + 1) * TT]
                xnc = biga[:, c * T + t0:c * T + t0 + TT]
                tmp = ctmp[:, (c % 2) * TT:(c % 2) * TT + TT]
                w.op(VE, lambda e, xc=xc, tmp=tmp: e.tensor_mul(tmp, xc, pB1[:, 0:TT]), sem="v")
                w.op(VE, lambda e, tmp=tmp, xnc=xnc: e.tensor_sub(xnc, tmp, pB2[:, 0:TT]), sem="v")
            mark_xn = w.mark("v")
            if A_LEVEL < 2:
                return

            qkv_list = [(None, sp_q), (None, sp_k), (None, None)]
            marks_c = [None] * 24
            vmark_t12 = [0] * 24
            vmark_mn = [0]
            gmarks = [0] * 24

            def a_stage1(i):
                wi, m = divmod(i, DC)
                wt = [wq, wk, wv][wi]
                ws = wring[:, (i % 2) * 1024:(i % 2) * 1024 + 1024]
                if i >= 2:
                    w.wait(GP, "t", gmarks[i - 2])
                else:
                    w.wait(GP, "t")
                w.dma(GP, ws.rearrange("p (c n) -> p c n", c=DC),
                      wt.rearrange("(c p) n -> p c n", p=128)[:, :, m * 128:(m + 1) * 128],
                      "lwg%d" % (i % 2))
                pm = pMain[i % 2]
                w.wait(TE, "lwg%d" % (i % 2))
                w.wait(TE, "v", mark_xn)
                if i >= 2 and marks_c[i - 2] is not None:
                    sem_n, val = marks_c[i - 2]
                    w.wait(TE, sem_n, val)
                else:
                    guard(w)
                for kc in range(DC):
                    w.op(TE, lambda e, ws=ws, kc=kc, pm=pm: e.matmul(
                        pm[:, 0:TT], ws[:, kc * 128:(kc + 1) * 128],
                        biga[:, kc * T + t0:kc * T + t0 + TT],
                        start=(kc == 0), stop=(kc == DC - 1)),
                        sem="t" if kc == DC - 1 else None)
                tmark = w.mark("t")
                gmarks[i] = tmark
                if wi < 2:
                    t1 = bsc[:, 2048 + (i % 2) * 2 * TT:2048 + (i % 2) * 2 * TT + TT]
                    t2 = bsc[:, 2048 + (i % 2) * 2 * TT + TT:2048 + (i % 2) * 2 * TT + 2 * TT]
                    w.wait(VE, "t", tmark)
                    w.wait(VE, "ldcs")
                    w.op(VE, lambda e, pm=pm, t1=t1: e.tensor_mul(t1, pm[:, 0:TT], cs1_sb[:, :]), sem="v")
                    w.op(VE, lambda e, pm=pm, t2=t2: e.tensor_mul(t2, pm[:, 0:TT], cs2_sb[:, :]), sem="v")
                    vmark_t12[i] = w.mark("v")
                    marks_c[i] = ("v", vmark_t12[i])
                else:
                    vb = bv[:, (i % 2) * TT:(i % 2) * TT + TT]
                    w.wait(SC, "t", tmark)
                    w.op(SC, lambda e, pm=pm, vb=vb: e.copy(vb, pm[:, 0:TT]), sem="s")
                    marks_c[i] = ("s", w.mark("s"))

            def a_stage2(i):
                wi, m = divmod(i, DC)
                _, dst_spill = qkv_list[wi]
                if wi < 2:
                    t1 = bsc[:, 2048 + (i % 2) * 2 * TT:2048 + (i % 2) * 2 * TT + TT]
                    t2 = bsc[:, 2048 + (i % 2) * 2 * TT + TT:2048 + (i % 2) * 2 * TT + 2 * TT]
                    w.wait(TE, "v", vmark_t12[i])
                    w.wait(TE, "v", vmark_mn[0])
                    w.op(TE, lambda e, t1=t1: e.matmul(
                        pRope[:, 0:TT], id_b_sb[:, :], t1, start=True, stop=False), sem=None)
                    w.op(TE, lambda e, t2=t2: e.matmul(
                        pRope[:, 0:TT], swap_sb[:, :], t2, start=False, stop=True), sem="t")
                    rmark = w.mark("t")
                    mn = bsc[:, 0:TT]
                    ex = bsc[:, TT:2 * TT]
                    qk_out = (bq if wi == 0 else bk)[:, (m % 2) * TT:(m % 2) * TT + TT]
                    w.wait(VE, "t", rmark)
                    w.op(VE, lambda e, mn=mn: e.tensor_single_scalar(
                        mn, pRope[:, 0:TT], 0.0, ALU.min), sem="v")
                    w.wait(SC, "v")
                    w.op(SC, lambda e, mn=mn, ex=ex: e.activation(ex, mn, AF.Exp), sem="s")
                    smark = w.mark("s")
                    w.wait(VE, "s", smark)
                    w.wait(VE, "spq%d" % (m % 2))
                    w.op(VE, lambda e, ex=ex, qk_out=qk_out: e.scalar_tensor_tensor(
                        qk_out, pRope[:, 0:TT], 0.0, ex, ALU.max, ALU.add), sem="v")
                    vmark_mn[0] = w.mark("v")
                    w.wait(SY, "v")
                    w.dma(SY, dst_spill[m * 128:(m + 1) * 128, t0:t0 + TT], qk_out,
                          "spq%d" % (m % 2))
                    if wi == 1:
                        for tc in range(NCH):
                            w.wait(TE, "v")
                            w.op(TE, lambda e, qk_out=qk_out, tc=tc: e.transpose(
                                pTp[tc % 2], qk_out[:, tc * 128:(tc + 1) * 128],
                                id_b_sb[:, :]), sem="t")
                            w.wait(VE, "t")
                            w.op(VE, lambda e, tc=tc, m=m: e.tensor_copy(
                                bkt[:, tc * 1024 + m * 128:tc * 1024 + (m + 1) * 128],
                                pTp[tc % 2]), sem="v")
                else:
                    vb = bv[:, (i % 2) * TT:(i % 2) * TT + TT]
                    for tc in range(NCH):
                        w.wait(TE, "s")
                        w.wait(TE, "v")
                        w.op(TE, lambda e, vb=vb, tc=tc: e.transpose(
                            pTp[tc % 2], vb[:, tc * 128:(tc + 1) * 128],
                            id_b_sb[:, :]), sem="t")
                        w.wait(VE, "t")
                        w.op(VE, lambda e, tc=tc, m=m: e.tensor_copy(
                            bva[:, tc * 1040 + 2 * m * NDST:tc * 1040 + (2 * m + 2) * NDST]
                            .rearrange("p (hh dd) -> p hh dd", hh=2)[:, :, 0:Dh],
                            pTp[tc % 2].rearrange("p (hh dd) -> p hh dd", hh=2)), sem="v")

            for step in range(25):
                if step < 24:
                    a_stage1(step)
                if step >= 1:
                    a_stage2(step - 1)
            w.op(VE, lambda e: e.memset(
                bva.rearrange("p (cc hh dd) -> p cc hh dd", cc=NCH, hh=H)[:, :, :, Dh:],
                1.0), sem="v")
            w.wait(SY, "v")
            w.dma(SY, sp_va[t0:t0 + TT, :].rearrange("(cc p) f -> p cc f", p=128),
                  bva.rearrange("p (cc f) -> p cc f", cc=NCH), "spv")
            if A_LEVEL < 3:
                return

            for tc in range(NCH):
                g = j * NCH + tc
                tc0 = t0 + tc * CH
                guard(w)
                for kc in range(DC):
                    w.op(TE, lambda e, kc=kc, tc0=tc0: e.matmul(
                        pBeta, biga[:, kc * T + tc0:kc * T + tc0 + CH],
                        wbeta_sb[:, kc * HC:(kc + 1) * HC],
                        start=(kc == 0), stop=(kc == DC - 1)), sem="t" if kc == DC - 1 else None)
                w.wait(VE, "t")
                w.op(VE, lambda e: e.tensor_add(zrow_sb[:, :], pBeta, bb_bc_sb[:, :]), sem="v")
                w.wait(SC, "v")
                w.op(SC, lambda e: e.activation(zrow_sb[:, :], zrow_sb[:, :], AF.Sigmoid), sem="s")
                w.wait(VE, "s")
                w.op(VE, lambda e: e.tensor_scalar(
                    zrow_sb[:, :], zrow_sb[:, :], 0.001, 0.999, ALU.max, ALU.min), sem="v")
                w.wait(SC, "v")
                w.op(SC, lambda e: e.activation(lb_sb[:, :], zrow_sb[:, :], AF.Ln), sem="s")
                w.wait(TE, "s")
                guard(w)
                w.op(TE, lambda e: e.matmul(pLloc, tri_f_sb[:, :], lb_sb[:, :],
                                            start=True, stop=True), sem=None)
                w.op(TE, lambda e: e.matmul(pLglob, tri_f_sb[:, :], lb_sb[:, :],
                                            start=True, stop=False), sem=None)
                w.op(TE, lambda e: e.matmul(pLglob, ones_f_sb[0:1, :], carry_sb[0:1, :],
                                            start=False, stop=True), sem=None)
                w.op(TE, lambda e: e.matmul(pMisc[0:1, :], onescol_f_sb[:, :], lb_sb[:, :],
                                            start=True, stop=True), sem="t")
                w.wait(VE, "t")
                w.wait(VE, "spl")
                w.op(VE, lambda e: e.tensor_copy(lloc_sb[:, :], pLloc), sem="v")
                w.op(VE, lambda e: e.tensor_copy(lglob_sb[:, :], pLglob), sem="v")
                w.op(VE, lambda e: e.tensor_copy(dsum_sb[:, :], pMisc[0:1, :]), sem="v")
                w.wait(SY, "v")
                w.dma(SY, sp_L[tc0:tc0 + CH, 0:HC], lloc_sb[:, :], "spl")
                w.dma(SY, sp_L[tc0:tc0 + CH, HC:2 * HC], lglob_sb[:, :], "spl")
                w.wait(TE, "v")
                w.op(TE, lambda e: e.transpose(pTcomb[0:HC, 0:CH], lloc_sb[:, :],
                                               id_f_sb[:, :]), sem="t")
                w.wait(VE, "t")
                w.op(VE, lambda e: e.tensor_copy(llocT_sb[:, :], pTcomb[0:HC, 0:CH]), sem="v")
                w.wait(SY, "v")
                w.dma(SY, sp_Lr[g, :, :], llocT_sb[:, :], "spl")
                w.wait(TE, "v")
                w.op(TE, lambda e: e.matmul(pScT, ones_f_sb[0:1, :], dsum_sb[0:1, :],
                                            start=True, stop=True), sem="t")
                w.wait(VE, "t")
                w.op(VE, lambda e: e.tensor_sub(scarg_sb[:, :], pScT, lloc_sb[:, :]), sem="v")
                w.wait(SC, "v")
                w.op(SC, lambda e: e.activation(sc_sb[:, :], scarg_sb[:, :], AF.Exp), sem="s")
                w.op(SC, lambda e: e.activation(adsum_sb[:, :], dsum_sb[:, :], AF.Exp), sem="s")
                w.wait(TE, "s")
                w.op(TE, lambda e: e.matmul(pAbc[0:64, 0:HC], ones_f_sb[0:1, 0:64],
                                            adsum_sb[0:1, :], start=True, stop=True), sem="t")
                w.wait(VE, "t")
                w.op(VE, lambda e: e.tensor_copy(asb_sb[:, :], pAbc[0:64, 0:HC]), sem="v")
                w.op(VE, lambda e: e.tensor_add(carry_sb[:, :], carry_sb[:, :], dsum_sb[:, :]), sem="v")
                if A_LEVEL < 4:
                    continue
                w.wait(SY, "v")
                w.dma(SY, sp_mst[g, :, :], m_bf[:, :], "spm")
                khat_all = bigb[:, 15424:17472]
                w.wait(VE, "s")
                w.op(VE, lambda e, tc=tc: e.tensor_tensor(
                    khat_all[:, :].rearrange("p (h c d) -> p h c d", h=H, c=C),
                    bkt[:, tc * 1024:(tc + 1) * 1024]
                    .rearrange("p (h d) -> p h d", h=H)[:, :, None, :]
                    .broadcast_to([128, H, C, Dh]),
                    sc_sb[:, :].rearrange("p (h c) -> p h c", h=H)[:, :, :, None]
                    .broadcast_to([128, H, C, Dh]),
                    ALU.mult), sem="v")
                dbanks = [pMain[0], pMain[1], ps_scan2[:, :], ps_bcast[:, :], ps_scan1[:, :]]
                w.wait(TE, "v")
                w.wait(TE, "s")
                for hcn in range(HC):
                    h = hcn // 2
                    dv = dbanks[hcn // 7][0:64, (hcn % 7) * 65:(hcn % 7) * 65 + 65]
                    w.op(TE, lambda e, hcn=hcn, h=h, tc=tc, dv=dv: e.matmul(
                        dv, khat_all[:, hcn * Dh:(hcn + 1) * Dh],
                        bva[:, tc * 1040 + h * NDST:tc * 1040 + (h + 1) * NDST],
                        start=True, stop=True), sem="t" if hcn == HC - 1 else None)
                w.wait(VE, "t")
                for hcn in range(HC):
                    dv = dbanks[hcn // 7][0:64, (hcn % 7) * 65:(hcn % 7) * 65 + 65]
                    w.op(VE, lambda e, hcn=hcn, dv=dv: e.scalar_tensor_tensor(
                        m_sb[:, hcn * NDST:(hcn + 1) * NDST],
                        m_sb[:, hcn * NDST:(hcn + 1) * NDST],
                        asb_sb[:, hcn:hcn + 1],
                        dv, ALU.mult, ALU.add), sem="v")
                w.wait(VE, "spm")
                w.op(VE, lambda e: e.tensor_copy(m_bf[:, :], m_sb[:, :]), sem="v")

        def phase_b_tile(w: Walk, j):
            t0 = j * TT
            w.wait(SY, "t")
            w.wait(SY, "v")
            w.wait(SC, "spa")
            w.wait(VE, "spa")
            w.dma(SY, bqt.rearrange("p (cc tt) -> p cc tt", cc=DC),
                  sp_q.rearrange("(cc p) tt -> p cc tt", p=128)[:, :, t0:t0 + TT], "rlq")
            w.dma(SY, bkt2.rearrange("p (cc tt) -> p cc tt", cc=DC),
                  sp_k.rearrange("(cc p) tt -> p cc tt", p=128)[:, :, t0:t0 + TT], "rlq")
            w.dma(SY, bva2.rearrange("p (cc f) -> p cc f", cc=NCH),
                  sp_va[t0:t0 + TT, :].rearrange("(cc p) f -> p cc f", p=128), "rlq")
            for tc in range(NCH):
                g = j * NCH + tc
                tc0 = t0 + tc * CH
                msl = mstl_sb[:, (tc % 2) * MCOLS:(tc % 2) * MCOLS + MCOLS]
                w.wait(SY, "t")
                w.wait(SY, "s")
                w.wait(SY, "v")
                w.dma(SY, msl[0:64, :], sp_mst[g, :, :], "rlm%d" % (tc % 2))
                w.dma(SY, msl[64:128, :], sp_mst[g, :, :], "rlm%d" % (tc % 2))
                w.dma(SY, lloc_sb[:, :], sp_L[tc0:tc0 + CH, 0:HC], "rlm2")
                w.dma(SY, lglob_sb[:, :], sp_L[tc0:tc0 + CH, HC:2 * HC], "rlm2")
                w.dma(SY, lrows_sb[:, :],
                      sp_Lr[g].rearrange("hh tt -> (hh tt)")[None, :], "rlm2")
                w.wait(VE, "rlm2")
                w.op(VE, lambda e: e.tensor_single_scalar(
                    negl_sb[:, :], lloc_sb[:, :], -1.0, ALU.mult), sem="v")
                w.wait(SC, "rlm2")
                w.op(SC, lambda e: e.activation(expll_sb[:, :], lloc_sb[:, :], AF.Exp), sem="s")
                w.op(SC, lambda e: e.activation(explg_sb[:, :], lglob_sb[:, :], AF.Exp), sem="s")
                w.wait(TE, "rlq")
                w.wait(TE, "rlm%d" % (tc % 2))
                w.wait(TE, "rlm2")
                w.wait(TE, "s")
                w.wait(TE, "v")
                bSt = [ps_scan0[:, 0:128], ps_scan2[:, 0:128]]
                bM0 = [ps_scan0[:, 128:256], ps_scan2[:, 128:256]]
                bM1 = [ps_scan0[:, 256:384], ps_scan2[:, 256:384]]
                bNum = [ps_scan1[:, 264:329], ps_main0[:, 264:329]]
                bInt = [ps_scan1[:, 0:260], ps_main0[:, 0:260]]
                vmark_p = [0] * H
                smark2 = [0] * H
                vmark2 = [0] * H

                def b_stage1(h):
                    hc0, hc1 = 2 * h, 2 * h + 1
                    cb = h // 2
                    pb = (h % 2) * 64
                    par = h % 2
                    qf = bqt[pb:pb + 64, cb * TT + tc * CH:cb * TT + tc * CH + CH]
                    kf = bkt2[pb:pb + 64, cb * TT + tc * CH:cb * TT + tc * CH + CH]
                    if h >= 2:
                        w.wait(TE, "v", vmark_p[h - 2])
                    else:
                        w.wait(TE, "v")
                        w.wait(TE, "s")
                    w.op(TE, lambda e: e.matmul(bSt[par], kf, qf, start=True, stop=True), sem=None)
                    for hci, (hcx, bM) in enumerate(((hc0, bM0[par]), (hc1, bM1[par]))):
                        w.op(TE, lambda e, hcx=hcx, bM=bM: e.matmul(
                            bM, ones_f_sb[0:1, :], lrows_sb[0:1, hcx * CH:(hcx + 1) * CH],
                            start=True, stop=False), sem=None)
                        w.op(TE, lambda e, hcx=hcx, bM=bM: e.matmul(
                            bM, lrows_sb[0:1, hcx * CH:(hcx + 1) * CH], negones_sb[0:1, :],
                            start=False, stop=True), sem="t" if hci == 1 else None)
                    tmark_st = w.mark("t")
                    stm = bsc2[:, par * 640:par * 640 + 128]
                    mk01 = ctmp[:, par * 512:par * 512 + 256]
                    a01 = ctmp[:, 1024 + par * 256:1024 + par * 256 + 256]
                    p01 = bsc2[:, par * 640 + 384:par * 640 + 640]
                    pM01 = [ps_scan0[:, 128:384], ps_scan2[:, 128:384]][par]
                    w.wait(VE, "t", tmark_st)
                    w.op(VE, lambda e: e.tensor_mul(stm, bSt[par], tri_b_sb[:, :]), sem="v")
                    w.op(VE, lambda e: e.tensor_single_scalar(
                        a01, pM01, 0.0, ALU.min), sem="v")
                    w.wait(SC, "v")
                    w.op(SC, lambda e: e.activation(mk01, a01, AF.Exp), sem="s")
                    w.wait(VE, "s")
                    w.op(VE, lambda e: e.tensor_tensor(
                        p01.rearrange("p (c f) -> p c f", c=2),
                        stm[:, None, :].broadcast_to([128, 2, 128]),
                        mk01.rearrange("p (c f) -> p c f", c=2),
                        ALU.mult), sem="v")
                    vmark_p[h] = w.mark("v")

                def b_stage2(h):
                    hc0, hc1 = 2 * h, 2 * h + 1
                    cb = h // 2
                    pb = (h % 2) * 64
                    par = h % 2
                    qf = bqt[pb:pb + 64, cb * TT + tc * CH:cb * TT + tc * CH + CH]
                    vaug_h = bva2[:, tc * 1040 + h * NDST:tc * 1040 + (h + 1) * NDST]
                    p0 = bsc2[:, par * 640 + 384:par * 640 + 512]
                    p1 = bsc2[:, par * 640 + 512:par * 640 + 640]
                    w.wait(TE, "v", vmark_p[h])
                    if h >= 2:
                        w.wait(TE, "s", smark2[h - 2])
                        w.wait(TE, "v", vmark2[h - 2])
                    w.op(TE, lambda e: e.matmul(bNum[par], p0, vaug_h, start=True, stop=False), sem=None)
                    w.op(TE, lambda e: e.matmul(bNum[par], p1, vaug_h, start=False, stop=True), sem=None)
                    w.op(TE, lambda e: e.matmul(
                        bInt[par][:, 0:65], qf, msl[pb:pb + 64, hc0 * NDST:(hc0 + 1) * NDST],
                        start=True, stop=True), sem=None)
                    w.op(TE, lambda e: e.matmul(
                        bInt[par][:, 65:130], qf, msl[pb:pb + 64, hc1 * NDST:(hc1 + 1) * NDST],
                        start=True, stop=True), sem=None)
                    w.op(TE, lambda e: e.matmul(
                        bInt[par][:, 130:195], qf, sinb_sb[pb:pb + 64, hc0 * NDST:(hc0 + 1) * NDST],
                        start=True, stop=True), sem=None)
                    w.op(TE, lambda e: e.matmul(
                        bInt[par][:, 195:260], qf, sinb_sb[pb:pb + 64, hc1 * NDST:(hc1 + 1) * NDST],
                        start=True, stop=True), sem="t")
                    tmark_num = w.mark("t")
                    nsb = num_sb[:, par * NDST:(par + 1) * NDST]
                    w.wait(SC, "t", tmark_num)
                    w.op(SC, lambda e: e.copy(nsb, bNum[par]), sem="s")
                    smark2[h] = w.mark("s")
                    w.wait(VE, "s", smark2[h])
                    for (off, esb, hcx) in [(0, expll_sb, hc0), (65, expll_sb, hc1),
                                            (130, explg_sb, hc0), (195, explg_sb, hc1)]:
                        w.op(VE, lambda e, off=off, esb=esb, hcx=hcx: e.scalar_tensor_tensor(
                            nsb, bInt[par][:, off:off + 65], esb[:, hcx:hcx + 1],
                            nsb, ALU.mult, ALU.add), sem="v")
                    w.op(VE, lambda e: e.tensor_single_scalar(
                        rden_sb[:, par * 2:par * 2 + 1], nsb[:, Dh:Dh + 1], EPS, ALU.add), sem="v")
                    w.op(VE, lambda e: e.reciprocal(
                        rden_sb[:, par * 2 + 1:par * 2 + 2], rden_sb[:, par * 2:par * 2 + 1]), sem="v")
                    w.op(VE, lambda e: e.tensor_single_scalar(
                        byt[:, h * Dh:(h + 1) * Dh], nsb[:, 0:Dh],
                        rden_sb[:, par * 2 + 1:par * 2 + 2], ALU.mult), sem="v")
                    vmark2[h] = w.mark("v")

                for step in range(H + 1):
                    if step < H:
                        b_stage1(step)
                    if step >= 1:
                        b_stage2(step - 1)
                for m in range(DC):
                    w.wait(TE, "v")
                    w.op(TE, lambda e, m=m: e.transpose(
                        pTp[m % 2], byt[:, m * 128:(m + 1) * 128], id_b_sb[:, :]), sem="t")
                    w.wait(VE, "t")
                    w.op(VE, lambda e, m=m, tc=tc: e.tensor_copy(
                        byf[:, m * TT + tc * CH:m * TT + tc * CH + CH], pTp[m % 2]), sem="v")
            w.dma(SY, stats_sb[0:1, 0:TT], sp_stats[0:1, t0:t0 + TT], "rlq")
            w.dma(SY, stats_sb[0:1, TT:2 * TT], sp_stats[1:2, t0:t0 + TT], "rlq")
            w.wait(TE, "rlq")
            guard(w)
            w.op(TE, lambda e: e.matmul(pB1[:, 0:TT], ones_f_sb[0:1, :],
                                        stats_sb[0:1, 0:TT], start=True, stop=True), sem=None)
            w.op(TE, lambda e: e.matmul(pB2[:, 0:TT], ones_f_sb[0:1, :],
                                        stats_sb[0:1, TT:2 * TT], start=True, stop=True), sem="t")
            mark_bcq = w.mark("t")
            w.wait(VE, "spa")
            for m in range(DC):
                pm = pMain[m % 2]
                ws = wring[:, (m % 2) * 1024:(m % 2) * 1024 + 1024]
                xc = cx[:, (m % 2) * TT:(m % 2) * TT + TT]
                xnf = ctmp[:, (m % 2) * TT:(m % 2) * TT + TT]
                w.wait(SY, "v")
                w.dma(SY, xc, x_fm[m * 128:(m + 1) * 128, t0:t0 + TT], "ldx%d" % (m % 2))
                w.wait(SY, "t")
                cache_load(w, UOF["wo"] + m, 1, ws, "ldw%d" % (m % 2))
                w.wait(TE, "ldw%d" % (m % 2))
                guard(w)
                for kc in range(DC):
                    w.op(TE, lambda e, kc=kc, pm=pm, ws=ws: e.matmul(
                        pm[:, 0:TT], ws[:, kc * 128:(kc + 1) * 128],
                        byf[:, kc * TT:(kc + 1) * TT],
                        start=(kc == 0), stop=(kc == DC - 1)), sem="t" if kc == DC - 1 else None)
                w.wait(VE, "ldx%d" % (m % 2))
                w.wait(VE, "t", mark_bcq)
                w.op(VE, lambda e, xc=xc, xnf=xnf: e.tensor_mul(xnf, xc, pB1[:, 0:TT]), sem="v")
                w.op(VE, lambda e, xnf=xnf: e.tensor_sub(xnf, xnf, pB2[:, 0:TT]), sem="v")
                w.wait(VE, "t")
                w.op(VE, lambda e, m=m, pm=pm, xnf=xnf: e.scalar_tensor_tensor(
                    ch_[:, m * TT:(m + 1) * TT], pm[:, 0:TT], vecs["bo"][:, m:m + 1],
                    xnf, ALU.add, ALU.add), sem="v")
            guard(w)
            for c in range(DC):
                hb = bsc2[:, 640:640 + TT]
                h2b = bsc2[:, 640 + TT:640 + 2 * TT]
                w.wait(SC, "v")
                if c >= 1:
                    w.wait(SC, "t")
                w.op(SC, lambda e, c=c, hb=hb: e.copy(hb, ch_[:, c * TT:(c + 1) * TT]), sem="s")
                w.op(SC, lambda e, c=c, h2b=h2b: e.activation(
                    h2b, ch_[:, c * TT:(c + 1) * TT], AF.Square), sem="s")
                w.wait(TE, "s")
                w.op(TE, lambda e, hb=hb, c=c: e.matmul(
                    pB1[0:1, 0:TT], onescol_b_sb[:, :], hb,
                    start=(c == 0), stop=(c == DC - 1)), sem=None)
                w.op(TE, lambda e, h2b=h2b, c=c: e.matmul(
                    pB2[0:1, 0:TT], onescol_b_sb[:, :], h2b,
                    start=(c == 0), stop=(c == DC - 1)), sem="t")
            mu = stats_sb[0:1, 0:TT]
            ex2 = stats_sb[0:1, TT:2 * TT]
            musq = stats_sb[0:1, 2 * TT:3 * TT]
            var = stats_sb[0:1, TT:2 * TT]
            sd = stats_sb[0:1, 2 * TT:3 * TT]
            rstd = stats_sb[0:1, TT:2 * TT]
            murstd = stats_sb[0:1, 2 * TT:3 * TT]
            w.wait(SC, "t")
            w.op(SC, lambda e: e.activation(mu, pB1[0:1, 0:TT], AF.Copy, scale=1.0 / D), sem="s")
            w.op(SC, lambda e: e.activation(ex2, pB2[0:1, 0:TT], AF.Copy, scale=1.0 / D), sem="s")
            w.wait(VE, "s")
            w.op(VE, lambda e: e.tensor_mul(musq, mu, mu), sem="v")
            w.op(VE, lambda e: e.tensor_sub(var, ex2, musq), sem="v")
            w.wait(SC, "v")
            w.op(SC, lambda e: e.activation(sd, var, AF.Sqrt, bias=eps_sb[0:1, 0:1]), sem="s")
            w.wait(VE, "s")
            w.op(VE, lambda e: e.reciprocal(rstd, sd), sem="v")
            w.op(VE, lambda e: e.tensor_mul(murstd, mu, rstd), sem="v")
            w.wait(TE, "v")
            w.op(TE, lambda e: e.matmul(pB1[:, 0:TT], ones_f_sb[0:1, :], rstd,
                                        start=True, stop=True), sem=None)
            w.op(TE, lambda e: e.matmul(pB2[:, 0:TT], ones_f_sb[0:1, :], murstd,
                                        start=True, stop=True), sem="t")
            mark_bc2 = w.mark("t")
            for c in range(DC):
                tmp = ctmp[:, 0:TT]
                tmp2 = ctmp[:, TT:2 * TT]
                w.wait(VE, "t", mark_bc2)
                w.wait(VE, "spa")
                w.op(VE, lambda e, c=c, tmp=tmp: e.tensor_mul(
                    tmp, ch_[:, c * TT:(c + 1) * TT], pB1[:, 0:TT]), sem="v")
                w.op(VE, lambda e, tmp=tmp, tmp2=tmp2: e.tensor_sub(tmp2, tmp, pB2[:, 0:TT]), sem="v")
                w.op(VE, lambda e, c=c, tmp2=tmp2: e.tensor_scalar(
                    tmp2, tmp2, vecs["ln_g"][:, c:c + 1], vecs["ln_b"][:, c:c + 1],
                    ALU.mult, ALU.add), sem="v")
                w.wait(SY, "v")
                w.dma(SY, sp_attn[c * 128:(c + 1) * 128, t0:t0 + TT], tmp2, "spa")

        def cache_load(w, unit, count, slot, sem):
            if count == 1:
                w.dma(SY, slot, wc[unit], sem)
            else:
                w.dma(SY, slot.rearrange("p (u f) -> p u f", u=count),
                      wc[unit:unit + count].rearrange("u p f -> p u f"), sem)

        def phase_c_half(w: Walk, hf):
            h0 = hf * HT
            njj = min(4, NTILES)
            npass = 2 if NTILES >= 4 else 1
            ring = Ring(w, depth=4)
            gring = Ring(w, depth=2)
            p4 = [pMain[0], pMain[1], ps_aux[:, :], ps_bcast[:, :]]
            for m in range(DC):
                ws = wring[:, (m % 2) * 1024:(m % 2) * 1024 + 1024]
                gring.wait_sy()
                cache_load(w, UOF["lin0_w"] + m, 1, ws, "ldw%d" % (m % 2))
                w.wait(TE, "ldw%d" % (m % 2))
                for jj in range(njj):
                    i = m * njj + jj
                    pm = p4[i % 4]
                    tl0 = h0 + jj * TT
                    ring.wait_te()
                    for kc in range(DC):
                        w.op(TE, lambda e, kc=kc, pm=pm, ws=ws, tl0=tl0: e.matmul(
                            pm[:, 0:TT], ws[:, kc * 128:(kc + 1) * 128],
                            biga[:, kc * T + tl0:kc * T + tl0 + TT],
                            start=(kc == 0), stop=(kc == DC - 1)), sem="t" if kc == DC - 1 else None)
                    w.wait(SC, "t")
                    w.op(SC, lambda e, m=m, jj=jj, pm=pm: e.activation(
                        bf0[:, m * HT + jj * TT:m * HT + (jj + 1) * TT], pm[:, 0:TT],
                        AF.Identity, bias=vecs["lin0_b"][:, m:m + 1]), sem="s")
                    ring.push()
                gring.push()
            ring = Ring(w, depth=2)
            gring = Ring(w, depth=2)
            pairs = [(pMain[0], pMain[1]), (ps_aux[:, :], ps_bcast[:, :])]
            for m in range(DC):
                ws1 = wring[:, (m % 2) * 1024:(m % 2) * 1024 + 1024]
                ws2 = wring[:, 2048 + (m % 2) * 1024:2048 + (m % 2) * 1024 + 1024]
                gring.wait_sy()
                cache_load(w, UOF["s1w1"] + m, 1, ws1, "ldw%d" % (m % 2))
                cache_load(w, UOF["s1w2"] + m, 1, ws2, "ldw%d" % (m % 2))
                w.wait(TE, "ldw%d" % (m % 2))
                for jj in range(njj):
                    i = m * njj + jj
                    pa, pb = pairs[i % 2]
                    tl0 = h0 + jj * TT
                    ring.wait_te()
                    for kc in range(DC):
                        w.op(TE, lambda e, kc=kc, ws1=ws1, jj=jj, pa=pa: e.matmul(
                            pa[:, 0:TT], ws1[:, kc * 128:(kc + 1) * 128],
                            bf0[:, kc * HT + jj * TT:kc * HT + (jj + 1) * TT],
                            start=(kc == 0), stop=(kc == DC - 1)), sem="t" if kc == DC - 1 else None)
                    for kc in range(DC):
                        w.op(TE, lambda e, kc=kc, ws2=ws2, jj=jj, pb=pb: e.matmul(
                            pb[:, 0:TT], ws2[:, kc * 128:(kc + 1) * 128],
                            bf0[:, kc * HT + jj * TT:kc * HT + (jj + 1) * TT],
                            start=(kc == 0), stop=(kc == DC - 1)), sem="t" if kc == DC - 1 else None)
                    gsl = bgg[:, (i % 2) * TT:(i % 2) * TT + TT]
                    w.wait(SC, "t")
                    w.op(SC, lambda e, m=m, gsl=gsl, pa=pa: e.activation(
                        gsl, pa[:, 0:TT], AF.Silu, bias=vecs["s1b1"][:, m:m + 1]), sem="s")
                    w.wait(VE, "s")
                    w.op(VE, lambda e, m=m, tl0=tl0, gsl=gsl, pb=pb: e.scalar_tensor_tensor(
                        biga[:, m * T + tl0:m * T + tl0 + TT],
                        pb[:, 0:TT], vecs["s1b2"][:, m:m + 1], gsl,
                        ALU.add, ALU.mult), sem="v")
                    ring.push()
                gring.push()
            for jj in range(njj):
                tl0 = h0 + jj * TT
                for tc in range(NCH):
                    tt0 = tl0 + tc * CH
                    guard(w)
                    for kc in range(DC):
                        w.op(TE, lambda e, kc=kc, tt0=tt0: e.matmul(
                            pGate, biga[:, kc * T + tt0:kc * T + tt0 + CH],
                            gatew_sb[:, kc * E:(kc + 1) * E],
                            start=(kc == 0), stop=(kc == DC - 1)), sem="t" if kc == DC - 1 else None)
                    sc_scores = csc[:, 0:8]
                    prob = csc[:, 8:16]
                    work = csc[:, 16:24]
                    sel = csc[:, 24:32]
                    msk = csc[:, 32:40]
                    rowm = csc[:, 40:41]
                    rows = csc[:, 41:42]
                    comb = csc[:, 48:56]
                    w.wait(VE, "t")
                    w.op(VE, lambda e: e.reduce_max(rowm, pGate, mybir.AxisListType.X), sem="v")
                    w.op(VE, lambda e: e.tensor_single_scalar(
                        sc_scores, pGate, rowm[:, 0:1], ALU.subtract), sem="v")
                    w.wait(SC, "v")
                    w.op(SC, lambda e: e.activation(prob, sc_scores, AF.Exp), sem="s")
                    w.wait(VE, "s")
                    w.op(VE, lambda e: e.reduce_sum(rows, prob, mybir.AxisListType.X), sem="v")
                    w.op(VE, lambda e: e.reciprocal(rowm[:, 0:1], rows[:, 0:1]), sem="v")
                    w.op(VE, lambda e: e.tensor_single_scalar(
                        prob, prob, rowm[:, 0:1], ALU.mult), sem="v")
                    w.op(VE, lambda e: e.tensor_copy(work, prob), sem="v")
                    w.op(VE, lambda e: e.memset(sel, 0.0), sem="v")
                    for it in range(4):
                        w.op(VE, lambda e: e.reduce_max(rowm, work, mybir.AxisListType.X), sem="v")
                        w.op(VE, lambda e: e.tensor_single_scalar(
                            msk, work, rowm[:, 0:1], ALU.is_ge), sem="v")
                        w.op(VE, lambda e: e.tensor_max(sel, sel, msk), sem="v")
                        w.op(VE, lambda e: e.scalar_tensor_tensor(
                            work, msk, -1e30, work, ALU.mult, ALU.add), sem="v")
                    w.op(VE, lambda e: e.tensor_mul(comb, prob, sel), sem="v")
                    w.op(VE, lambda e: e.reduce_sum(rows, comb, mybir.AxisListType.X), sem="v")
                    w.op(VE, lambda e: e.reciprocal(rowm[:, 0:1], rows[:, 0:1]), sem="v")
                    w.op(VE, lambda e: e.tensor_single_scalar(
                        comb, comb, rowm[:, 0:1], ALU.mult), sem="v")
                    w.wait(TE, "v")
                    w.op(TE, lambda e, comb=comb: e.transpose(
                        pTcomb[0:8, 0:128], comb, id_f_sb[:, :]), sem="t")
                    w.wait(VE, "t")
                    w.wait(VE, "bounce")
                    w.op(VE, lambda e, tc=tc: e.tensor_copy(
                        combf_sb[:, tc * CH:(tc + 1) * CH], pTcomb[0:8, 0:128]), sem="v")
                w.wait(GP, "v")
                w.dma(GP, sp_comb[:, tl0:tl0 + TT], combf_sb[:, :], "bounce")
            bc_sb = bigb[:, 8192:9216]
            for pp in range(npass):
                p0 = h0 + pp * PT
                w.wait(VE, "t")
                w.op(VE, lambda e: e.memset(routed[:, :], 0.0), sem="v")
                w.wait(GP, "bounce")
                pairs = [(pMain[0], pMain[1]), (ps_aux[:, :], ps_bcast[:, :])]
                gring13 = Ring(w, depth=3)
                gring2 = Ring(w, depth=2)
                for ee in range(E):
                    w.wait(GP, "t")
                    w.dma(GP, combrow_sb[:, :], sp_comb[ee:ee + 1, p0:p0 + PT], "bounce")
                    w.wait(TE, "bounce")
                    w.wait(TE, "v")
                    w.wait(TE, "s")
                    for t2 in range(2):
                        w.op(TE, lambda e, t2=t2: e.matmul(
                            pMain[t2][:, 0:TT], ones_b_sb[0:1, :],
                            combrow_sb[0:1, t2 * TT:(t2 + 1) * TT],
                            start=True, stop=True), sem="t")
                    w.wait(VE, "t")
                    w.op(VE, lambda e: e.tensor_copy(bc_sb[:, 0:TT], pMain[0][:, 0:TT]), sem="v")
                    w.op(VE, lambda e: e.tensor_copy(bc_sb[:, TT:2 * TT], pMain[1][:, 0:TT]), sem="v")
                    ring = Ring(w, depth=2)
                    for m in range(DC):
                        ws1 = bigb[:, 9216 + (m % 3) * 1024:9216 + (m % 3) * 1024 + 1024]
                        ws3 = bigb[:, 12288 + (m % 3) * 1024:12288 + (m % 3) * 1024 + 1024]
                        gring13.wait_sy()
                        cache_load(w, UOF["exw1"] + ee * 8 + m, 1, ws1, "ldw%d" % (m % 3))
                        cache_load(w, UOF["exw3"] + ee * 8 + m, 1, ws3, "ldw%d" % (m % 3))
                        w.wait(TE, "ldw%d" % (m % 3))
                        for t2 in range(2):
                            i = m * 2 + t2
                            pa, pb = pairs[i % 2]
                            tl0 = p0 + t2 * TT
                            ring.wait_te()
                            for kc in range(DC):
                                w.op(TE, lambda e, kc=kc, ws1=ws1, tl0=tl0, pa=pa: e.matmul(
                                    pa[:, 0:TT], ws1[:, kc * 128:(kc + 1) * 128],
                                    biga[:, kc * T + tl0:kc * T + tl0 + TT],
                                    start=(kc == 0), stop=(kc == DC - 1)), sem="t" if kc == DC - 1 else None)
                            for kc in range(DC):
                                w.op(TE, lambda e, kc=kc, ws3=ws3, tl0=tl0, pb=pb: e.matmul(
                                    pb[:, 0:TT], ws3[:, kc * 128:(kc + 1) * 128],
                                    biga[:, kc * T + tl0:kc * T + tl0 + TT],
                                    start=(kc == 0), stop=(kc == DC - 1)), sem="t" if kc == DC - 1 else None)
                            gsl = bgg[:, (i % 2) * TT:(i % 2) * TT + TT]
                            tb = btmp[:, (i % 2) * TT:(i % 2) * TT + TT]
                            w.wait(SC, "t")
                            w.op(SC, lambda e, gsl=gsl, pa=pa: e.activation(
                                gsl, pa[:, 0:TT], AF.Silu), sem="s")
                            w.wait(VE, "s")
                            w.op(VE, lambda e, tb=tb, gsl=gsl, pb=pb: e.tensor_mul(
                                tb, pb[:, 0:TT], gsl), sem="v")
                            w.op(VE, lambda e, m=m, t2=t2, tb=tb: e.tensor_mul(
                                bhh[:, m * PT + t2 * TT:m * PT + (t2 + 1) * TT],
                                tb, bc_sb[:, t2 * TT:(t2 + 1) * TT]), sem="v")
                            ring.push()
                        gring13.push()
                    ring = Ring(w, depth=4)
                    p4 = [pMain[0], pMain[1], ps_aux[:, :], ps_bcast[:, :]]
                    for mo in range(DC):
                        ws2 = wring[:, (mo % 2) * 1024:(mo % 2) * 1024 + 1024]
                        gring2.wait_sy()
                        cache_load(w, UOF["exw2"] + ee * 8 + mo, 1, ws2, "ldw%s" % "ab"[mo % 2])
                        w.wait(TE, "ldw%s" % "ab"[mo % 2])
                        for t2 in range(2):
                            i = mo * 2 + t2
                            pm = p4[i % 4]
                            ring.wait_te()
                            for kc in range(DC):
                                w.op(TE, lambda e, kc=kc, ws2=ws2, t2=t2, pm=pm: e.matmul(
                                    pm[:, 0:TT], ws2[:, kc * 128:(kc + 1) * 128],
                                    bhh[:, kc * PT + t2 * TT:kc * PT + (t2 + 1) * TT],
                                    start=(kc == 0), stop=(kc == DC - 1)), sem="t" if kc == DC - 1 else None)
                            w.wait(VE, "t")
                            w.op(VE, lambda e, mo=mo, t2=t2, pm=pm: e.tensor_add(
                                routed[:, mo * PT + t2 * TT:mo * PT + (t2 + 1) * TT],
                                routed[:, mo * PT + t2 * TT:mo * PT + (t2 + 1) * TT],
                                pm[:, 0:TT]), sem="v")
                            ring.push()
                        gring2.push()
                phase_c4_pass(w, p0)

        def phase_c4_pass(w: Walk, p0):
            for t2 in range(2):
                tl0 = p0 + t2 * TT
                ring = Ring(w, depth=2)
                gring = Ring(w, depth=2)
                pairs = [(pMain[0], pMain[1]), (ps_aux[:, :], ps_bcast[:, :])]
                for mh in range(16):
                    ws1 = wring[:, (mh % 2) * 1024:(mh % 2) * 1024 + 1024]
                    ws3 = wring[:, 2048 + (mh % 2) * 1024:2048 + (mh % 2) * 1024 + 1024]
                    gring.wait_sy()
                    cache_load(w, UOF["shw1"] + mh, 1, ws1, "ldw%d" % (mh % 2))
                    cache_load(w, UOF["shw3"] + mh, 1, ws3, "ldw%d" % (mh % 2))
                    w.wait(TE, "ldw%d" % (mh % 2))
                    pa, pb = pairs[mh % 2]
                    ring.wait_te()
                    for kc in range(DC):
                        w.op(TE, lambda e, kc=kc, ws1=ws1, tl0=tl0, pa=pa: e.matmul(
                            pa[:, 0:TT], ws1[:, kc * 128:(kc + 1) * 128],
                            biga[:, kc * T + tl0:kc * T + tl0 + TT],
                            start=(kc == 0), stop=(kc == DC - 1)), sem="t" if kc == DC - 1 else None)
                    for kc in range(DC):
                        w.op(TE, lambda e, kc=kc, ws3=ws3, tl0=tl0, pb=pb: e.matmul(
                            pb[:, 0:TT], ws3[:, kc * 128:(kc + 1) * 128],
                            biga[:, kc * T + tl0:kc * T + tl0 + TT],
                            start=(kc == 0), stop=(kc == DC - 1)), sem="t" if kc == DC - 1 else None)
                    gsl = bgg[:, (mh % 2) * TT:(mh % 2) * TT + TT]
                    w.wait(SC, "t")
                    w.op(SC, lambda e, gsl=gsl, pa=pa: e.activation(gsl, pa[:, 0:TT], AF.Silu), sem="s")
                    w.wait(VE, "s")
                    w.op(VE, lambda e, mh=mh, gsl=gsl, pb=pb: e.tensor_mul(
                        bshh[:, mh * TT:(mh + 1) * TT], pb[:, 0:TT], gsl), sem="v")
                    ring.push()
                    gring.push()
                ring = Ring(w, depth=4)
                gring = Ring(w, depth=2)
                p4 = [pMain[0], pMain[1], ps_aux[:, :], ps_bcast[:, :]]
                for mo in range(DC):
                    ws2 = wring[:, (mo % 2) * 2048:(mo % 2) * 2048 + 2048]
                    gring.wait_sy()
                    cache_load(w, UOF["shw2"] + 2 * mo, 2, ws2, "ldw%d" % (mo % 2))
                    w.wait(TE, "ldw%d" % (mo % 2))
                    pm = p4[mo % 4]
                    ring.wait_te()
                    for kc in range(16):
                        w.op(TE, lambda e, kc=kc, ws2=ws2, pm=pm: e.matmul(
                            pm[:, 0:TT], ws2[:, kc * 128:(kc + 1) * 128],
                            bshh[:, kc * TT:(kc + 1) * TT],
                            start=(kc == 0), stop=(kc == 15)), sem="t" if kc == 15 else None)
                    w.wait(VE, "t")
                    w.op(VE, lambda e, mo=mo, t2=t2, pm=pm: e.tensor_add(
                        bm1[:, t2 * 4096 + mo * TT:t2 * 4096 + (mo + 1) * TT], pm[:, 0:TT],
                        routed[:, mo * PT + t2 * TT:mo * PT + (t2 + 1) * TT]), sem="v")
                    ring.push()
                    gring.push()
            for t2 in range(2):
                tl0 = p0 + t2 * TT
                ring = Ring(w, depth=4)
                gring = Ring(w, depth=2)
                p4 = [pMain[0], pMain[1], ps_aux[:, :], ps_bcast[:, :]]
                for m in range(DC):
                    ws = wring[:, (m % 2) * 1024:(m % 2) * 1024 + 1024]
                    gring.wait_sy()
                    cache_load(w, UOF["lin1_w"] + m, 1, ws, "ldw%d" % (m % 2))
                    w.wait(TE, "ldw%d" % (m % 2))
                    pm = p4[m % 4]
                    ring.wait_te()
                    for kc in range(DC):
                        w.op(TE, lambda e, kc=kc, ws=ws, t2=t2, pm=pm: e.matmul(
                            pm[:, 0:TT], ws[:, kc * 128:(kc + 1) * 128],
                            bm1[:, t2 * 4096 + kc * TT:t2 * 4096 + (kc + 1) * TT],
                            start=(kc == 0), stop=(kc == DC - 1)), sem="t" if kc == DC - 1 else None)
                    w.wait(SC, "t")
                    w.op(SC, lambda e, m=m, pm=pm: e.activation(
                        bm2[:, m * TT:(m + 1) * TT], pm[:, 0:TT], AF.Identity,
                        bias=vecs["lin1_b"][:, m:m + 1]), sem="s")
                    ring.push()
                    gring.push()
                ring = Ring(w, depth=2)
                gring = Ring(w, depth=2)
                pairs = [(pMain[0], pMain[1]), (ps_aux[:, :], ps_bcast[:, :])]
                for m in range(DC):
                    ws1 = wring[:, (m % 2) * 1024:(m % 2) * 1024 + 1024]
                    ws2 = wring[:, 2048 + (m % 2) * 1024:2048 + (m % 2) * 1024 + 1024]
                    gring.wait_sy()
                    cache_load(w, UOF["s2w1"] + m, 1, ws1, "ldw%d" % (m % 2))
                    cache_load(w, UOF["s2w2"] + m, 1, ws2, "ldw%d" % (m % 2))
                    w.wait(TE, "ldw%d" % (m % 2))
                    pa, pb = pairs[m % 2]
                    ring.wait_te()
                    for kc in range(DC):
                        w.op(TE, lambda e, kc=kc, ws1=ws1, pa=pa: e.matmul(
                            pa[:, 0:TT], ws1[:, kc * 128:(kc + 1) * 128],
                            bm2[:, kc * TT:(kc + 1) * TT],
                            start=(kc == 0), stop=(kc == DC - 1)), sem="t" if kc == DC - 1 else None)
                    for kc in range(DC):
                        w.op(TE, lambda e, kc=kc, ws2=ws2, pb=pb: e.matmul(
                            pb[:, 0:TT], ws2[:, kc * 128:(kc + 1) * 128],
                            bm2[:, kc * TT:(kc + 1) * TT],
                            start=(kc == 0), stop=(kc == DC - 1)), sem="t" if kc == DC - 1 else None)
                    gsl = bgg[:, (m % 2) * TT:(m % 2) * TT + TT]
                    w.wait(SC, "t")
                    w.op(SC, lambda e, m=m, gsl=gsl, pa=pa: e.activation(
                        gsl, pa[:, 0:TT], AF.Silu, bias=vecs["s2b1"][:, m:m + 1]), sem="s")
                    w.wait(VE, "s")
                    if m == 0:
                        w.wait(VE, "outd")
                    w.op(VE, lambda e, m=m, gsl=gsl, pb=pb: e.scalar_tensor_tensor(
                        cffn[:, m * TT:(m + 1) * TT], pb[:, 0:TT],
                        vecs["s2b2"][:, m:m + 1], gsl, ALU.add, ALU.mult), sem="v")
                    ring.push()
                    gring.push()
                    xa = cxa[:, 0:TT]
                    at = ctmp2[:, 0:TT]
                    w.wait(SY, "v")
                    w.dma(SY, xa, x_fm[m * 128:(m + 1) * 128, tl0:tl0 + TT], "ldx0")
                    w.dma(SY, at, sp_attn[m * 128:(m + 1) * 128, tl0:tl0 + TT], "ldx0")
                    w.wait(VE, "ldx0")
                    w.op(VE, lambda e, m=m, xa=xa: e.tensor_add(
                        cffn[:, m * TT:(m + 1) * TT], cffn[:, m * TT:(m + 1) * TT], xa), sem="v")
                    w.op(VE, lambda e, m=m, at=at: e.tensor_add(
                        cffn[:, m * TT:(m + 1) * TT], cffn[:, m * TT:(m + 1) * TT], at), sem="v")
                    w.wait(SY, "v")
                    w.dma(SY, out_fm[m * 128:(m + 1) * 128, tl0:tl0 + TT],
                          cffn[:, m * TT:(m + 1) * TT], "outd")

        @block.sync
        def _(e):
            body(Walk("sync", e, sems))

        @block.tensor
        def _(e):
            body(Walk("tensor", e, sems))

        @block.vector
        def _(e):
            body(Walk("vector", e, sems))

        @block.scalar
        def _(e):
            body(Walk("scalar", e, sems))

        @block.gpsimd
        def _(e):
            body(Walk("gpsimd", e, sems))

    return nc


def _rope_tables(core):
    inv = 1.0 / 10000.0 ** (np.arange(0, Dh, 2, dtype=np.float64) / Dh)
    t0 = (core % 4) * T
    t = np.arange(t0, t0 + T, dtype=np.float64)
    fr = t[:, None] * inv[None, :]
    c = np.cos(fr).astype(np.float32).T
    s = np.sin(fr).astype(np.float32).T
    cs1 = np.zeros((128, T), np.float32)
    cs2 = np.zeros((128, T), np.float32)
    for blk in range(2):
        b0 = blk * 64
        cs1[b0:b0 + 32] = c
        cs1[b0 + 32:b0 + 64] = c
        cs2[b0:b0 + 32] = s
        cs2[b0 + 32:b0 + 64] = -s
    return cs1, cs2


_nc_cache = {}


def kernel(**inputs):
    f32 = lambda k: np.ascontiguousarray(np.asarray(inputs[k], np.float32))
    x = f32("L0_attn_x")

    tri = np.triu(np.ones((CH, CH), np.float32))
    ident = np.eye(CH, dtype=np.float32)
    swap = np.zeros((CH, CH), np.float32)
    for mcol in range(CH):
        blk, i = mcol // 64, mcol % 64
        swap[blk * 64 + (i + 32) % 64, mcol] = 1.0
    ones_row = np.ones((1, CH), np.float32)
    ones_col = np.ones((CH, 1), np.float32)

    vec = lambda k: np.ascontiguousarray(f32(k).reshape(DC, 128).T)
    gate_wt = np.ascontiguousarray(
        f32("L0_ffn_gate_w").reshape(E, DC, 128).transpose(2, 1, 0).reshape(128, DC * E))

    common = dict(
        wq=f32("L0_attn_Wq"), wk=f32("L0_attn_Wk"), wv=f32("L0_attn_Wv"),
        wbeta=f32("L0_attn_Wbeta"), wo=f32("L0_attn_Wo"),
        bbeta=f32("L0_attn_bbeta").reshape(1, HC), bo=vec("L0_attn_bo"),
        ln_g=vec("L0_attn_ln_gamma"), ln_b=vec("L0_attn_ln_beta"),
        lin0_w=f32("L0_ffn_lin0_w"), lin0_b=vec("L0_ffn_lin0_b"),
        s1w1=f32("L0_ffn_swi1_w1"), s1b1=vec("L0_ffn_swi1_b1"),
        s1w2=f32("L0_ffn_swi1_w2"), s1b2=vec("L0_ffn_swi1_b2"),
        gate_wt=gate_wt,
        exw1=f32("L0_ffn_exp_w1"), exw3=f32("L0_ffn_exp_w3"), exw2=f32("L0_ffn_exp_w2"),
        shw1=f32("L0_ffn_sh_w1"), shw3=f32("L0_ffn_sh_w3"), shw2=f32("L0_ffn_sh_w2"),
        lin1_w=f32("L0_ffn_lin1_w"), lin1_b=vec("L0_ffn_lin1_b"),
        s2w1=f32("L0_ffn_swi2_w1"), s2b1=vec("L0_ffn_swi2_b1"),
        s2w2=f32("L0_ffn_swi2_w2"), s2b2=vec("L0_ffn_swi2_b2"),
        tri_f=tri, tri_b=tri.astype(_bf), id_b=ident.astype(_bf), id_f=ident,
        swap_b=swap.astype(_bf), ones_f=ones_row, ones_b=ones_row.astype(_bf),
        onescol_f=ones_col, onescol_b=ones_col.astype(_bf),
        negones_f=-ones_row,
    )

    in_maps = []
    for core in range(NCORE):
        b = core // 4
        r = core % 4
        xs = np.ascontiguousarray(x[b, r * T:(r + 1) * T, :].T)
        cs1v, cs2v = _rope_tables(core)
        gm = np.zeros((1, NCORE), np.float32)
        for i in range(NCORE):
            gm[0, i] = 1.0 if (i // 4 == b and i % 4 < r) else 0.0
        m = dict(common)
        m.update(x_fm=xs, cs1=cs1v.astype(_bf), cs2=cs2v.astype(_bf), gmask=gm)
        in_maps.append(m)

    if "prog" not in _nc_cache:
        _nc_cache["prog"] = build_program()
    nc = _nc_cache["prog"]

    import os as _os
    res = run_bass_kernel_spmd(nc, in_maps, core_ids=list(range(NCORE)),
                               trace=bool(_os.environ.get("KTRACE")))
    global LAST_RESULTS
    LAST_RESULTS = res

    out = np.empty((B, S, D), np.float32)
    for core in range(NCORE):
        b = core // 4
        r = core % 4
        out[b, r * T:(r + 1) * T, :] = np.asarray(res.results[core]["out_fm"]).T
    return out


# version 14
# speedup vs baseline: 1.1001x; 1.0030x over previous
import numpy as np
import ml_dtypes
import concourse.bass as bass
import concourse.mybir as mybir
from concourse.bass_utils import run_bass_kernel_spmd

F32 = mybir.dt.float32
BF16 = mybir.dt.bfloat16
AF = mybir.ActivationFunctionType
ALU = mybir.AluOpType

NCORE = 8
B, S, D, H, Dh, C = 2, 16384, 1024, 16, 64, 2
T = 4096
NT = 8
TT = 512
NCH = 4
CH = 128
DC = 8
HC = H * C
E = 8
SHD = 2048
PT = 1024
NHALF = 2
HT = 2048
LN_EPS = 1e-5
EPS = 1e-6
NDST = Dh + 1
MCOLS = HC * NDST

DEBUG = False
PHASES = "AGBC"
CLEVEL = 4
ALSUB = 4
A_LEVEL = 4
NTILES = NT
SHARED_CC = True
STRICT_SELF_WAITS = True
_bf = ml_dtypes.bfloat16
LAST_RESULTS = None


class Walk:

    def __init__(self, eng_name, eng, sems):
        self.eng_name = eng_name
        self.eng = eng
        self.sems = sems
        self.counts = {k: 0 for k in sems}

    def mark(self, sem):
        return self.counts[sem]

    def on(self, eng_name):
        return eng_name == self.eng_name

    def wait(self, eng_name, sem, val=None):
        v = self.counts[sem] if val is None else val
        if self.on(eng_name) and v > 0:
            self.eng.wait_ge(self.sems[sem], v)

    def dma(self, qname, out, in_, sem, **kw):
        self.counts[sem] += 16
        if self.on(qname):
            self.eng.dma_start(out=out, in_=in_, **kw).then_inc(self.sems[sem], 16)

    _SELF = {"vector": "v", "scalar": "s"}

    def op(self, eng_name, fn, sem=None):
        own = self._SELF.get(eng_name) if STRICT_SELF_WAITS else None
        if own is not None and self.on(eng_name) and self.counts[own] > 0:
            self.eng.wait_ge(self.sems[own], self.counts[own])
        if sem is not None:
            self.counts[sem] += 1
        if self.on(eng_name):
            last = fn(self.eng)
            if sem is not None:
                last.then_inc(self.sems[sem], 1)


def build_program():
    nc = bass.Bass("TRN2", target_bir_lowering=False)

    def din(name, shape, dt=F32):
        return nc.dram_tensor(name, shape, dt, kind="ExternalInput").ap()

    x_fm = din("x_fm", [D, T])
    wq = din("wq", [D, D]); wk = din("wk", [D, D]); wv = din("wv", [D, D])
    wbeta = din("wbeta", [D, HC]); wo = din("wo", [D, D])
    bbeta = din("bbeta", [1, HC]); bo = din("bo", [128, DC])
    ln_g = din("ln_g", [128, DC]); ln_b = din("ln_b", [128, DC])
    lin0_w = din("lin0_w", [D, D]); lin0_b = din("lin0_b", [128, DC])
    s1w1 = din("s1w1", [D, D]); s1b1 = din("s1b1", [128, DC])
    s1w2 = din("s1w2", [D, D]); s1b2 = din("s1b2", [128, DC])
    gate_wt = din("gate_wt", [128, DC * E])
    exw1 = din("exw1", [E, D, D]); exw3 = din("exw3", [E, D, D]); exw2 = din("exw2", [E, D, D])
    shw1 = din("shw1", [D, SHD]); shw3 = din("shw3", [D, SHD]); shw2 = din("shw2", [SHD, D])
    lin1_w = din("lin1_w", [D, D]); lin1_b = din("lin1_b", [128, DC])
    s2w1 = din("s2w1", [D, D]); s2b1 = din("s2b1", [128, DC])
    s2w2 = din("s2w2", [D, D]); s2b2 = din("s2b2", [128, DC])
    cs1 = din("cs1", [CH, T], BF16); cs2 = din("cs2", [CH, T], BF16)
    tri_f = din("tri_f", [CH, CH]); tri_b = din("tri_b", [CH, CH], BF16)
    id_b = din("id_b", [CH, CH], BF16); id_f = din("id_f", [CH, CH])
    swap_b = din("swap_b", [CH, CH], BF16)
    ones_f = din("ones_f", [1, CH]); ones_b = din("ones_b", [1, CH], BF16)
    negones_f = din("negones_f", [1, CH])
    onescol_f = din("onescol_f", [CH, 1])
    onescol_b = din("onescol_b", [CH, 1], BF16)
    gmask = din("gmask", [1, NCORE])

    out_fm = nc.dram_tensor("out_fm", [D, T], F32, kind="ExternalOutput").ap()

    skind = "ExternalOutput" if DEBUG else "Internal"

    def scr(name, shape, dt=F32):
        return nc.dram_tensor(name, shape, dt, kind=skind).ap()

    sp_q = scr("sp_q", [D, T], BF16)
    sp_k = scr("sp_k", [D, T], BF16)
    sp_va = scr("sp_va", [T, H * NDST], BF16)
    sp_L = scr("sp_L", [T, 2 * HC])
    sp_Lr = scr("sp_Lr", [T // CH, HC, CH])
    sp_mst = scr("sp_mst", [T // CH, 64, MCOLS], BF16)
    sp_attn = scr("sp_attn", [D, T])
    sp_stats = scr("sp_stats", [2, T])
    wc = nc.dram_tensor("wc", [320, 128, 1024], BF16, kind="Internal").ap()
    UOF = {"wq": 0, "wk": 8, "wv": 16, "wo": 24, "lin0_w": 32, "s1w1": 40,
           "s1w2": 48, "lin1_w": 56, "s2w1": 64, "s2w2": 72,
           "shw1": 80, "shw3": 96, "shw2": 112,
           "exw1": 128, "exw3": 192, "exw2": 256}
    sp_comb = scr("sp_comb", [E, T], BF16)
    cc_in = nc.dram_tensor("cc_in", [1, HC + 64 * MCOLS], F32, kind="Internal").ap()
    dbg_cc = (nc.dram_tensor("dbg_cc", [1, HC + 64 * MCOLS], F32,
                             kind="ExternalOutput").ap() if DEBUG else None)
    cc_out = nc.dram_tensor("cc_out", [NCORE, HC + 64 * MCOLS], F32,
                            addr_space="Shared" if SHARED_CC else "Local",
                            kind="Internal").ap()

    from contextlib import ExitStack
    ctx = ExitStack()

    def mk_sb(name, shape, dt=F32):
        return ctx.enter_context(nc.sbuf_tensor(name, shape, dt))

    with ctx:
        biga = mk_sb("biga", [128, DC * T], BF16)
        bigb = mk_sb("bigb", [128, 19200], BF16)
        bigc = mk_sb("bigc", [128, 8192], F32)
        wring = mk_sb("wring", [128, 4096], BF16)
        m_sb = mk_sb("m_sb", [64, MCOLS], F32)
        sin_sb = mk_sb("sin_sb", [64, MCOLS], F32)
        sinb_sb = mk_sb("sinb_sb", [128, MCOLS], BF16)
        m_bf = mk_sb("m_bf", [64, MCOLS], BF16)
        u64a = mk_sb("u64a", [128, 2 * MCOLS], BF16)
        gtmp_sb = u64a[:, :].bitcast(F32)[0:64, 0:MCOLS]
        arow_sb = mk_sb("arow_sb", [1, NCORE * HC], F32)
        gm_sb = mk_sb("gm_sb", [1, NCORE], F32)
        gmb_sb = mk_sb("gmb_sb", [64, NCORE], F32)
        ap_row = mk_sb("ap_row", [1, HC], F32)
        ap_bc = mk_sb("ap_bc", [64, HC], F32)
        cs1_sb = mk_sb("cs1_sb", [128, TT], BF16)
        cs2_sb = mk_sb("cs2_sb", [128, TT], BF16)
        tri_f_sb = mk_sb("tri_f_sb", [128, CH], F32)
        tri_b_sb = mk_sb("tri_b_sb", [128, CH], BF16)
        id_b_sb = mk_sb("id_b_sb", [128, CH], BF16)
        id_f_sb = mk_sb("id_f_sb", [128, CH], F32)
        swap_sb = mk_sb("swap_sb", [128, CH], BF16)
        ones_f_sb = mk_sb("ones_f_sb", [1, CH], F32)
        negones_sb = mk_sb("negones_sb", [1, CH], F32)
        ones_b_sb = mk_sb("ones_b_sb", [1, CH], BF16)
        onescol_f_sb = mk_sb("onescol_f_sb", [128, 1], F32)
        onescol_b_sb = mk_sb("onescol_b_sb", [128, 1], BF16)
        bbr_sb = mk_sb("bbr_sb", [1, HC], F32)
        bb_bc_sb = mk_sb("bb_bc_sb", [128, HC], F32)
        wbeta_sb = mk_sb("wbeta_sb", [128, DC * HC], BF16)
        gatew_sb = mk_sb("gatew_sb", [128, DC * E], BF16)
        vec_names = ["bo", "ln_g", "ln_b", "lin0_b", "s1b1", "s1b2",
                     "lin1_b", "s2b1", "s2b2"]
        vecs = {n: mk_sb("vec_" + n, [128, DC], F32) for n in vec_names}
        carry_sb = mk_sb("carry_sb", [1, HC], F32)
        dsum_sb = mk_sb("dsum_sb", [1, HC], F32)
        adsum_sb = mk_sb("adsum_sb", [1, HC], F32)
        acore_sb = mk_sb("acore_sb", [1, HC], F32)
        lrows_sb = mk_sb("lrows_sb", [1, HC * CH], F32)
        combrow_sb = mk_sb("combrow_sb", [1, PT], BF16)
        combf_sb = mk_sb("combf_sb", [8, TT], BF16)
        eps_sb = mk_sb("eps_sb", [1, 1], F32)
        dmy_sb = mk_sb("dmy_sb", [1, 1], F32)
        lloc_sb = mk_sb("lloc_sb", [128, HC], F32)
        lglob_sb = mk_sb("lglob_sb", [128, HC], F32)
        negl_sb = mk_sb("negl_sb", [128, HC], F32)
        expll_sb = mk_sb("expll_sb", [128, HC], F32)
        explg_sb = mk_sb("explg_sb", [128, HC], F32)
        sc_sb = mk_sb("sc_sb", [128, HC], F32)
        asb_sb = mk_sb("asb_sb", [64, HC], F32)
        scarg_sb = mk_sb("scarg_sb", [128, HC], F32)
        zrow_sb = mk_sb("zrow_sb", [128, HC], F32)
        lb_sb = mk_sb("lb_sb", [128, HC], F32)
        num_sb = mk_sb("num_sb", [128, 2 * NDST], F32)
        rden_sb = mk_sb("rden_sb", [128, 4], F32)
        llocT_sb = mk_sb("llocT_sb", [HC, CH], F32)
        mstl_sb = u64a[:, :]

        ps_main0 = ctx.enter_context(nc.psum_tensor("ps_main0", [128, 512], F32))
        ps_main1 = ctx.enter_context(nc.psum_tensor("ps_main1", [128, 512], F32))
        ps_aux = ctx.enter_context(nc.psum_tensor("ps_aux", [128, 512], F32))
        ps_scan0 = ctx.enter_context(nc.psum_tensor("ps_scan0", [128, 512], F32))
        ps_scan1 = ctx.enter_context(nc.psum_tensor("ps_scan1", [128, 512], F32))
        ps_scan2 = ctx.enter_context(nc.psum_tensor("ps_scan2", [128, 512], F32))
        ps_tp = ctx.enter_context(nc.psum_tensor("ps_tp", [128, 1024], BF16))
        ps_bcast = ctx.enter_context(nc.psum_tensor("ps_bcast", [128, 512], F32))

        pMain = [ps_main0[:, :], ps_main1[:, :]]
        pB1 = ps_bcast[:, :]
        pB2 = ps_aux[:, :]
        pRope = ps_aux[:, :]
        pSt = [ps_scan0[:, 0:128], ps_scan0[:, 128:256]]
        pBeta = ps_scan0[:, 256:288]
        pGate = ps_scan0[:, 288:296]
        pTcomb = ps_scan0[:, 352:480]
        pM0 = ps_scan1[:, 0:128]
        pM1 = ps_scan1[:, 128:256]
        pScT = ps_scan1[:, 256:288]
        pAbc = ps_scan1[:, 288:320]
        pStt = [ps_scan1[:, 320:385], ps_scan1[:, 385:450]]
        pMisc = ps_scan1[:, 450:482]
        pLloc = ps_scan2[:, 0:32]
        pLglob = ps_scan2[:, 32:64]
        pNum = [ps_scan2[:, 64:129], ps_scan2[:, 129:194]]
        pIntr = ps_scan2[:, 194:454]
        pTp = [ps_tp[:, 0:128], ps_tp[:, 128:256]]
        pBc = [ps_bcast[:, :], ps_aux[:, :]]

        bq = bigb[:, 0:1024]
        bk = bigb[:, 1024:2048]
        bv = bigb[:, 2048:3072]
        bva = bigb[:, 3072:7232]
        bkt = bigb[:, 7232:11328]
        bsc = bigb[:, 11328:15424]
        bqt = bigb[:, 0:4096]
        bkt2 = bigb[:, 4096:8192]
        bva2 = bigb[:, 8192:12352]
        byt = bigb[:, 12352:13376]
        byf = bigb[:, 13376:17472]
        bsc2 = bigb[:, 17472:19200]
        bf0 = bigb[:, 0:16384]
        bgg = bigb[:, 16384:17408]
        btmp = bigb[:, 17408:18432]
        bhh = bigb[:, 0:8192]
        bshh = bigb[:, 0:8192]
        bm1 = bigb[:, 8192:16384]
        bm2 = bigb[:, 0:4096]

        cx = bigc[:, 0:1024]
        ch_ = bigc[:, 1024:5120]
        ctmp = bigc[:, 5120:7168]
        csc = bigc[:, 7168:8192]
        stats_sb = bigc[0:1, 6656:6656 + 3 * TT].rearrange("o f -> o f")
        routed = bigc[:, 0:8192]
        cffn = bigc[:, 0:4096]
        cxa = bigc[:, 4096:5120]
        ctmp2 = bigc[:, 5120:7168]

        sems = {}
        for n in ["ldw0", "ldw1", "ldw2", "ldwa", "ldwb", "lwg0", "lwg1", "ldx0", "ldx1", "ldcs", "ldm", "ldmg",
                  "spq0", "spq1", "spv", "spl", "spm", "spa", "rlq",
                  "rlm0", "rlm1", "rlm2",
                  "cc", "ccd", "bounce", "outd", "pcast", "t", "v", "s", "g"]:
            sems[n] = ctx.enter_context(nc.semaphore(n))

        block = ctx.enter_context(nc.Block())

        TE, VE, SC, SY, GP = "tensor", "vector", "scalar", "sync", "gpsimd"

        def guard(w):
            w.wait(TE, "v")
            w.wait(TE, "s")

        class Ring:

            def __init__(self, w, depth=2):
                self.w = w
                self.depth = depth
                self.hist = []
                self.tm0 = w.mark("t")

            def wait_te(self):
                if len(self.hist) >= self.depth:
                    vm, sm, _ = self.hist[-self.depth]
                    self.w.wait(TE, "v", vm)
                    self.w.wait(TE, "s", sm)
                else:
                    guard(self.w)

            def wait_gp(self):
                if len(self.hist) >= self.depth:
                    _, _, tm = self.hist[-self.depth]
                    self.w.wait(GP, "t", tm)
                else:
                    self.w.wait(GP, "t", self.tm0)

            def wait_sy(self):
                if len(self.hist) >= self.depth:
                    _, _, tm = self.hist[-self.depth]
                    self.w.wait(SY, "t", tm)
                else:
                    self.w.wait(SY, "t", self.tm0)

            def push(self):
                self.hist.append((self.w.mark("v"), self.w.mark("s"), self.w.mark("t")))

        def body(w: Walk):
            if PHASES == "MICRO":
                for c in range(DC):
                    w.dma(SY, out_fm[c * 128:(c + 1) * 128, :],
                          x_fm[c * 128:(c + 1) * 128, :], "outd")
                w.wait(SY, "outd")
                return
            w.dma(SY, tri_f_sb[:, :], tri_f, "ldm")
            w.dma(SY, tri_b_sb[:, :], tri_b, "ldm")
            w.dma(SY, id_b_sb[:, :], id_b, "ldm")
            w.dma(SY, id_f_sb[:, :], id_f, "ldm")
            w.dma(SY, swap_sb[:, :], swap_b, "ldm")
            w.dma(SY, ones_f_sb[:, :], ones_f, "ldm")
            w.dma(SY, negones_sb[:, :], negones_f, "ldm")
            w.dma(SY, ones_b_sb[:, :], ones_b, "ldm")
            w.dma(SY, onescol_f_sb[:, :], onescol_f, "ldm")
            w.dma(SY, onescol_b_sb[:, :], onescol_b, "ldm")
            w.dma(SY, gm_sb[:, :], gmask, "ldm")
            w.dma(SY, bbr_sb[:, :], bbeta, "ldm")
            bias_srcs = {"bo": bo, "ln_g": ln_g, "ln_b": ln_b, "lin0_b": lin0_b,
                         "s1b1": s1b1, "s1b2": s1b2, "lin1_b": lin1_b,
                         "s2b1": s2b1, "s2b2": s2b2}
            for n in vec_names:
                w.dma(SY, vecs[n][:, :], bias_srcs[n], "ldm")
            w.dma(GP, wbeta_sb[:, :].rearrange("p (c n) -> p c n", c=DC),
                  wbeta.rearrange("(c p) n -> p c n", p=128), "ldmg")
            w.dma(GP, gatew_sb[:, :], gate_wt, "ldmg")
            def precast_dxd(name, wt, nm):
                for m in range(nm):
                    w.dma(GP, wc[UOF[name] + m].rearrange("p (kc n) -> p kc n", kc=DC),
                          wt.rearrange("(kc p) n -> p kc n", p=128)[:, :, m * 128:(m + 1) * 128],
                          "pcast")
            precast_dxd("wo", wo, 8)
            w.wait(GP, "pcast")
            w.op(GP, lambda e: e.memset(dmy_sb[:, :], 0.0), sem="g")
            g1mark = w.mark("g")
            w.wait(SY, "g", g1mark)
            w.wait(TE, "ldm")
            w.wait(TE, "ldmg")
            w.op(TE, lambda e: e.matmul(pScT, ones_f_sb[0:1, :], bbr_sb[0:1, :],
                                        start=True, stop=True), sem="t")
            w.wait(VE, "t")
            w.wait(VE, "ldm")
            w.op(VE, lambda e: e.tensor_copy(bb_bc_sb[:, :], pScT), sem="v")
            if CLEVEL >= 3:
                w.op(VE, lambda e: e.memset(m_sb[:, :], 0.0), sem="v")
                w.op(VE, lambda e: e.memset(m_bf[:, :], 0.0), sem="v")
                w.op(VE, lambda e: e.memset(sin_sb[:, :], 0.0), sem="v")
                w.op(VE, lambda e: e.memset(carry_sb[:, :], 0.0), sem="v")
                w.op(VE, lambda e: e.memset(eps_sb[:, :], LN_EPS), sem="v")
            if CLEVEL >= 4:
                w.wait(TE, "v")
                w.op(TE, lambda e: e.matmul(pAbc[0:64, 0:NCORE], ones_f_sb[0:1, 0:64],
                                            gm_sb[0:1, :], start=True, stop=True), sem="t")
                w.wait(VE, "t")
                w.op(VE, lambda e: e.tensor_copy(gmb_sb[:, :], pAbc[0:64, 0:NCORE]), sem="v")

            if "A" in PHASES and A_LEVEL >= 1:
                for j in range(NTILES):
                    phase_a_tile(w, j)
            if "G" in PHASES:
                do_gather(w)
            precast_dxd("lin0_w", lin0_w, 8)
            precast_dxd("s1w1", s1w1, 8)
            precast_dxd("s1w2", s1w2, 8)
            precast_dxd("lin1_w", lin1_w, 8)
            precast_dxd("s2w1", s2w1, 8)
            precast_dxd("s2w2", s2w2, 8)
            precast_dxd("shw1", shw1, 16)
            precast_dxd("shw3", shw3, 16)
            for m in range(DC):
                for u in range(2):
                    w.dma(GP, wc[UOF["shw2"] + 2 * m + u].rearrange("p (kc n) -> p kc n", kc=DC),
                          shw2.rearrange("(u kc p) n -> u p kc n", u=2, p=128)[u][:, :, m * 128:(m + 1) * 128],
                          "pcast")
            for ee in range(E):
                for nm, wt in (("exw1", exw1), ("exw3", exw3), ("exw2", exw2)):
                    for m in range(DC):
                        w.dma(GP, wc[UOF[nm] + ee * 8 + m].rearrange("p (kc n) -> p kc n", kc=DC),
                              wt[ee].rearrange("(kc p) n -> p kc n", p=128)[:, :, m * 128:(m + 1) * 128],
                              "pcast")
            w.wait(GP, "pcast")
            w.op(GP, lambda e: e.memset(dmy_sb[:, :], 0.0), sem="g")
            g2mark = w.mark("g")
            if "B" in PHASES:
                for sname in ["spq0", "spq1", "spv", "spl", "spm"]:
                    w.wait(SY, sname)
                w.wait(TE, "ccd")
                w.wait(VE, "ccd")
                for j in range(NTILES):
                    phase_b_tile(w, j)
            if "C" in PHASES:
                w.wait(SY, "g", g2mark)
                w.wait(SY, "spa")
                w.wait(VE, "spa")
                w.wait(SC, "spa")
                w.wait(TE, "spa")
                for hf in range(NHALF if NTILES >= 8 else 1):
                    phase_c_half(w, hf)
            for sname in ["outd", "spq0", "spq1", "spv", "spl", "spa", "spm",
                          "ldx0", "ldx1", "rlq", "rlm0", "rlm1", "rlm2",
                          "ldcs", "ldm", "ldw0", "ldw1", "ldw2", "ldwa", "ldwb"]:
                w.wait(SY, sname)
            for sname in ["ldmg", "ccd", "bounce", "lwg0", "lwg1"]:
                w.wait(GP, sname)

        def do_gather(w: Walk):
            w.wait(SC, "v")
            w.op(SC, lambda e: e.activation(acore_sb[:, :], carry_sb[:, :], AF.Exp), sem="s")
            w.wait(GP, "s")
            w.wait(GP, "v")
            w.dma(GP, cc_in[0:1, 0:HC], acore_sb[:, :], "ccd")
            w.dma(GP, cc_in[0, HC:].rearrange("(p f) -> p f", p=64), m_sb[:, :], "ccd")
            w.wait(GP, "ccd")
            if DEBUG:
                w.dma(GP, dbg_cc, cc_in[:, :], "ccd")
                w.wait(GP, "ccd")

            w.op(GP, lambda e: e.collective_compute(
                "AllGather", ALU.bypass, replica_groups=[list(range(NCORE))],
                ins=[cc_in[:, :]], outs=[cc_out[:, :]]), sem="cc")
            w.wait(GP, "cc")
            w.dma(GP, arow_sb[:, :].rearrange("o (r a) -> o r a", r=NCORE),
                  cc_out[None, :, 0:HC], "ccd")
            for i in range(NCORE):
                w.wait(GP, "v")
                w.dma(GP, gtmp_sb[:, :],
                      cc_out[i, HC:].rearrange("(p f) -> p f", p=64), "ccd")
                w.wait(VE, "ccd")
                w.op(VE, lambda e, i=i: e.tensor_single_scalar(
                    ap_row[:, :], arow_sb[0:1, i * HC:(i + 1) * HC], -1.0, ALU.add), sem="v")
                w.op(VE, lambda e, i=i: e.tensor_single_scalar(
                    ap_row[:, :], ap_row[:, :], gm_sb[0:1, i:i + 1], ALU.mult), sem="v")
                w.op(VE, lambda e: e.tensor_single_scalar(
                    ap_row[:, :], ap_row[:, :], 1.0, ALU.add), sem="v")
                w.wait(TE, "v")
                w.op(TE, lambda e: e.matmul(pAbc[0:64, 0:HC], ones_f_sb[0:1, 0:64],
                                            ap_row[0:1, :], start=True, stop=True), sem="t")
                w.wait(VE, "t")
                w.op(VE, lambda e: e.tensor_copy(ap_bc[:, :], pAbc[0:64, 0:HC]), sem="v")
                w.op(VE, lambda e, i=i: e.tensor_single_scalar(
                    gtmp_sb[:, :], gtmp_sb[:, :], gmb_sb[:, i:i + 1], ALU.mult), sem="v")
                for hcn in range(HC):
                    w.op(VE, lambda e, hcn=hcn: e.scalar_tensor_tensor(
                        sin_sb[:, hcn * NDST:(hcn + 1) * NDST],
                        sin_sb[:, hcn * NDST:(hcn + 1) * NDST],
                        ap_bc[:, hcn:hcn + 1],
                        gtmp_sb[:, hcn * NDST:(hcn + 1) * NDST],
                        ALU.mult, ALU.add), sem="v")
            w.op(VE, lambda e: e.tensor_copy(sinb_sb[0:64, :], sin_sb[:, :]), sem="v")
            w.wait(GP, "v")
            w.dma(GP, sinb_sb[64:128, :], sinb_sb[0:64, :], "ccd")

        def ln_sweep(w, read_chunk, nst, write_out):
            raise NotImplementedError

        def phase_a_tile(w: Walk, j):
            t0 = j * TT
            w.wait(VE, "t")
            w.wait(VE, "spq0")
            w.wait(VE, "spq1")
            w.wait(VE, "spv")
            w.wait(VE, "spl")
            w.wait(VE, "spm")
            w.wait(SY, "v")
            w.wait(SY, "s")
            if A_LEVEL >= 2 and not (A_LEVEL == 2 and ALSUB == 1):
                w.dma(SY, cs1_sb[:, :], cs1[:, t0:t0 + TT], "ldcs")
                w.dma(SY, cs2_sb[:, :], cs2[:, t0:t0 + TT], "ldcs")

            guard(w)
            cxa = bigc[:, 0:4096]
            for c in range(DC):
                w.dma(SY, cxa[:, c * TT:(c + 1) * TT],
                      x_fm[c * 128:(c + 1) * 128, t0:t0 + TT], "ldx0")
            w.wait(SC, "ldx0")
            for c in range(DC):
                xc = cxa[:, c * TT:(c + 1) * TT]
                xb = bsc[:, (c % 2) * TT:(c % 2) * TT + TT]
                x2b = bsc[:, 1024 + (c % 2) * TT:1024 + (c % 2) * TT + TT]
                if c >= 2:
                    w.wait(SC, "t")
                w.op(SC, lambda e, xc=xc, xb=xb: e.copy(xb, xc), sem="s")
                w.op(SC, lambda e, xc=xc, x2b=x2b: e.activation(x2b, xc, AF.Square), sem="s")
                w.wait(TE, "s")
                w.op(TE, lambda e, xb=xb, c=c: e.matmul(
                    pB1[0:1, 0:TT], onescol_b_sb[:, :], xb,
                    start=(c == 0), stop=(c == DC - 1)), sem=None)
                w.op(TE, lambda e, x2b=x2b, c=c: e.matmul(
                    pB2[0:1, 0:TT], onescol_b_sb[:, :], x2b,
                    start=(c == 0), stop=(c == DC - 1)), sem="t")
            if A_LEVEL == 1 and ALSUB < 2:
                return
            mu = stats_sb[0:1, 0:TT]
            ex2 = stats_sb[0:1, TT:2 * TT]
            musq = stats_sb[0:1, 2 * TT:3 * TT]
            var = stats_sb[0:1, TT:2 * TT]
            sd = stats_sb[0:1, 2 * TT:3 * TT]
            rstd = stats_sb[0:1, TT:2 * TT]
            murstd = stats_sb[0:1, 2 * TT:3 * TT]
            w.wait(SC, "t")
            w.op(SC, lambda e: e.activation(mu, pB1[0:1, 0:TT], AF.Copy, scale=1.0 / D), sem="s")
            w.op(SC, lambda e: e.activation(ex2, pB2[0:1, 0:TT], AF.Copy, scale=1.0 / D), sem="s")
            w.wait(VE, "s")
            w.op(VE, lambda e: e.tensor_mul(musq, mu, mu), sem="v")
            w.op(VE, lambda e: e.tensor_sub(var, ex2, musq), sem="v")
            w.wait(SC, "v")
            w.op(SC, lambda e: e.activation(sd, var, AF.Sqrt, bias=eps_sb[0:1, 0:1]), sem="s")
            w.wait(VE, "s")
            w.op(VE, lambda e: e.reciprocal(rstd, sd), sem="v")
            w.op(VE, lambda e: e.tensor_mul(murstd, mu, rstd), sem="v")
            if A_LEVEL == 1 and ALSUB < 3:
                return
            w.wait(SY, "v")
            w.dma(SY, sp_stats[0:1, t0:t0 + TT], rstd, "spl")
            w.dma(SY, sp_stats[1:2, t0:t0 + TT], murstd, "spl")
            w.wait(TE, "v")
            w.op(TE, lambda e: e.matmul(pB1[:, 0:TT], ones_f_sb[0:1, :], rstd,
                                        start=True, stop=True), sem=None)
            w.op(TE, lambda e: e.matmul(pB2[:, 0:TT], ones_f_sb[0:1, :], murstd,
                                        start=True, stop=True), sem="t")
            mark_bc = w.mark("t")
            if A_LEVEL == 1 and ALSUB < 4:
                return
            w.wait(VE, "t", mark_bc)
            for c in range(DC):
                xc = cxa[:, c * TT:(c + 1) * TT]
                xnc = biga[:, c * T + t0:c * T + t0 + TT]
                tmp = ctmp[:, (c % 2) * TT:(c % 2) * TT + TT]
                w.op(VE, lambda e, xc=xc, tmp=tmp: e.tensor_mul(tmp, xc, pB1[:, 0:TT]), sem="v")
                w.op(VE, lambda e, tmp=tmp, xnc=xnc: e.tensor_sub(xnc, tmp, pB2[:, 0:TT]), sem="v")
            mark_xn = w.mark("v")
            if A_LEVEL < 2:
                return

            qkv_list = [(None, sp_q), (None, sp_k), (None, None)]
            marks_c = [None] * 24
            vmark_t12 = [0] * 24
            vmark_mn = [0]
            gmarks = [0] * 24

            def a_stage1(i):
                wi, m = divmod(i, DC)
                wt = [wq, wk, wv][wi]
                ws = wring[:, (i % 2) * 1024:(i % 2) * 1024 + 1024]
                if i >= 2:
                    w.wait(GP, "t", gmarks[i - 2])
                else:
                    w.wait(GP, "t")
                w.dma(GP, ws.rearrange("p (c n) -> p c n", c=DC),
                      wt.rearrange("(c p) n -> p c n", p=128)[:, :, m * 128:(m + 1) * 128],
                      "lwg%d" % (i % 2))
                pm = pMain[i % 2]
                w.wait(TE, "lwg%d" % (i % 2))
                w.wait(TE, "v", mark_xn)
                if i >= 2 and marks_c[i - 2] is not None:
                    sem_n, val = marks_c[i - 2]
                    w.wait(TE, sem_n, val)
                else:
                    guard(w)
                for kc in range(DC):
                    w.op(TE, lambda e, ws=ws, kc=kc, pm=pm: e.matmul(
                        pm[:, 0:TT], ws[:, kc * 128:(kc + 1) * 128],
                        biga[:, kc * T + t0:kc * T + t0 + TT],
                        start=(kc == 0), stop=(kc == DC - 1)),
                        sem="t" if kc == DC - 1 else None)
                tmark = w.mark("t")
                gmarks[i] = tmark
                if wi < 2:
                    t1 = bsc[:, 2048 + (i % 2) * 2 * TT:2048 + (i % 2) * 2 * TT + TT]
                    t2 = bsc[:, 2048 + (i % 2) * 2 * TT + TT:2048 + (i % 2) * 2 * TT + 2 * TT]
                    w.wait(VE, "t", tmark)
                    w.wait(VE, "ldcs")
                    w.op(VE, lambda e, pm=pm, t1=t1: e.tensor_mul(t1, pm[:, 0:TT], cs1_sb[:, :]), sem="v")
                    w.op(VE, lambda e, pm=pm, t2=t2: e.tensor_mul(t2, pm[:, 0:TT], cs2_sb[:, :]), sem="v")
                    vmark_t12[i] = w.mark("v")
                    marks_c[i] = ("v", vmark_t12[i])
                else:
                    vb = bv[:, (i % 2) * TT:(i % 2) * TT + TT]
                    w.wait(SC, "t", tmark)
                    w.op(SC, lambda e, pm=pm, vb=vb: e.copy(vb, pm[:, 0:TT]), sem="s")
                    marks_c[i] = ("s", w.mark("s"))

            def a_stage2(i):
                wi, m = divmod(i, DC)
                _, dst_spill = qkv_list[wi]
                if wi < 2:
                    t1 = bsc[:, 2048 + (i % 2) * 2 * TT:2048 + (i % 2) * 2 * TT + TT]
                    t2 = bsc[:, 2048 + (i % 2) * 2 * TT + TT:2048 + (i % 2) * 2 * TT + 2 * TT]
                    w.wait(TE, "v", vmark_t12[i])
                    w.wait(TE, "v", vmark_mn[0])
                    w.op(TE, lambda e, t1=t1: e.matmul(
                        pRope[:, 0:TT], id_b_sb[:, :], t1, start=True, stop=False), sem=None)
                    w.op(TE, lambda e, t2=t2: e.matmul(
                        pRope[:, 0:TT], swap_sb[:, :], t2, start=False, stop=True), sem="t")
                    rmark = w.mark("t")
                    mn = bsc[:, 0:TT]
                    ex = bsc[:, TT:2 * TT]
                    qk_out = (bq if wi == 0 else bk)[:, (m % 2) * TT:(m % 2) * TT + TT]
                    w.wait(VE, "t", rmark)
                    w.op(VE, lambda e, mn=mn: e.tensor_single_scalar(
                        mn, pRope[:, 0:TT], 0.0, ALU.min), sem="v")
                    w.wait(SC, "v")
                    w.op(SC, lambda e, mn=mn, ex=ex: e.activation(ex, mn, AF.Exp), sem="s")
                    smark = w.mark("s")
                    w.wait(VE, "s", smark)
                    w.wait(VE, "spq%d" % (m % 2))
                    w.op(VE, lambda e, ex=ex, qk_out=qk_out: e.scalar_tensor_tensor(
                        qk_out, pRope[:, 0:TT], 0.0, ex, ALU.max, ALU.add), sem="v")
                    vmark_mn[0] = w.mark("v")
                    w.wait(SY, "v")
                    w.dma(SY, dst_spill[m * 128:(m + 1) * 128, t0:t0 + TT], qk_out,
                          "spq%d" % (m % 2))
                    if wi == 1:
                        for tc in range(NCH):
                            w.wait(TE, "v")
                            w.op(TE, lambda e, qk_out=qk_out, tc=tc: e.transpose(
                                pTp[tc % 2], qk_out[:, tc * 128:(tc + 1) * 128],
                                id_b_sb[:, :]), sem="t")
                            w.wait(VE, "t")
                            w.op(VE, lambda e, tc=tc, m=m: e.tensor_copy(
                                bkt[:, tc * 1024 + m * 128:tc * 1024 + (m + 1) * 128],
                                pTp[tc % 2]), sem="v")
                else:
                    vb = bv[:, (i % 2) * TT:(i % 2) * TT + TT]
                    for tc in range(NCH):
                        w.wait(TE, "s")
                        w.wait(TE, "v")
                        w.op(TE, lambda e, vb=vb, tc=tc: e.transpose(
                            pTp[tc % 2], vb[:, tc * 128:(tc + 1) * 128],
                            id_b_sb[:, :]), sem="t")
                        w.wait(VE, "t")
                        w.op(VE, lambda e, tc=tc, m=m: e.tensor_copy(
                            bva[:, tc * 1040 + 2 * m * NDST:tc * 1040 + (2 * m + 2) * NDST]
                            .rearrange("p (hh dd) -> p hh dd", hh=2)[:, :, 0:Dh],
                            pTp[tc % 2].rearrange("p (hh dd) -> p hh dd", hh=2)), sem="v")

            for step in range(25):
                if step < 24:
                    a_stage1(step)
                if step >= 1:
                    a_stage2(step - 1)
            w.op(VE, lambda e: e.memset(
                bva.rearrange("p (cc hh dd) -> p cc hh dd", cc=NCH, hh=H)[:, :, :, Dh:],
                1.0), sem="v")
            w.wait(SY, "v")
            w.dma(SY, sp_va[t0:t0 + TT, :].rearrange("(cc p) f -> p cc f", p=128),
                  bva.rearrange("p (cc f) -> p cc f", cc=NCH), "spv")
            if A_LEVEL < 3:
                return

            for tc in range(NCH):
                g = j * NCH + tc
                tc0 = t0 + tc * CH
                guard(w)
                for kc in range(DC):
                    w.op(TE, lambda e, kc=kc, tc0=tc0: e.matmul(
                        pBeta, biga[:, kc * T + tc0:kc * T + tc0 + CH],
                        wbeta_sb[:, kc * HC:(kc + 1) * HC],
                        start=(kc == 0), stop=(kc == DC - 1)), sem="t" if kc == DC - 1 else None)
                w.wait(VE, "t")
                w.op(VE, lambda e: e.tensor_add(zrow_sb[:, :], pBeta, bb_bc_sb[:, :]), sem="v")
                w.wait(SC, "v")
                w.op(SC, lambda e: e.activation(zrow_sb[:, :], zrow_sb[:, :], AF.Sigmoid), sem="s")
                w.wait(VE, "s")
                w.op(VE, lambda e: e.tensor_scalar(
                    zrow_sb[:, :], zrow_sb[:, :], 0.001, 0.999, ALU.max, ALU.min), sem="v")
                w.wait(SC, "v")
                w.op(SC, lambda e: e.activation(lb_sb[:, :], zrow_sb[:, :], AF.Ln), sem="s")
                w.wait(TE, "s")
                guard(w)
                w.op(TE, lambda e: e.matmul(pLloc, tri_f_sb[:, :], lb_sb[:, :],
                                            start=True, stop=True), sem=None)
                w.op(TE, lambda e: e.matmul(pLglob, tri_f_sb[:, :], lb_sb[:, :],
                                            start=True, stop=False), sem=None)
                w.op(TE, lambda e: e.matmul(pLglob, ones_f_sb[0:1, :], carry_sb[0:1, :],
                                            start=False, stop=True), sem=None)
                w.op(TE, lambda e: e.matmul(pMisc[0:1, :], onescol_f_sb[:, :], lb_sb[:, :],
                                            start=True, stop=True), sem="t")
                w.wait(VE, "t")
                w.wait(VE, "spl")
                w.op(VE, lambda e: e.tensor_copy(lloc_sb[:, :], pLloc), sem="v")
                w.op(VE, lambda e: e.tensor_copy(lglob_sb[:, :], pLglob), sem="v")
                w.op(VE, lambda e: e.tensor_copy(dsum_sb[:, :], pMisc[0:1, :]), sem="v")
                w.wait(SY, "v")
                w.dma(SY, sp_L[tc0:tc0 + CH, 0:HC], lloc_sb[:, :], "spl")
                w.dma(SY, sp_L[tc0:tc0 + CH, HC:2 * HC], lglob_sb[:, :], "spl")
                w.wait(TE, "v")
                w.op(TE, lambda e: e.transpose(pTcomb[0:HC, 0:CH], lloc_sb[:, :],
                                               id_f_sb[:, :]), sem="t")
                w.wait(VE, "t")
                w.op(VE, lambda e: e.tensor_copy(llocT_sb[:, :], pTcomb[0:HC, 0:CH]), sem="v")
                w.wait(SY, "v")
                w.dma(SY, sp_Lr[g, :, :], llocT_sb[:, :], "spl")
                w.wait(TE, "v")
                w.op(TE, lambda e: e.matmul(pScT, ones_f_sb[0:1, :], dsum_sb[0:1, :],
                                            start=True, stop=True), sem="t")
                w.wait(VE, "t")
                w.op(VE, lambda e: e.tensor_sub(scarg_sb[:, :], pScT, lloc_sb[:, :]), sem="v")
                w.wait(SC, "v")
                w.op(SC, lambda e: e.activation(sc_sb[:, :], scarg_sb[:, :], AF.Exp), sem="s")
                w.op(SC, lambda e: e.activation(adsum_sb[:, :], dsum_sb[:, :], AF.Exp), sem="s")
                w.wait(TE, "s")
                w.op(TE, lambda e: e.matmul(pAbc[0:64, 0:HC], ones_f_sb[0:1, 0:64],
                                            adsum_sb[0:1, :], start=True, stop=True), sem="t")
                w.wait(VE, "t")
                w.op(VE, lambda e: e.tensor_copy(asb_sb[:, :], pAbc[0:64, 0:HC]), sem="v")
                w.op(VE, lambda e: e.tensor_add(carry_sb[:, :], carry_sb[:, :], dsum_sb[:, :]), sem="v")
                if A_LEVEL < 4:
                    continue
                w.wait(SY, "v")
                w.dma(SY, sp_mst[g, :, :], m_bf[:, :], "spm")
                khat_all = bigb[:, 15424:17472]
                w.wait(VE, "s")
                w.op(VE, lambda e, tc=tc: e.tensor_tensor(
                    khat_all[:, :].rearrange("p (h c d) -> p h c d", h=H, c=C),
                    bkt[:, tc * 1024:(tc + 1) * 1024]
                    .rearrange("p (h d) -> p h d", h=H)[:, :, None, :]
                    .broadcast_to([128, H, C, Dh]),
                    sc_sb[:, :].rearrange("p (h c) -> p h c", h=H)[:, :, :, None]
                    .broadcast_to([128, H, C, Dh]),
                    ALU.mult), sem="v")
                dbanks = [pMain[0], pMain[1], ps_scan2[:, :], ps_bcast[:, :], ps_scan1[:, :]]
                w.wait(TE, "v")
                w.wait(TE, "s")
                for hcn in range(HC):
                    h = hcn // 2
                    dv = dbanks[hcn // 7][0:64, (hcn % 7) * 65:(hcn % 7) * 65 + 65]
                    w.op(TE, lambda e, hcn=hcn, h=h, tc=tc, dv=dv: e.matmul(
                        dv, khat_all[:, hcn * Dh:(hcn + 1) * Dh],
                        bva[:, tc * 1040 + h * NDST:tc * 1040 + (h + 1) * NDST],
                        start=True, stop=True), sem="t" if hcn == HC - 1 else None)
                w.wait(VE, "t")
                for hcn in range(HC):
                    dv = dbanks[hcn // 7][0:64, (hcn % 7) * 65:(hcn % 7) * 65 + 65]
                    w.op(VE, lambda e, hcn=hcn, dv=dv: e.scalar_tensor_tensor(
                        m_sb[:, hcn * NDST:(hcn + 1) * NDST],
                        m_sb[:, hcn * NDST:(hcn + 1) * NDST],
                        asb_sb[:, hcn:hcn + 1],
                        dv, ALU.mult, ALU.add), sem="v")
                w.wait(VE, "spm")
                w.op(VE, lambda e: e.tensor_copy(m_bf[:, :], m_sb[:, :]), sem="v")

        def phase_b_tile(w: Walk, j):
            t0 = j * TT
            w.wait(SY, "t")
            w.wait(SY, "v")
            w.wait(SC, "spa")
            w.wait(VE, "spa")
            w.dma(SY, bqt.rearrange("p (cc tt) -> p cc tt", cc=DC),
                  sp_q.rearrange("(cc p) tt -> p cc tt", p=128)[:, :, t0:t0 + TT], "rlq")
            w.dma(SY, bkt2.rearrange("p (cc tt) -> p cc tt", cc=DC),
                  sp_k.rearrange("(cc p) tt -> p cc tt", p=128)[:, :, t0:t0 + TT], "rlq")
            w.dma(SY, bva2.rearrange("p (cc f) -> p cc f", cc=NCH),
                  sp_va[t0:t0 + TT, :].rearrange("(cc p) f -> p cc f", p=128), "rlq")
            for tc in range(NCH):
                g = j * NCH + tc
                tc0 = t0 + tc * CH
                msl = mstl_sb[:, (tc % 2) * MCOLS:(tc % 2) * MCOLS + MCOLS]
                w.wait(SY, "t")
                w.wait(SY, "s")
                w.wait(SY, "v")
                w.dma(SY, msl[0:64, :], sp_mst[g, :, :], "rlm%d" % (tc % 2))
                w.dma(SY, msl[64:128, :], sp_mst[g, :, :], "rlm%d" % (tc % 2))
                w.dma(SY, lloc_sb[:, :], sp_L[tc0:tc0 + CH, 0:HC], "rlm2")
                w.dma(SY, lglob_sb[:, :], sp_L[tc0:tc0 + CH, HC:2 * HC], "rlm2")
                w.dma(SY, lrows_sb[:, :],
                      sp_Lr[g].rearrange("hh tt -> (hh tt)")[None, :], "rlm2")
                w.wait(VE, "rlm2")
                w.op(VE, lambda e: e.tensor_single_scalar(
                    negl_sb[:, :], lloc_sb[:, :], -1.0, ALU.mult), sem="v")
                w.wait(SC, "rlm2")
                w.op(SC, lambda e: e.activation(expll_sb[:, :], lloc_sb[:, :], AF.Exp), sem="s")
                w.op(SC, lambda e: e.activation(explg_sb[:, :], lglob_sb[:, :], AF.Exp), sem="s")
                w.wait(TE, "rlq")
                w.wait(TE, "rlm%d" % (tc % 2))
                w.wait(TE, "rlm2")
                w.wait(TE, "s")
                w.wait(TE, "v")
                bSt = [ps_scan0[:, 0:128], ps_scan2[:, 0:128]]
                bM0 = [ps_scan0[:, 128:256], ps_scan2[:, 128:256]]
                bM1 = [ps_scan0[:, 256:384], ps_scan2[:, 256:384]]
                bNum = [ps_scan1[:, 264:329], ps_main0[:, 264:329]]
                bInt = [ps_scan1[:, 0:260], ps_main0[:, 0:260]]
                vmark_p = [0] * H
                smark2 = [0] * H
                vmark2 = [0] * H

                def b_stage1(h):
                    hc0, hc1 = 2 * h, 2 * h + 1
                    cb = h // 2
                    pb = (h % 2) * 64
                    par = h % 2
                    qf = bqt[pb:pb + 64, cb * TT + tc * CH:cb * TT + tc * CH + CH]
                    kf = bkt2[pb:pb + 64, cb * TT + tc * CH:cb * TT + tc * CH + CH]
                    if h >= 2:
                        w.wait(TE, "v", vmark_p[h - 2])
                    else:
                        w.wait(TE, "v")
                        w.wait(TE, "s")
                    w.op(TE, lambda e: e.matmul(bSt[par], kf, qf, start=True, stop=True), sem=None)
                    for hci, (hcx, bM) in enumerate(((hc0, bM0[par]), (hc1, bM1[par]))):
                        w.op(TE, lambda e, hcx=hcx, bM=bM: e.matmul(
                            bM, ones_f_sb[0:1, :], lrows_sb[0:1, hcx * CH:(hcx + 1) * CH],
                            start=True, stop=False), sem=None)
                        w.op(TE, lambda e, hcx=hcx, bM=bM: e.matmul(
                            bM, lrows_sb[0:1, hcx * CH:(hcx + 1) * CH], negones_sb[0:1, :],
                            start=False, stop=True), sem="t" if hci == 1 else None)
                    tmark_st = w.mark("t")
                    stm = bsc2[:, par * 640:par * 640 + 128]
                    mk01 = ctmp[:, par * 512:par * 512 + 256]
                    a01 = ctmp[:, 1024 + par * 256:1024 + par * 256 + 256]
                    p01 = bsc2[:, par * 640 + 384:par * 640 + 640]
                    pM01 = [ps_scan0[:, 128:384], ps_scan2[:, 128:384]][par]
                    w.wait(VE, "t", tmark_st)
                    w.op(VE, lambda e: e.tensor_mul(stm, bSt[par], tri_b_sb[:, :]), sem="v")
                    w.op(VE, lambda e: e.tensor_single_scalar(
                        a01, pM01, 0.0, ALU.min), sem="v")
                    w.wait(SC, "v")
                    w.op(SC, lambda e: e.activation(mk01, a01, AF.Exp), sem="s")
                    w.wait(VE, "s")
                    w.op(VE, lambda e: e.tensor_tensor(
                        p01.rearrange("p (c f) -> p c f", c=2),
                        stm[:, None, :].broadcast_to([128, 2, 128]),
                        mk01.rearrange("p (c f) -> p c f", c=2),
                        ALU.mult), sem="v")
                    vmark_p[h] = w.mark("v")

                def b_stage2(h):
                    hc0, hc1 = 2 * h, 2 * h + 1
                    cb = h // 2
                    pb = (h % 2) * 64
                    par = h % 2
                    qf = bqt[pb:pb + 64, cb * TT + tc * CH:cb * TT + tc * CH + CH]
                    vaug_h = bva2[:, tc * 1040 + h * NDST:tc * 1040 + (h + 1) * NDST]
                    p0 = bsc2[:, par * 640 + 384:par * 640 + 512]
                    p1 = bsc2[:, par * 640 + 512:par * 640 + 640]
                    w.wait(TE, "v", vmark_p[h])
                    if h >= 2:
                        w.wait(TE, "s", smark2[h - 2])
                        w.wait(TE, "v", vmark2[h - 2])
                    w.op(TE, lambda e: e.matmul(bNum[par], p0, vaug_h, start=True, stop=False), sem=None)
                    w.op(TE, lambda e: e.matmul(bNum[par], p1, vaug_h, start=False, stop=True), sem=None)
                    w.op(TE, lambda e: e.matmul(
                        bInt[par][:, 0:65], qf, msl[pb:pb + 64, hc0 * NDST:(hc0 + 1) * NDST],
                        start=True, stop=True), sem=None)
                    w.op(TE, lambda e: e.matmul(
                        bInt[par][:, 65:130], qf, msl[pb:pb + 64, hc1 * NDST:(hc1 + 1) * NDST],
                        start=True, stop=True), sem=None)
                    w.op(TE, lambda e: e.matmul(
                        bInt[par][:, 130:195], qf, sinb_sb[pb:pb + 64, hc0 * NDST:(hc0 + 1) * NDST],
                        start=True, stop=True), sem=None)
                    w.op(TE, lambda e: e.matmul(
                        bInt[par][:, 195:260], qf, sinb_sb[pb:pb + 64, hc1 * NDST:(hc1 + 1) * NDST],
                        start=True, stop=True), sem="t")
                    tmark_num = w.mark("t")
                    nsb = num_sb[:, par * NDST:(par + 1) * NDST]
                    w.wait(SC, "t", tmark_num)
                    w.op(SC, lambda e: e.copy(nsb, bNum[par]), sem="s")
                    smark2[h] = w.mark("s")
                    w.wait(VE, "s", smark2[h])
                    for (off, esb, hcx) in [(0, expll_sb, hc0), (65, expll_sb, hc1),
                                            (130, explg_sb, hc0), (195, explg_sb, hc1)]:
                        w.op(VE, lambda e, off=off, esb=esb, hcx=hcx: e.scalar_tensor_tensor(
                            nsb, bInt[par][:, off:off + 65], esb[:, hcx:hcx + 1],
                            nsb, ALU.mult, ALU.add), sem="v")
                    w.op(VE, lambda e: e.tensor_single_scalar(
                        rden_sb[:, par * 2:par * 2 + 1], nsb[:, Dh:Dh + 1], EPS, ALU.add), sem="v")
                    w.op(VE, lambda e: e.reciprocal(
                        rden_sb[:, par * 2 + 1:par * 2 + 2], rden_sb[:, par * 2:par * 2 + 1]), sem="v")
                    w.op(VE, lambda e: e.tensor_single_scalar(
                        byt[:, h * Dh:(h + 1) * Dh], nsb[:, 0:Dh],
                        rden_sb[:, par * 2 + 1:par * 2 + 2], ALU.mult), sem="v")
                    vmark2[h] = w.mark("v")

                for step in range(H + 1):
                    if step < H:
                        b_stage1(step)
                    if step >= 1:
                        b_stage2(step - 1)
                for m in range(DC):
                    w.wait(TE, "v")
                    w.op(TE, lambda e, m=m: e.transpose(
                        pTp[m % 2], byt[:, m * 128:(m + 1) * 128], id_b_sb[:, :]), sem="t")
                    w.wait(VE, "t")
                    w.op(VE, lambda e, m=m, tc=tc: e.tensor_copy(
                        byf[:, m * TT + tc * CH:m * TT + tc * CH + CH], pTp[m % 2]), sem="v")
            w.dma(SY, stats_sb[0:1, 0:TT], sp_stats[0:1, t0:t0 + TT], "rlq")
            w.dma(SY, stats_sb[0:1, TT:2 * TT], sp_stats[1:2, t0:t0 + TT], "rlq")
            w.wait(TE, "rlq")
            guard(w)
            w.op(TE, lambda e: e.matmul(pB1[:, 0:TT], ones_f_sb[0:1, :],
                                        stats_sb[0:1, 0:TT], start=True, stop=True), sem=None)
            w.op(TE, lambda e: e.matmul(pB2[:, 0:TT], ones_f_sb[0:1, :],
                                        stats_sb[0:1, TT:2 * TT], start=True, stop=True), sem="t")
            mark_bcq = w.mark("t")
            w.wait(VE, "spa")
            for m in range(DC):
                pm = pMain[m % 2]
                ws = wring[:, (m % 2) * 1024:(m % 2) * 1024 + 1024]
                xc = cx[:, (m % 2) * TT:(m % 2) * TT + TT]
                xnf = ctmp[:, (m % 2) * TT:(m % 2) * TT + TT]
                w.wait(SY, "v")
                w.dma(SY, xc, x_fm[m * 128:(m + 1) * 128, t0:t0 + TT], "ldx%d" % (m % 2))
                w.wait(SY, "t")
                cache_load(w, UOF["wo"] + m, 1, ws, "ldw%d" % (m % 2))
                w.wait(TE, "ldw%d" % (m % 2))
                guard(w)
                for kc in range(DC):
                    w.op(TE, lambda e, kc=kc, pm=pm, ws=ws: e.matmul(
                        pm[:, 0:TT], ws[:, kc * 128:(kc + 1) * 128],
                        byf[:, kc * TT:(kc + 1) * TT],
                        start=(kc == 0), stop=(kc == DC - 1)), sem="t" if kc == DC - 1 else None)
                w.wait(VE, "ldx%d" % (m % 2))
                w.wait(VE, "t", mark_bcq)
                w.op(VE, lambda e, xc=xc, xnf=xnf: e.tensor_mul(xnf, xc, pB1[:, 0:TT]), sem="v")
                w.op(VE, lambda e, xnf=xnf: e.tensor_sub(xnf, xnf, pB2[:, 0:TT]), sem="v")
                w.wait(VE, "t")
                w.op(VE, lambda e, m=m, pm=pm, xnf=xnf: e.scalar_tensor_tensor(
                    ch_[:, m * TT:(m + 1) * TT], pm[:, 0:TT], vecs["bo"][:, m:m + 1],
                    xnf, ALU.add, ALU.add), sem="v")
            guard(w)
            for c in range(DC):
                hb = bsc2[:, 640:640 + TT]
                h2b = bsc2[:, 640 + TT:640 + 2 * TT]
                w.wait(SC, "v")
                if c >= 1:
                    w.wait(SC, "t")
                w.op(SC, lambda e, c=c, hb=hb: e.copy(hb, ch_[:, c * TT:(c + 1) * TT]), sem="s")
                w.op(SC, lambda e, c=c, h2b=h2b: e.activation(
                    h2b, ch_[:, c * TT:(c + 1) * TT], AF.Square), sem="s")
                w.wait(TE, "s")
                w.op(TE, lambda e, hb=hb, c=c: e.matmul(
                    pB1[0:1, 0:TT], onescol_b_sb[:, :], hb,
                    start=(c == 0), stop=(c == DC - 1)), sem=None)
                w.op(TE, lambda e, h2b=h2b, c=c: e.matmul(
                    pB2[0:1, 0:TT], onescol_b_sb[:, :], h2b,
                    start=(c == 0), stop=(c == DC - 1)), sem="t")
            mu = stats_sb[0:1, 0:TT]
            ex2 = stats_sb[0:1, TT:2 * TT]
            musq = stats_sb[0:1, 2 * TT:3 * TT]
            var = stats_sb[0:1, TT:2 * TT]
            sd = stats_sb[0:1, 2 * TT:3 * TT]
            rstd = stats_sb[0:1, TT:2 * TT]
            murstd = stats_sb[0:1, 2 * TT:3 * TT]
            w.wait(SC, "t")
            w.op(SC, lambda e: e.activation(mu, pB1[0:1, 0:TT], AF.Copy, scale=1.0 / D), sem="s")
            w.op(SC, lambda e: e.activation(ex2, pB2[0:1, 0:TT], AF.Copy, scale=1.0 / D), sem="s")
            w.wait(VE, "s")
            w.op(VE, lambda e: e.tensor_mul(musq, mu, mu), sem="v")
            w.op(VE, lambda e: e.tensor_sub(var, ex2, musq), sem="v")
            w.wait(SC, "v")
            w.op(SC, lambda e: e.activation(sd, var, AF.Sqrt, bias=eps_sb[0:1, 0:1]), sem="s")
            w.wait(VE, "s")
            w.op(VE, lambda e: e.reciprocal(rstd, sd), sem="v")
            w.op(VE, lambda e: e.tensor_mul(murstd, mu, rstd), sem="v")
            w.wait(TE, "v")
            w.op(TE, lambda e: e.matmul(pB1[:, 0:TT], ones_f_sb[0:1, :], rstd,
                                        start=True, stop=True), sem=None)
            w.op(TE, lambda e: e.matmul(pB2[:, 0:TT], ones_f_sb[0:1, :], murstd,
                                        start=True, stop=True), sem="t")
            mark_bc2 = w.mark("t")
            for c in range(DC):
                tmp = ctmp[:, 0:TT]
                tmp2 = ctmp[:, TT:2 * TT]
                w.wait(VE, "t", mark_bc2)
                w.wait(VE, "spa")
                w.op(VE, lambda e, c=c, tmp=tmp: e.tensor_mul(
                    tmp, ch_[:, c * TT:(c + 1) * TT], pB1[:, 0:TT]), sem="v")
                w.op(VE, lambda e, tmp=tmp, tmp2=tmp2: e.tensor_sub(tmp2, tmp, pB2[:, 0:TT]), sem="v")
                w.op(VE, lambda e, c=c, tmp2=tmp2: e.tensor_scalar(
                    tmp2, tmp2, vecs["ln_g"][:, c:c + 1], vecs["ln_b"][:, c:c + 1],
                    ALU.mult, ALU.add), sem="v")
                w.wait(SY, "v")
                w.dma(SY, sp_attn[c * 128:(c + 1) * 128, t0:t0 + TT], tmp2, "spa")

        def cache_load(w, unit, count, slot, sem):
            if count == 1:
                w.dma(SY, slot, wc[unit], sem)
            else:
                w.dma(SY, slot.rearrange("p (u f) -> p u f", u=count),
                      wc[unit:unit + count].rearrange("u p f -> p u f"), sem)

        def phase_c_half(w: Walk, hf):
            h0 = hf * HT
            njj = min(4, NTILES)
            npass = 2 if NTILES >= 4 else 1
            ring = Ring(w, depth=4)
            gring = Ring(w, depth=2)
            p4 = [pMain[0], pMain[1], ps_aux[:, :], ps_bcast[:, :]]
            for m in range(DC):
                ws = wring[:, (m % 2) * 1024:(m % 2) * 1024 + 1024]
                gring.wait_sy()
                cache_load(w, UOF["lin0_w"] + m, 1, ws, "ldw%d" % (m % 2))
                w.wait(TE, "ldw%d" % (m % 2))
                for jj in range(njj):
                    i = m * njj + jj
                    pm = p4[i % 4]
                    tl0 = h0 + jj * TT
                    ring.wait_te()
                    for kc in range(DC):
                        w.op(TE, lambda e, kc=kc, pm=pm, ws=ws, tl0=tl0: e.matmul(
                            pm[:, 0:TT], ws[:, kc * 128:(kc + 1) * 128],
                            biga[:, kc * T + tl0:kc * T + tl0 + TT],
                            start=(kc == 0), stop=(kc == DC - 1)), sem="t" if kc == DC - 1 else None)
                    w.wait(SC, "t")
                    w.op(SC, lambda e, m=m, jj=jj, pm=pm: e.activation(
                        bf0[:, m * HT + jj * TT:m * HT + (jj + 1) * TT], pm[:, 0:TT],
                        AF.Identity, bias=vecs["lin0_b"][:, m:m + 1]), sem="s")
                    ring.push()
                gring.push()
            ring = Ring(w, depth=2)
            gring = Ring(w, depth=2)
            pairs = [(pMain[0], pMain[1]), (ps_aux[:, :], ps_bcast[:, :])]
            for m in range(DC):
                ws1 = wring[:, (m % 2) * 1024:(m % 2) * 1024 + 1024]
                ws2 = wring[:, 2048 + (m % 2) * 1024:2048 + (m % 2) * 1024 + 1024]
                gring.wait_sy()
                cache_load(w, UOF["s1w1"] + m, 1, ws1, "ldw%d" % (m % 2))
                cache_load(w, UOF["s1w2"] + m, 1, ws2, "ldw%d" % (m % 2))
                w.wait(TE, "ldw%d" % (m % 2))
                for jj in range(njj):
                    i = m * njj + jj
                    pa, pb = pairs[i % 2]
                    tl0 = h0 + jj * TT
                    ring.wait_te()
                    for kc in range(DC):
                        w.op(TE, lambda e, kc=kc, ws1=ws1, jj=jj, pa=pa: e.matmul(
                            pa[:, 0:TT], ws1[:, kc * 128:(kc + 1) * 128],
                            bf0[:, kc * HT + jj * TT:kc * HT + (jj + 1) * TT],
                            start=(kc == 0), stop=(kc == DC - 1)), sem="t" if kc == DC - 1 else None)
                    for kc in range(DC):
                        w.op(TE, lambda e, kc=kc, ws2=ws2, jj=jj, pb=pb: e.matmul(
                            pb[:, 0:TT], ws2[:, kc * 128:(kc + 1) * 128],
                            bf0[:, kc * HT + jj * TT:kc * HT + (jj + 1) * TT],
                            start=(kc == 0), stop=(kc == DC - 1)), sem="t" if kc == DC - 1 else None)
                    gsl = bgg[:, (i % 2) * TT:(i % 2) * TT + TT]
                    w.wait(SC, "t")
                    w.op(SC, lambda e, m=m, gsl=gsl, pa=pa: e.activation(
                        gsl, pa[:, 0:TT], AF.Silu, bias=vecs["s1b1"][:, m:m + 1]), sem="s")
                    w.wait(VE, "s")
                    w.op(VE, lambda e, m=m, tl0=tl0, gsl=gsl, pb=pb: e.scalar_tensor_tensor(
                        biga[:, m * T + tl0:m * T + tl0 + TT],
                        pb[:, 0:TT], vecs["s1b2"][:, m:m + 1], gsl,
                        ALU.add, ALU.mult), sem="v")
                    ring.push()
                gring.push()
            for jj in range(njj):
                tl0 = h0 + jj * TT
                for tc in range(NCH):
                    tt0 = tl0 + tc * CH
                    guard(w)
                    for kc in range(DC):
                        w.op(TE, lambda e, kc=kc, tt0=tt0: e.matmul(
                            pGate, biga[:, kc * T + tt0:kc * T + tt0 + CH],
                            gatew_sb[:, kc * E:(kc + 1) * E],
                            start=(kc == 0), stop=(kc == DC - 1)), sem="t" if kc == DC - 1 else None)
                    sc_scores = csc[:, 0:8]
                    prob = csc[:, 8:16]
                    work = csc[:, 16:24]
                    sel = csc[:, 24:32]
                    msk = csc[:, 32:40]
                    rowm = csc[:, 40:41]
                    rows = csc[:, 41:42]
                    comb = csc[:, 48:56]
                    w.wait(VE, "t")
                    w.op(VE, lambda e: e.reduce_max(rowm, pGate, mybir.AxisListType.X), sem="v")
                    w.op(VE, lambda e: e.tensor_single_scalar(
                        sc_scores, pGate, rowm[:, 0:1], ALU.subtract), sem="v")
                    w.wait(SC, "v")
                    w.op(SC, lambda e: e.activation(prob, sc_scores, AF.Exp), sem="s")
                    w.wait(VE, "s")
                    w.op(VE, lambda e: e.reduce_sum(rows, prob, mybir.AxisListType.X), sem="v")
                    w.op(VE, lambda e: e.reciprocal(rowm[:, 0:1], rows[:, 0:1]), sem="v")
                    w.op(VE, lambda e: e.tensor_single_scalar(
                        prob, prob, rowm[:, 0:1], ALU.mult), sem="v")
                    w.op(VE, lambda e: e.tensor_copy(work, prob), sem="v")
                    w.op(VE, lambda e: e.memset(sel, 0.0), sem="v")
                    for it in range(4):
                        w.op(VE, lambda e: e.reduce_max(rowm, work, mybir.AxisListType.X), sem="v")
                        w.op(VE, lambda e: e.tensor_single_scalar(
                            msk, work, rowm[:, 0:1], ALU.is_ge), sem="v")
                        w.op(VE, lambda e: e.tensor_max(sel, sel, msk), sem="v")
                        w.op(VE, lambda e: e.scalar_tensor_tensor(
                            work, msk, -1e30, work, ALU.mult, ALU.add), sem="v")
                    w.op(VE, lambda e: e.tensor_mul(comb, prob, sel), sem="v")
                    w.op(VE, lambda e: e.reduce_sum(rows, comb, mybir.AxisListType.X), sem="v")
                    w.op(VE, lambda e: e.reciprocal(rowm[:, 0:1], rows[:, 0:1]), sem="v")
                    w.op(VE, lambda e: e.tensor_single_scalar(
                        comb, comb, rowm[:, 0:1], ALU.mult), sem="v")
                    w.wait(TE, "v")
                    w.op(TE, lambda e, comb=comb: e.transpose(
                        pTcomb[0:8, 0:128], comb, id_f_sb[:, :]), sem="t")
                    w.wait(VE, "t")
                    w.wait(VE, "bounce")
                    w.op(VE, lambda e, tc=tc: e.tensor_copy(
                        combf_sb[:, tc * CH:(tc + 1) * CH], pTcomb[0:8, 0:128]), sem="v")
                w.wait(GP, "v")
                w.dma(GP, sp_comb[:, tl0:tl0 + TT], combf_sb[:, :], "bounce")
            bc_sb = bigb[:, 8192:9216]
            cr_mark = [None]
            for pp in range(npass):
                p0 = h0 + pp * PT
                w.wait(VE, "t")
                w.op(VE, lambda e: e.memset(routed[:, :], 0.0), sem="v")
                w.wait(GP, "bounce")
                pairs = [(pMain[0], pMain[1]), (ps_aux[:, :], ps_bcast[:, :])]
                gring13 = Ring(w, depth=3)
                gring2 = Ring(w, depth=2)
                for ee in range(E):
                    if cr_mark[0] is None:
                        w.wait(GP, "t")
                    else:
                        w.wait(GP, "t", cr_mark[0])
                    w.dma(GP, combrow_sb[:, :], sp_comb[ee:ee + 1, p0:p0 + PT], "bounce")
                    w.wait(TE, "bounce")
                    w.wait(TE, "v")
                    w.wait(TE, "s")
                    for t2 in range(2):
                        w.op(TE, lambda e, t2=t2: e.matmul(
                            pMain[t2][:, 0:TT], ones_b_sb[0:1, :],
                            combrow_sb[0:1, t2 * TT:(t2 + 1) * TT],
                            start=True, stop=True), sem="t")
                    cr_mark[0] = w.mark("t")
                    w.wait(VE, "t")
                    w.op(VE, lambda e: e.tensor_copy(bc_sb[:, 0:TT], pMain[0][:, 0:TT]), sem="v")
                    w.op(VE, lambda e: e.tensor_copy(bc_sb[:, TT:2 * TT], pMain[1][:, 0:TT]), sem="v")
                    ring = Ring(w, depth=2)
                    for m in range(DC):
                        ws1 = bigb[:, 9216 + (m % 3) * 1024:9216 + (m % 3) * 1024 + 1024]
                        ws3 = bigb[:, 12288 + (m % 3) * 1024:12288 + (m % 3) * 1024 + 1024]
                        gring13.wait_sy()
                        cache_load(w, UOF["exw1"] + ee * 8 + m, 1, ws1, "ldw%d" % (m % 3))
                        cache_load(w, UOF["exw3"] + ee * 8 + m, 1, ws3, "ldw%d" % (m % 3))
                        w.wait(TE, "ldw%d" % (m % 3))
                        for t2 in range(2):
                            i = m * 2 + t2
                            pa, pb = pairs[i % 2]
                            tl0 = p0 + t2 * TT
                            ring.wait_te()
                            for kc in range(DC):
                                w.op(TE, lambda e, kc=kc, ws1=ws1, tl0=tl0, pa=pa: e.matmul(
                                    pa[:, 0:TT], ws1[:, kc * 128:(kc + 1) * 128],
                                    biga[:, kc * T + tl0:kc * T + tl0 + TT],
                                    start=(kc == 0), stop=(kc == DC - 1)), sem="t" if kc == DC - 1 else None)
                            for kc in range(DC):
                                w.op(TE, lambda e, kc=kc, ws3=ws3, tl0=tl0, pb=pb: e.matmul(
                                    pb[:, 0:TT], ws3[:, kc * 128:(kc + 1) * 128],
                                    biga[:, kc * T + tl0:kc * T + tl0 + TT],
                                    start=(kc == 0), stop=(kc == DC - 1)), sem="t" if kc == DC - 1 else None)
                            gsl = bgg[:, (i % 2) * TT:(i % 2) * TT + TT]
                            tb = btmp[:, (i % 2) * TT:(i % 2) * TT + TT]
                            w.wait(SC, "t")
                            w.op(SC, lambda e, gsl=gsl, pa=pa: e.activation(
                                gsl, pa[:, 0:TT], AF.Silu), sem="s")
                            w.wait(VE, "s")
                            w.op(VE, lambda e, tb=tb, gsl=gsl, pb=pb: e.tensor_mul(
                                tb, pb[:, 0:TT], gsl), sem="v")
                            w.op(VE, lambda e, m=m, t2=t2, tb=tb: e.tensor_mul(
                                bhh[:, m * PT + t2 * TT:m * PT + (t2 + 1) * TT],
                                tb, bc_sb[:, t2 * TT:(t2 + 1) * TT]), sem="v")
                            ring.push()
                        gring13.push()
                    ring = Ring(w, depth=4)
                    p4 = [pMain[0], pMain[1], ps_aux[:, :], ps_bcast[:, :]]
                    for mo in range(DC):
                        ws2 = wring[:, (mo % 2) * 1024:(mo % 2) * 1024 + 1024]
                        gring2.wait_sy()
                        cache_load(w, UOF["exw2"] + ee * 8 + mo, 1, ws2, "ldw%s" % "ab"[mo % 2])
                        w.wait(TE, "ldw%s" % "ab"[mo % 2])
                        for t2 in range(2):
                            i = mo * 2 + t2
                            pm = p4[i % 4]
                            ring.wait_te()
                            for kc in range(DC):
                                w.op(TE, lambda e, kc=kc, ws2=ws2, t2=t2, pm=pm: e.matmul(
                                    pm[:, 0:TT], ws2[:, kc * 128:(kc + 1) * 128],
                                    bhh[:, kc * PT + t2 * TT:kc * PT + (t2 + 1) * TT],
                                    start=(kc == 0), stop=(kc == DC - 1)), sem="t" if kc == DC - 1 else None)
                            w.wait(VE, "t")
                            w.op(VE, lambda e, mo=mo, t2=t2, pm=pm: e.tensor_add(
                                routed[:, mo * PT + t2 * TT:mo * PT + (t2 + 1) * TT],
                                routed[:, mo * PT + t2 * TT:mo * PT + (t2 + 1) * TT],
                                pm[:, 0:TT]), sem="v")
                            ring.push()
                        gring2.push()
                phase_c4_pass(w, p0)

        def phase_c4_pass(w: Walk, p0):
            for t2 in range(2):
                tl0 = p0 + t2 * TT
                ring = Ring(w, depth=2)
                gring = Ring(w, depth=2)
                pairs = [(pMain[0], pMain[1]), (ps_aux[:, :], ps_bcast[:, :])]
                for mh in range(16):
                    ws1 = wring[:, (mh % 2) * 1024:(mh % 2) * 1024 + 1024]
                    ws3 = wring[:, 2048 + (mh % 2) * 1024:2048 + (mh % 2) * 1024 + 1024]
                    gring.wait_sy()
                    cache_load(w, UOF["shw1"] + mh, 1, ws1, "ldw%d" % (mh % 2))
                    cache_load(w, UOF["shw3"] + mh, 1, ws3, "ldw%d" % (mh % 2))
                    w.wait(TE, "ldw%d" % (mh % 2))
                    pa, pb = pairs[mh % 2]
                    ring.wait_te()
                    for kc in range(DC):
                        w.op(TE, lambda e, kc=kc, ws1=ws1, tl0=tl0, pa=pa: e.matmul(
                            pa[:, 0:TT], ws1[:, kc * 128:(kc + 1) * 128],
                            biga[:, kc * T + tl0:kc * T + tl0 + TT],
                            start=(kc == 0), stop=(kc == DC - 1)), sem="t" if kc == DC - 1 else None)
                    for kc in range(DC):
                        w.op(TE, lambda e, kc=kc, ws3=ws3, tl0=tl0, pb=pb: e.matmul(
                            pb[:, 0:TT], ws3[:, kc * 128:(kc + 1) * 128],
                            biga[:, kc * T + tl0:kc * T + tl0 + TT],
                            start=(kc == 0), stop=(kc == DC - 1)), sem="t" if kc == DC - 1 else None)
                    gsl = bgg[:, (mh % 2) * TT:(mh % 2) * TT + TT]
                    w.wait(SC, "t")
                    w.op(SC, lambda e, gsl=gsl, pa=pa: e.activation(gsl, pa[:, 0:TT], AF.Silu), sem="s")
                    w.wait(VE, "s")
                    w.op(VE, lambda e, mh=mh, gsl=gsl, pb=pb: e.tensor_mul(
                        bshh[:, mh * TT:(mh + 1) * TT], pb[:, 0:TT], gsl), sem="v")
                    ring.push()
                    gring.push()
                ring = Ring(w, depth=4)
                gring = Ring(w, depth=2)
                p4 = [pMain[0], pMain[1], ps_aux[:, :], ps_bcast[:, :]]
                for mo in range(DC):
                    ws2 = wring[:, (mo % 2) * 2048:(mo % 2) * 2048 + 2048]
                    gring.wait_sy()
                    cache_load(w, UOF["shw2"] + 2 * mo, 2, ws2, "ldw%d" % (mo % 2))
                    w.wait(TE, "ldw%d" % (mo % 2))
                    pm = p4[mo % 4]
                    ring.wait_te()
                    for kc in range(16):
                        w.op(TE, lambda e, kc=kc, ws2=ws2, pm=pm: e.matmul(
                            pm[:, 0:TT], ws2[:, kc * 128:(kc + 1) * 128],
                            bshh[:, kc * TT:(kc + 1) * TT],
                            start=(kc == 0), stop=(kc == 15)), sem="t" if kc == 15 else None)
                    w.wait(VE, "t")
                    w.op(VE, lambda e, mo=mo, t2=t2, pm=pm: e.tensor_add(
                        bm1[:, t2 * 4096 + mo * TT:t2 * 4096 + (mo + 1) * TT], pm[:, 0:TT],
                        routed[:, mo * PT + t2 * TT:mo * PT + (t2 + 1) * TT]), sem="v")
                    ring.push()
                    gring.push()
            for t2 in range(2):
                tl0 = p0 + t2 * TT
                ring = Ring(w, depth=4)
                gring = Ring(w, depth=2)
                p4 = [pMain[0], pMain[1], ps_aux[:, :], ps_bcast[:, :]]
                for m in range(DC):
                    ws = wring[:, (m % 2) * 1024:(m % 2) * 1024 + 1024]
                    gring.wait_sy()
                    cache_load(w, UOF["lin1_w"] + m, 1, ws, "ldw%d" % (m % 2))
                    w.wait(TE, "ldw%d" % (m % 2))
                    pm = p4[m % 4]
                    ring.wait_te()
                    for kc in range(DC):
                        w.op(TE, lambda e, kc=kc, ws=ws, t2=t2, pm=pm: e.matmul(
                            pm[:, 0:TT], ws[:, kc * 128:(kc + 1) * 128],
                            bm1[:, t2 * 4096 + kc * TT:t2 * 4096 + (kc + 1) * TT],
                            start=(kc == 0), stop=(kc == DC - 1)), sem="t" if kc == DC - 1 else None)
                    w.wait(SC, "t")
                    w.op(SC, lambda e, m=m, pm=pm: e.activation(
                        bm2[:, m * TT:(m + 1) * TT], pm[:, 0:TT], AF.Identity,
                        bias=vecs["lin1_b"][:, m:m + 1]), sem="s")
                    ring.push()
                    gring.push()
                ring = Ring(w, depth=2)
                gring = Ring(w, depth=2)
                pairs = [(pMain[0], pMain[1]), (ps_aux[:, :], ps_bcast[:, :])]
                for m in range(DC):
                    ws1 = wring[:, (m % 2) * 1024:(m % 2) * 1024 + 1024]
                    ws2 = wring[:, 2048 + (m % 2) * 1024:2048 + (m % 2) * 1024 + 1024]
                    gring.wait_sy()
                    cache_load(w, UOF["s2w1"] + m, 1, ws1, "ldw%d" % (m % 2))
                    cache_load(w, UOF["s2w2"] + m, 1, ws2, "ldw%d" % (m % 2))
                    w.wait(TE, "ldw%d" % (m % 2))
                    pa, pb = pairs[m % 2]
                    ring.wait_te()
                    for kc in range(DC):
                        w.op(TE, lambda e, kc=kc, ws1=ws1, pa=pa: e.matmul(
                            pa[:, 0:TT], ws1[:, kc * 128:(kc + 1) * 128],
                            bm2[:, kc * TT:(kc + 1) * TT],
                            start=(kc == 0), stop=(kc == DC - 1)), sem="t" if kc == DC - 1 else None)
                    for kc in range(DC):
                        w.op(TE, lambda e, kc=kc, ws2=ws2, pb=pb: e.matmul(
                            pb[:, 0:TT], ws2[:, kc * 128:(kc + 1) * 128],
                            bm2[:, kc * TT:(kc + 1) * TT],
                            start=(kc == 0), stop=(kc == DC - 1)), sem="t" if kc == DC - 1 else None)
                    gsl = bgg[:, (m % 2) * TT:(m % 2) * TT + TT]
                    w.wait(SC, "t")
                    w.op(SC, lambda e, m=m, gsl=gsl, pa=pa: e.activation(
                        gsl, pa[:, 0:TT], AF.Silu, bias=vecs["s2b1"][:, m:m + 1]), sem="s")
                    w.wait(VE, "s")
                    if m == 0:
                        w.wait(VE, "outd")
                    w.op(VE, lambda e, m=m, gsl=gsl, pb=pb: e.scalar_tensor_tensor(
                        cffn[:, m * TT:(m + 1) * TT], pb[:, 0:TT],
                        vecs["s2b2"][:, m:m + 1], gsl, ALU.add, ALU.mult), sem="v")
                    ring.push()
                    gring.push()
                    xa = cxa[:, 0:TT]
                    at = ctmp2[:, 0:TT]
                    w.wait(SY, "v")
                    w.dma(SY, xa, x_fm[m * 128:(m + 1) * 128, tl0:tl0 + TT], "ldx0")
                    w.dma(SY, at, sp_attn[m * 128:(m + 1) * 128, tl0:tl0 + TT], "ldx0")
                    w.wait(VE, "ldx0")
                    w.op(VE, lambda e, m=m, xa=xa: e.tensor_add(
                        cffn[:, m * TT:(m + 1) * TT], cffn[:, m * TT:(m + 1) * TT], xa), sem="v")
                    w.op(VE, lambda e, m=m, at=at: e.tensor_add(
                        cffn[:, m * TT:(m + 1) * TT], cffn[:, m * TT:(m + 1) * TT], at), sem="v")
                    w.wait(SY, "v")
                    w.dma(SY, out_fm[m * 128:(m + 1) * 128, tl0:tl0 + TT],
                          cffn[:, m * TT:(m + 1) * TT], "outd")

        @block.sync
        def _(e):
            body(Walk("sync", e, sems))

        @block.tensor
        def _(e):
            body(Walk("tensor", e, sems))

        @block.vector
        def _(e):
            body(Walk("vector", e, sems))

        @block.scalar
        def _(e):
            body(Walk("scalar", e, sems))

        @block.gpsimd
        def _(e):
            body(Walk("gpsimd", e, sems))

    return nc


def _rope_tables(core):
    inv = 1.0 / 10000.0 ** (np.arange(0, Dh, 2, dtype=np.float64) / Dh)
    t0 = (core % 4) * T
    t = np.arange(t0, t0 + T, dtype=np.float64)
    fr = t[:, None] * inv[None, :]
    c = np.cos(fr).astype(np.float32).T
    s = np.sin(fr).astype(np.float32).T
    cs1 = np.zeros((128, T), np.float32)
    cs2 = np.zeros((128, T), np.float32)
    for blk in range(2):
        b0 = blk * 64
        cs1[b0:b0 + 32] = c
        cs1[b0 + 32:b0 + 64] = c
        cs2[b0:b0 + 32] = s
        cs2[b0 + 32:b0 + 64] = -s
    return cs1, cs2


_nc_cache = {}


def kernel(**inputs):
    f32 = lambda k: np.ascontiguousarray(np.asarray(inputs[k], np.float32))
    x = f32("L0_attn_x")

    tri = np.triu(np.ones((CH, CH), np.float32))
    ident = np.eye(CH, dtype=np.float32)
    swap = np.zeros((CH, CH), np.float32)
    for mcol in range(CH):
        blk, i = mcol // 64, mcol % 64
        swap[blk * 64 + (i + 32) % 64, mcol] = 1.0
    ones_row = np.ones((1, CH), np.float32)
    ones_col = np.ones((CH, 1), np.float32)

    vec = lambda k: np.ascontiguousarray(f32(k).reshape(DC, 128).T)
    gate_wt = np.ascontiguousarray(
        f32("L0_ffn_gate_w").reshape(E, DC, 128).transpose(2, 1, 0).reshape(128, DC * E))

    common = dict(
        wq=f32("L0_attn_Wq"), wk=f32("L0_attn_Wk"), wv=f32("L0_attn_Wv"),
        wbeta=f32("L0_attn_Wbeta"), wo=f32("L0_attn_Wo"),
        bbeta=f32("L0_attn_bbeta").reshape(1, HC), bo=vec("L0_attn_bo"),
        ln_g=vec("L0_attn_ln_gamma"), ln_b=vec("L0_attn_ln_beta"),
        lin0_w=f32("L0_ffn_lin0_w"), lin0_b=vec("L0_ffn_lin0_b"),
        s1w1=f32("L0_ffn_swi1_w1"), s1b1=vec("L0_ffn_swi1_b1"),
        s1w2=f32("L0_ffn_swi1_w2"), s1b2=vec("L0_ffn_swi1_b2"),
        gate_wt=gate_wt,
        exw1=f32("L0_ffn_exp_w1"), exw3=f32("L0_ffn_exp_w3"), exw2=f32("L0_ffn_exp_w2"),
        shw1=f32("L0_ffn_sh_w1"), shw3=f32("L0_ffn_sh_w3"), shw2=f32("L0_ffn_sh_w2"),
        lin1_w=f32("L0_ffn_lin1_w"), lin1_b=vec("L0_ffn_lin1_b"),
        s2w1=f32("L0_ffn_swi2_w1"), s2b1=vec("L0_ffn_swi2_b1"),
        s2w2=f32("L0_ffn_swi2_w2"), s2b2=vec("L0_ffn_swi2_b2"),
        tri_f=tri, tri_b=tri.astype(_bf), id_b=ident.astype(_bf), id_f=ident,
        swap_b=swap.astype(_bf), ones_f=ones_row, ones_b=ones_row.astype(_bf),
        onescol_f=ones_col, onescol_b=ones_col.astype(_bf),
        negones_f=-ones_row,
    )

    in_maps = []
    for core in range(NCORE):
        b = core // 4
        r = core % 4
        xs = np.ascontiguousarray(x[b, r * T:(r + 1) * T, :].T)
        cs1v, cs2v = _rope_tables(core)
        gm = np.zeros((1, NCORE), np.float32)
        for i in range(NCORE):
            gm[0, i] = 1.0 if (i // 4 == b and i % 4 < r) else 0.0
        m = dict(common)
        m.update(x_fm=xs, cs1=cs1v.astype(_bf), cs2=cs2v.astype(_bf), gmask=gm)
        in_maps.append(m)

    if "prog" not in _nc_cache:
        _nc_cache["prog"] = build_program()
    nc = _nc_cache["prog"]

    import os as _os
    res = run_bass_kernel_spmd(nc, in_maps, core_ids=list(range(NCORE)),
                               trace=bool(_os.environ.get("KTRACE")))
    global LAST_RESULTS
    LAST_RESULTS = res

    out = np.empty((B, S, D), np.float32)
    for core in range(NCORE):
        b = core // 4
        r = core % 4
        out[b, r * T:(r + 1) * T, :] = np.asarray(res.results[core]["out_fm"]).T
    return out
